# version 02
# baseline (speedup 1.0000x reference)
import numpy as np
import ml_dtypes
import concourse.bass as bass
import concourse.mybir as mybir
from concourse.bass_utils import run_bass_kernel_spmd
from contextlib import ExitStack

F32 = mybir.dt.float32
BF16 = mybir.dt.bfloat16
AF = mybir.ActivationFunctionType
ALU = mybir.AluOpType
BF = ml_dtypes.bfloat16

B, L, D = 2, 16384, 1024
DI, DS, DCONV, DTR = 2048, 16, 4, 64
NE, DFF = 8, 512
NC = 8
TC = (B * L) // NC
DDI = DI // NC
EPS = 1e-5

ENGINES = ("sp", "act", "dve", "pool", "pe")


class Op:
    __slots__ = ("eng", "fn", "deps", "dma", "idx", "inc_val", "needed", "key",
                 "seq", "wait_val")

    def __init__(self, eng, fn, deps, dma):
        self.eng, self.fn, self.deps, self.dma = eng, fn, deps, dma
        self.idx = -1
        self.inc_val = None
        self.needed = False


class Buf:
    __slots__ = ("name", "last_writer", "readers", "exclusive")

    def __init__(self, name="", exclusive=False):
        self.name = name
        self.last_writer = None
        self.readers = []
        self.exclusive = exclusive


class Sched:

    def __init__(self, nc):
        self.nc = nc
        self.streams = {e: [] for e in ENGINES}
        self._seq = 0

    def op(self, eng, fn, reads=(), writes=(), deps=(), dma=False, key=None):
        reads = list(reads)
        writes = list(writes)
        excl = [b for b in reads if b.exclusive]
        if excl:
            reads = [b for b in reads if not b.exclusive]
            writes = writes + excl
        dd = list(deps)
        for b in reads:
            if b.last_writer is not None:
                dd.append(b.last_writer)
        for b in writes:
            if b.last_writer is not None:
                dd.append(b.last_writer)
            dd.extend(b.readers)
        o = Op(eng, fn, dd, dma)
        o.key = key
        o.idx = len(self.streams[eng])
        o.seq = self._seq
        self._seq += 1
        self.streams[eng].append(o)
        for b in reads:
            b.readers.append(o)
        for b in writes:
            b.last_writer = o
            b.readers = []
        return o

    def dma(self, eng, fn, reads=(), writes=(), deps=(), key=None):
        return self.op(eng, fn, reads=reads, writes=writes, deps=deps, dma=True,
                       key=key)

    def barrier(self):
        lasts = [s[-1] for s in self.streams.values() if s]
        nops = [self.op(e, lambda eng: eng.nop(), deps=lasts) for e in ENGINES]
        for e in ENGINES:
            self.op(e, lambda eng: eng.nop(), deps=nops)

    def _resolve(self):
        for e in ENGINES:
            for o in self.streams[e]:
                for d in o.deps:
                    d.needed = True
        self._dma_hist = {}
        for e in ENGINES:
            c_comp = 0
            c_dma = {}
            for o in self.streams[e]:
                if o.dma:
                    kk = o.key or (e + "_dma")
                    c_dma[kk] = c_dma.get(kk, 0) + 16
                    o.inc_val = c_dma[kk]
                    o.needed = True
                    self._dma_hist.setdefault(kk, []).append((o.seq, o.inc_val))
                elif o.needed:
                    c_comp += 1
                    o.inc_val = c_comp
                else:
                    o.inc_val = None

    def emit(self, block, sems, final_eng="dve"):
        self._resolve()
        sched = self
        finals = {}
        for e in ENGINES:
            for o in self.streams[e]:
                if o.dma:
                    kk = o.key or (e + "_dma")
                    finals[kk] = max(finals.get(kk, 0), o.inc_val)
                elif o.inc_val is not None:
                    finals[e] = max(finals.get(e, 0), o.inc_val)

        def mk_runner(ename):
            def run(eng):
                water = {}
                prev_completed = 0
                for o in sched.streams[ename]:
                    want = {}
                    for dp in o.deps:
                        if dp.inc_val is None:
                            continue
                        if dp.eng == o.eng and not dp.dma:
                            if dp.inc_val <= prev_completed:
                                continue
                        if dp.dma:
                            key = dp.key or dp.eng + "_dma"
                            v = 0
                            for sq, cum in sched._dma_hist[key]:
                                if sq < o.seq:
                                    v = cum
                                else:
                                    break
                            if v > want.get(key, 0):
                                want[key] = v
                            continue
                        key = dp.eng
                        if dp.inc_val > want.get(key, 0):
                            want[key] = dp.inc_val
                    for key in sorted(want):
                        v = want[key]
                        if key == ename:
                            if v > prev_completed:
                                eng.wait_ge(sems[key], v)
                                prev_completed = v
                            continue
                        if water.get(key, 0) >= v:
                            continue
                        eng.wait_ge(sems[key], v)
                        water[key] = v
                    inst = o.fn(eng)
                    if o.inc_val is not None:
                        key = (o.key or o.eng + "_dma") if o.dma else o.eng
                        inst.then_inc(sems[key], 16 if o.dma else 1)
                if ename == final_eng:
                    for key in sorted(finals):
                        if key in ENGINES:
                            continue
                        if finals[key] > water.get(key, 0) and finals[key] > 0:
                            eng.wait_ge(sems[key], finals[key])
            return run

        block.sync(mk_runner("sp"))
        block.scalar(mk_runner("act"))
        block.vector(mk_runner("dve"))
        block.gpsimd(mk_runner("pool"))
        block.tensor(mk_runner("pe"))


def make_sems(nc, ctx, extra=("wload",)):
    sems = {}
    for e in ENGINES:
        sems[e] = ctx.enter_context(nc.semaphore(f"s_{e}"))
        sems[e + "_dma"] = ctx.enter_context(nc.semaphore(f"s_{e}_dma"))
    for k in extra:
        sems[k] = ctx.enter_context(nc.semaphore(f"s_{k}"))
    return sems


def sbuf(nc, ctx, name, shape, dtype):
    h = ctx.enter_context(nc.sbuf_tensor(name, list(shape), dtype))
    return h, Buf(name)


def psum(nc, ctx, name, shape, dtype=F32):
    h = ctx.enter_context(nc.psum_tensor(name, list(shape), dtype))
    return h, Buf(name, exclusive=True)


def ln_rowmath(S, pm_h, pm_b, psq_h, psq_b, negm_h, negm_b, rstd_h, rstd_b,
               tmp_h, tmp_b, col0, w, nfeat):
    sl = slice(col0, col0 + w)
    S.op("act", lambda e: e.activation(out=negm_h[0:1, sl], in_=pm_h[0:1, 0:w],
                                       func=AF.Copy, scale=-1.0 / nfeat),
         reads=[pm_b], writes=[negm_b])
    S.op("act", lambda e: e.activation(out=tmp_h[0:1, sl], in_=pm_h[0:1, 0:w],
                                       func=AF.Square, scale=1.0 / nfeat),
         reads=[pm_b], writes=[tmp_b])
    S.op("act", lambda e: e.activation(out=rstd_h[0:1, sl], in_=psq_h[0:1, 0:w],
                                       func=AF.Copy, scale=1.0 / nfeat),
         reads=[psq_b], writes=[rstd_b])
    S.op("dve", lambda e: e.tensor_tensor(out=rstd_h[0:1, sl], in0=rstd_h[0:1, sl],
                                          in1=tmp_h[0:1, sl], op=ALU.subtract),
         reads=[tmp_b], writes=[rstd_b])
    S.op("dve", lambda e: e.tensor_scalar(out=rstd_h[0:1, sl], in0=rstd_h[0:1, sl],
                                          scalar1=EPS, scalar2=None, op0=ALU.add),
         writes=[rstd_b])
    S.op("act", lambda e: e.activation(out=rstd_h[0:1, sl], in_=rstd_h[0:1, sl],
                                       func=AF.Ln),
         writes=[rstd_b])
    S.op("act", lambda e: e.activation(out=rstd_h[0:1, sl], in_=rstd_h[0:1, sl],
                                       func=AF.Exp, scale=-0.5),
         writes=[rstd_b])


def build_l1(debug=False, p2_nchunks=None):
    nc = bass.Bass("TRN2", target_bir_lowering=False)
    x_d = nc.dram_tensor("x", [D, TC + 4], F32, kind="ExternalInput")
    dbg_d = nc.dram_tensor("dbg", [128, 16 * 8], F32, kind="ExternalOutput") if debug else None
    dbg2_d = nc.dram_tensor("dbg2", [128, 96], F32, kind="ExternalOutput") if debug else None
    dbg3_d = nc.dram_tensor("dbg3", [128, 8], F32, kind="ExternalOutput") if debug else None
    win_d = nc.dram_tensor("w_in", [D, 2 * DI], F32, kind="ExternalInput")
    wxp_d = nc.dram_tensor("w_xp", [DI, 96], F32, kind="ExternalInput")
    cw_d = nc.dram_tensor("conv_w", [128, (DI // 128) * DCONV], F32, kind="ExternalInput")
    cb_d = nc.dram_tensor("conv_b", [128, DI // 128], F32, kind="ExternalInput")
    ones_d = nc.dram_tensor("ones", [128, 128], BF16, kind="ExternalInput")
    u_d = nc.dram_tensor("u", [DI, TC], BF16, kind="ExternalOutput")
    zs_d = nc.dram_tensor("zs", [DI, TC], BF16, kind="ExternalOutput")
    xdbl_d = nc.dram_tensor("xdbl", [96, TC], F32, kind="ExternalOutput")

    KT = D // 128
    MT = 2 * DI // 128
    NCH = TC // 512
    W = 512
    chunks = [(0, 4)] + [(4 + W * c, W) for c in range(NCH)]
    x_ap = x_d.ap()

    with ExitStack() as ctx:
        sems = make_sems(nc, ctx)
        S = Sched(nc)
        win_h, win_b = sbuf(nc, ctx, "win_bf", [128, KT * 2 * DI], BF16)
        wxp_h, wxp_b = sbuf(nc, ctx, "wxp_bf", [128, (DI // 128) * 96], BF16)
        cw_h, cw_b = sbuf(nc, ctx, "cw", [128, (DI // 128) * DCONV], F32)
        cb_h, cb_b = sbuf(nc, ctx, "cb", [128, DI // 128], F32)
        ones_h, ones_b = sbuf(nc, ctx, "ones_sb", [128, 128], BF16)
        negmb_h, negmb_b = sbuf(nc, ctx, "negm_bf", [1, TC + 4], BF16)
        rstdb_h, rstdb_b = sbuf(nc, ctx, "rstd_bf", [1, TC + 4], BF16)
        nm_s, nm_sb = sbuf(nc, ctx, "nm_s", [1, W], F32)
        rs_s, rs_sb = sbuf(nc, ctx, "rs_s", [1, W], F32)
        rt_s, rt_sb = sbuf(nc, ctx, "rt_s", [1, W], F32)
        xbf_h, xbf_b = sbuf(nc, ctx, "xbf", [128, KT * W], BF16)
        sqf_h, sqf_b = sbuf(nc, ctx, "sqf", [128, KT * W], BF16)
        xn_h, xn_b = sbuf(nc, ctx, "xn", [128, KT * W], BF16)
        tfa_h, tfa_b = sbuf(nc, ctx, "tfa", [128, W], F32)
        xin = [sbuf(nc, ctx, f"xin{i}", [128, 4 + W], F32) for i in range(DI // 128)]
        u_h, u_b = sbuf(nc, ctx, "usb", [128, (DI // 128) * W], BF16)
        ur_h, ur_b = sbuf(nc, ctx, "uraw", [128, W], F32)
        zs_h, zs_b = sbuf(nc, ctx, "zsb", [128, (DI // 128) * W], BF16)
        xdb_h, xdb_b = sbuf(nc, ctx, "xdbuf", [96, W], F32)
        pmean_h, pmean_b = psum(nc, ctx, "pmean", [1, W])
        psq_h, psq_b = psum(nc, ctx, "psq", [1, W])
        pnm_h, pnm_b = psum(nc, ctx, "pnm", [128, W])
        prs_h, prs_b = psum(nc, ctx, "prs", [128, W])
        pxz = [psum(nc, ctx, f"pxz{i}", [128, W]) for i in range(2)]
        pxd_h, pxd_b = psum(nc, ctx, "pxd", [96, W])

        S.dma("pool", lambda e: e.dma_start(
            out=win_h[:, :].rearrange("p (a m) -> p a m", a=KT),
            in_=win_d.ap().rearrange("(a p) m -> p a m", p=128)),
            writes=[win_b], key="wload")
        S.dma("pool", lambda e: e.dma_start(
            out=wxp_h[:, :].rearrange("p (a m) -> p a m", a=DI // 128),
            in_=wxp_d.ap().rearrange("(a p) m -> p a m", p=128)),
            writes=[wxp_b], key="wload")
        S.dma("sp", lambda e: e.dma_start(out=cw_h[:, :], in_=cw_d.ap()),
              writes=[cw_b])
        S.dma("sp", lambda e: e.dma_start(out=cb_h[:, :], in_=cb_d.ap()),
              writes=[cb_b])
        S.dma("sp", lambda e: e.dma_start(out=ones_h[:, :], in_=ones_d.ap()),
              writes=[ones_b])

        for col0, w in chunks:
            S.dma("pool", lambda e, col0=col0, w=w: e.dma_start(
                out=xbf_h[:, 0:KT * w].rearrange("p (a t) -> p a t", a=KT),
                in_=x_ap[:, col0:col0 + w].rearrange("(a p) t -> p a t", p=128)),
                writes=[xbf_b])
            S.op("act", lambda e, w=w: e.activation(
                out=sqf_h[:, 0:KT * w], in_=xbf_h[:, 0:KT * w], func=AF.Square),
                reads=[xbf_b], writes=[sqf_b])
            for k in range(KT):
                S.op("pe", lambda e, k=k, w=w: e.matmul(
                    pmean_h[0:1, 0:w], ones_h[:, 0:1], xbf_h[:, k * w:(k + 1) * w],
                    start=(k == 0), stop=(k == KT - 1)),
                    reads=[xbf_b, ones_b], writes=[pmean_b])
                S.op("pe", lambda e, k=k, w=w: e.matmul(
                    psq_h[0:1, 0:w], ones_h[:, 0:1], sqf_h[:, k * w:(k + 1) * w],
                    start=(k == 0), stop=(k == KT - 1)),
                    reads=[sqf_b, ones_b], writes=[psq_b])
            ln_rowmath(S, pmean_h, pmean_b, psq_h, psq_b, nm_s, nm_sb,
                       rs_s, rs_sb, rt_s, rt_sb, 0, w, D)
            S.op("act", lambda e, col0=col0, w=w: e.activation(
                out=negmb_h[0:1, col0:col0 + w], in_=nm_s[0:1, 0:w], func=AF.Copy),
                reads=[nm_sb], writes=[negmb_b])
            S.op("act", lambda e, col0=col0, w=w: e.activation(
                out=rstdb_h[0:1, col0:col0 + w], in_=rs_s[0:1, 0:w], func=AF.Copy),
                reads=[rs_sb], writes=[rstdb_b])

        p2chunks = chunks if p2_nchunks is None else chunks[:p2_nchunks]
        for col0, w in p2chunks:
            mini = (w == 4)
            S.dma("pool", lambda e, col0=col0, w=w: e.dma_start(
                out=xbf_h[:, 0:KT * w].rearrange("p (a t) -> p a t", a=KT),
                in_=x_ap[:, col0:col0 + w].rearrange("(a p) t -> p a t", p=128)),
                writes=[xbf_b])
            S.op("pe", lambda e, col0=col0, w=w: e.matmul(
                pnm_h[:, 0:w], ones_h[0:1, :], negmb_h[0:1, col0:col0 + w],
                start=True, stop=True), reads=[negmb_b, ones_b], writes=[pnm_b])
            S.op("pe", lambda e, col0=col0, w=w: e.matmul(
                prs_h[:, 0:w], ones_h[0:1, :], rstdb_h[0:1, col0:col0 + w],
                start=True, stop=True), reads=[rstdb_b, ones_b], writes=[prs_b])
            for k in range(KT):
                S.op("dve", lambda e, k=k, w=w: e.tensor_tensor(
                    out=tfa_h[:, 0:w], in0=xbf_h[:, k * w:(k + 1) * w],
                    in1=pnm_h[:, 0:w], op=ALU.add),
                    reads=[xbf_b, pnm_b], writes=[tfa_b])
                S.op("dve", lambda e, k=k, w=w: e.tensor_tensor(
                    out=xn_h[:, k * w:(k + 1) * w], in0=tfa_h[:, 0:w],
                    in1=prs_h[:, 0:w], op=ALU.mult),
                    reads=[tfa_b, prs_b], writes=[xn_b])
            if debug and mini:
                S.dma("pool", lambda e: e.dma_start(out=dbg2_d.ap()[:, 0:32],
                                                  in_=xbf_h[:, 0:32]), reads=[xbf_b])
                S.dma("pool", lambda e: e.dma_start(out=dbg2_d.ap()[:, 32:64],
                                                  in_=xn_h[:, 0:32]), reads=[xn_b])
                S.dma("pool", lambda e: e.dma_start(out=dbg2_d.ap()[0:1, 64:68],
                                                  in_=negmb_h[0:1, 0:4]), reads=[negmb_b])
                S.dma("pool", lambda e: e.dma_start(out=dbg2_d.ap()[0:1, 68:72],
                                                  in_=rstdb_h[0:1, 0:4]), reads=[rstdb_b])
            mt_range = range(16) if mini else range(MT)
            _dbg_mini = debug and mini
            for mi in mt_range:
                ph, pb = pxz[mi % 2]
                for k in range(KT):
                    S.op("pe", lambda e, mi=mi, k=k, ph=ph, w=w: e.matmul(
                        ph[:, 0:w],
                        win_h[:, k * 2 * DI + mi * 128:k * 2 * DI + mi * 128 + 128],
                        xn_h[:, k * w:(k + 1) * w],
                        start=(k == 0), stop=(k == KT - 1)),
                        reads=[xn_b, win_b], writes=[pb])
                if mi < 16:
                    xih, xib = xin[mi]
                    if mini:
                        if debug and mi == 0:
                            S.op("dve", lambda e, ph=ph: e.tensor_copy(
                                out=tfa_h[:, 0:4], in_=ph[:, 0:4]),
                                reads=[pb], writes=[tfa_b])
                            S.dma("sp", lambda e: e.dma_start(
                                out=dbg3_d.ap()[:, 0:4], in_=tfa_h[:, 0:4]),
                                reads=[tfa_b])
                        S.op("act", lambda e, ph=ph, xih=xih: e.activation(
                            out=xih[:, 0:4], in_=ph[:, 0:4], func=AF.Copy),
                            reads=[pb], writes=[xib])
                    else:
                        S.op("act", lambda e, ph=ph, xih=xih, w=w: e.activation(
                            out=xih[:, 4:4 + w], in_=ph[:, 0:w], func=AF.Copy),
                            reads=[pb], writes=[xib])
                        if debug and col0 == 4:
                            S.dma("sp", lambda e, xih=xih, mi=mi: e.dma_start(
                                out=dbg_d.ap()[:, mi * 8:(mi + 1) * 8],
                                in_=xih[:, 0:8]), reads=[xib])
                        S.op("dve", lambda e, xih=xih, mi=mi, w=w: e.tensor_scalar(
                            out=ur_h[:, 0:w], in0=xih[:, 1:1 + w],
                            scalar1=cw_h[:, mi * DCONV:mi * DCONV + 1],
                            scalar2=cb_h[:, mi:mi + 1], op0=ALU.mult, op1=ALU.add),
                            reads=[xib, cw_b, cb_b], writes=[ur_b])
                        for j in range(1, DCONV):
                            S.op("dve", lambda e, xih=xih, mi=mi, j=j, w=w: e.scalar_tensor_tensor(
                                out=ur_h[:, 0:w], in0=xih[:, 1 + j:1 + j + w],
                                scalar=cw_h[:, mi * DCONV + j:mi * DCONV + j + 1],
                                in1=ur_h[:, 0:w], op0=ALU.mult, op1=ALU.add),
                                reads=[xib, cw_b], writes=[ur_b])
                        S.op("act", lambda e, mi=mi, w=w: e.activation(
                            out=u_h[:, mi * w:(mi + 1) * w], in_=ur_h[:, 0:w],
                            func=AF.Silu), reads=[ur_b], writes=[u_b])
                        S.op("dve", lambda e, xih=xih, w=w: e.tensor_copy(
                            out=xih[:, 0:4], in_=xih[:, w:w + 4]),
                            reads=[xib], writes=[xib])
                        S.op("pe", lambda e, mi=mi, w=w: e.matmul(
                            pxd_h[0:96, 0:w], wxp_h[:, mi * 96:(mi + 1) * 96],
                            u_h[:, mi * w:(mi + 1) * w],
                            start=(mi == 0), stop=(mi == 15)),
                            reads=[u_b, wxp_b], writes=[pxd_b])

                else:
                    S.op("act", lambda e, ph=ph, mi=mi, w=w: e.activation(
                        out=zs_h[:, (mi - 16) * w:(mi - 15) * w], in_=ph[:, 0:w],
                        func=AF.Silu), reads=[pb], writes=[zs_b])
            if _dbg_mini:
                xih0, xib0 = xin[0]
                S.dma("pool", lambda e, xih0=xih0: e.dma_start(
                    out=dbg2_d.ap()[:, 72:76], in_=xih0[:, 0:4]), reads=[xib0])
                xih1, xib1 = xin[1]
                S.dma("pool", lambda e, xih1=xih1: e.dma_start(
                    out=dbg2_d.ap()[:, 76:80], in_=xih1[:, 0:4]), reads=[xib1])
            if not mini:
                S.dma("sp", lambda e, col0=col0, w=w: e.dma_start(
                    out=u_d.ap()[:, col0 - 4:col0 - 4 + w].rearrange(
                        "(a p) t -> p a t", p=128),
                    in_=u_h[:, 0:16 * w].rearrange("p (a t) -> p a t", a=16)),
                    reads=[u_b])
                S.dma("act", lambda e, col0=col0, w=w: e.dma_start(
                    out=zs_d.ap()[:, col0 - 4:col0 - 4 + w].rearrange(
                        "(a p) t -> p a t", p=128),
                    in_=zs_h[:, 0:16 * w].rearrange("p (a t) -> p a t", a=16)),
                    reads=[zs_b])
                S.op("act", lambda e, w=w: e.activation(
                    out=xdb_h[0:96, 0:w], in_=pxd_h[0:96, 0:w], func=AF.Copy),
                    reads=[pxd_b], writes=[xdb_b])
                S.dma("act", lambda e, col0=col0, w=w: e.dma_start(
                    out=xdbl_d.ap()[0:96, col0 - 4:col0 - 4 + w],
                    in_=xdb_h[0:96, 0:w]), reads=[xdb_b])

        block = ctx.enter_context(nc.Block())
        S.emit(block, sems)
    return nc


_L1 = None


def _np(a):
    return np.asarray(a)


def _host_l1(inputs):
    global _L1
    if _L1 is None:
        _L1 = build_l1()
    x = _np(inputs["L0_attn_x"])
    w_in = np.ascontiguousarray(_np(inputs["L0_attn_W_in"]), np.float32)
    w_xp = np.ascontiguousarray(_np(inputs["L0_attn_W_xproj"]), np.float32)
    cw = np.ascontiguousarray(_np(inputs["L0_attn_conv_w"]).reshape(DI // 128, 128, DCONV)
                              .transpose(1, 0, 2).reshape(128, -1), np.float32)
    cb = np.ascontiguousarray(_np(inputs["L0_attn_conv_b"]).reshape(DI // 128, 128).T,
                              np.float32)
    ones = np.ones((128, 128), BF)
    in_maps = []
    for c in range(NC):
        b, s = c // 4, (c % 4) * TC
        xs = np.zeros((D, TC + 4), np.float32)
        lo = max(0, s - 4)
        xs[:, 4 - (s - lo):] = x[b, lo:s + TC, :].T
        in_maps.append(dict(x=np.ascontiguousarray(xs), w_in=w_in, w_xp=w_xp,
                            conv_w=cw, conv_b=cb, ones=ones))
    res = run_bass_kernel_spmd(_L1, in_maps, core_ids=list(range(NC)))
    u = np.concatenate([np.asarray(r["u"]) for r in res.results], axis=1)
    zs = np.concatenate([np.asarray(r["zs"]) for r in res.results], axis=1)
    xdbl = np.concatenate([np.asarray(r["xdbl"]) for r in res.results], axis=1)
    return u, zs, xdbl


def build_l2(nchu=None, nbatch=B, skip=()):
    nc = bass.Bass("TRN2", target_bir_lowering=False)
    TT = B * L
    TCH = 1024
    NSUB = TCH // 512
    NCHU = (L // TCH) if nchu is None else nchu
    NDT = DDI // 128
    u_d = nc.dram_tensor("u", [DDI, TT], BF16, kind="ExternalInput")
    xdbl_d = nc.dram_tensor("xdbl", [96, TT], F32, kind="ExternalInput")
    wdt_d = nc.dram_tensor("wdt", [DTR, DDI], F32, kind="ExternalInput")
    bdt_d = nc.dram_tensor("bdt", [128, NDT], F32, kind="ExternalInput")
    alog_d = nc.dram_tensor("alog", [DDI, DS], F32, kind="ExternalInput")
    ones_d = nc.dram_tensor("ones", [128, 128], BF16, kind="ExternalInput")
    ident_d = nc.dram_tensor("ident", [128, 128], BF16, kind="ExternalInput")
    y_d = nc.dram_tensor("y", [DDI, TT], BF16, kind="ExternalOutput")

    with ExitStack() as ctx:
        sems = make_sems(nc, ctx, extra=("wload", "wloadp", "uld", "yst", "bcld"))
        S = Sched(nc)
        ones_h, ones_b = sbuf(nc, ctx, "ones_sb", [128, 128], BF16)
        ident_h, ident_b = sbuf(nc, ctx, "ident_sb", [128, 128], BF16)
        wdt_h, wdt_b = sbuf(nc, ctx, "wdt_bf", [DTR, DDI], BF16)
        bdt_h, bdt_b = sbuf(nc, ctx, "bdt_sb", [128, NDT], F32)
        an_h = [sbuf(nc, ctx, f"aneg{i}", [128, DS], F32) for i in range(NDT)]
        dtlr_h, dtlr_b = sbuf(nc, ctx, "dtlr_bf", [DTR, TCH], BF16)
        bc_h, bc_b = sbuf(nc, ctx, "bc_bf", [1, 32 * TCH], BF16)
        brep = [sbuf(nc, ctx, f"brep{i}", [128, TCH], BF16) for i in range(2)]
        crep = [sbuf(nc, ctx, f"crep{i}", [128, TCH], BF16) for i in range(2)]
        ut = [sbuf(nc, ctx, f"ut{i}", [128, TCH], BF16) for i in range(NDT)]
        dl = [sbuf(nc, ctx, f"dl{i}", [128, TCH], F32) for i in range(NDT)]
        du = [sbuf(nc, ctx, f"du{i}", [128, TCH], BF16) for i in range(NDT)]
        spa_h, spa_b = sbuf(nc, ctx, "sp_abs", [128, 512], F32)
        spb_h, spb_b = sbuf(nc, ctx, "sp_mx", [128, 512], F32)
        dA = [sbuf(nc, ctx, f"dA{i}", [128, TCH], F32) for i in range(2)]
        dBu = [sbuf(nc, ctx, f"dBu{i}", [128, TCH], BF16) for i in range(2)]
        hh = [sbuf(nc, ctx, f"h{i}", [128, TCH], BF16) for i in range(2)]
        gg = [sbuf(nc, ctx, f"g{i}", [128, TCH], BF16) for i in range(2)]
        carry = [sbuf(nc, ctx, f"carry{i}", [128, DS], F32) for i in range(NDT)]
        yo = [sbuf(nc, ctx, f"yo{i}", [128, TCH], BF16) for i in range(NDT)]
        pdl_h, pdl_b = psum(nc, ctx, "pdl", [128, 512])
        pbc = [psum(nc, ctx, f"pbc{i}", [128, 512]) for i in range(2)]
        py = [[psum(nc, ctx, f"py{i}_{s}", [128, 512]) for s in range(NSUB)]
              for i in range(NDT)]

        S.dma("sp", lambda e: e.dma_start(out=ones_h[:, :], in_=ones_d.ap()),
              writes=[ones_b], key="wload")
        S.dma("sp", lambda e: e.dma_start(out=ident_h[:, :], in_=ident_d.ap()),
              writes=[ident_b], key="wload")
        S.dma("pool", lambda e: e.dma_start(out=wdt_h[:, :], in_=wdt_d.ap()),
              writes=[wdt_b], key="wloadp")
        S.dma("sp", lambda e: e.dma_start(out=bdt_h[:, :], in_=bdt_d.ap()),
              writes=[bdt_b], key="wload")
        al_bufs = []
        for i in range(NDT):
            ah, ab = sbuf(nc, ctx, f"alog{i}", [128, DS], F32)
            al_bufs.append((ah, ab))
            S.dma("sp", lambda e, ah=ah, i=i: e.dma_start(
                out=ah[:, :], in_=alog_d.ap()[128 * i:128 * (i + 1), :]),
                writes=[ab], key="wload")
        for i in range(NDT):
            ah, ab = al_bufs[i]
            nh, nb = an_h[i]
            S.op("act", lambda e, ah=ah, nh=nh: e.activation(
                out=nh[:, :], in_=ah[:, :], func=AF.Exp), reads=[ab], writes=[nb])
            S.op("dve", lambda e, nh=nh: e.tensor_scalar(
                out=nh[:, :], in0=nh[:, :], scalar1=-1.0, scalar2=None,
                op0=ALU.mult), writes=[nb])
        for i in range(NDT):
            ch, cb_ = carry[i]
            S.op("dve", lambda e, ch=ch: e.memset(ch[:, :], 0.0), writes=[cb_])

        for b in range(nbatch):
            if b > 0:
                for i in range(NDT):
                    ch, cb_ = carry[i]
                    S.op("dve", lambda e, ch=ch: e.memset(ch[:, :], 0.0),
                         writes=[cb_])
            for chk in range(NCHU):
                c0 = b * L + chk * TCH
                S.dma("pool", lambda e, c0=c0: e.dma_start(
                    out=dtlr_h[:, :], in_=xdbl_d.ap()[0:DTR, c0:c0 + TCH]),
                    writes=[dtlr_b], key="bcld")
                S.dma("pool", lambda e, c0=c0: e.dma_start(
                    out=bc_h[:, :].rearrange("p (a t) -> p a t", a=32),
                    in_=bass.AP(xdbl_d, DTR * TT + c0, [[1, 1], [TT, 32], [1, TCH]])),
                    writes=[bc_b], key="bcld")
                for i in range(NDT):
                    uh, ub = ut[i]
                    S.dma("sp", lambda e, uh=uh, i=i, c0=c0: e.dma_start(
                        out=uh[:, :], in_=u_d.ap()[128 * i:128 * (i + 1), c0:c0 + TCH]),
                        writes=[ub], key="uld")
                for i in range(NDT) if "delta" not in skip else []:
                    dlh, dlb = dl[i]
                    for s in range(NSUB):
                        if "deltamm" not in skip:
                            S.op("pe", lambda e, i=i, s=s: e.matmul(
                                pdl_h[:, :], wdt_h[:, 128 * i:128 * (i + 1)],
                                dtlr_h[:, 512 * s:512 * (s + 1)], start=True, stop=True),
                                reads=[dtlr_b, wdt_b], writes=[pdl_b])
                        else:
                            S.op("pe", lambda e, i=i, s=s: e.matmul(
                                pdl_h[:, :], ident_h[:, :],
                                ut[i][0][:, 512 * s:512 * (s + 1)], start=True, stop=True),
                                reads=[ut[i][1], ident_b], writes=[pdl_b])
                        _af_abs = AF.Square if "abs" in skip else AF.Abs
                        if "spact" not in skip:
                            S.op("act", lambda e, i=i, _af_abs=_af_abs: e.activation(
                                out=spa_h[:, :], in_=pdl_h[:, :], func=_af_abs,
                                bias=bdt_h[:, i:i + 1]),
                                reads=[pdl_b, bdt_b], writes=[spa_b])
                            S.op("act", lambda e: e.activation(
                                out=spa_h[:, :], in_=spa_h[:, :], func=AF.Exp,
                                scale=-1.0), writes=[spa_b])
                            S.op("act", lambda e: e.activation(
                                out=spa_h[:, :], in_=spa_h[:, :], func=AF.Ln,
                                bias=1.0), writes=[spa_b])
                        else:
                            S.op("act", lambda e: e.activation(
                                out=spa_h[:, :], in_=pdl_h[:, :], func=AF.Copy),
                                reads=[pdl_b], writes=[spa_b])
                        if "spdve" not in skip:
                            S.op("dve", lambda e, i=i: e.tensor_scalar(
                                out=spb_h[:, :], in0=pdl_h[:, :],
                                scalar1=bdt_h[:, i:i + 1], scalar2=0.0,
                                op0=ALU.add, op1=ALU.max),
                                reads=[pdl_b, bdt_b], writes=[spb_b])
                        else:
                            S.op("dve", lambda e: e.tensor_copy(
                                out=spb_h[:, :], in_=pdl_h[:, :]),
                                reads=[pdl_b], writes=[spb_b])
                        S.op("dve", lambda e, dlh=dlh, s=s: e.tensor_tensor(
                            out=dlh[:, 512 * s:512 * (s + 1)], in0=spa_h[:, :],
                            in1=spb_h[:, :], op=ALU.add),
                            reads=[spa_b, spb_b], writes=[dlb])
                    duh, dub = du[i]
                    uh, ub = ut[i]
                    S.op("dve", lambda e, duh=duh, dlh=dlh, uh=uh: e.tensor_tensor(
                        out=duh[:, :], in0=dlh[:, :], in1=uh[:, :], op=ALU.mult),
                        reads=[dlb, ub], writes=[dub])
                for n in range(DS) if "nloop" not in skip else []:
                    bh, bb = brep[n % 2]
                    ch2, cb2 = crep[n % 2]
                    for s in range(NSUB):
                        ph, pb = pbc[0]
                        S.op("pe", lambda e, ph=ph, n=n, s=s: e.matmul(
                            ph[:, :], ones_h[0:1, :],
                            bc_h[0:1, n * TCH + 512 * s:n * TCH + 512 * (s + 1)],
                            start=True, stop=True),
                            reads=[bc_b, ones_b], writes=[pb])
                        S.op("act", lambda e, ph=ph, bh=bh, s=s: e.activation(
                            out=bh[:, 512 * s:512 * (s + 1)], in_=ph[:, :],
                            func=AF.Copy), reads=[pb], writes=[bb])
                        ph2, pb2 = pbc[1]
                        S.op("pe", lambda e, ph2=ph2, n=n, s=s: e.matmul(
                            ph2[:, :], ones_h[0:1, :],
                            bc_h[0:1, (16 + n) * TCH + 512 * s:(16 + n) * TCH + 512 * (s + 1)],
                            start=True, stop=True),
                            reads=[bc_b, ones_b], writes=[pb2])
                        S.op("act", lambda e, ph2=ph2, ch2=ch2, s=s: e.activation(
                            out=ch2[:, 512 * s:512 * (s + 1)], in_=ph2[:, :],
                            func=AF.Copy), reads=[pb2], writes=[cb2])
                    for i in range(NDT):
                        pp = (n * NDT + i) % 2
                        dAh, dAb = dA[pp]
                        dBh, dBb = dBu[pp]
                        hhh, hhb = hh[pp]
                        ggh, ggb = gg[pp]
                        dlh, dlb = dl[i]
                        duh, dub = du[i]
                        anh, anb = an_h[i]
                        ch, cb_ = carry[i]
                        S.op("act", lambda e, dAh=dAh, dlh=dlh, anh=anh, n=n: e.activation(
                            out=dAh[:, :], in_=dlh[:, :], func=AF.Exp,
                            scale=anh[:, n:n + 1]), reads=[dlb, anb], writes=[dAb])
                        S.op("pool", lambda e, dBh=dBh, duh=duh, bh=bh: e.tensor_tensor(
                            out=dBh[:, :], in0=duh[:, :], in1=bh[:, :], op=ALU.mult),
                            reads=[dub, bb], writes=[dBb])
                        if "scan" in skip:
                            S.op("dve", lambda e, hhh=hhh, dBh=dBh: e.tensor_copy(
                                out=hhh[:, :], in_=dBh[:, :]),
                                reads=[dBb], writes=[hhb])
                        else:
                            S.op("dve", lambda e, hhh=hhh, dAh=dAh, dBh=dBh, ch=ch, n=n: e.tensor_tensor_scan(
                                out=hhh[:, :], data0=dAh[:, :], data1=dBh[:, :],
                                initial=ch[:, n:n + 1], op0=ALU.mult, op1=ALU.add),
                                reads=[dAb, dBb, cb_], writes=[hhb])
                        if "carry" not in skip:
                            S.op("dve", lambda e, hhh=hhh, ch=ch, n=n: e.tensor_copy(
                                out=ch[:, n:n + 1], in_=hhh[:, TCH - 1:TCH]),
                                reads=[hhb], writes=[cb_])
                        S.op("dve", lambda e, ggh=ggh, hhh=hhh, ch2=ch2: e.tensor_tensor(
                            out=ggh[:, :], in0=hhh[:, :], in1=ch2[:, :], op=ALU.mult),
                            reads=[hhb, cb2], writes=[ggb])
                        for s in range(NSUB):
                            pyh, pyb = py[i][s]
                            S.op("pe", lambda e, pyh=pyh, ggh=ggh, s=s, n=n: e.matmul(
                                pyh[:, :], ident_h[:, :],
                                ggh[:, 512 * s:512 * (s + 1)],
                                start=(n == 0), stop=(n == DS - 1)),
                                reads=[ggb, ident_b], writes=[pyb])
                for i in range(NDT) if "yevac" not in skip else []:
                    yh, yb = yo[i]
                    for s in range(NSUB):
                        pyh, pyb = py[i][s]
                        S.op("act", lambda e, yh=yh, pyh=pyh, s=s: e.activation(
                            out=yh[:, 512 * s:512 * (s + 1)], in_=pyh[:, :],
                            func=AF.Copy), reads=[pyb], writes=[yb])
                    S.dma("sp", lambda e, yh=yh, i=i, c0=c0: e.dma_start(
                        out=y_d.ap()[128 * i:128 * (i + 1), c0:c0 + TCH],
                        in_=yh[:, :]), reads=[yb], key="yst")

        block = ctx.enter_context(nc.Block())
        S.emit(block, sems)
    return nc


_BUILT = {}


def _get(name, builder):
    if name not in _BUILT:
        _BUILT[name] = builder()
    return _BUILT[name]


def _host_l2(u_full, xdbl_full, inputs):
    nc = _get("l2", build_l2)
    wdt = np.asarray(inputs["L0_attn_W_dt"], np.float32)
    bdt = np.asarray(inputs["L0_attn_b_dt"], np.float32)
    alog = np.asarray(inputs["L0_attn_A_log"], np.float32)
    ones = np.ones((128, 128), BF)
    ident = np.eye(128).astype(BF)
    in_maps = []
    for c in range(NC):
        dsl = slice(c * DDI, (c + 1) * DDI)
        in_maps.append(dict(
            u=np.ascontiguousarray(u_full[dsl]),
            xdbl=np.ascontiguousarray(xdbl_full),
            wdt=np.ascontiguousarray(wdt[:, dsl]),
            bdt=np.ascontiguousarray(bdt[dsl].reshape(DDI // 128, 128).T),
            alog=np.ascontiguousarray(alog[dsl]),
            ones=ones, ident=ident))
    res = run_bass_kernel_spmd(nc, in_maps, core_ids=list(range(NC)))
    return np.concatenate([np.asarray(r["y"]) for r in res.results], axis=0)


def ln_rowmath_sb(S, pm_h, pm_b, psq_h, psq_b, negm_h, negm_b, rstd_h, rstd_b,
                  tmp_h, tmp_b, icol0, ocol0, w, nfeat):
    isl = slice(icol0, icol0 + w)
    osl = slice(ocol0, ocol0 + w)
    S.op("act", lambda e: e.activation(out=negm_h[0:1, osl], in_=pm_h[0:1, isl],
                                       func=AF.Copy, scale=-1.0 / nfeat),
         reads=[pm_b], writes=[negm_b])
    S.op("act", lambda e: e.activation(out=tmp_h[0:1, 0:w], in_=pm_h[0:1, isl],
                                       func=AF.Square, scale=1.0 / nfeat),
         reads=[pm_b], writes=[tmp_b])
    S.op("act", lambda e: e.activation(out=rstd_h[0:1, osl], in_=psq_h[0:1, isl],
                                       func=AF.Copy, scale=1.0 / nfeat),
         reads=[psq_b], writes=[rstd_b])
    S.op("dve", lambda e: e.tensor_tensor(out=rstd_h[0:1, osl], in0=rstd_h[0:1, osl],
                                          in1=tmp_h[0:1, 0:w], op=ALU.subtract),
         reads=[tmp_b], writes=[rstd_b])
    S.op("dve", lambda e: e.tensor_scalar(out=rstd_h[0:1, osl], in0=rstd_h[0:1, osl],
                                          scalar1=EPS, scalar2=None, op0=ALU.add),
         writes=[rstd_b])
    S.op("act", lambda e: e.activation(out=rstd_h[0:1, osl], in_=rstd_h[0:1, osl],
                                       func=AF.Ln), writes=[rstd_b])
    S.op("act", lambda e: e.activation(out=rstd_h[0:1, osl], in_=rstd_h[0:1, osl],
                                       func=AF.Exp, scale=-0.5), writes=[rstd_b])


def build_l3a(nchunks=None):
    nc = bass.Bass("TRN2", target_bir_lowering=False)
    W = 512
    NCH = TC // W if nchunks is None else nchunks
    KTI = DI // 128
    KTM = D // 128
    y_dd = nc.dram_tensor("y", [DI, TC], BF16, kind="ExternalInput")
    u_dd = nc.dram_tensor("u", [DI, TC], BF16, kind="ExternalInput")
    zs_dd = nc.dram_tensor("zs", [DI, TC], BF16, kind="ExternalInput")
    x_dd = nc.dram_tensor("x", [D, TC], F32, kind="ExternalInput")
    dsk_d = nc.dram_tensor("dskip", [128, KTI], F32, kind="ExternalInput")
    wout_d = nc.dram_tensor("wout", [DI, D], F32, kind="ExternalInput")
    rtr_d = nc.dram_tensor("router", [D, NE], F32, kind="ExternalInput")
    ones_d = nc.dram_tensor("ones", [128, 128], BF16, kind="ExternalInput")
    ident_d = nc.dram_tensor("ident", [128, 128], BF16, kind="ExternalInput")
    x2_d = nc.dram_tensor("x2", [D, TC], F32, kind="ExternalOutput")
    xn3_d = nc.dram_tensor("xn3", [D, TC], BF16, kind="ExternalOutput")
    coef_d = nc.dram_tensor("coef", [1, NE * TC], BF16, kind="ExternalOutput")

    with ExitStack() as ctx:
        sems = make_sems(nc, ctx, extra=("wload", "wloadp", "yld", "uld", "zld",
                                         "xld", "ost"))
        S = Sched(nc)
        ones_h, ones_b = sbuf(nc, ctx, "ones_sb", [128, 128], BF16)
        ident_h, ident_b = sbuf(nc, ctx, "ident_sb", [128, 128], BF16)
        dsk_h, dsk_b = sbuf(nc, ctx, "dsk", [128, KTI], F32)
        wout_h, wout_b = sbuf(nc, ctx, "wout_bf", [128, KTI * D], BF16)
        rtr_h, rtr_b = sbuf(nc, ctx, "rtr_bf", [128, KTM * NE], BF16)
        ysb_h, ysb_b = sbuf(nc, ctx, "ysb", [128, KTI * W], BF16)
        usb_h, usb_b = sbuf(nc, ctx, "usb", [128, KTI * W], BF16)
        zsb_h, zsb_b = sbuf(nc, ctx, "zsb", [128, KTI * W], BF16)
        xsb_h, xsb_b = sbuf(nc, ctx, "xsb", [128, KTM * W], F32)
        tg_h, tg_b = sbuf(nc, ctx, "tg", [128, W], F32)
        yg_h, yg_b = sbuf(nc, ctx, "yg", [128, KTI * W], BF16)
        mo_h, mo_b = sbuf(nc, ctx, "mo_bf", [128, KTM * W], BF16)
        sq_h, sq_b = sbuf(nc, ctx, "sq_bf", [128, KTM * W], BF16)
        x2sb_h, x2sb_b = sbuf(nc, ctx, "x2sb", [128, KTM * W], F32)
        x2bf_h, x2bf_b = sbuf(nc, ctx, "x2bf", [128, KTM * W], BF16)
        xn3_h, xn3_b = sbuf(nc, ctx, "xn3sb", [128, KTM * W], BF16)
        nm2_h, nm2_b = sbuf(nc, ctx, "nm2", [1, TC], BF16)
        rs2_h, rs2_b = sbuf(nc, ctx, "rs2", [1, TC], BF16)
        nm3_h, nm3_b = sbuf(nc, ctx, "nm3", [1, TC], BF16)
        rs3_h, rs3_b = sbuf(nc, ctx, "rs3", [1, TC], BF16)
        nmf_h, nmf_b = sbuf(nc, ctx, "nmf", [1, W], F32)
        rsf_h, rsf_b = sbuf(nc, ctx, "rsf", [1, W], F32)
        rtf_h, rtf_b = sbuf(nc, ctx, "rtf", [1, W], F32)
        lg_h, lg_b = sbuf(nc, ctx, "lg_bf", [8, W], BF16)
        lgT_h, lgT_b = sbuf(nc, ctx, "lgT", [128, 8], F32)
        r1_h, r1_b = sbuf(nc, ctx, "r1", [128, 1], F32)
        r2_h, r2_b = sbuf(nc, ctx, "r2", [128, 1], F32)
        ee_h, ee_b = sbuf(nc, ctx, "ee", [128, 8], F32)
        pp_h, pp_b = sbuf(nc, ctx, "ppb", [128, 8], F32)
        gx_h, gx_b = sbuf(nc, ctx, "gx", [128, 8], F32)
        cf_h, cf_b = sbuf(nc, ctx, "cfbf", [128, 8], BF16)
        cT_h, cT_b = sbuf(nc, ctx, "coefT", [8, W], BF16)
        pwo = [psum(nc, ctx, f"pwo{i}", [128, W]) for i in range(2)]
        pm_h, pm_b = psum(nc, ctx, "pm", [1, W])
        pq_h, pq_b = psum(nc, ctx, "pq", [1, W])
        pnm_h, pnm_b = psum(nc, ctx, "pnm", [128, W])
        prs_h, prs_b = psum(nc, ctx, "prs", [128, W])
        plg_h, plg_b = psum(nc, ctx, "plg", [8, W])
        ptb_h, ptb_b = psum(nc, ctx, "ptb", [128, 128], BF16)

        S.dma("sp", lambda e: e.dma_start(out=ones_h[:, :], in_=ones_d.ap()),
              writes=[ones_b], key="wload")
        S.dma("sp", lambda e: e.dma_start(out=ident_h[:, :], in_=ident_d.ap()),
              writes=[ident_b], key="wload")
        S.dma("sp", lambda e: e.dma_start(out=dsk_h[:, :], in_=dsk_d.ap()),
              writes=[dsk_b], key="wload")
        S.dma("pool", lambda e: e.dma_start(
            out=wout_h[:, :].rearrange("p (a m) -> p a m", a=KTI),
            in_=wout_d.ap().rearrange("(a p) m -> p a m", p=128)),
            writes=[wout_b], key="wloadp")
        S.dma("pool", lambda e: e.dma_start(
            out=rtr_h[:, :].rearrange("p (a m) -> p a m", a=KTM),
            in_=rtr_d.ap().rearrange("(a p) m -> p a m", p=128)),
            writes=[rtr_b], key="wloadp")

        for chn in range(NCH):
            c0 = chn * W
            for nm_, hh_, bb_, dd_, kt_, ky_ in (
                    ("y", ysb_h, ysb_b, y_dd, KTI, "yld"),
                    ("u", usb_h, usb_b, u_dd, KTI, "uld"),
                    ("z", zsb_h, zsb_b, zs_dd, KTI, "zld"),
                    ("x", xsb_h, xsb_b, x_dd, KTM, "xld")):
                S.dma("sp", lambda e, hh_=hh_, dd_=dd_, kt_=kt_, c0=c0: e.dma_start(
                    out=hh_[:, 0:kt_ * W].rearrange("p (a t) -> p a t", a=kt_),
                    in_=dd_.ap()[:, c0:c0 + W].rearrange("(a p) t -> p a t", p=128)),
                    writes=[bb_], key=ky_)
            for k in range(KTI):
                sl = slice(k * W, (k + 1) * W)
                S.op("dve", lambda e, k=k, sl=sl: e.scalar_tensor_tensor(
                    out=tg_h[:, :], in0=usb_h[:, sl], scalar=dsk_h[:, k:k + 1],
                    in1=ysb_h[:, sl], op0=ALU.mult, op1=ALU.add),
                    reads=[usb_b, dsk_b, ysb_b], writes=[tg_b])
                S.op("dve", lambda e, k=k, sl=sl: e.tensor_tensor(
                    out=yg_h[:, sl], in0=tg_h[:, :], in1=zsb_h[:, sl],
                    op=ALU.mult), reads=[tg_b, zsb_b], writes=[yg_b])
            for mi in range(KTM):
                ph, pb = pwo[mi % 2]
                for k in range(KTI):
                    S.op("pe", lambda e, mi=mi, k=k, ph=ph: e.matmul(
                        ph[:, :], wout_h[:, k * D + mi * 128:k * D + mi * 128 + 128],
                        yg_h[:, k * W:(k + 1) * W],
                        start=(k == 0), stop=(k == KTI - 1)),
                        reads=[yg_b, wout_b], writes=[pb])
                msl = slice(mi * W, (mi + 1) * W)
                S.op("act", lambda e, ph=ph, msl=msl: e.activation(
                    out=mo_h[:, msl], in_=ph[:, :], func=AF.Copy),
                    reads=[pb], writes=[mo_b])
                S.op("act", lambda e, ph=ph, msl=msl: e.activation(
                    out=sq_h[:, msl], in_=ph[:, :], func=AF.Square),
                    reads=[pb], writes=[sq_b])
            for mi in range(KTM):
                msl = slice(mi * W, (mi + 1) * W)
                S.op("pe", lambda e, mi=mi, msl=msl: e.matmul(
                    pm_h[0:1, :], ones_h[:, 0:1], mo_h[:, msl],
                    start=(mi == 0), stop=(mi == KTM - 1)),
                    reads=[mo_b, ones_b], writes=[pm_b])
                S.op("pe", lambda e, mi=mi, msl=msl: e.matmul(
                    pq_h[0:1, :], ones_h[:, 0:1], sq_h[:, msl],
                    start=(mi == 0), stop=(mi == KTM - 1)),
                    reads=[sq_b, ones_b], writes=[pq_b])
            ln_rowmath(S, pm_h, pm_b, pq_h, pq_b, nmf_h, nmf_b, rsf_h, rsf_b,
                       rtf_h, rtf_b, 0, W, D)
            S.op("act", lambda e, c0=c0: e.activation(
                out=nm2_h[0:1, c0:c0 + W], in_=nmf_h[0:1, 0:W], func=AF.Copy),
                reads=[nmf_b], writes=[nm2_b])
            S.op("act", lambda e, c0=c0: e.activation(
                out=rs2_h[0:1, c0:c0 + W], in_=rsf_h[0:1, 0:W], func=AF.Copy),
                reads=[rsf_b], writes=[rs2_b])
            S.op("pe", lambda e, c0=c0: e.matmul(
                pnm_h[:, :], ones_h[0:1, :], nm2_h[0:1, c0:c0 + W],
                start=True, stop=True), reads=[nm2_b, ones_b], writes=[pnm_b])
            S.op("pe", lambda e, c0=c0: e.matmul(
                prs_h[:, :], ones_h[0:1, :], rs2_h[0:1, c0:c0 + W],
                start=True, stop=True), reads=[rs2_b, ones_b], writes=[prs_b])
            for mi in range(KTM):
                msl = slice(mi * W, (mi + 1) * W)
                S.op("dve", lambda e, msl=msl: e.tensor_tensor(
                    out=tg_h[:, :], in0=mo_h[:, msl], in1=pnm_h[:, :], op=ALU.add),
                    reads=[mo_b, pnm_b], writes=[tg_b])
                S.op("dve", lambda e, msl=msl: e.tensor_tensor(
                    out=tg_h[:, :], in0=tg_h[:, :], in1=prs_h[:, :], op=ALU.mult),
                    reads=[prs_b], writes=[tg_b])
                S.op("dve", lambda e, msl=msl: e.tensor_tensor(
                    out=x2sb_h[:, msl], in0=tg_h[:, :], in1=xsb_h[:, msl],
                    op=ALU.add), reads=[tg_b, xsb_b], writes=[x2sb_b])
                S.op("act", lambda e, msl=msl: e.activation(
                    out=x2bf_h[:, msl], in_=x2sb_h[:, msl], func=AF.Copy),
                    reads=[x2sb_b], writes=[x2bf_b])
                S.op("act", lambda e, msl=msl: e.activation(
                    out=sq_h[:, msl], in_=x2sb_h[:, msl], func=AF.Square),
                    reads=[x2sb_b], writes=[sq_b])
            S.dma("sp", lambda e, c0=c0: e.dma_start(
                out=x2_d.ap()[:, c0:c0 + W].rearrange("(a p) t -> p a t", p=128),
                in_=x2sb_h[:, :].rearrange("p (a t) -> p a t", a=KTM)),
                reads=[x2sb_b], key="ost")
            for mi in range(KTM):
                msl = slice(mi * W, (mi + 1) * W)
                S.op("pe", lambda e, msl=msl, mi=mi: e.matmul(
                    pm_h[0:1, :], ones_h[:, 0:1], x2bf_h[:, msl],
                    start=(mi == 0), stop=(mi == KTM - 1)),
                    reads=[x2bf_b, ones_b], writes=[pm_b])
                S.op("pe", lambda e, msl=msl, mi=mi: e.matmul(
                    pq_h[0:1, :], ones_h[:, 0:1], sq_h[:, msl],
                    start=(mi == 0), stop=(mi == KTM - 1)),
                    reads=[sq_b, ones_b], writes=[pq_b])
            ln_rowmath(S, pm_h, pm_b, pq_h, pq_b, nmf_h, nmf_b, rsf_h, rsf_b,
                       rtf_h, rtf_b, 0, W, D)
            S.op("act", lambda e, c0=c0: e.activation(
                out=nm3_h[0:1, c0:c0 + W], in_=nmf_h[0:1, 0:W], func=AF.Copy),
                reads=[nmf_b], writes=[nm3_b])
            S.op("act", lambda e, c0=c0: e.activation(
                out=rs3_h[0:1, c0:c0 + W], in_=rsf_h[0:1, 0:W], func=AF.Copy),
                reads=[rsf_b], writes=[rs3_b])
            S.op("pe", lambda e, c0=c0: e.matmul(
                pnm_h[:, :], ones_h[0:1, :], nm3_h[0:1, c0:c0 + W],
                start=True, stop=True), reads=[nm3_b, ones_b], writes=[pnm_b])
            S.op("pe", lambda e, c0=c0: e.matmul(
                prs_h[:, :], ones_h[0:1, :], rs3_h[0:1, c0:c0 + W],
                start=True, stop=True), reads=[rs3_b, ones_b], writes=[prs_b])
            for mi in range(KTM):
                msl = slice(mi * W, (mi + 1) * W)
                S.op("dve", lambda e, msl=msl: e.tensor_tensor(
                    out=tg_h[:, :], in0=x2sb_h[:, msl], in1=pnm_h[:, :],
                    op=ALU.add), reads=[x2sb_b, pnm_b], writes=[tg_b])
                S.op("dve", lambda e, msl=msl: e.tensor_tensor(
                    out=xn3_h[:, msl], in0=tg_h[:, :], in1=prs_h[:, :],
                    op=ALU.mult), reads=[tg_b, prs_b], writes=[xn3_b])
            S.dma("sp", lambda e, c0=c0: e.dma_start(
                out=xn3_d.ap()[:, c0:c0 + W].rearrange("(a p) t -> p a t", p=128),
                in_=xn3_h[:, :].rearrange("p (a t) -> p a t", a=KTM)),
                reads=[xn3_b], key="ost")
            for k in range(KTM):
                S.op("pe", lambda e, k=k: e.matmul(
                    plg_h[0:8, :], rtr_h[:, k * NE:(k + 1) * NE],
                    xn3_h[:, k * W:(k + 1) * W],
                    start=(k == 0), stop=(k == KTM - 1)),
                    reads=[xn3_b, rtr_b], writes=[plg_b])
            S.op("act", lambda e: e.activation(
                out=lg_h[0:8, :], in_=plg_h[0:8, :], func=AF.Copy),
                reads=[plg_b], writes=[lg_b])
            for j in range(4):
                S.op("pe", lambda e, j=j: e.transpose(
                    ptb_h[:, 0:8], lg_h[0:8, j * 128:(j + 1) * 128],
                    ident_h[0:8, 0:8]), reads=[lg_b, ident_b], writes=[ptb_b])
                S.op("act", lambda e: e.activation(
                    out=lgT_h[:, :], in_=ptb_h[:, 0:8], func=AF.Copy),
                    reads=[ptb_b], writes=[lgT_b])
                S.op("dve", lambda e: e.reduce_max(
                    out=r1_h[:, :], in_=lgT_h[:, :], axis=mybir.AxisListType.X),
                    reads=[lgT_b], writes=[r1_b])
                S.op("dve", lambda e: e.tensor_scalar(
                    out=r1_h[:, :], in0=r1_h[:, :], scalar1=-1.0, scalar2=None,
                    op0=ALU.mult), writes=[r1_b])
                S.op("act", lambda e: e.activation(
                    out=ee_h[:, :], in_=lgT_h[:, :], func=AF.Exp,
                    bias=r1_h[:, 0:1]), reads=[lgT_b, r1_b], writes=[ee_b])
                S.op("dve", lambda e: e.reduce_sum(
                    out=r2_h[:, :], in_=ee_h[:, :], axis=mybir.AxisListType.X),
                    reads=[ee_b], writes=[r2_b])
                S.op("dve", lambda e: e.reciprocal(out=r2_h[:, :], in_=r2_h[:, :]),
                     writes=[r2_b])
                S.op("dve", lambda e: e.tensor_scalar(
                    out=pp_h[:, :], in0=ee_h[:, :], scalar1=r2_h[:, 0:1],
                    scalar2=None, op0=ALU.mult),
                    reads=[ee_b, r2_b], writes=[pp_b])
                S.op("dve", lambda e: e.reduce_max(
                    out=r1_h[:, :], in_=pp_h[:, :], axis=mybir.AxisListType.X),
                    reads=[pp_b], writes=[r1_b])
                S.op("dve", lambda e: e.tensor_scalar(
                    out=gx_h[:, :], in0=pp_h[:, :], scalar1=r1_h[:, 0:1],
                    scalar2=None, op0=ALU.is_ge),
                    reads=[pp_b, r1_b], writes=[gx_b])
                S.op("dve", lambda e: e.tensor_tensor(
                    out=gx_h[:, :], in0=pp_h[:, :], in1=gx_h[:, :], op=ALU.mult),
                    reads=[pp_b], writes=[gx_b])
                S.op("dve", lambda e: e.tensor_tensor(
                    out=gx_h[:, :], in0=pp_h[:, :], in1=gx_h[:, :],
                    op=ALU.subtract), reads=[pp_b], writes=[gx_b])
                S.op("dve", lambda e: e.reduce_max(
                    out=r2_h[:, :], in_=gx_h[:, :], axis=mybir.AxisListType.X),
                    reads=[gx_b], writes=[r2_b])
                S.op("dve", lambda e: e.tensor_scalar(
                    out=gx_h[:, :], in0=pp_h[:, :], scalar1=r2_h[:, 0:1],
                    scalar2=None, op0=ALU.is_ge),
                    reads=[pp_b, r2_b], writes=[gx_b])
                S.op("dve", lambda e: e.tensor_tensor(
                    out=pp_h[:, :], in0=pp_h[:, :], in1=gx_h[:, :], op=ALU.mult),
                    reads=[gx_b], writes=[pp_b])
                S.op("act", lambda e: e.activation(
                    out=cf_h[:, :], in_=pp_h[:, :], func=AF.Copy),
                    reads=[pp_b], writes=[cf_b])
                S.op("pe", lambda e: e.transpose(
                    ptb_h[0:8, :], cf_h[:, 0:8], ident_h[:, :]),
                    reads=[cf_b, ident_b], writes=[ptb_b])
                S.op("act", lambda e, j=j: e.activation(
                    out=cT_h[0:8, j * 128:(j + 1) * 128], in_=ptb_h[0:8, :],
                    func=AF.Copy), reads=[ptb_b], writes=[cT_b])
            S.dma("sp", lambda e, c0=c0: e.dma_start(
                out=bass.AP(coef_d, c0, [[TC, 8], [1, W]]), in_=cT_h[0:8, :]),
                reads=[cT_b], key="ost")

        block = ctx.enter_context(nc.Block())
        S.emit(block, sems)
    return nc


def build_l3b(nchunks=None):
    nc = bass.Bass("TRN2", target_bir_lowering=False)
    W = 512
    NCH = TC // W if nchunks is None else nchunks
    KTM = D // 128
    KTF = DFF // 128
    xn3_d = nc.dram_tensor("xn3", [D, TC], BF16, kind="ExternalInput")
    coef_d = nc.dram_tensor("coef", [1, NE * TC], BF16, kind="ExternalInput")
    x2_d = nc.dram_tensor("x2", [D, TC], F32, kind="ExternalInput")
    weg_d = nc.dram_tensor("weg", [NE * D, DFF], F32, kind="ExternalInput")
    weu_d = nc.dram_tensor("weu", [NE * D, DFF], F32, kind="ExternalInput")
    wed_d = nc.dram_tensor("wed", [NE * DFF, D], F32, kind="ExternalInput")
    wsg_d = nc.dram_tensor("wsg", [D, D], F32, kind="ExternalInput")
    wsu_d = nc.dram_tensor("wsu", [D, D], F32, kind="ExternalInput")
    wsd_d = nc.dram_tensor("wsd", [D, D], F32, kind="ExternalInput")
    ones_d = nc.dram_tensor("ones", [128, 128], BF16, kind="ExternalInput")
    out_d = nc.dram_tensor("out", [D, TC], F32, kind="ExternalOutput")
    ffn_d = nc.dram_tensor("ffn_scr", [D, TC], BF16, kind="Internal")

    with ExitStack() as ctx:
        sems = make_sems(nc, ctx, extra=("wload", "wloadp", "we0",
                                         "xnld", "x2ld", "ffst", "ffld", "ost",
                                         "cfld"))
        S = Sched(nc)
        ones_h, ones_b = sbuf(nc, ctx, "ones_sb", [128, 128], BF16)
        wsg_h, wsg_b = sbuf(nc, ctx, "wsg_bf", [128, KTM * D], BF16)
        wsu_h, wsu_b = sbuf(nc, ctx, "wsu_bf", [128, KTM * D], BF16)
        wsd_h, wsd_b = sbuf(nc, ctx, "wsd_bf", [128, KTM * D], BF16)
        wg_h, wg_b = sbuf(nc, ctx, "weg0", [128, KTM * DFF], BF16)
        wu_h, wu_b = sbuf(nc, ctx, "weu0", [128, KTM * DFF], BF16)
        wd_h, wd_b = sbuf(nc, ctx, "wed0", [128, KTF * D], BF16)
        xn3_h, xn3_b = sbuf(nc, ctx, "xn3sb", [128, KTM * W], BF16)
        coef_h, coef_b = sbuf(nc, ctx, "coefsb", [1, NE * W], BF16)
        gsb_h, gsb_b = sbuf(nc, ctx, "gsb", [128, KTF * W], BF16)
        usb_h, usb_b = sbuf(nc, ctx, "uub", [128, KTF * W], BF16)
        cme_h, cme_b = sbuf(nc, ctx, "cme", [128, KTF * W], BF16)
        sm_h, sm_b = sbuf(nc, ctx, "smb", [128, KTM * W], BF16)
        rsb_h, rsb_b = sbuf(nc, ctx, "routed", [128, KTM * W], F32)
        ffn_h, ffn_b = sbuf(nc, ctx, "ffnb", [128, KTM * W], BF16)
        sq_h, sq_b = sbuf(nc, ctx, "sqb", [128, KTM * W], BF16)
        m4_h, m4_b = sbuf(nc, ctx, "m4row", [1, TC], BF16)
        q4_h, q4_b = sbuf(nc, ctx, "q4row", [1, TC], BF16)
        nm4_h, nm4_b = sbuf(nc, ctx, "nm4", [1, TC], BF16)
        rs4_h, rs4_b = sbuf(nc, ctx, "rs4", [1, TC], BF16)
        rtf_h, rtf_b = sbuf(nc, ctx, "rtf", [1, W], F32)
        x2sb_h, x2sb_b = sbuf(nc, ctx, "x2sb", [128, KTM * W], F32)
        tg_h, tg_b = sbuf(nc, ctx, "tg", [128, W], F32)
        pg_h, pg_b = psum(nc, ctx, "pg", [128, W])
        pu_h, pu_b = psum(nc, ctx, "pu", [128, W])
        pc_h, pc_b = psum(nc, ctx, "pc", [128, W])
        pd_h, pd_b = psum(nc, ctx, "pd", [128, W])
        pm_h, pm_b = psum(nc, ctx, "pm", [1, W])
        pq_h, pq_b = psum(nc, ctx, "pq", [1, W])
        pnm_h, pnm_b = psum(nc, ctx, "pnm", [128, W])
        prs_h, prs_b = psum(nc, ctx, "prs", [128, W])

        S.dma("pool", lambda e: e.dma_start(
            out=wsg_h[:, :].rearrange("p (a m) -> p a m", a=KTM),
            in_=wsg_d.ap().rearrange("(a p) m -> p a m", p=128)),
            writes=[wsg_b], key="wloadp")
        S.dma("pool", lambda e: e.dma_start(
            out=wsu_h[:, :].rearrange("p (a m) -> p a m", a=KTM),
            in_=wsu_d.ap().rearrange("(a p) m -> p a m", p=128)),
            writes=[wsu_b], key="wloadp")
        S.dma("pool", lambda e: e.dma_start(
            out=wsd_h[:, :].rearrange("p (a m) -> p a m", a=KTM),
            in_=wsd_d.ap().rearrange("(a p) m -> p a m", p=128)),
            writes=[wsd_b], key="wloadp")
        S.dma("sp", lambda e: e.dma_start(out=ones_h[:, :], in_=ones_d.ap()),
              writes=[ones_b], key="wload")

        for chn in range(NCH):
            c0 = chn * W
            S.dma("sp", lambda e, c0=c0: e.dma_start(
                out=xn3_h[:, :].rearrange("p (a t) -> p a t", a=KTM),
                in_=xn3_d.ap()[:, c0:c0 + W].rearrange("(a p) t -> p a t", p=128)),
                writes=[xn3_b], key="xnld")
            S.dma("sp", lambda e, c0=c0: e.dma_start(
                out=coef_h[:, :].rearrange("p (a t) -> p a t", a=NE),
                in_=bass.AP(coef_d, c0, [[1, 1], [TC, NE], [1, W]])),
                writes=[coef_b], key="cfld")
            for ei in range(NE):
                S.dma("pool", lambda e, ei=ei: e.dma_start(
                    out=wg_h[:, :].rearrange("p (a m) -> p a m", a=KTM),
                    in_=weg_d.ap()[ei * D:(ei + 1) * D, :].rearrange(
                        "(a p) m -> p a m", p=128)), writes=[wg_b], key="we0")
                S.dma("pool", lambda e, ei=ei: e.dma_start(
                    out=wu_h[:, :].rearrange("p (a m) -> p a m", a=KTM),
                    in_=weu_d.ap()[ei * D:(ei + 1) * D, :].rearrange(
                        "(a p) m -> p a m", p=128)), writes=[wu_b], key="we0")
                S.dma("pool", lambda e, ei=ei: e.dma_start(
                    out=wd_h[:, :].rearrange("p (a m) -> p a m", a=KTF),
                    in_=wed_d.ap()[ei * DFF:(ei + 1) * DFF, :].rearrange(
                        "(a p) m -> p a m", p=128)), writes=[wd_b], key="we0")
                S.op("pe", lambda e, ei=ei: e.matmul(
                    pc_h[:, :], ones_h[0:1, :],
                    coef_h[0:1, ei * W:(ei + 1) * W],
                    start=True, stop=True), reads=[coef_b, ones_b], writes=[pc_b])
                for mi2 in range(KTF):
                    fsl = slice(mi2 * W, (mi2 + 1) * W)
                    for k in range(KTM):
                        S.op("pe", lambda e, mi2=mi2, k=k: e.matmul(
                            pg_h[:, :], wg_h[:, k * DFF + mi2 * 128:k * DFF + mi2 * 128 + 128],
                            xn3_h[:, k * W:(k + 1) * W],
                            start=(k == 0), stop=(k == KTM - 1)),
                            reads=[xn3_b, wg_b], writes=[pg_b])
                    S.op("act", lambda e, fsl=fsl: e.activation(
                        out=gsb_h[:, fsl], in_=pg_h[:, :], func=AF.Silu),
                        reads=[pg_b], writes=[gsb_b])
                    for k in range(KTM):
                        S.op("pe", lambda e, mi2=mi2, k=k: e.matmul(
                            pu_h[:, :], wu_h[:, k * DFF + mi2 * 128:k * DFF + mi2 * 128 + 128],
                            xn3_h[:, k * W:(k + 1) * W],
                            start=(k == 0), stop=(k == KTM - 1)),
                            reads=[xn3_b, wu_b], writes=[pu_b])
                    S.op("act", lambda e, fsl=fsl: e.activation(
                        out=usb_h[:, fsl], in_=pu_h[:, :], func=AF.Copy),
                        reads=[pu_b], writes=[usb_b])
                    S.op("dve", lambda e, fsl=fsl: e.tensor_tensor(
                        out=gsb_h[:, fsl], in0=gsb_h[:, fsl], in1=usb_h[:, fsl],
                        op=ALU.mult), reads=[usb_b], writes=[gsb_b])
                    S.op("dve", lambda e, fsl=fsl: e.tensor_tensor(
                        out=cme_h[:, fsl], in0=gsb_h[:, fsl], in1=pc_h[:, :],
                        op=ALU.mult), reads=[gsb_b, pc_b], writes=[cme_b])
                for mo in range(KTM):
                    for k in range(KTF):
                        S.op("pe", lambda e, mo=mo, k=k: e.matmul(
                            pd_h[:, :], wd_h[:, k * D + mo * 128:k * D + mo * 128 + 128],
                            cme_h[:, k * W:(k + 1) * W],
                            start=(k == 0), stop=(k == KTF - 1)),
                            reads=[cme_b, wd_b], writes=[pd_b])
                    msl = slice(mo * W, (mo + 1) * W)
                    if ei == 0:
                        S.op("act", lambda e, msl=msl: e.activation(
                            out=rsb_h[:, msl], in_=pd_h[:, :], func=AF.Copy),
                            reads=[pd_b], writes=[rsb_b])
                    else:
                        S.op("dve", lambda e, msl=msl: e.tensor_tensor(
                            out=rsb_h[:, msl], in0=rsb_h[:, msl], in1=pd_h[:, :],
                            op=ALU.add), reads=[pd_b], writes=[rsb_b])
            for mi in range(KTM):
                msl = slice(mi * W, (mi + 1) * W)
                for k in range(KTM):
                    S.op("pe", lambda e, mi=mi, k=k: e.matmul(
                        pg_h[:, :], wsg_h[:, k * D + mi * 128:k * D + mi * 128 + 128],
                        xn3_h[:, k * W:(k + 1) * W],
                        start=(k == 0), stop=(k == KTM - 1)),
                        reads=[xn3_b, wsg_b], writes=[pg_b])
                S.op("act", lambda e, msl=msl: e.activation(
                    out=sq_h[:, msl], in_=pg_h[:, :], func=AF.Silu),
                    reads=[pg_b], writes=[sq_b])
                for k in range(KTM):
                    S.op("pe", lambda e, mi=mi, k=k: e.matmul(
                        pu_h[:, :], wsu_h[:, k * D + mi * 128:k * D + mi * 128 + 128],
                        xn3_h[:, k * W:(k + 1) * W],
                        start=(k == 0), stop=(k == KTM - 1)),
                        reads=[xn3_b, wsu_b], writes=[pu_b])
                S.op("act", lambda e, msl=msl: e.activation(
                    out=ffn_h[:, msl], in_=pu_h[:, :], func=AF.Copy),
                    reads=[pu_b], writes=[ffn_b])
                S.op("dve", lambda e, msl=msl: e.tensor_tensor(
                    out=sm_h[:, msl], in0=sq_h[:, msl], in1=ffn_h[:, msl],
                    op=ALU.mult), reads=[sq_b, ffn_b], writes=[sm_b])
            for mo in range(KTM):
                msl = slice(mo * W, (mo + 1) * W)
                for k in range(KTM):
                    S.op("pe", lambda e, mo=mo, k=k: e.matmul(
                        pd_h[:, :], wsd_h[:, k * D + mo * 128:k * D + mo * 128 + 128],
                        sm_h[:, k * W:(k + 1) * W],
                        start=(k == 0), stop=(k == KTM - 1)),
                        reads=[sm_b, wsd_b], writes=[pd_b])
                S.op("dve", lambda e, msl=msl: e.tensor_tensor(
                    out=rsb_h[:, msl], in0=rsb_h[:, msl], in1=pd_h[:, :],
                    op=ALU.add), reads=[pd_b], writes=[rsb_b])
                S.op("act", lambda e, msl=msl: e.activation(
                    out=ffn_h[:, msl], in_=rsb_h[:, msl], func=AF.Copy),
                    reads=[rsb_b], writes=[ffn_b])
                S.op("act", lambda e, msl=msl: e.activation(
                    out=sq_h[:, msl], in_=rsb_h[:, msl], func=AF.Square),
                    reads=[rsb_b], writes=[sq_b])
            for mo in range(KTM):
                msl = slice(mo * W, (mo + 1) * W)
                S.op("pe", lambda e, msl=msl, mo=mo: e.matmul(
                    pm_h[0:1, :], ones_h[:, 0:1], ffn_h[:, msl],
                    start=(mo == 0), stop=(mo == KTM - 1)),
                    reads=[ffn_b, ones_b], writes=[pm_b])
                S.op("pe", lambda e, msl=msl, mo=mo: e.matmul(
                    pq_h[0:1, :], ones_h[:, 0:1], sq_h[:, msl],
                    start=(mo == 0), stop=(mo == KTM - 1)),
                    reads=[sq_b, ones_b], writes=[pq_b])
            S.op("act", lambda e, c0=c0: e.activation(
                out=m4_h[0:1, c0:c0 + W], in_=pm_h[0:1, :], func=AF.Copy),
                reads=[pm_b], writes=[m4_b])
            S.op("act", lambda e, c0=c0: e.activation(
                out=q4_h[0:1, c0:c0 + W], in_=pq_h[0:1, :], func=AF.Copy),
                reads=[pq_b], writes=[q4_b])
            S.dma("sp", lambda e, c0=c0: e.dma_start(
                out=ffn_d.ap()[:, c0:c0 + W].rearrange("(a p) t -> p a t", p=128),
                in_=ffn_h[:, :].rearrange("p (a t) -> p a t", a=KTM)),
                reads=[ffn_b], key="ffst")

        for chn in range(NCH):
            c0 = chn * W
            ln_rowmath_sb(S, m4_h, m4_b, q4_h, q4_b, nm4_h, nm4_b, rs4_h, rs4_b,
                          rtf_h, rtf_b, c0, c0, W, D)
        for chn in range(NCH):
            c0 = chn * W
            S.dma("sp", lambda e, c0=c0: e.dma_start(
                out=x2sb_h[:, :].rearrange("p (a t) -> p a t", a=KTM),
                in_=x2_d.ap()[:, c0:c0 + W].rearrange("(a p) t -> p a t", p=128)),
                writes=[x2sb_b], key="x2ld")
            S.dma("sp", lambda e, c0=c0: e.dma_start(
                out=ffn_h[:, :].rearrange("p (a t) -> p a t", a=KTM),
                in_=ffn_d.ap()[:, c0:c0 + W].rearrange("(a p) t -> p a t", p=128)),
                writes=[ffn_b], key="ffld")
            S.op("pe", lambda e, c0=c0: e.matmul(
                pnm_h[:, :], ones_h[0:1, :], nm4_h[0:1, c0:c0 + W],
                start=True, stop=True), reads=[nm4_b, ones_b], writes=[pnm_b])
            S.op("pe", lambda e, c0=c0: e.matmul(
                prs_h[:, :], ones_h[0:1, :], rs4_h[0:1, c0:c0 + W],
                start=True, stop=True), reads=[rs4_b, ones_b], writes=[prs_b])
            for mi in range(KTM):
                msl = slice(mi * W, (mi + 1) * W)
                S.op("dve", lambda e, msl=msl: e.tensor_tensor(
                    out=tg_h[:, :], in0=ffn_h[:, msl], in1=pnm_h[:, :],
                    op=ALU.add), reads=[ffn_b, pnm_b], writes=[tg_b])
                S.op("dve", lambda e, msl=msl: e.tensor_tensor(
                    out=tg_h[:, :], in0=tg_h[:, :], in1=prs_h[:, :], op=ALU.mult),
                    reads=[prs_b], writes=[tg_b])
                S.op("dve", lambda e, msl=msl: e.tensor_tensor(
                    out=x2sb_h[:, msl], in0=tg_h[:, :], in1=x2sb_h[:, msl],
                    op=ALU.add), reads=[tg_b], writes=[x2sb_b])
            S.dma("sp", lambda e, c0=c0: e.dma_start(
                out=out_d.ap()[:, c0:c0 + W].rearrange("(a p) t -> p a t", p=128),
                in_=x2sb_h[:, :].rearrange("p (a t) -> p a t", a=KTM)),
                reads=[x2sb_b], key="ost")

        block = ctx.enter_context(nc.Block())
        S.emit(block, sems)
    return nc


def _host_l3(y_full, u_full, zs_full, inputs):
    nc_a = _get("l3a", build_l3a)
    nc_b = _get("l3b", build_l3b)
    x = _np(inputs["L0_attn_x"])
    dskip = np.asarray(inputs["L0_attn_D_skip"], np.float32).reshape(DI // 128, 128).T
    wout = np.ascontiguousarray(_np(inputs["L0_attn_W_out"]), np.float32)
    router = np.ascontiguousarray(_np(inputs["L0_ffn_router"]), np.float32)
    ones = np.ones((128, 128), BF)
    ident = np.eye(128).astype(BF)
    in_maps = []
    for c in range(NC):
        b, s = c // 4, (c % 4) * TC
        tsl = slice(c * TC, (c + 1) * TC)
        in_maps.append(dict(
            y=np.ascontiguousarray(np.asarray(y_full)[:, tsl]),
            u=np.ascontiguousarray(np.asarray(u_full)[:, tsl]),
            zs=np.ascontiguousarray(np.asarray(zs_full)[:, tsl]),
            x=np.ascontiguousarray(x[b, s:s + TC, :].T.astype(np.float32)),
            dskip=np.ascontiguousarray(dskip), wout=wout, router=router,
            ones=ones, ident=ident))
    res = run_bass_kernel_spmd(nc_a, in_maps, core_ids=list(range(NC)))
    weg = np.ascontiguousarray(_np(inputs["L0_ffn_experts_gate"]).reshape(NE * D, DFF), np.float32)
    weu = np.ascontiguousarray(_np(inputs["L0_ffn_experts_up"]).reshape(NE * D, DFF), np.float32)
    wed = np.ascontiguousarray(_np(inputs["L0_ffn_experts_down"]).reshape(NE * DFF, D), np.float32)
    wsg = np.ascontiguousarray(_np(inputs["L0_ffn_shared_gate"]), np.float32)
    wsu = np.ascontiguousarray(_np(inputs["L0_ffn_shared_up"]), np.float32)
    wsd = np.ascontiguousarray(_np(inputs["L0_ffn_shared_down"]), np.float32)
    in_maps_b = []
    for c in range(NC):
        r = res.results[c]
        in_maps_b.append(dict(
            xn3=np.asarray(r["xn3"]), coef=np.asarray(r["coef"]),
            x2=np.asarray(r["x2"]), weg=weg, weu=weu, wed=wed,
            wsg=wsg, wsu=wsu, wsd=wsd, ones=ones))
    res_b = run_bass_kernel_spmd(nc_b, in_maps_b, core_ids=list(range(NC)))
    return np.concatenate([np.asarray(r["out"]) for r in res_b.results], axis=1)


def kernel(**inputs):
    u, zs, xdbl = _host_l1(inputs)
    y = _host_l2(u, xdbl, inputs)
    out_fm = _host_l3(y, u, zs, inputs)
    return np.ascontiguousarray(out_fm.T.reshape(B, L, D)).astype(np.float32)




# version 04
# speedup vs baseline: 1.0024x; 1.0024x over previous
import numpy as np
import ml_dtypes
import concourse.bass as bass
import concourse.mybir as mybir
from concourse.bass_utils import run_bass_kernel_spmd
from contextlib import ExitStack

F32 = mybir.dt.float32
BF16 = mybir.dt.bfloat16
AF = mybir.ActivationFunctionType
ALU = mybir.AluOpType
BF = ml_dtypes.bfloat16

B, L, D = 2, 16384, 1024
DI, DS, DCONV, DTR = 2048, 16, 4, 64
NE, DFF = 8, 512
NC = 8
TC = (B * L) // NC
DDI = DI // NC
EPS = 1e-5

ENGINES = ("sp", "act", "dve", "pool", "pe")


class Op:
    __slots__ = ("eng", "fn", "deps", "dma", "idx", "inc_val", "needed", "key",
                 "seq", "wait_val")

    def __init__(self, eng, fn, deps, dma):
        self.eng, self.fn, self.deps, self.dma = eng, fn, deps, dma
        self.idx = -1
        self.inc_val = None
        self.needed = False


class Buf:
    __slots__ = ("name", "last_writer", "readers", "exclusive")

    def __init__(self, name="", exclusive=False):
        self.name = name
        self.last_writer = None
        self.readers = []
        self.exclusive = exclusive


class Sched:

    def __init__(self, nc):
        self.nc = nc
        self.streams = {e: [] for e in ENGINES}
        self._seq = 0
        self._last_dma = {}

    def op(self, eng, fn, reads=(), writes=(), deps=(), dma=False, key=None):
        reads = list(reads)
        writes = list(writes)
        excl = [b for b in reads if b.exclusive]
        if excl:
            reads = [b for b in reads if not b.exclusive]
            writes = writes + excl
        dd = list(deps)
        for b in reads:
            if b.last_writer is not None:
                dd.append(b.last_writer)
        for b in writes:
            if b.last_writer is not None:
                dd.append(b.last_writer)
            dd.extend(b.readers)
        o = Op(eng, fn, dd, dma)
        o.key = key
        o.idx = len(self.streams[eng])
        o.seq = self._seq
        self._seq += 1
        self.streams[eng].append(o)
        for b in reads:
            b.readers.append(o)
        for b in writes:
            b.last_writer = o
            b.readers = []
        return o

    def dma(self, eng, fn, reads=(), writes=(), deps=(), key=None):
        o = self.op(eng, fn, reads=reads, writes=writes, deps=deps, dma=True,
                    key=key)
        self._last_dma[self._key(o)] = o
        return o

    def create_sems(self, nc, ctx):
        sems = {}
        keys = set(ENGINES)
        for e in ENGINES:
            for o in self.streams[e]:
                if o.dma:
                    keys.add(self._key(o))
        for k in sorted(keys):
            sems[k] = ctx.enter_context(nc.semaphore("s_" + k.replace(".", "_")))
        return sems

    def barrier(self):
        lasts = [s[-1] for s in self.streams.values() if s]
        lasts += list(self._last_dma.values())
        nops = [self.op(e, lambda eng: eng.nop(), deps=lasts) for e in ENGINES]
        for e in ENGINES:
            self.op(e, lambda eng: eng.nop(), deps=nops)

    def _key(self, o):
        return o.eng + "." + (o.key or "dma")

    def _resolve(self):
        for e in ENGINES:
            for o in self.streams[e]:
                for d in o.deps:
                    d.needed = True
        self._dma_hist = {}
        for e in ENGINES:
            c_comp = 0
            c_dma = {}
            for o in self.streams[e]:
                if o.dma:
                    kk = self._key(o)
                    c_dma[kk] = c_dma.get(kk, 0) + 16
                    o.inc_val = c_dma[kk]
                    o.needed = True
                    self._dma_hist.setdefault(kk, []).append((o.seq, o.inc_val))
                elif o.needed:
                    c_comp += 1
                    o.inc_val = c_comp
                else:
                    o.inc_val = None

    def emit(self, block, sems, final_eng="dve"):
        self._resolve()
        sched = self
        finals = {}
        for e in ENGINES:
            for o in self.streams[e]:
                if o.dma:
                    kk = self._key(o)
                    finals[kk] = max(finals.get(kk, 0), o.inc_val)
                elif o.inc_val is not None:
                    finals[e] = max(finals.get(e, 0), o.inc_val)

        def mk_runner(ename):
            def run(eng):
                water = {}
                prev_completed = 0
                for o in sched.streams[ename]:
                    want = {}
                    for dp in o.deps:
                        if dp.inc_val is None:
                            continue
                        if dp.eng == o.eng and not dp.dma:
                            if dp.inc_val <= prev_completed:
                                continue
                        if dp.dma:
                            key = sched._key(dp)
                            v = 0
                            for sq, cum in sched._dma_hist[key]:
                                if sq < o.seq:
                                    v = cum
                                else:
                                    break
                            if v > want.get(key, 0):
                                want[key] = v
                            continue
                        key = dp.eng
                        if dp.inc_val > want.get(key, 0):
                            want[key] = dp.inc_val
                    for key in sorted(want):
                        v = want[key]
                        if key == ename:
                            if v > prev_completed:
                                eng.wait_ge(sems[key], v)
                                prev_completed = v
                            continue
                        if water.get(key, 0) >= v:
                            continue
                        eng.wait_ge(sems[key], v)
                        water[key] = v
                    inst = o.fn(eng)
                    if o.inc_val is not None:
                        key = sched._key(o) if o.dma else o.eng
                        inst.then_inc(sems[key], 16 if o.dma else 1)
                if ename == final_eng:
                    for key in sorted(finals):
                        if key in ENGINES:
                            continue
                        if finals[key] > water.get(key, 0) and finals[key] > 0:
                            eng.wait_ge(sems[key], finals[key])
            return run

        block.sync(mk_runner("sp"))
        block.scalar(mk_runner("act"))
        block.vector(mk_runner("dve"))
        block.gpsimd(mk_runner("pool"))
        block.tensor(mk_runner("pe"))


def make_sems(nc, ctx, extra=("wload",)):
    sems = {}
    for e in ENGINES:
        sems[e] = ctx.enter_context(nc.semaphore(f"s_{e}"))
        sems[e + "_dma"] = ctx.enter_context(nc.semaphore(f"s_{e}_dma"))
    for k in extra:
        sems[k] = ctx.enter_context(nc.semaphore(f"s_{k}"))
    return sems


_PREFIX = [""]


def sbuf(nc, ctx, name, shape, dtype):
    name = _PREFIX[0] + name
    h = ctx.enter_context(nc.sbuf_tensor(name, list(shape), dtype))
    return h, Buf(name)


def psum(nc, ctx, name, shape, dtype=F32):
    name = _PREFIX[0] + name
    h = ctx.enter_context(nc.psum_tensor(name, list(shape), dtype))
    return h, Buf(name, exclusive=True)


def ln_rowmath(S, pm_h, pm_b, psq_h, psq_b, negm_h, negm_b, rstd_h, rstd_b,
               tmp_h, tmp_b, col0, w, nfeat):
    sl = slice(col0, col0 + w)
    S.op("act", lambda e: e.activation(out=negm_h[0:1, sl], in_=pm_h[0:1, 0:w],
                                       func=AF.Copy, scale=-1.0 / nfeat),
         reads=[pm_b], writes=[negm_b])
    S.op("act", lambda e: e.activation(out=tmp_h[0:1, sl], in_=pm_h[0:1, 0:w],
                                       func=AF.Square, scale=1.0 / nfeat),
         reads=[pm_b], writes=[tmp_b])
    S.op("act", lambda e: e.activation(out=rstd_h[0:1, sl], in_=psq_h[0:1, 0:w],
                                       func=AF.Copy, scale=1.0 / nfeat),
         reads=[psq_b], writes=[rstd_b])
    S.op("dve", lambda e: e.tensor_tensor(out=rstd_h[0:1, sl], in0=rstd_h[0:1, sl],
                                          in1=tmp_h[0:1, sl], op=ALU.subtract),
         reads=[tmp_b], writes=[rstd_b])
    S.op("dve", lambda e: e.tensor_scalar(out=rstd_h[0:1, sl], in0=rstd_h[0:1, sl],
                                          scalar1=EPS, scalar2=None, op0=ALU.add),
         writes=[rstd_b])
    S.op("act", lambda e: e.activation(out=rstd_h[0:1, sl], in_=rstd_h[0:1, sl],
                                       func=AF.Ln),
         writes=[rstd_b])
    S.op("act", lambda e: e.activation(out=rstd_h[0:1, sl], in_=rstd_h[0:1, sl],
                                       func=AF.Exp, scale=-0.5),
         writes=[rstd_b])


def stage_l1(nc, S, sems, E, debug=False, p2_nchunks=None):
    x_d = E["x"]
    win_d = E["w_in"]
    wxp_d = E["w_xp"]
    cw_d = E["conv_w"]
    cb_d = E["conv_b"]
    ones_d = E["ones_in"]
    u_d = E["u_scr"]
    zs_d = E["zs_scr"]
    xdbl_d = E["xdbl_scr"]

    KT = D // 128
    MT = 2 * DI // 128
    NCH = TC // 512
    W = 512
    chunks = [(0, 4)] + [(4 + W * c, W) for c in range(NCH)]
    x_ap = x_d.ap()

    with ExitStack() as ctx:
        win_h, win_b = sbuf(nc, ctx, "win_bf", [128, KT * 2 * DI], BF16)
        wxp_h, wxp_b = sbuf(nc, ctx, "wxp_bf", [128, (DI // 128) * 96], BF16)
        cw_h, cw_b = sbuf(nc, ctx, "cw", [128, (DI // 128) * DCONV], F32)
        cb_h, cb_b = sbuf(nc, ctx, "cb", [128, DI // 128], F32)
        ones_h, ones_b = sbuf(nc, ctx, "ones_sb", [128, 128], BF16)
        negmb_h, negmb_b = sbuf(nc, ctx, "negm_bf", [1, TC + 4], BF16)
        rstdb_h, rstdb_b = sbuf(nc, ctx, "rstd_bf", [1, TC + 4], BF16)
        nm_s, nm_sb = sbuf(nc, ctx, "nm_s", [1, W], F32)
        rs_s, rs_sb = sbuf(nc, ctx, "rs_s", [1, W], F32)
        rt_s, rt_sb = sbuf(nc, ctx, "rt_s", [1, W], F32)
        xbf_h, xbf_b = sbuf(nc, ctx, "xbf", [128, KT * W], BF16)
        sqf_h, sqf_b = sbuf(nc, ctx, "sqf", [128, KT * W], BF16)
        xn_h, xn_b = sbuf(nc, ctx, "xn", [128, KT * W], BF16)
        tfa_h, tfa_b = sbuf(nc, ctx, "tfa", [128, W], F32)
        xin = [sbuf(nc, ctx, f"xin{i}", [128, 4 + W], F32) for i in range(DI // 128)]
        u_h, u_b = sbuf(nc, ctx, "usb", [128, (DI // 128) * W], BF16)
        ur_h, ur_b = sbuf(nc, ctx, "uraw", [128, W], F32)
        zs_h, zs_b = sbuf(nc, ctx, "zsb", [128, (DI // 128) * W], BF16)
        xdb_h, xdb_b = sbuf(nc, ctx, "xdbuf", [96, W], F32)
        pmean_h, pmean_b = psum(nc, ctx, "pmean", [1, W])
        psq_h, psq_b = psum(nc, ctx, "psq", [1, W])
        pnm_h, pnm_b = psum(nc, ctx, "pnm", [128, W])
        prs_h, prs_b = psum(nc, ctx, "prs", [128, W])
        pxz = [psum(nc, ctx, f"pxz{i}", [128, W]) for i in range(2)]
        pxd_h, pxd_b = psum(nc, ctx, "pxd", [96, W])

        S.dma("pool", lambda e: e.dma_start(
            out=win_h[:, :].rearrange("p (a m) -> p a m", a=KT),
            in_=win_d.ap().rearrange("(a p) m -> p a m", p=128)),
            writes=[win_b], key="wload")
        S.dma("pool", lambda e: e.dma_start(
            out=wxp_h[:, :].rearrange("p (a m) -> p a m", a=DI // 128),
            in_=wxp_d.ap().rearrange("(a p) m -> p a m", p=128)),
            writes=[wxp_b], key="wload")
        S.dma("sp", lambda e: e.dma_start(out=cw_h[:, :], in_=cw_d.ap()),
              writes=[cw_b])
        S.dma("sp", lambda e: e.dma_start(out=cb_h[:, :], in_=cb_d.ap()),
              writes=[cb_b])
        S.dma("sp", lambda e: e.dma_start(out=ones_h[:, :], in_=ones_d.ap()),
              writes=[ones_b])

        for col0, w in chunks:
            S.dma("pool", lambda e, col0=col0, w=w: e.dma_start(
                out=xbf_h[:, 0:KT * w].rearrange("p (a t) -> p a t", a=KT),
                in_=x_ap[:, col0:col0 + w].rearrange("(a p) t -> p a t", p=128)),
                writes=[xbf_b])
            S.op("act", lambda e, w=w: e.activation(
                out=sqf_h[:, 0:KT * w], in_=xbf_h[:, 0:KT * w], func=AF.Square),
                reads=[xbf_b], writes=[sqf_b])
            for k in range(KT):
                S.op("pe", lambda e, k=k, w=w: e.matmul(
                    pmean_h[0:1, 0:w], ones_h[:, 0:1], xbf_h[:, k * w:(k + 1) * w],
                    start=(k == 0), stop=(k == KT - 1)),
                    reads=[xbf_b, ones_b], writes=[pmean_b])
                S.op("pe", lambda e, k=k, w=w: e.matmul(
                    psq_h[0:1, 0:w], ones_h[:, 0:1], sqf_h[:, k * w:(k + 1) * w],
                    start=(k == 0), stop=(k == KT - 1)),
                    reads=[sqf_b, ones_b], writes=[psq_b])
            ln_rowmath(S, pmean_h, pmean_b, psq_h, psq_b, nm_s, nm_sb,
                       rs_s, rs_sb, rt_s, rt_sb, 0, w, D)
            S.op("act", lambda e, col0=col0, w=w: e.activation(
                out=negmb_h[0:1, col0:col0 + w], in_=nm_s[0:1, 0:w], func=AF.Copy),
                reads=[nm_sb], writes=[negmb_b])
            S.op("act", lambda e, col0=col0, w=w: e.activation(
                out=rstdb_h[0:1, col0:col0 + w], in_=rs_s[0:1, 0:w], func=AF.Copy),
                reads=[rs_sb], writes=[rstdb_b])

        p2chunks = chunks if p2_nchunks is None else chunks[:p2_nchunks]
        for col0, w in p2chunks:
            mini = (w == 4)
            S.dma("pool", lambda e, col0=col0, w=w: e.dma_start(
                out=xbf_h[:, 0:KT * w].rearrange("p (a t) -> p a t", a=KT),
                in_=x_ap[:, col0:col0 + w].rearrange("(a p) t -> p a t", p=128)),
                writes=[xbf_b])
            S.op("pe", lambda e, col0=col0, w=w: e.matmul(
                pnm_h[:, 0:w], ones_h[0:1, :], negmb_h[0:1, col0:col0 + w],
                start=True, stop=True), reads=[negmb_b, ones_b], writes=[pnm_b])
            S.op("pe", lambda e, col0=col0, w=w: e.matmul(
                prs_h[:, 0:w], ones_h[0:1, :], rstdb_h[0:1, col0:col0 + w],
                start=True, stop=True), reads=[rstdb_b, ones_b], writes=[prs_b])
            for k in range(KT):
                S.op("dve", lambda e, k=k, w=w: e.tensor_tensor(
                    out=tfa_h[:, 0:w], in0=xbf_h[:, k * w:(k + 1) * w],
                    in1=pnm_h[:, 0:w], op=ALU.add),
                    reads=[xbf_b, pnm_b], writes=[tfa_b])
                S.op("dve", lambda e, k=k, w=w: e.tensor_tensor(
                    out=xn_h[:, k * w:(k + 1) * w], in0=tfa_h[:, 0:w],
                    in1=prs_h[:, 0:w], op=ALU.mult),
                    reads=[tfa_b, prs_b], writes=[xn_b])
            mt_range = range(16) if mini else range(MT)
            for mi in mt_range:
                ph, pb = pxz[mi % 2]
                for k in range(KT):
                    S.op("pe", lambda e, mi=mi, k=k, ph=ph, w=w: e.matmul(
                        ph[:, 0:w],
                        win_h[:, k * 2 * DI + mi * 128:k * 2 * DI + mi * 128 + 128],
                        xn_h[:, k * w:(k + 1) * w],
                        start=(k == 0), stop=(k == KT - 1)),
                        reads=[xn_b, win_b], writes=[pb])
                if mi < 16:
                    xih, xib = xin[mi]
                    if mini:
                        S.op("act", lambda e, ph=ph, xih=xih: e.activation(
                            out=xih[:, 0:4], in_=ph[:, 0:4], func=AF.Copy),
                            reads=[pb], writes=[xib])
                    else:
                        S.op("act", lambda e, ph=ph, xih=xih, w=w: e.activation(
                            out=xih[:, 4:4 + w], in_=ph[:, 0:w], func=AF.Copy),
                            reads=[pb], writes=[xib])
                        S.op("dve", lambda e, xih=xih, mi=mi, w=w: e.tensor_scalar(
                            out=ur_h[:, 0:w], in0=xih[:, 1:1 + w],
                            scalar1=cw_h[:, mi * DCONV:mi * DCONV + 1],
                            scalar2=cb_h[:, mi:mi + 1], op0=ALU.mult, op1=ALU.add),
                            reads=[xib, cw_b, cb_b], writes=[ur_b])
                        for j in range(1, DCONV):
                            S.op("dve", lambda e, xih=xih, mi=mi, j=j, w=w: e.scalar_tensor_tensor(
                                out=ur_h[:, 0:w], in0=xih[:, 1 + j:1 + j + w],
                                scalar=cw_h[:, mi * DCONV + j:mi * DCONV + j + 1],
                                in1=ur_h[:, 0:w], op0=ALU.mult, op1=ALU.add),
                                reads=[xib, cw_b], writes=[ur_b])
                        S.op("act", lambda e, mi=mi, w=w: e.activation(
                            out=u_h[:, mi * w:(mi + 1) * w], in_=ur_h[:, 0:w],
                            func=AF.Silu), reads=[ur_b], writes=[u_b])
                        S.op("dve", lambda e, xih=xih, w=w: e.tensor_copy(
                            out=xih[:, 0:4], in_=xih[:, w:w + 4]),
                            reads=[xib], writes=[xib])
                        S.op("pe", lambda e, mi=mi, w=w: e.matmul(
                            pxd_h[0:96, 0:w], wxp_h[:, mi * 96:(mi + 1) * 96],
                            u_h[:, mi * w:(mi + 1) * w],
                            start=(mi == 0), stop=(mi == 15)),
                            reads=[u_b, wxp_b], writes=[pxd_b])

                else:
                    S.op("act", lambda e, ph=ph, mi=mi, w=w: e.activation(
                        out=zs_h[:, (mi - 16) * w:(mi - 15) * w], in_=ph[:, 0:w],
                        func=AF.Silu), reads=[pb], writes=[zs_b])
            if not mini:
                S.dma("sp", lambda e, col0=col0, w=w: e.dma_start(
                    out=u_d.ap()[:, col0 - 4:col0 - 4 + w].rearrange(
                        "(a p) t -> p a t", p=128),
                    in_=u_h[:, 0:16 * w].rearrange("p (a t) -> p a t", a=16)),
                    reads=[u_b])
                S.dma("act", lambda e, col0=col0, w=w: e.dma_start(
                    out=zs_d.ap()[:, col0 - 4:col0 - 4 + w].rearrange(
                        "(a p) t -> p a t", p=128),
                    in_=zs_h[:, 0:16 * w].rearrange("p (a t) -> p a t", a=16)),
                    reads=[zs_b])
                S.op("act", lambda e, w=w: e.activation(
                    out=xdb_h[0:96, 0:w], in_=pxd_h[0:96, 0:w], func=AF.Copy),
                    reads=[pxd_b], writes=[xdb_b])
                S.dma("act", lambda e, col0=col0, w=w: e.dma_start(
                    out=xdbl_d.ap()[0:96, col0 - 4:col0 - 4 + w],
                    in_=xdb_h[0:96, 0:w]), reads=[xdb_b])


def _np(a):
    return np.asarray(a)


def stage_l2(nc, S, sems, E, nchu=None, nbatch=B, skip=()):
    TCH = 1024
    NSUB = TCH // 512
    NCHU = (L // TCH) if nchu is None else nchu
    NDT = DDI // 128
    u_d = E["u_my"]
    xdbl_d = E["xdbl_ag"]
    wdt_d = E["wdt"]
    bdt_d = E["bdt"]
    alog_d = E["alog"]
    ones_d = E["ones_in"]
    ident_d = E["ident_in"]
    y_d = E["y_scr"]

    with ExitStack() as ctx:
        ones_h, ones_b = sbuf(nc, ctx, "l2_ones", [128, 128], BF16)
        ident_h, ident_b = sbuf(nc, ctx, "ident_sb", [128, 128], BF16)
        wdt_h, wdt_b = sbuf(nc, ctx, "wdt_bf", [DTR, DDI], BF16)
        bdt_h, bdt_b = sbuf(nc, ctx, "bdt_sb", [128, NDT], F32)
        an_h = [sbuf(nc, ctx, f"aneg{i}", [128, DS], F32) for i in range(NDT)]
        dtlr_h, dtlr_b = sbuf(nc, ctx, "dtlr_bf", [DTR, TCH], BF16)
        bc_h, bc_b = sbuf(nc, ctx, "bc_bf", [1, 32 * TCH], BF16)
        brep = [sbuf(nc, ctx, f"brep{i}", [128, TCH], BF16) for i in range(2)]
        crep = [sbuf(nc, ctx, f"crep{i}", [128, TCH], BF16) for i in range(2)]
        ut = [sbuf(nc, ctx, f"ut{i}", [128, TCH], BF16) for i in range(NDT)]
        dl = [sbuf(nc, ctx, f"dl{i}", [128, TCH], F32) for i in range(NDT)]
        du = [sbuf(nc, ctx, f"du{i}", [128, TCH], BF16) for i in range(NDT)]
        spa_h, spa_b = sbuf(nc, ctx, "sp_abs", [128, 512], F32)
        spb_h, spb_b = sbuf(nc, ctx, "sp_mx", [128, 512], F32)
        dA = [sbuf(nc, ctx, f"dA{i}", [128, TCH], F32) for i in range(2)]
        dBu = [sbuf(nc, ctx, f"dBu{i}", [128, TCH], BF16) for i in range(2)]
        hh = [sbuf(nc, ctx, f"h{i}", [128, TCH], BF16) for i in range(2)]
        gg = [sbuf(nc, ctx, f"g{i}", [128, TCH], BF16) for i in range(2)]
        carry = [sbuf(nc, ctx, f"carry{i}", [128, DS], F32) for i in range(NDT)]
        yo = [sbuf(nc, ctx, f"yo{i}", [128, TCH], BF16) for i in range(NDT)]
        pdl_h, pdl_b = psum(nc, ctx, "pdl", [128, 512])
        pbc = [psum(nc, ctx, f"pbc{i}", [128, 512]) for i in range(2)]
        py = [[psum(nc, ctx, f"py{i}_{s}", [128, 512]) for s in range(NSUB)]
              for i in range(NDT)]

        S.dma("sp", lambda e: e.dma_start(out=ones_h[:, :], in_=ones_d.ap()),
              writes=[ones_b], key="wload")
        S.dma("sp", lambda e: e.dma_start(out=ident_h[:, :], in_=ident_d.ap()),
              writes=[ident_b], key="wload")
        S.dma("pool", lambda e: e.dma_start(out=wdt_h[:, :], in_=wdt_d.ap()),
              writes=[wdt_b], key="wloadp")
        S.dma("sp", lambda e: e.dma_start(out=bdt_h[:, :], in_=bdt_d.ap()),
              writes=[bdt_b], key="wload")
        al_bufs = []
        for i in range(NDT):
            ah, ab = sbuf(nc, ctx, f"alog{i}", [128, DS], F32)
            al_bufs.append((ah, ab))
            S.dma("sp", lambda e, ah=ah, i=i: e.dma_start(
                out=ah[:, :], in_=alog_d.ap()[128 * i:128 * (i + 1), :]),
                writes=[ab], key="wload")
        for i in range(NDT):
            ah, ab = al_bufs[i]
            nh, nb = an_h[i]
            S.op("act", lambda e, ah=ah, nh=nh: e.activation(
                out=nh[:, :], in_=ah[:, :], func=AF.Exp), reads=[ab], writes=[nb])
            S.op("dve", lambda e, nh=nh: e.tensor_scalar(
                out=nh[:, :], in0=nh[:, :], scalar1=-1.0, scalar2=None,
                op0=ALU.mult), writes=[nb])
        for i in range(NDT):
            ch, cb_ = carry[i]
            S.op("dve", lambda e, ch=ch: e.memset(ch[:, :], 0.0), writes=[cb_])

        for b in range(nbatch):
            if b > 0:
                for i in range(NDT):
                    ch, cb_ = carry[i]
                    S.op("dve", lambda e, ch=ch: e.memset(ch[:, :], 0.0),
                         writes=[cb_])
            for chk in range(NCHU):
                c0 = b * L + chk * TCH
                rr = c0 // TC
                cl = c0 % TC
                S.dma("pool", lambda e, rr=rr, cl=cl: e.dma_start(
                    out=dtlr_h[:, :],
                    in_=xdbl_d.ap()[rr * 96:rr * 96 + DTR, cl:cl + TCH]),
                    writes=[dtlr_b], key="bcld")
                S.dma("pool", lambda e, rr=rr, cl=cl: e.dma_start(
                    out=bc_h[:, :].rearrange("p (a t) -> p a t", a=32),
                    in_=bass.AP(xdbl_d, (rr * 96 + DTR) * TC + cl,
                                [[1, 1], [TC, 32], [1, TCH]])),
                    writes=[bc_b], key="bcld")
                for i in range(NDT):
                    uh, ub = ut[i]
                    S.dma("sp", lambda e, uh=uh, i=i, rr=rr, cl=cl: e.dma_start(
                        out=uh[:, :],
                        in_=u_d.ap()[rr * DDI + 128 * i:rr * DDI + 128 * (i + 1),
                                     cl:cl + TCH]),
                        writes=[ub], key="uld")
                for i in range(NDT) if "delta" not in skip else []:
                    dlh, dlb = dl[i]
                    for s in range(NSUB):
                        if "deltamm" not in skip:
                            S.op("pe", lambda e, i=i, s=s: e.matmul(
                                pdl_h[:, :], wdt_h[:, 128 * i:128 * (i + 1)],
                                dtlr_h[:, 512 * s:512 * (s + 1)], start=True, stop=True),
                                reads=[dtlr_b, wdt_b], writes=[pdl_b])
                        else:
                            S.op("pe", lambda e, i=i, s=s: e.matmul(
                                pdl_h[:, :], ident_h[:, :],
                                ut[i][0][:, 512 * s:512 * (s + 1)], start=True, stop=True),
                                reads=[ut[i][1], ident_b], writes=[pdl_b])
                        _af_abs = AF.Square if "abs" in skip else AF.Abs
                        if "spact" not in skip:
                            S.op("act", lambda e, i=i, _af_abs=_af_abs: e.activation(
                                out=spa_h[:, :], in_=pdl_h[:, :], func=_af_abs,
                                bias=bdt_h[:, i:i + 1]),
                                reads=[pdl_b, bdt_b], writes=[spa_b])
                            S.op("act", lambda e: e.activation(
                                out=spa_h[:, :], in_=spa_h[:, :], func=AF.Exp,
                                scale=-1.0), writes=[spa_b])
                            S.op("act", lambda e: e.activation(
                                out=spa_h[:, :], in_=spa_h[:, :], func=AF.Ln,
                                bias=1.0), writes=[spa_b])
                        else:
                            S.op("act", lambda e: e.activation(
                                out=spa_h[:, :], in_=pdl_h[:, :], func=AF.Copy),
                                reads=[pdl_b], writes=[spa_b])
                        if "spdve" not in skip:
                            S.op("dve", lambda e, i=i: e.tensor_scalar(
                                out=spb_h[:, :], in0=pdl_h[:, :],
                                scalar1=bdt_h[:, i:i + 1], scalar2=0.0,
                                op0=ALU.add, op1=ALU.max),
                                reads=[pdl_b, bdt_b], writes=[spb_b])
                        else:
                            S.op("dve", lambda e: e.tensor_copy(
                                out=spb_h[:, :], in_=pdl_h[:, :]),
                                reads=[pdl_b], writes=[spb_b])
                        S.op("dve", lambda e, dlh=dlh, s=s: e.tensor_tensor(
                            out=dlh[:, 512 * s:512 * (s + 1)], in0=spa_h[:, :],
                            in1=spb_h[:, :], op=ALU.add),
                            reads=[spa_b, spb_b], writes=[dlb])
                    duh, dub = du[i]
                    uh, ub = ut[i]
                    S.op("dve", lambda e, duh=duh, dlh=dlh, uh=uh: e.tensor_tensor(
                        out=duh[:, :], in0=dlh[:, :], in1=uh[:, :], op=ALU.mult),
                        reads=[dlb, ub], writes=[dub])
                for n in range(DS) if "nloop" not in skip else []:
                    bh, bb = brep[n % 2]
                    ch2, cb2 = crep[n % 2]
                    for s in range(NSUB):
                        ph, pb = pbc[0]
                        S.op("pe", lambda e, ph=ph, n=n, s=s: e.matmul(
                            ph[:, :], ones_h[0:1, :],
                            bc_h[0:1, n * TCH + 512 * s:n * TCH + 512 * (s + 1)],
                            start=True, stop=True),
                            reads=[bc_b, ones_b], writes=[pb])
                        S.op("act", lambda e, ph=ph, bh=bh, s=s: e.activation(
                            out=bh[:, 512 * s:512 * (s + 1)], in_=ph[:, :],
                            func=AF.Copy), reads=[pb], writes=[bb])
                        ph2, pb2 = pbc[1]
                        S.op("pe", lambda e, ph2=ph2, n=n, s=s: e.matmul(
                            ph2[:, :], ones_h[0:1, :],
                            bc_h[0:1, (16 + n) * TCH + 512 * s:(16 + n) * TCH + 512 * (s + 1)],
                            start=True, stop=True),
                            reads=[bc_b, ones_b], writes=[pb2])
                        S.op("act", lambda e, ph2=ph2, ch2=ch2, s=s: e.activation(
                            out=ch2[:, 512 * s:512 * (s + 1)], in_=ph2[:, :],
                            func=AF.Copy), reads=[pb2], writes=[cb2])
                    for i in range(NDT):
                        pp = (n * NDT + i) % 2
                        dAh, dAb = dA[pp]
                        dBh, dBb = dBu[pp]
                        hhh, hhb = hh[pp]
                        ggh, ggb = gg[pp]
                        dlh, dlb = dl[i]
                        duh, dub = du[i]
                        anh, anb = an_h[i]
                        ch, cb_ = carry[i]
                        S.op("act", lambda e, dAh=dAh, dlh=dlh, anh=anh, n=n: e.activation(
                            out=dAh[:, :], in_=dlh[:, :], func=AF.Exp,
                            scale=anh[:, n:n + 1]), reads=[dlb, anb], writes=[dAb])
                        S.op("pool", lambda e, dBh=dBh, duh=duh, bh=bh: e.tensor_tensor(
                            out=dBh[:, :], in0=duh[:, :], in1=bh[:, :], op=ALU.mult),
                            reads=[dub, bb], writes=[dBb])
                        if "scan" in skip:
                            S.op("dve", lambda e, hhh=hhh, dBh=dBh: e.tensor_copy(
                                out=hhh[:, :], in_=dBh[:, :]),
                                reads=[dBb], writes=[hhb])
                        else:
                            S.op("dve", lambda e, hhh=hhh, dAh=dAh, dBh=dBh, ch=ch, n=n: e.tensor_tensor_scan(
                                out=hhh[:, :], data0=dAh[:, :], data1=dBh[:, :],
                                initial=ch[:, n:n + 1], op0=ALU.mult, op1=ALU.add),
                                reads=[dAb, dBb, cb_], writes=[hhb])
                        if "carry" not in skip:
                            S.op("dve", lambda e, hhh=hhh, ch=ch, n=n: e.tensor_copy(
                                out=ch[:, n:n + 1], in_=hhh[:, TCH - 1:TCH]),
                                reads=[hhb], writes=[cb_])
                        S.op("dve", lambda e, ggh=ggh, hhh=hhh, ch2=ch2: e.tensor_tensor(
                            out=ggh[:, :], in0=hhh[:, :], in1=ch2[:, :], op=ALU.mult),
                            reads=[hhb, cb2], writes=[ggb])
                        for s in range(NSUB):
                            pyh, pyb = py[i][s]
                            S.op("pe", lambda e, pyh=pyh, ggh=ggh, s=s, n=n: e.matmul(
                                pyh[:, :], ident_h[:, :],
                                ggh[:, 512 * s:512 * (s + 1)],
                                start=(n == 0), stop=(n == DS - 1)),
                                reads=[ggb, ident_b], writes=[pyb])
                for i in range(NDT) if "yevac" not in skip else []:
                    yh, yb = yo[i]
                    for s in range(NSUB):
                        pyh, pyb = py[i][s]
                        S.op("act", lambda e, yh=yh, pyh=pyh, s=s: e.activation(
                            out=yh[:, 512 * s:512 * (s + 1)], in_=pyh[:, :],
                            func=AF.Copy), reads=[pyb], writes=[yb])
                    S.dma("sp", lambda e, yh=yh, i=i, rr=rr, cl=cl: e.dma_start(
                        out=y_d.ap()[rr * DDI + 128 * i:rr * DDI + 128 * (i + 1),
                                     cl:cl + TCH],
                        in_=yh[:, :]), reads=[yb], key="yst")


def ln_rowmath_sb(S, pm_h, pm_b, psq_h, psq_b, negm_h, negm_b, rstd_h, rstd_b,
                  tmp_h, tmp_b, icol0, ocol0, w, nfeat):
    isl = slice(icol0, icol0 + w)
    osl = slice(ocol0, ocol0 + w)
    S.op("act", lambda e: e.activation(out=negm_h[0:1, osl], in_=pm_h[0:1, isl],
                                       func=AF.Copy, scale=-1.0 / nfeat),
         reads=[pm_b], writes=[negm_b])
    S.op("act", lambda e: e.activation(out=tmp_h[0:1, 0:w], in_=pm_h[0:1, isl],
                                       func=AF.Square, scale=1.0 / nfeat),
         reads=[pm_b], writes=[tmp_b])
    S.op("act", lambda e: e.activation(out=rstd_h[0:1, osl], in_=psq_h[0:1, isl],
                                       func=AF.Copy, scale=1.0 / nfeat),
         reads=[psq_b], writes=[rstd_b])
    S.op("dve", lambda e: e.tensor_tensor(out=rstd_h[0:1, osl], in0=rstd_h[0:1, osl],
                                          in1=tmp_h[0:1, 0:w], op=ALU.subtract),
         reads=[tmp_b], writes=[rstd_b])
    S.op("dve", lambda e: e.tensor_scalar(out=rstd_h[0:1, osl], in0=rstd_h[0:1, osl],
                                          scalar1=EPS, scalar2=None, op0=ALU.add),
         writes=[rstd_b])
    S.op("act", lambda e: e.activation(out=rstd_h[0:1, osl], in_=rstd_h[0:1, osl],
                                       func=AF.Ln), writes=[rstd_b])
    S.op("act", lambda e: e.activation(out=rstd_h[0:1, osl], in_=rstd_h[0:1, osl],
                                       func=AF.Exp, scale=-0.5), writes=[rstd_b])


def stage_l3a(nc, S, sems, E, nchunks=None):
    W = 512
    NCH = TC // W if nchunks is None else nchunks
    KTI = DI // 128
    KTM = D // 128
    y_dd = E["y_my"]
    u_dd = E["u_scr"]
    zs_dd = E["zs_scr"]
    x_dd = E["x_nohalo"]
    dsk_d = E["dskip"]
    wout_d = E["wout"]
    rtr_d = E["router"]
    ones_d = E["ones_in"]
    ident_d = E["ident_in"]
    x2_d = E["x2_scr"]
    xn3_d = E["xn3_scr"]
    coef_d = E["coef_scr"]

    with ExitStack() as ctx:
        ones_h, ones_b = sbuf(nc, ctx, "ones_sb", [128, 128], BF16)
        ident_h, ident_b = sbuf(nc, ctx, "ident_sb", [128, 128], BF16)
        dsk_h, dsk_b = sbuf(nc, ctx, "dsk", [128, KTI], F32)
        wout_h, wout_b = sbuf(nc, ctx, "wout_bf", [128, KTI * D], BF16)
        rtr_h, rtr_b = sbuf(nc, ctx, "rtr_bf", [128, KTM * NE], BF16)
        ysb_h, ysb_b = sbuf(nc, ctx, "ysb", [128, KTI * W], BF16)
        usb_h, usb_b = sbuf(nc, ctx, "usb", [128, KTI * W], BF16)
        zsb_h, zsb_b = sbuf(nc, ctx, "zsb", [128, KTI * W], BF16)
        xsb_h, xsb_b = sbuf(nc, ctx, "xsb", [128, KTM * W], F32)
        tg_h, tg_b = sbuf(nc, ctx, "tg", [128, W], F32)
        yg_h, yg_b = sbuf(nc, ctx, "yg", [128, KTI * W], BF16)
        mo_h, mo_b = sbuf(nc, ctx, "mo_bf", [128, KTM * W], BF16)
        sq_h, sq_b = sbuf(nc, ctx, "sq_bf", [128, KTM * W], BF16)
        x2sb_h, x2sb_b = sbuf(nc, ctx, "x2sb", [128, KTM * W], F32)
        x2bf_h, x2bf_b = sbuf(nc, ctx, "x2bf", [128, KTM * W], BF16)
        xn3_h, xn3_b = sbuf(nc, ctx, "xn3sb", [128, KTM * W], BF16)
        nm2_h, nm2_b = sbuf(nc, ctx, "nm2", [1, TC], BF16)
        rs2_h, rs2_b = sbuf(nc, ctx, "rs2", [1, TC], BF16)
        nm3_h, nm3_b = sbuf(nc, ctx, "nm3", [1, TC], BF16)
        rs3_h, rs3_b = sbuf(nc, ctx, "rs3", [1, TC], BF16)
        nmf_h, nmf_b = sbuf(nc, ctx, "nmf", [1, W], F32)
        rsf_h, rsf_b = sbuf(nc, ctx, "rsf", [1, W], F32)
        rtf_h, rtf_b = sbuf(nc, ctx, "rtf", [1, W], F32)
        lg_h, lg_b = sbuf(nc, ctx, "lg_bf", [8, W], BF16)
        lgT_h, lgT_b = sbuf(nc, ctx, "lgT", [128, 8], F32)
        r1_h, r1_b = sbuf(nc, ctx, "r1", [128, 1], F32)
        r2_h, r2_b = sbuf(nc, ctx, "r2", [128, 1], F32)
        ee_h, ee_b = sbuf(nc, ctx, "ee", [128, 8], F32)
        pp_h, pp_b = sbuf(nc, ctx, "ppb", [128, 8], F32)
        gx_h, gx_b = sbuf(nc, ctx, "gx", [128, 8], F32)
        cf_h, cf_b = sbuf(nc, ctx, "cfbf", [128, 8], BF16)
        cT_h, cT_b = sbuf(nc, ctx, "coefT", [8, W], BF16)
        pwo = [psum(nc, ctx, f"pwo{i}", [128, W]) for i in range(2)]
        pm_h, pm_b = psum(nc, ctx, "pm", [1, W])
        pq_h, pq_b = psum(nc, ctx, "pq", [1, W])
        pnm_h, pnm_b = psum(nc, ctx, "pnm", [128, W])
        prs_h, prs_b = psum(nc, ctx, "prs", [128, W])
        plg_h, plg_b = psum(nc, ctx, "plg", [8, W])
        ptb_h, ptb_b = psum(nc, ctx, "ptb", [128, 128], BF16)

        S.dma("sp", lambda e: e.dma_start(out=ones_h[:, :], in_=ones_d.ap()),
              writes=[ones_b], key="wload")
        S.dma("sp", lambda e: e.dma_start(out=ident_h[:, :], in_=ident_d.ap()),
              writes=[ident_b], key="wload")
        S.dma("sp", lambda e: e.dma_start(out=dsk_h[:, :], in_=dsk_d.ap()),
              writes=[dsk_b], key="wload")
        S.dma("pool", lambda e: e.dma_start(
            out=wout_h[:, :].rearrange("p (a m) -> p a m", a=KTI),
            in_=wout_d.ap().rearrange("(a p) m -> p a m", p=128)),
            writes=[wout_b], key="wloadp")
        S.dma("pool", lambda e: e.dma_start(
            out=rtr_h[:, :].rearrange("p (a m) -> p a m", a=KTM),
            in_=rtr_d.ap().rearrange("(a p) m -> p a m", p=128)),
            writes=[rtr_b], key="wloadp")

        for chn in range(NCH):
            c0 = chn * W
            for nm_, hh_, bb_, dd_, kt_, ky_, xo_ in (
                    ("y", ysb_h, ysb_b, y_dd, KTI, "yld", 0),
                    ("u", usb_h, usb_b, u_dd, KTI, "uld", 0),
                    ("z", zsb_h, zsb_b, zs_dd, KTI, "zld", 0),
                    ("x", xsb_h, xsb_b, x_dd, KTM, "xld", 4)):
                S.dma("sp", lambda e, hh_=hh_, dd_=dd_, kt_=kt_, c0=c0, xo_=xo_: e.dma_start(
                    out=hh_[:, 0:kt_ * W].rearrange("p (a t) -> p a t", a=kt_),
                    in_=dd_.ap()[:, xo_ + c0:xo_ + c0 + W].rearrange(
                        "(a p) t -> p a t", p=128)),
                    writes=[bb_], key=ky_)
            for k in range(KTI):
                sl = slice(k * W, (k + 1) * W)
                S.op("dve", lambda e, k=k, sl=sl: e.scalar_tensor_tensor(
                    out=tg_h[:, :], in0=usb_h[:, sl], scalar=dsk_h[:, k:k + 1],
                    in1=ysb_h[:, sl], op0=ALU.mult, op1=ALU.add),
                    reads=[usb_b, dsk_b, ysb_b], writes=[tg_b])
                S.op("dve", lambda e, k=k, sl=sl: e.tensor_tensor(
                    out=yg_h[:, sl], in0=tg_h[:, :], in1=zsb_h[:, sl],
                    op=ALU.mult), reads=[tg_b, zsb_b], writes=[yg_b])
            for mi in range(KTM):
                ph, pb = pwo[mi % 2]
                for k in range(KTI):
                    S.op("pe", lambda e, mi=mi, k=k, ph=ph: e.matmul(
                        ph[:, :], wout_h[:, k * D + mi * 128:k * D + mi * 128 + 128],
                        yg_h[:, k * W:(k + 1) * W],
                        start=(k == 0), stop=(k == KTI - 1)),
                        reads=[yg_b, wout_b], writes=[pb])
                msl = slice(mi * W, (mi + 1) * W)
                S.op("act", lambda e, ph=ph, msl=msl: e.activation(
                    out=mo_h[:, msl], in_=ph[:, :], func=AF.Copy),
                    reads=[pb], writes=[mo_b])
                S.op("act", lambda e, ph=ph, msl=msl: e.activation(
                    out=sq_h[:, msl], in_=ph[:, :], func=AF.Square),
                    reads=[pb], writes=[sq_b])
            for mi in range(KTM):
                msl = slice(mi * W, (mi + 1) * W)
                S.op("pe", lambda e, mi=mi, msl=msl: e.matmul(
                    pm_h[0:1, :], ones_h[:, 0:1], mo_h[:, msl],
                    start=(mi == 0), stop=(mi == KTM - 1)),
                    reads=[mo_b, ones_b], writes=[pm_b])
                S.op("pe", lambda e, mi=mi, msl=msl: e.matmul(
                    pq_h[0:1, :], ones_h[:, 0:1], sq_h[:, msl],
                    start=(mi == 0), stop=(mi == KTM - 1)),
                    reads=[sq_b, ones_b], writes=[pq_b])
            ln_rowmath(S, pm_h, pm_b, pq_h, pq_b, nmf_h, nmf_b, rsf_h, rsf_b,
                       rtf_h, rtf_b, 0, W, D)
            S.op("act", lambda e, c0=c0: e.activation(
                out=nm2_h[0:1, c0:c0 + W], in_=nmf_h[0:1, 0:W], func=AF.Copy),
                reads=[nmf_b], writes=[nm2_b])
            S.op("act", lambda e, c0=c0: e.activation(
                out=rs2_h[0:1, c0:c0 + W], in_=rsf_h[0:1, 0:W], func=AF.Copy),
                reads=[rsf_b], writes=[rs2_b])
            S.op("pe", lambda e, c0=c0: e.matmul(
                pnm_h[:, :], ones_h[0:1, :], nm2_h[0:1, c0:c0 + W],
                start=True, stop=True), reads=[nm2_b, ones_b], writes=[pnm_b])
            S.op("pe", lambda e, c0=c0: e.matmul(
                prs_h[:, :], ones_h[0:1, :], rs2_h[0:1, c0:c0 + W],
                start=True, stop=True), reads=[rs2_b, ones_b], writes=[prs_b])
            for mi in range(KTM):
                msl = slice(mi * W, (mi + 1) * W)
                S.op("dve", lambda e, msl=msl: e.tensor_tensor(
                    out=tg_h[:, :], in0=mo_h[:, msl], in1=pnm_h[:, :], op=ALU.add),
                    reads=[mo_b, pnm_b], writes=[tg_b])
                S.op("dve", lambda e, msl=msl: e.tensor_tensor(
                    out=tg_h[:, :], in0=tg_h[:, :], in1=prs_h[:, :], op=ALU.mult),
                    reads=[prs_b], writes=[tg_b])
                S.op("dve", lambda e, msl=msl: e.tensor_tensor(
                    out=x2sb_h[:, msl], in0=tg_h[:, :], in1=xsb_h[:, msl],
                    op=ALU.add), reads=[tg_b, xsb_b], writes=[x2sb_b])
                S.op("act", lambda e, msl=msl: e.activation(
                    out=x2bf_h[:, msl], in_=x2sb_h[:, msl], func=AF.Copy),
                    reads=[x2sb_b], writes=[x2bf_b])
                S.op("act", lambda e, msl=msl: e.activation(
                    out=sq_h[:, msl], in_=x2sb_h[:, msl], func=AF.Square),
                    reads=[x2sb_b], writes=[sq_b])
            S.dma("sp", lambda e, c0=c0: e.dma_start(
                out=x2_d.ap()[:, c0:c0 + W].rearrange("(a p) t -> p a t", p=128),
                in_=x2sb_h[:, :].rearrange("p (a t) -> p a t", a=KTM)),
                reads=[x2sb_b], key="ost")
            for mi in range(KTM):
                msl = slice(mi * W, (mi + 1) * W)
                S.op("pe", lambda e, msl=msl, mi=mi: e.matmul(
                    pm_h[0:1, :], ones_h[:, 0:1], x2bf_h[:, msl],
                    start=(mi == 0), stop=(mi == KTM - 1)),
                    reads=[x2bf_b, ones_b], writes=[pm_b])
                S.op("pe", lambda e, msl=msl, mi=mi: e.matmul(
                    pq_h[0:1, :], ones_h[:, 0:1], sq_h[:, msl],
                    start=(mi == 0), stop=(mi == KTM - 1)),
                    reads=[sq_b, ones_b], writes=[pq_b])
            ln_rowmath(S, pm_h, pm_b, pq_h, pq_b, nmf_h, nmf_b, rsf_h, rsf_b,
                       rtf_h, rtf_b, 0, W, D)
            S.op("act", lambda e, c0=c0: e.activation(
                out=nm3_h[0:1, c0:c0 + W], in_=nmf_h[0:1, 0:W], func=AF.Copy),
                reads=[nmf_b], writes=[nm3_b])
            S.op("act", lambda e, c0=c0: e.activation(
                out=rs3_h[0:1, c0:c0 + W], in_=rsf_h[0:1, 0:W], func=AF.Copy),
                reads=[rsf_b], writes=[rs3_b])
            S.op("pe", lambda e, c0=c0: e.matmul(
                pnm_h[:, :], ones_h[0:1, :], nm3_h[0:1, c0:c0 + W],
                start=True, stop=True), reads=[nm3_b, ones_b], writes=[pnm_b])
            S.op("pe", lambda e, c0=c0: e.matmul(
                prs_h[:, :], ones_h[0:1, :], rs3_h[0:1, c0:c0 + W],
                start=True, stop=True), reads=[rs3_b, ones_b], writes=[prs_b])
            for mi in range(KTM):
                msl = slice(mi * W, (mi + 1) * W)
                S.op("dve", lambda e, msl=msl: e.tensor_tensor(
                    out=tg_h[:, :], in0=x2sb_h[:, msl], in1=pnm_h[:, :],
                    op=ALU.add), reads=[x2sb_b, pnm_b], writes=[tg_b])
                S.op("dve", lambda e, msl=msl: e.tensor_tensor(
                    out=xn3_h[:, msl], in0=tg_h[:, :], in1=prs_h[:, :],
                    op=ALU.mult), reads=[tg_b, prs_b], writes=[xn3_b])
            S.dma("sp", lambda e, c0=c0: e.dma_start(
                out=xn3_d.ap()[:, c0:c0 + W].rearrange("(a p) t -> p a t", p=128),
                in_=xn3_h[:, :].rearrange("p (a t) -> p a t", a=KTM)),
                reads=[xn3_b], key="ost")
            for k in range(KTM):
                S.op("pe", lambda e, k=k: e.matmul(
                    plg_h[0:8, :], rtr_h[:, k * NE:(k + 1) * NE],
                    xn3_h[:, k * W:(k + 1) * W],
                    start=(k == 0), stop=(k == KTM - 1)),
                    reads=[xn3_b, rtr_b], writes=[plg_b])
            S.op("act", lambda e: e.activation(
                out=lg_h[0:8, :], in_=plg_h[0:8, :], func=AF.Copy),
                reads=[plg_b], writes=[lg_b])
            for j in range(4):
                S.op("pe", lambda e, j=j: e.transpose(
                    ptb_h[:, 0:8], lg_h[0:8, j * 128:(j + 1) * 128],
                    ident_h[0:8, 0:8]), reads=[lg_b, ident_b], writes=[ptb_b])
                S.op("act", lambda e: e.activation(
                    out=lgT_h[:, :], in_=ptb_h[:, 0:8], func=AF.Copy),
                    reads=[ptb_b], writes=[lgT_b])
                S.op("dve", lambda e: e.reduce_max(
                    out=r1_h[:, :], in_=lgT_h[:, :], axis=mybir.AxisListType.X),
                    reads=[lgT_b], writes=[r1_b])
                S.op("dve", lambda e: e.tensor_scalar(
                    out=r1_h[:, :], in0=r1_h[:, :], scalar1=-1.0, scalar2=None,
                    op0=ALU.mult), writes=[r1_b])
                S.op("act", lambda e: e.activation(
                    out=ee_h[:, :], in_=lgT_h[:, :], func=AF.Exp,
                    bias=r1_h[:, 0:1]), reads=[lgT_b, r1_b], writes=[ee_b])
                S.op("dve", lambda e: e.reduce_sum(
                    out=r2_h[:, :], in_=ee_h[:, :], axis=mybir.AxisListType.X),
                    reads=[ee_b], writes=[r2_b])
                S.op("dve", lambda e: e.reciprocal(out=r2_h[:, :], in_=r2_h[:, :]),
                     writes=[r2_b])
                S.op("dve", lambda e: e.tensor_scalar(
                    out=pp_h[:, :], in0=ee_h[:, :], scalar1=r2_h[:, 0:1],
                    scalar2=None, op0=ALU.mult),
                    reads=[ee_b, r2_b], writes=[pp_b])
                S.op("dve", lambda e: e.reduce_max(
                    out=r1_h[:, :], in_=pp_h[:, :], axis=mybir.AxisListType.X),
                    reads=[pp_b], writes=[r1_b])
                S.op("dve", lambda e: e.tensor_scalar(
                    out=gx_h[:, :], in0=pp_h[:, :], scalar1=r1_h[:, 0:1],
                    scalar2=None, op0=ALU.is_ge),
                    reads=[pp_b, r1_b], writes=[gx_b])
                S.op("dve", lambda e: e.tensor_tensor(
                    out=gx_h[:, :], in0=pp_h[:, :], in1=gx_h[:, :], op=ALU.mult),
                    reads=[pp_b], writes=[gx_b])
                S.op("dve", lambda e: e.tensor_tensor(
                    out=gx_h[:, :], in0=pp_h[:, :], in1=gx_h[:, :],
                    op=ALU.subtract), reads=[pp_b], writes=[gx_b])
                S.op("dve", lambda e: e.reduce_max(
                    out=r2_h[:, :], in_=gx_h[:, :], axis=mybir.AxisListType.X),
                    reads=[gx_b], writes=[r2_b])
                S.op("dve", lambda e: e.tensor_scalar(
                    out=gx_h[:, :], in0=pp_h[:, :], scalar1=r2_h[:, 0:1],
                    scalar2=None, op0=ALU.is_ge),
                    reads=[pp_b, r2_b], writes=[gx_b])
                S.op("dve", lambda e: e.tensor_tensor(
                    out=pp_h[:, :], in0=pp_h[:, :], in1=gx_h[:, :], op=ALU.mult),
                    reads=[gx_b], writes=[pp_b])
                S.op("act", lambda e: e.activation(
                    out=cf_h[:, :], in_=pp_h[:, :], func=AF.Copy),
                    reads=[pp_b], writes=[cf_b])
                S.op("pe", lambda e: e.transpose(
                    ptb_h[0:8, :], cf_h[:, 0:8], ident_h[:, :]),
                    reads=[cf_b, ident_b], writes=[ptb_b])
                S.op("act", lambda e, j=j: e.activation(
                    out=cT_h[0:8, j * 128:(j + 1) * 128], in_=ptb_h[0:8, :],
                    func=AF.Copy), reads=[ptb_b], writes=[cT_b])
            S.dma("sp", lambda e, c0=c0: e.dma_start(
                out=bass.AP(coef_d, c0, [[TC, 8], [1, W]]), in_=cT_h[0:8, :]),
                reads=[cT_b], key="ost")


def stage_l3b(nc, S, sems, E, nchunks=None):
    W = 512
    NCH = TC // W if nchunks is None else nchunks
    KTM = D // 128
    KTF = DFF // 128
    xn3_d = E["xn3_scr"]
    coef_d = E["coef_scr"]
    x2_d = E["x2_scr"]
    weg_d = E["weg"]
    weu_d = E["weu"]
    wed_d = E["wed"]
    wsg_d = E["wsg"]
    wsu_d = E["wsu"]
    wsd_d = E["wsd"]
    ones_d = E["ones_in"]
    out_d = E["out"]
    ffn_d = E["ffn_scr"]

    with ExitStack() as ctx:
        ones_h, ones_b = sbuf(nc, ctx, "ones_sb", [128, 128], BF16)
        wsg_h, wsg_b = sbuf(nc, ctx, "wsg_bf", [128, KTM * D], BF16)
        wsu_h, wsu_b = sbuf(nc, ctx, "wsu_bf", [128, KTM * D], BF16)
        wsd_h, wsd_b = sbuf(nc, ctx, "wsd_bf", [128, KTM * D], BF16)
        wg_h, wg_b = sbuf(nc, ctx, "weg0", [128, KTM * DFF], BF16)
        wu_h, wu_b = sbuf(nc, ctx, "weu0", [128, KTM * DFF], BF16)
        wd_h, wd_b = sbuf(nc, ctx, "wed0", [128, KTF * D], BF16)
        xn3_h, xn3_b = sbuf(nc, ctx, "xn3sb", [128, KTM * W], BF16)
        coef_h, coef_b = sbuf(nc, ctx, "coefsb", [1, NE * W], BF16)
        gsb_h, gsb_b = sbuf(nc, ctx, "gsb", [128, KTF * W], BF16)
        usb_h, usb_b = sbuf(nc, ctx, "uub", [128, KTF * W], BF16)
        cme_h, cme_b = sbuf(nc, ctx, "cme", [128, KTF * W], BF16)
        sm_h, sm_b = sbuf(nc, ctx, "smb", [128, KTM * W], BF16)
        rsb_h, rsb_b = sbuf(nc, ctx, "routed", [128, KTM * W], F32)
        ffn_h, ffn_b = sbuf(nc, ctx, "ffnb", [128, KTM * W], BF16)
        sq_h, sq_b = sbuf(nc, ctx, "sqb", [128, KTM * W], BF16)
        m4_h, m4_b = sbuf(nc, ctx, "m4row", [1, TC], BF16)
        q4_h, q4_b = sbuf(nc, ctx, "q4row", [1, TC], BF16)
        nm4_h, nm4_b = sbuf(nc, ctx, "nm4", [1, TC], BF16)
        rs4_h, rs4_b = sbuf(nc, ctx, "rs4", [1, TC], BF16)
        rtf_h, rtf_b = sbuf(nc, ctx, "rtf", [1, W], F32)
        x2sb_h, x2sb_b = sbuf(nc, ctx, "x2sb", [128, KTM * W], F32)
        tg_h, tg_b = sbuf(nc, ctx, "tg", [128, W], F32)
        pg_h, pg_b = psum(nc, ctx, "pg", [128, W])
        pu_h, pu_b = psum(nc, ctx, "pu", [128, W])
        pc_h, pc_b = psum(nc, ctx, "pc", [128, W])
        pd_h, pd_b = psum(nc, ctx, "pd", [128, W])
        pm_h, pm_b = psum(nc, ctx, "pm", [1, W])
        pq_h, pq_b = psum(nc, ctx, "pq", [1, W])
        pnm_h, pnm_b = psum(nc, ctx, "pnm", [128, W])
        prs_h, prs_b = psum(nc, ctx, "prs", [128, W])

        S.dma("pool", lambda e: e.dma_start(
            out=wsg_h[:, :].rearrange("p (a m) -> p a m", a=KTM),
            in_=wsg_d.ap().rearrange("(a p) m -> p a m", p=128)),
            writes=[wsg_b], key="wloadp")
        S.dma("pool", lambda e: e.dma_start(
            out=wsu_h[:, :].rearrange("p (a m) -> p a m", a=KTM),
            in_=wsu_d.ap().rearrange("(a p) m -> p a m", p=128)),
            writes=[wsu_b], key="wloadp")
        S.dma("pool", lambda e: e.dma_start(
            out=wsd_h[:, :].rearrange("p (a m) -> p a m", a=KTM),
            in_=wsd_d.ap().rearrange("(a p) m -> p a m", p=128)),
            writes=[wsd_b], key="wloadp")
        S.dma("sp", lambda e: e.dma_start(out=ones_h[:, :], in_=ones_d.ap()),
              writes=[ones_b], key="wload")

        for chn in range(NCH):
            c0 = chn * W
            S.dma("sp", lambda e, c0=c0: e.dma_start(
                out=xn3_h[:, :].rearrange("p (a t) -> p a t", a=KTM),
                in_=xn3_d.ap()[:, c0:c0 + W].rearrange("(a p) t -> p a t", p=128)),
                writes=[xn3_b], key="xnld")
            S.dma("sp", lambda e, c0=c0: e.dma_start(
                out=coef_h[:, :].rearrange("p (a t) -> p a t", a=NE),
                in_=bass.AP(coef_d, c0, [[1, 1], [TC, NE], [1, W]])),
                writes=[coef_b], key="cfld")
            for ei in range(NE):
                S.dma("pool", lambda e, ei=ei: e.dma_start(
                    out=wg_h[:, :].rearrange("p (a m) -> p a m", a=KTM),
                    in_=weg_d.ap()[ei * D:(ei + 1) * D, :].rearrange(
                        "(a p) m -> p a m", p=128)), writes=[wg_b], key="we0")
                S.dma("pool", lambda e, ei=ei: e.dma_start(
                    out=wu_h[:, :].rearrange("p (a m) -> p a m", a=KTM),
                    in_=weu_d.ap()[ei * D:(ei + 1) * D, :].rearrange(
                        "(a p) m -> p a m", p=128)), writes=[wu_b], key="we0")
                S.dma("pool", lambda e, ei=ei: e.dma_start(
                    out=wd_h[:, :].rearrange("p (a m) -> p a m", a=KTF),
                    in_=wed_d.ap()[ei * DFF:(ei + 1) * DFF, :].rearrange(
                        "(a p) m -> p a m", p=128)), writes=[wd_b], key="we0")
                S.op("pe", lambda e, ei=ei: e.matmul(
                    pc_h[:, :], ones_h[0:1, :],
                    coef_h[0:1, ei * W:(ei + 1) * W],
                    start=True, stop=True), reads=[coef_b, ones_b], writes=[pc_b])
                for mi2 in range(KTF):
                    fsl = slice(mi2 * W, (mi2 + 1) * W)
                    for k in range(KTM):
                        S.op("pe", lambda e, mi2=mi2, k=k: e.matmul(
                            pg_h[:, :], wg_h[:, k * DFF + mi2 * 128:k * DFF + mi2 * 128 + 128],
                            xn3_h[:, k * W:(k + 1) * W],
                            start=(k == 0), stop=(k == KTM - 1)),
                            reads=[xn3_b, wg_b], writes=[pg_b])
                    S.op("act", lambda e, fsl=fsl: e.activation(
                        out=gsb_h[:, fsl], in_=pg_h[:, :], func=AF.Silu),
                        reads=[pg_b], writes=[gsb_b])
                    for k in range(KTM):
                        S.op("pe", lambda e, mi2=mi2, k=k: e.matmul(
                            pu_h[:, :], wu_h[:, k * DFF + mi2 * 128:k * DFF + mi2 * 128 + 128],
                            xn3_h[:, k * W:(k + 1) * W],
                            start=(k == 0), stop=(k == KTM - 1)),
                            reads=[xn3_b, wu_b], writes=[pu_b])
                    S.op("act", lambda e, fsl=fsl: e.activation(
                        out=usb_h[:, fsl], in_=pu_h[:, :], func=AF.Copy),
                        reads=[pu_b], writes=[usb_b])
                    S.op("dve", lambda e, fsl=fsl: e.tensor_tensor(
                        out=gsb_h[:, fsl], in0=gsb_h[:, fsl], in1=usb_h[:, fsl],
                        op=ALU.mult), reads=[usb_b], writes=[gsb_b])
                    S.op("dve", lambda e, fsl=fsl: e.tensor_tensor(
                        out=cme_h[:, fsl], in0=gsb_h[:, fsl], in1=pc_h[:, :],
                        op=ALU.mult), reads=[gsb_b, pc_b], writes=[cme_b])
                for mo in range(KTM):
                    for k in range(KTF):
                        S.op("pe", lambda e, mo=mo, k=k: e.matmul(
                            pd_h[:, :], wd_h[:, k * D + mo * 128:k * D + mo * 128 + 128],
                            cme_h[:, k * W:(k + 1) * W],
                            start=(k == 0), stop=(k == KTF - 1)),
                            reads=[cme_b, wd_b], writes=[pd_b])
                    msl = slice(mo * W, (mo + 1) * W)
                    if ei == 0:
                        S.op("act", lambda e, msl=msl: e.activation(
                            out=rsb_h[:, msl], in_=pd_h[:, :], func=AF.Copy),
                            reads=[pd_b], writes=[rsb_b])
                    else:
                        S.op("dve", lambda e, msl=msl: e.tensor_tensor(
                            out=rsb_h[:, msl], in0=rsb_h[:, msl], in1=pd_h[:, :],
                            op=ALU.add), reads=[pd_b], writes=[rsb_b])
            for mi in range(KTM):
                msl = slice(mi * W, (mi + 1) * W)
                for k in range(KTM):
                    S.op("pe", lambda e, mi=mi, k=k: e.matmul(
                        pg_h[:, :], wsg_h[:, k * D + mi * 128:k * D + mi * 128 + 128],
                        xn3_h[:, k * W:(k + 1) * W],
                        start=(k == 0), stop=(k == KTM - 1)),
                        reads=[xn3_b, wsg_b], writes=[pg_b])
                S.op("act", lambda e, msl=msl: e.activation(
                    out=sq_h[:, msl], in_=pg_h[:, :], func=AF.Silu),
                    reads=[pg_b], writes=[sq_b])
                for k in range(KTM):
                    S.op("pe", lambda e, mi=mi, k=k: e.matmul(
                        pu_h[:, :], wsu_h[:, k * D + mi * 128:k * D + mi * 128 + 128],
                        xn3_h[:, k * W:(k + 1) * W],
                        start=(k == 0), stop=(k == KTM - 1)),
                        reads=[xn3_b, wsu_b], writes=[pu_b])
                S.op("act", lambda e, msl=msl: e.activation(
                    out=ffn_h[:, msl], in_=pu_h[:, :], func=AF.Copy),
                    reads=[pu_b], writes=[ffn_b])
                S.op("dve", lambda e, msl=msl: e.tensor_tensor(
                    out=sm_h[:, msl], in0=sq_h[:, msl], in1=ffn_h[:, msl],
                    op=ALU.mult), reads=[sq_b, ffn_b], writes=[sm_b])
            for mo in range(KTM):
                msl = slice(mo * W, (mo + 1) * W)
                for k in range(KTM):
                    S.op("pe", lambda e, mo=mo, k=k: e.matmul(
                        pd_h[:, :], wsd_h[:, k * D + mo * 128:k * D + mo * 128 + 128],
                        sm_h[:, k * W:(k + 1) * W],
                        start=(k == 0), stop=(k == KTM - 1)),
                        reads=[sm_b, wsd_b], writes=[pd_b])
                S.op("dve", lambda e, msl=msl: e.tensor_tensor(
                    out=rsb_h[:, msl], in0=rsb_h[:, msl], in1=pd_h[:, :],
                    op=ALU.add), reads=[pd_b], writes=[rsb_b])
                S.op("act", lambda e, msl=msl: e.activation(
                    out=ffn_h[:, msl], in_=rsb_h[:, msl], func=AF.Copy),
                    reads=[rsb_b], writes=[ffn_b])
                S.op("act", lambda e, msl=msl: e.activation(
                    out=sq_h[:, msl], in_=rsb_h[:, msl], func=AF.Square),
                    reads=[rsb_b], writes=[sq_b])
            for mo in range(KTM):
                msl = slice(mo * W, (mo + 1) * W)
                S.op("pe", lambda e, msl=msl, mo=mo: e.matmul(
                    pm_h[0:1, :], ones_h[:, 0:1], ffn_h[:, msl],
                    start=(mo == 0), stop=(mo == KTM - 1)),
                    reads=[ffn_b, ones_b], writes=[pm_b])
                S.op("pe", lambda e, msl=msl, mo=mo: e.matmul(
                    pq_h[0:1, :], ones_h[:, 0:1], sq_h[:, msl],
                    start=(mo == 0), stop=(mo == KTM - 1)),
                    reads=[sq_b, ones_b], writes=[pq_b])
            S.op("act", lambda e, c0=c0: e.activation(
                out=m4_h[0:1, c0:c0 + W], in_=pm_h[0:1, :], func=AF.Copy),
                reads=[pm_b], writes=[m4_b])
            S.op("act", lambda e, c0=c0: e.activation(
                out=q4_h[0:1, c0:c0 + W], in_=pq_h[0:1, :], func=AF.Copy),
                reads=[pq_b], writes=[q4_b])
            S.dma("sp", lambda e, c0=c0: e.dma_start(
                out=ffn_d.ap()[:, c0:c0 + W].rearrange("(a p) t -> p a t", p=128),
                in_=ffn_h[:, :].rearrange("p (a t) -> p a t", a=KTM)),
                reads=[ffn_b], key="ffst")

        for chn in range(NCH):
            c0 = chn * W
            ln_rowmath_sb(S, m4_h, m4_b, q4_h, q4_b, nm4_h, nm4_b, rs4_h, rs4_b,
                          rtf_h, rtf_b, c0, c0, W, D)
        for chn in range(NCH):
            c0 = chn * W
            S.dma("sp", lambda e, c0=c0: e.dma_start(
                out=x2sb_h[:, :].rearrange("p (a t) -> p a t", a=KTM),
                in_=x2_d.ap()[:, c0:c0 + W].rearrange("(a p) t -> p a t", p=128)),
                writes=[x2sb_b], key="x2ld")
            S.dma("sp", lambda e, c0=c0: e.dma_start(
                out=ffn_h[:, :].rearrange("p (a t) -> p a t", a=KTM),
                in_=ffn_d.ap()[:, c0:c0 + W].rearrange("(a p) t -> p a t", p=128)),
                writes=[ffn_b], key="ffld")
            S.op("pe", lambda e, c0=c0: e.matmul(
                pnm_h[:, :], ones_h[0:1, :], nm4_h[0:1, c0:c0 + W],
                start=True, stop=True), reads=[nm4_b, ones_b], writes=[pnm_b])
            S.op("pe", lambda e, c0=c0: e.matmul(
                prs_h[:, :], ones_h[0:1, :], rs4_h[0:1, c0:c0 + W],
                start=True, stop=True), reads=[rs4_b, ones_b], writes=[prs_b])
            for mi in range(KTM):
                msl = slice(mi * W, (mi + 1) * W)
                S.op("dve", lambda e, msl=msl: e.tensor_tensor(
                    out=tg_h[:, :], in0=ffn_h[:, msl], in1=pnm_h[:, :],
                    op=ALU.add), reads=[ffn_b, pnm_b], writes=[tg_b])
                S.op("dve", lambda e, msl=msl: e.tensor_tensor(
                    out=tg_h[:, :], in0=tg_h[:, :], in1=prs_h[:, :], op=ALU.mult),
                    reads=[prs_b], writes=[tg_b])
                S.op("dve", lambda e, msl=msl: e.tensor_tensor(
                    out=x2sb_h[:, msl], in0=tg_h[:, :], in1=x2sb_h[:, msl],
                    op=ALU.add), reads=[tg_b], writes=[x2sb_b])
            S.dma("sp", lambda e, c0=c0: e.dma_start(
                out=out_d.ap()[:, c0:c0 + W].rearrange("(a p) t -> p a t", p=128),
                in_=x2sb_h[:, :].rearrange("p (a t) -> p a t", a=KTM)),
                reads=[x2sb_b], key="ost")




def build_l3():
    nc = bass.Bass("TRN2", target_bir_lowering=False)
    E = {}
    E["y_my"] = nc.dram_tensor("y", [DI, TC], BF16, kind="ExternalInput")
    E["u_scr"] = nc.dram_tensor("u", [DI, TC], BF16, kind="ExternalInput")
    E["zs_scr"] = nc.dram_tensor("zs", [DI, TC], BF16, kind="ExternalInput")
    E["x_nohalo"] = nc.dram_tensor("x", [D, TC + 4], F32, kind="ExternalInput")
    E["dskip"] = nc.dram_tensor("dskip", [128, DI // 128], F32, kind="ExternalInput")
    E["wout"] = nc.dram_tensor("wout", [DI, D], F32, kind="ExternalInput")
    E["router"] = nc.dram_tensor("router", [D, NE], F32, kind="ExternalInput")
    E["weg"] = nc.dram_tensor("weg", [NE * D, DFF], F32, kind="ExternalInput")
    E["weu"] = nc.dram_tensor("weu", [NE * D, DFF], F32, kind="ExternalInput")
    E["wed"] = nc.dram_tensor("wed", [NE * DFF, D], F32, kind="ExternalInput")
    E["wsg"] = nc.dram_tensor("wsg", [D, D], F32, kind="ExternalInput")
    E["wsu"] = nc.dram_tensor("wsu", [D, D], F32, kind="ExternalInput")
    E["wsd"] = nc.dram_tensor("wsd", [D, D], F32, kind="ExternalInput")
    E["ones_in"] = nc.dram_tensor("ones", [128, 128], BF16, kind="ExternalInput")
    E["ident_in"] = nc.dram_tensor("ident", [128, 128], BF16, kind="ExternalInput")
    E["out"] = nc.dram_tensor("out", [D, TC], F32, kind="ExternalOutput")
    E["x2_scr"] = nc.dram_tensor("x2_scr", [D, TC], F32, kind="Internal")
    E["xn3_scr"] = nc.dram_tensor("xn3_scr", [D, TC], BF16, kind="Internal")
    E["coef_scr"] = nc.dram_tensor("coef_scr", [1, NE * TC], BF16, kind="Internal")
    E["ffn_scr"] = nc.dram_tensor("ffn_scr", [D, TC], BF16, kind="Internal")
    with ExitStack() as ctx:
        S = Sched(nc)
        _PREFIX[0] = "c_"
        stage_l3a(nc, S, None, E)
        S.barrier()
        _PREFIX[0] = "d_"
        stage_l3b(nc, S, None, E)
        _PREFIX[0] = ""
        sems = S.create_sems(nc, ctx)
        block = ctx.enter_context(nc.Block())
        S.emit(block, sems)
    return nc


def build_l1_only():
    nc = bass.Bass("TRN2", target_bir_lowering=False)
    E = {}
    E["x"] = nc.dram_tensor("x", [D, TC + 4], F32, kind="ExternalInput")
    E["w_in"] = nc.dram_tensor("w_in", [D, 2 * DI], F32, kind="ExternalInput")
    E["w_xp"] = nc.dram_tensor("w_xp", [DI, 96], F32, kind="ExternalInput")
    E["conv_w"] = nc.dram_tensor("conv_w", [128, (DI // 128) * DCONV], F32, kind="ExternalInput")
    E["conv_b"] = nc.dram_tensor("conv_b", [128, DI // 128], F32, kind="ExternalInput")
    E["ones_in"] = nc.dram_tensor("ones", [128, 128], BF16, kind="ExternalInput")
    E["u_scr"] = nc.dram_tensor("u", [DI, TC], BF16, kind="ExternalOutput")
    E["zs_scr"] = nc.dram_tensor("zs", [DI, TC], BF16, kind="ExternalOutput")
    E["xdbl_scr"] = nc.dram_tensor("xdbl", [96, TC], F32, kind="ExternalOutput")
    with ExitStack() as ctx:
        S = Sched(nc)
        _PREFIX[0] = "a_"
        stage_l1(nc, S, None, E)
        _PREFIX[0] = ""
        sems = S.create_sems(nc, ctx)
        block = ctx.enter_context(nc.Block())
        S.emit(block, sems)
    return nc


def build_l2_only():
    nc = bass.Bass("TRN2", target_bir_lowering=False)
    E = {}
    E["u_my"] = nc.dram_tensor("u", [DI, TC], BF16, kind="ExternalInput")
    E["xdbl_ag"] = nc.dram_tensor("xdbl", [NC * 96, TC], F32, kind="ExternalInput")
    E["wdt"] = nc.dram_tensor("wdt", [DTR, DDI], F32, kind="ExternalInput")
    E["bdt"] = nc.dram_tensor("bdt", [128, DDI // 128], F32, kind="ExternalInput")
    E["alog"] = nc.dram_tensor("alog", [DDI, DS], F32, kind="ExternalInput")
    E["ones_in"] = nc.dram_tensor("ones", [128, 128], BF16, kind="ExternalInput")
    E["ident_in"] = nc.dram_tensor("ident", [128, 128], BF16, kind="ExternalInput")
    E["y_scr"] = nc.dram_tensor("y", [DI, TC], BF16, kind="ExternalOutput")
    with ExitStack() as ctx:
        S = Sched(nc)
        _PREFIX[0] = "b_"
        stage_l2(nc, S, None, E)
        _PREFIX[0] = ""
        sems = S.create_sems(nc, ctx)
        block = ctx.enter_context(nc.Block())
        S.emit(block, sems)
    return nc


RG = [[0, 1, 2, 3, 4, 5, 6, 7]]


def build_full():
    nc = bass.Bass("TRN2", target_bir_lowering=False)
    E = {}
    E["x"] = nc.dram_tensor("x", [D, TC + 4], F32, kind="ExternalInput")
    E["x_nohalo"] = E["x"]
    E["w_in"] = nc.dram_tensor("w_in", [D, 2 * DI], F32, kind="ExternalInput")
    E["w_xp"] = nc.dram_tensor("w_xp", [DI, 96], F32, kind="ExternalInput")
    E["conv_w"] = nc.dram_tensor("conv_w", [128, (DI // 128) * DCONV], F32, kind="ExternalInput")
    E["conv_b"] = nc.dram_tensor("conv_b", [128, DI // 128], F32, kind="ExternalInput")
    E["wdt"] = nc.dram_tensor("wdt", [DTR, DDI], F32, kind="ExternalInput")
    E["bdt"] = nc.dram_tensor("bdt", [128, DDI // 128], F32, kind="ExternalInput")
    E["alog"] = nc.dram_tensor("alog", [DDI, DS], F32, kind="ExternalInput")
    E["dskip"] = nc.dram_tensor("dskip", [128, DI // 128], F32, kind="ExternalInput")
    E["wout"] = nc.dram_tensor("wout", [DI, D], F32, kind="ExternalInput")
    E["router"] = nc.dram_tensor("router", [D, NE], F32, kind="ExternalInput")
    E["weg"] = nc.dram_tensor("weg", [NE * D, DFF], F32, kind="ExternalInput")
    E["weu"] = nc.dram_tensor("weu", [NE * D, DFF], F32, kind="ExternalInput")
    E["wed"] = nc.dram_tensor("wed", [NE * DFF, D], F32, kind="ExternalInput")
    E["wsg"] = nc.dram_tensor("wsg", [D, D], F32, kind="ExternalInput")
    E["wsu"] = nc.dram_tensor("wsu", [D, D], F32, kind="ExternalInput")
    E["wsd"] = nc.dram_tensor("wsd", [D, D], F32, kind="ExternalInput")
    E["ones_in"] = nc.dram_tensor("ones", [128, 128], BF16, kind="ExternalInput")
    E["ident_in"] = nc.dram_tensor("ident", [128, 128], BF16, kind="ExternalInput")
    E["out"] = nc.dram_tensor("out", [D, TC], F32, kind="ExternalOutput")
    E["u_scr"] = nc.dram_tensor("u_scr", [DI, TC], BF16, kind="Internal")
    E["zs_scr"] = nc.dram_tensor("zs_scr", [DI, TC], BF16, kind="Internal")
    E["xdbl_scr"] = nc.dram_tensor("xdbl_scr", [96, TC], F32, kind="Internal")
    E["u_ag"] = nc.dram_tensor("u_ag", [NC * DI, TC], BF16, kind="Internal")
    E["cofs_in"] = nc.dram_tensor("cofs", [1, 1], mybir.dt.int32, kind="ExternalInput")
    E["xdbl_ag"] = nc.dram_tensor("xdbl_ag", [NC * 96, TC], F32, kind="Internal")
    E["y_scr"] = nc.dram_tensor("y_scr", [DI, TC], BF16, kind="Internal")
    E["y_ag"] = nc.dram_tensor("y_ag", [NC * DI, TC], BF16, kind="Internal")
    E["u_my"] = nc.dram_tensor("u_my", [DI, TC], BF16, kind="Internal")
    E["y_my"] = nc.dram_tensor("y_my", [DI, TC], BF16, kind="Internal")
    E["x2_scr"] = nc.dram_tensor("x2_scr", [D, TC], F32, kind="Internal")
    E["xn3_scr"] = nc.dram_tensor("xn3_scr", [D, TC], BF16, kind="Internal")
    E["coef_scr"] = nc.dram_tensor("coef_scr", [1, NE * TC], BF16, kind="Internal")
    E["ffn_scr"] = nc.dram_tensor("ffn_scr", [D, TC], BF16, kind="Internal")

    with ExitStack() as ctx:
        sems = None
        S = Sched(nc)
        cofs_sb = ctx.enter_context(nc.sbuf_tensor("cofs_sb", [1, 1], mybir.dt.int32))
        cofs_buf = Buf("cofs")
        E["cofs"] = {}

        def _cofs_setup(e, E=E, cofs_sb=cofs_sb):
            reg = e.alloc_register("cofs_reg")
            inst = e.reg_load(reg, cofs_sb[0:1, 0:1])
            E["cofs"]["v"] = e.snap(reg)
            return inst

        S.dma("pool", lambda e: e.dma_start(out=cofs_sb[0:1, 0:1],
                                            in_=E["cofs_in"].ap()),
              writes=[cofs_buf], key="wloadp")
        S.op("pool", _cofs_setup, reads=[cofs_buf])
        _PREFIX[0] = "a_"
        stage_l1(nc, S, sems, E)
        S.barrier()
        b_uag, b_umy = Buf("u_ag"), Buf("u_my")
        S.dma("pool", lambda e: e.collective_compute(
            "AllGather", ALU.bypass, replica_groups=RG,
            ins=[E["xdbl_scr"].ap()], outs=[E["xdbl_ag"].ap()]), key="coll")
        S.dma("pool", lambda e: e.collective_compute(
            "AllGather", ALU.bypass, replica_groups=RG,
            ins=[E["u_scr"].ap()], outs=[E["u_ag"].ap()]), key="coll",
            writes=[b_uag])
        for rr in range(NC):
            S.dma("pool", lambda e, rr=rr: e.dma_start(
                out=E["u_my"].ap()[rr * DDI:(rr + 1) * DDI, :],
                in_=E["u_ag"].ap()[bass.ds(E["cofs"]["v"] + rr * DI, DDI), :]),
                key="uld", reads=[b_uag], writes=[b_umy])
        S.barrier()
        _PREFIX[0] = "b_"
        stage_l2(nc, S, sems, E)
        S.barrier()
        b_yag, b_ymy = Buf("y_ag"), Buf("y_my")
        S.dma("pool", lambda e: e.collective_compute(
            "AllGather", ALU.bypass, replica_groups=RG,
            ins=[E["y_scr"].ap()], outs=[E["y_ag"].ap()]), key="coll",
            writes=[b_yag])
        for j in range(NC):
            S.dma("pool", lambda e, j=j: e.dma_start(
                out=E["y_my"].ap()[j * DDI:(j + 1) * DDI, :],
                in_=E["y_ag"].ap()[bass.ds(E["cofs"]["v"] + j * DI, DDI), :]),
                key="yld", reads=[b_yag], writes=[b_ymy])
        S.barrier()
        _PREFIX[0] = "c_"
        stage_l3a(nc, S, sems, E)
        S.barrier()
        _PREFIX[0] = "d_"
        stage_l3b(nc, S, sems, E)
        _PREFIX[0] = ""
        sems = S.create_sems(nc, ctx)
        block = ctx.enter_context(nc.Block())
        S.emit(block, sems)
    return nc


_FULL = [None, None, None]


def _prep_common(inputs):
    p = {}
    p["w_in"] = np.ascontiguousarray(_np(inputs["L0_attn_W_in"]), np.float32)
    p["w_xp"] = np.ascontiguousarray(_np(inputs["L0_attn_W_xproj"]), np.float32)
    p["cw"] = np.ascontiguousarray(_np(inputs["L0_attn_conv_w"]).reshape(DI // 128, 128, DCONV)
                                   .transpose(1, 0, 2).reshape(128, -1), np.float32)
    p["cb"] = np.ascontiguousarray(_np(inputs["L0_attn_conv_b"]).reshape(DI // 128, 128).T,
                                   np.float32)
    p["wdt"] = np.asarray(inputs["L0_attn_W_dt"], np.float32)
    p["bdt"] = np.asarray(inputs["L0_attn_b_dt"], np.float32)
    p["alog"] = np.asarray(inputs["L0_attn_A_log"], np.float32)
    p["dskip"] = np.ascontiguousarray(
        np.asarray(inputs["L0_attn_D_skip"], np.float32).reshape(DI // 128, 128).T)
    p["wout"] = np.ascontiguousarray(_np(inputs["L0_attn_W_out"]), np.float32)
    p["router"] = np.ascontiguousarray(_np(inputs["L0_ffn_router"]), np.float32)
    p["weg"] = np.ascontiguousarray(_np(inputs["L0_ffn_experts_gate"]).reshape(NE * D, DFF), np.float32)
    p["weu"] = np.ascontiguousarray(_np(inputs["L0_ffn_experts_up"]).reshape(NE * D, DFF), np.float32)
    p["wed"] = np.ascontiguousarray(_np(inputs["L0_ffn_experts_down"]).reshape(NE * DFF, D), np.float32)
    p["wsg"] = np.ascontiguousarray(_np(inputs["L0_ffn_shared_gate"]), np.float32)
    p["wsu"] = np.ascontiguousarray(_np(inputs["L0_ffn_shared_up"]), np.float32)
    p["wsd"] = np.ascontiguousarray(_np(inputs["L0_ffn_shared_down"]), np.float32)
    p["ones"] = np.ones((128, 128), BF)
    p["ident"] = np.eye(128).astype(BF)
    return p


def kernel(**inputs):
    if _FULL[0] is None:
        _FULL[0] = build_l1_only()
        _FULL[1] = build_l2_only()
        _FULL[2] = build_l3()
    p = _prep_common(inputs)
    x = _np(inputs["L0_attn_x"])
    xs_all = []
    for c in range(NC):
        b, s = c // 4, (c % 4) * TC
        xs = np.zeros((D, TC + 4), np.float32)
        lo = max(0, s - 4)
        xs[:, 4 - (s - lo):] = x[b, lo:s + TC, :].T
        xs_all.append(np.ascontiguousarray(xs))
    in1 = [dict(x=xs_all[c], w_in=p["w_in"], w_xp=p["w_xp"], conv_w=p["cw"],
                conv_b=p["cb"], ones=p["ones"]) for c in range(NC)]
    r1 = run_bass_kernel_spmd(_FULL[0], in1, core_ids=list(range(NC)))
    u = np.concatenate([np.asarray(r["u"]) for r in r1.results], axis=1)
    zs = [np.asarray(r["zs"]) for r in r1.results]
    xdbl = np.ascontiguousarray(
        np.concatenate([np.asarray(r["xdbl"]) for r in r1.results], axis=0))
    in2 = []
    for c in range(NC):
        dsl = slice(c * DDI, (c + 1) * DDI)
        u_my = np.concatenate([u[dsl][:, k * TC:(k + 1) * TC] for k in range(NC)],
                              axis=0)
        in2.append(dict(
            u=np.ascontiguousarray(u_my), xdbl=xdbl,
            wdt=np.ascontiguousarray(p["wdt"][:, dsl]),
            bdt=np.ascontiguousarray(p["bdt"][dsl].reshape(DDI // 128, 128).T),
            alog=np.ascontiguousarray(p["alog"][dsl]),
            ones=p["ones"], ident=p["ident"]))
    r2 = run_bass_kernel_spmd(_FULL[1], in2, core_ids=list(range(NC)))
    yfull = np.concatenate([np.asarray(r["y"]) for r in r2.results], axis=0)
    in3 = []
    for c in range(NC):
        yv = np.concatenate([yfull[k * DI + c * DDI:k * DI + (c + 1) * DDI]
                             for k in range(NC)], axis=0)
        tsl = slice(c * TC, (c + 1) * TC)
        in3.append(dict(
            y=np.ascontiguousarray(yv),
            u=np.ascontiguousarray(u[:, tsl]), zs=zs[c], x=xs_all[c],
            dskip=p["dskip"], wout=p["wout"], router=p["router"],
            weg=p["weg"], weu=p["weu"], wed=p["wed"], wsg=p["wsg"],
            wsu=p["wsu"], wsd=p["wsd"], ones=p["ones"], ident=p["ident"]))
    r3 = run_bass_kernel_spmd(_FULL[2], in3, core_ids=list(range(NC)))
    out_fm = np.concatenate([np.asarray(r["out"]) for r in r3.results], axis=1)
    return np.ascontiguousarray(out_fm.T.reshape(B, L, D)).astype(np.float32)


# version 06
# speedup vs baseline: 1.0275x; 1.0250x over previous
import numpy as np
import ml_dtypes
import concourse.bass as bass
import concourse.mybir as mybir
from concourse.bass_utils import run_bass_kernel_spmd
from contextlib import ExitStack

F32 = mybir.dt.float32
BF16 = mybir.dt.bfloat16
AF = mybir.ActivationFunctionType
ALU = mybir.AluOpType
BF = ml_dtypes.bfloat16

B, L, D = 2, 16384, 1024
DI, DS, DCONV, DTR = 2048, 16, 4, 64
NE, DFF = 8, 512
NC = 8
TC = (B * L) // NC
DDI = DI // NC
EPS = 1e-5

ENGINES = ("sp", "act", "dve", "pool", "pe")


class Op:
    __slots__ = ("eng", "fn", "deps", "dma", "idx", "inc_val", "needed", "key",
                 "seq", "wait_val")

    def __init__(self, eng, fn, deps, dma):
        self.eng, self.fn, self.deps, self.dma = eng, fn, deps, dma
        self.idx = -1
        self.inc_val = None
        self.needed = False


class Buf:
    __slots__ = ("name", "last_writer", "readers", "exclusive")

    def __init__(self, name="", exclusive=False):
        self.name = name
        self.last_writer = None
        self.readers = []
        self.exclusive = exclusive


SYNC_SAME_ENGINE = [True]


class Sched:

    def __init__(self, nc):
        self.nc = nc
        self.streams = {e: [] for e in ENGINES}
        self._seq = 0
        self._last_dma = {}

    def op(self, eng, fn, reads=(), writes=(), deps=(), dma=False, key=None):
        reads = list(reads)
        writes = list(writes)
        excl = [b for b in reads if b.exclusive]
        if excl:
            reads = [b for b in reads if not b.exclusive]
            writes = writes + excl
        dd = list(deps)
        for b in reads:
            if b.last_writer is not None:
                dd.append(b.last_writer)
        for b in writes:
            if b.last_writer is not None:
                dd.append(b.last_writer)
            dd.extend(b.readers)
        o = Op(eng, fn, dd, dma)
        o.key = key
        o.idx = len(self.streams[eng])
        o.seq = self._seq
        self._seq += 1
        self.streams[eng].append(o)
        for b in reads:
            b.readers.append(o)
        for b in writes:
            b.last_writer = o
            b.readers = []
        return o

    def dma(self, eng, fn, reads=(), writes=(), deps=(), key=None):
        o = self.op(eng, fn, reads=reads, writes=writes, deps=deps, dma=True,
                    key=key)
        self._last_dma[self._key(o)] = o
        return o

    def create_sems(self, nc, ctx):
        sems = {}
        keys = set(ENGINES)
        for e in ENGINES:
            for o in self.streams[e]:
                if o.dma:
                    keys.add(self._key(o))
        for k in sorted(keys):
            sems[k] = ctx.enter_context(nc.semaphore("s_" + k.replace(".", "_")))
        return sems

    def barrier(self):
        lasts = [s[-1] for s in self.streams.values() if s]
        lasts += list(self._last_dma.values())
        nops = [self.op(e, lambda eng: eng.nop(), deps=lasts) for e in ENGINES]
        for e in ENGINES:
            self.op(e, lambda eng: eng.nop(), deps=nops)

    def _key(self, o):
        return o.eng + "." + (o.key or "dma")

    def _resolve(self):
        for e in ENGINES:
            for o in self.streams[e]:
                for d in o.deps:
                    if (not SYNC_SAME_ENGINE[0] and d.eng == o.eng
                            and not d.dma and not o.dma):
                        continue
                    d.needed = True
        self._dma_hist = {}
        for e in ENGINES:
            c_comp = 0
            c_dma = {}
            for o in self.streams[e]:
                if o.dma:
                    kk = self._key(o)
                    c_dma[kk] = c_dma.get(kk, 0) + 16
                    o.inc_val = c_dma[kk]
                    o.needed = True
                    self._dma_hist.setdefault(kk, []).append((o.seq, o.inc_val))
                elif o.needed:
                    c_comp += 1
                    o.inc_val = c_comp
                else:
                    o.inc_val = None

    def emit(self, block, sems, final_eng="dve"):
        self._resolve()
        sched = self
        finals = {}
        for e in ENGINES:
            for o in self.streams[e]:
                if o.dma:
                    kk = self._key(o)
                    finals[kk] = max(finals.get(kk, 0), o.inc_val)
                elif o.inc_val is not None:
                    finals[e] = max(finals.get(e, 0), o.inc_val)

        def mk_runner(ename):
            def run(eng):
                water = {}
                prev_completed = 0
                for o in sched.streams[ename]:
                    want = {}
                    for dp in o.deps:
                        if dp.inc_val is None:
                            continue
                        if dp.eng == o.eng and not dp.dma:
                            if not SYNC_SAME_ENGINE[0] and not o.dma:
                                continue
                            if dp.inc_val <= prev_completed:
                                continue
                        if dp.dma:
                            key = sched._key(dp)
                            v = 0
                            for sq, cum in sched._dma_hist[key]:
                                if sq < o.seq:
                                    v = cum
                                else:
                                    break
                            if v > want.get(key, 0):
                                want[key] = v
                            continue
                        key = dp.eng
                        if dp.inc_val > want.get(key, 0):
                            want[key] = dp.inc_val
                    for key in sorted(want):
                        v = want[key]
                        if key == ename:
                            if v > prev_completed:
                                eng.wait_ge(sems[key], v)
                                prev_completed = v
                            continue
                        if water.get(key, 0) >= v:
                            continue
                        eng.wait_ge(sems[key], v)
                        water[key] = v
                    inst = o.fn(eng)
                    if o.inc_val is not None:
                        key = sched._key(o) if o.dma else o.eng
                        inst.then_inc(sems[key], 16 if o.dma else 1)
                if ename == final_eng:
                    for key in sorted(finals):
                        if key in ENGINES:
                            continue
                        if finals[key] > water.get(key, 0) and finals[key] > 0:
                            eng.wait_ge(sems[key], finals[key])
            return run

        block.sync(mk_runner("sp"))
        block.scalar(mk_runner("act"))
        block.vector(mk_runner("dve"))
        block.gpsimd(mk_runner("pool"))
        block.tensor(mk_runner("pe"))


def make_sems(nc, ctx, extra=("wload",)):
    sems = {}
    for e in ENGINES:
        sems[e] = ctx.enter_context(nc.semaphore(f"s_{e}"))
        sems[e + "_dma"] = ctx.enter_context(nc.semaphore(f"s_{e}_dma"))
    for k in extra:
        sems[k] = ctx.enter_context(nc.semaphore(f"s_{k}"))
    return sems


_PREFIX = [""]


def sbuf(nc, ctx, name, shape, dtype):
    name = _PREFIX[0] + name
    h = ctx.enter_context(nc.sbuf_tensor(name, list(shape), dtype))
    return h, Buf(name)


def psum(nc, ctx, name, shape, dtype=F32):
    name = _PREFIX[0] + name
    h = ctx.enter_context(nc.psum_tensor(name, list(shape), dtype))
    return h, Buf(name, exclusive=True)


def ln_rowmath(S, pm_h, pm_b, psq_h, psq_b, negm_h, negm_b, rstd_h, rstd_b,
               tmp_h, tmp_b, col0, w, nfeat):
    sl = slice(col0, col0 + w)
    S.op("act", lambda e: e.activation(out=negm_h[0:1, sl], in_=pm_h[0:1, 0:w],
                                       func=AF.Copy, scale=-1.0 / nfeat),
         reads=[pm_b], writes=[negm_b])
    S.op("act", lambda e: e.activation(out=tmp_h[0:1, sl], in_=pm_h[0:1, 0:w],
                                       func=AF.Square, scale=1.0 / nfeat),
         reads=[pm_b], writes=[tmp_b])
    S.op("act", lambda e: e.activation(out=rstd_h[0:1, sl], in_=psq_h[0:1, 0:w],
                                       func=AF.Copy, scale=1.0 / nfeat),
         reads=[psq_b], writes=[rstd_b])
    S.op("dve", lambda e: e.tensor_tensor(out=rstd_h[0:1, sl], in0=rstd_h[0:1, sl],
                                          in1=tmp_h[0:1, sl], op=ALU.subtract),
         reads=[tmp_b], writes=[rstd_b])
    S.op("dve", lambda e: e.tensor_scalar(out=rstd_h[0:1, sl], in0=rstd_h[0:1, sl],
                                          scalar1=EPS, scalar2=None, op0=ALU.add),
         writes=[rstd_b])
    S.op("act", lambda e: e.activation(out=rstd_h[0:1, sl], in_=rstd_h[0:1, sl],
                                       func=AF.Ln),
         writes=[rstd_b])
    S.op("act", lambda e: e.activation(out=rstd_h[0:1, sl], in_=rstd_h[0:1, sl],
                                       func=AF.Exp, scale=-0.5),
         writes=[rstd_b])


def stage_l1(nc, S, sems, E, debug=False, p2_nchunks=None):
    x_d = E["x"]
    win_d = E["w_in"]
    wxp_d = E["w_xp"]
    cw_d = E["conv_w"]
    cb_d = E["conv_b"]
    ones_d = E["ones_in"]
    u_d = E["u_scr"]
    zs_d = E["zs_scr"]
    xdbl_d = E["xdbl_scr"]

    KT = D // 128
    MT = 2 * DI // 128
    NCH = TC // 512
    W = 512
    chunks = [(0, 4)] + [(4 + W * c, W) for c in range(NCH)]
    x_ap = x_d.ap()

    with ExitStack() as ctx:
        win_h, win_b = sbuf(nc, ctx, "win_bf", [128, KT * 2 * DI], BF16)
        wxp_h, wxp_b = sbuf(nc, ctx, "wxp_bf", [128, (DI // 128) * 96], BF16)
        cw_h, cw_b = sbuf(nc, ctx, "cw", [128, (DI // 128) * DCONV], F32)
        cb_h, cb_b = sbuf(nc, ctx, "cb", [128, DI // 128], F32)
        ones_h, ones_b = sbuf(nc, ctx, "ones_sb", [128, 128], BF16)
        negmb_h, negmb_b = sbuf(nc, ctx, "negm_bf", [1, TC + 4], BF16)
        rstdb_h, rstdb_b = sbuf(nc, ctx, "rstd_bf", [1, TC + 4], BF16)
        nm_s, nm_sb = sbuf(nc, ctx, "nm_s", [1, W], F32)
        rs_s, rs_sb = sbuf(nc, ctx, "rs_s", [1, W], F32)
        rt_s, rt_sb = sbuf(nc, ctx, "rt_s", [1, W], F32)
        xbf_h, xbf_b = sbuf(nc, ctx, "xbf", [128, KT * W], BF16)
        sqf_h, sqf_b = sbuf(nc, ctx, "sqf", [128, KT * W], BF16)
        xn_h, xn_b = sbuf(nc, ctx, "xn", [128, KT * W], BF16)
        tfa_h, tfa_b = sbuf(nc, ctx, "tfa", [128, W], F32)
        xin = [sbuf(nc, ctx, f"xin{i}", [128, 4 + W], F32) for i in range(DI // 128)]
        u_h, u_b = sbuf(nc, ctx, "usb", [128, (DI // 128) * W], BF16)
        ur_h, ur_b = sbuf(nc, ctx, "uraw", [128, W], F32)
        zs_h, zs_b = sbuf(nc, ctx, "zsb", [128, (DI // 128) * W], BF16)
        xdb_h, xdb_b = sbuf(nc, ctx, "xdbuf", [96, W], F32)
        pmean_h, pmean_b = psum(nc, ctx, "pmean", [1, W])
        psq_h, psq_b = psum(nc, ctx, "psq", [1, W])
        pnm_h, pnm_b = psum(nc, ctx, "pnm", [128, W])
        prs_h, prs_b = psum(nc, ctx, "prs", [128, W])
        pxz = [psum(nc, ctx, f"pxz{i}", [128, W]) for i in range(2)]
        pxd_h, pxd_b = psum(nc, ctx, "pxd", [96, W])

        S.dma("pool", lambda e: e.dma_start(
            out=win_h[:, :].rearrange("p (a m) -> p a m", a=KT),
            in_=win_d.ap().rearrange("(a p) m -> p a m", p=128)),
            writes=[win_b], key="wload")
        S.dma("pool", lambda e: e.dma_start(
            out=wxp_h[:, :].rearrange("p (a m) -> p a m", a=DI // 128),
            in_=wxp_d.ap().rearrange("(a p) m -> p a m", p=128)),
            writes=[wxp_b], key="wload")
        S.dma("sp", lambda e: e.dma_start(out=cw_h[:, :], in_=cw_d.ap()),
              writes=[cw_b])
        S.dma("sp", lambda e: e.dma_start(out=cb_h[:, :], in_=cb_d.ap()),
              writes=[cb_b])
        S.dma("sp", lambda e: e.dma_start(out=ones_h[:, :], in_=ones_d.ap()),
              writes=[ones_b])

        for col0, w in chunks:
            S.dma("pool", lambda e, col0=col0, w=w: e.dma_start(
                out=xbf_h[:, 0:KT * w].rearrange("p (a t) -> p a t", a=KT),
                in_=x_ap[:, col0:col0 + w].rearrange("(a p) t -> p a t", p=128)),
                writes=[xbf_b])
            S.op("act", lambda e, w=w: e.activation(
                out=sqf_h[:, 0:KT * w], in_=xbf_h[:, 0:KT * w], func=AF.Square),
                reads=[xbf_b], writes=[sqf_b])
            for k in range(KT):
                S.op("pe", lambda e, k=k, w=w: e.matmul(
                    pmean_h[0:1, 0:w], ones_h[:, 0:1], xbf_h[:, k * w:(k + 1) * w],
                    start=(k == 0), stop=(k == KT - 1)),
                    reads=[xbf_b, ones_b], writes=[pmean_b])
                S.op("pe", lambda e, k=k, w=w: e.matmul(
                    psq_h[0:1, 0:w], ones_h[:, 0:1], sqf_h[:, k * w:(k + 1) * w],
                    start=(k == 0), stop=(k == KT - 1)),
                    reads=[sqf_b, ones_b], writes=[psq_b])
            ln_rowmath(S, pmean_h, pmean_b, psq_h, psq_b, nm_s, nm_sb,
                       rs_s, rs_sb, rt_s, rt_sb, 0, w, D)
            S.op("act", lambda e, col0=col0, w=w: e.activation(
                out=negmb_h[0:1, col0:col0 + w], in_=nm_s[0:1, 0:w], func=AF.Copy),
                reads=[nm_sb], writes=[negmb_b])
            S.op("act", lambda e, col0=col0, w=w: e.activation(
                out=rstdb_h[0:1, col0:col0 + w], in_=rs_s[0:1, 0:w], func=AF.Copy),
                reads=[rs_sb], writes=[rstdb_b])

        p2chunks = chunks if p2_nchunks is None else chunks[:p2_nchunks]
        for col0, w in p2chunks:
            mini = (w == 4)
            S.dma("pool", lambda e, col0=col0, w=w: e.dma_start(
                out=xbf_h[:, 0:KT * w].rearrange("p (a t) -> p a t", a=KT),
                in_=x_ap[:, col0:col0 + w].rearrange("(a p) t -> p a t", p=128)),
                writes=[xbf_b])
            S.op("pe", lambda e, col0=col0, w=w: e.matmul(
                pnm_h[:, 0:w], ones_h[0:1, :], negmb_h[0:1, col0:col0 + w],
                start=True, stop=True), reads=[negmb_b, ones_b], writes=[pnm_b])
            S.op("pe", lambda e, col0=col0, w=w: e.matmul(
                prs_h[:, 0:w], ones_h[0:1, :], rstdb_h[0:1, col0:col0 + w],
                start=True, stop=True), reads=[rstdb_b, ones_b], writes=[prs_b])
            for k in range(KT):
                S.op("dve", lambda e, k=k, w=w: e.tensor_tensor(
                    out=tfa_h[:, 0:w], in0=xbf_h[:, k * w:(k + 1) * w],
                    in1=pnm_h[:, 0:w], op=ALU.add),
                    reads=[xbf_b, pnm_b], writes=[tfa_b])
                S.op("dve", lambda e, k=k, w=w: e.tensor_tensor(
                    out=xn_h[:, k * w:(k + 1) * w], in0=tfa_h[:, 0:w],
                    in1=prs_h[:, 0:w], op=ALU.mult),
                    reads=[tfa_b, prs_b], writes=[xn_b])
            mt_range = range(16) if mini else range(MT)
            for mi in mt_range:
                ph, pb = pxz[mi % 2]
                for k in range(KT):
                    S.op("pe", lambda e, mi=mi, k=k, ph=ph, w=w: e.matmul(
                        ph[:, 0:w],
                        win_h[:, k * 2 * DI + mi * 128:k * 2 * DI + mi * 128 + 128],
                        xn_h[:, k * w:(k + 1) * w],
                        start=(k == 0), stop=(k == KT - 1)),
                        reads=[xn_b, win_b], writes=[pb])
                if mi < 16:
                    xih, xib = xin[mi]
                    if mini:
                        S.op("act", lambda e, ph=ph, xih=xih: e.activation(
                            out=xih[:, 0:4], in_=ph[:, 0:4], func=AF.Copy),
                            reads=[pb], writes=[xib])
                    else:
                        S.op("act", lambda e, ph=ph, xih=xih, w=w: e.activation(
                            out=xih[:, 4:4 + w], in_=ph[:, 0:w], func=AF.Copy),
                            reads=[pb], writes=[xib])
                        S.op("dve", lambda e, xih=xih, mi=mi, w=w: e.tensor_scalar(
                            out=ur_h[:, 0:w], in0=xih[:, 1:1 + w],
                            scalar1=cw_h[:, mi * DCONV:mi * DCONV + 1],
                            scalar2=cb_h[:, mi:mi + 1], op0=ALU.mult, op1=ALU.add),
                            reads=[xib, cw_b, cb_b], writes=[ur_b])
                        for j in range(1, DCONV):
                            S.op("dve", lambda e, xih=xih, mi=mi, j=j, w=w: e.scalar_tensor_tensor(
                                out=ur_h[:, 0:w], in0=xih[:, 1 + j:1 + j + w],
                                scalar=cw_h[:, mi * DCONV + j:mi * DCONV + j + 1],
                                in1=ur_h[:, 0:w], op0=ALU.mult, op1=ALU.add),
                                reads=[xib, cw_b], writes=[ur_b])
                        S.op("act", lambda e, mi=mi, w=w: e.activation(
                            out=u_h[:, mi * w:(mi + 1) * w], in_=ur_h[:, 0:w],
                            func=AF.Silu), reads=[ur_b], writes=[u_b])
                        S.op("dve", lambda e, xih=xih, w=w: e.tensor_copy(
                            out=xih[:, 0:4], in_=xih[:, w:w + 4]),
                            reads=[xib], writes=[xib])
                        S.op("pe", lambda e, mi=mi, w=w: e.matmul(
                            pxd_h[0:96, 0:w], wxp_h[:, mi * 96:(mi + 1) * 96],
                            u_h[:, mi * w:(mi + 1) * w],
                            start=(mi == 0), stop=(mi == 15)),
                            reads=[u_b, wxp_b], writes=[pxd_b])

                else:
                    S.op("act", lambda e, ph=ph, mi=mi, w=w: e.activation(
                        out=zs_h[:, (mi - 16) * w:(mi - 15) * w], in_=ph[:, 0:w],
                        func=AF.Silu), reads=[pb], writes=[zs_b])
            if not mini:
                S.dma("sp", lambda e, col0=col0, w=w: e.dma_start(
                    out=u_d.ap()[:, col0 - 4:col0 - 4 + w].rearrange(
                        "(a p) t -> p a t", p=128),
                    in_=u_h[:, 0:16 * w].rearrange("p (a t) -> p a t", a=16)),
                    reads=[u_b])
                S.dma("act", lambda e, col0=col0, w=w: e.dma_start(
                    out=zs_d.ap()[:, col0 - 4:col0 - 4 + w].rearrange(
                        "(a p) t -> p a t", p=128),
                    in_=zs_h[:, 0:16 * w].rearrange("p (a t) -> p a t", a=16)),
                    reads=[zs_b])
                S.op("act", lambda e, w=w: e.activation(
                    out=xdb_h[0:96, 0:w], in_=pxd_h[0:96, 0:w], func=AF.Copy),
                    reads=[pxd_b], writes=[xdb_b])
                S.dma("act", lambda e, col0=col0, w=w: e.dma_start(
                    out=xdbl_d.ap()[0:96, col0 - 4:col0 - 4 + w],
                    in_=xdb_h[0:96, 0:w]), reads=[xdb_b])


def _np(a):
    return np.asarray(a)


def stage_l2(nc, S, sems, E, nchu=None, nbatch=B, skip=()):
    TCH = 1024
    NSUB = TCH // 512
    NCHU = (L // TCH) if nchu is None else nchu
    NDT = DDI // 128
    u_d = E["u_my"]
    xdbl_d = E["xdbl_ag"]
    wdt_d = E["wdt"]
    bdt_d = E["bdt"]
    alog_d = E["alog"]
    ones_d = E["ones_in"]
    ident_d = E["ident_in"]
    y_d = E["y_scr"]

    with ExitStack() as ctx:
        ones_h, ones_b = sbuf(nc, ctx, "l2_ones", [128, 128], BF16)
        ident_h, ident_b = sbuf(nc, ctx, "ident_sb", [128, 128], BF16)
        wdt_h, wdt_b = sbuf(nc, ctx, "wdt_bf", [DTR, DDI], BF16)
        bdt_h, bdt_b = sbuf(nc, ctx, "bdt_sb", [128, NDT], F32)
        an_h = [sbuf(nc, ctx, f"aneg{i}", [128, DS], F32) for i in range(NDT)]
        dtlr_p = [sbuf(nc, ctx, f"dtlr_bf{p}", [DTR, TCH], BF16) for p in range(2)]
        bc_p = [sbuf(nc, ctx, f"bc_bf{p}", [1, 32 * TCH], BF16) for p in range(2)]
        brep = [sbuf(nc, ctx, f"brep{i}", [128, TCH], BF16) for i in range(2)]
        crep = [sbuf(nc, ctx, f"crep{i}", [128, TCH], BF16) for i in range(2)]
        ut_p = [[sbuf(nc, ctx, f"ut{i}_{p}", [128, TCH], BF16) for i in range(NDT)]
                for p in range(2)]
        dl_p = [[sbuf(nc, ctx, f"dl{i}_{p}", [128, TCH], F32) for i in range(NDT)]
                for p in range(2)]
        du_p = [[sbuf(nc, ctx, f"du{i}_{p}", [128, TCH], BF16) for i in range(NDT)]
                for p in range(2)]
        spa_h, spa_b = sbuf(nc, ctx, "sp_abs", [128, 512], F32)
        spb_h, spb_b = sbuf(nc, ctx, "sp_mx", [128, 512], F32)
        dA = [sbuf(nc, ctx, f"dA{i}", [128, TCH], F32) for i in range(2)]
        dBu = [sbuf(nc, ctx, f"dBu{i}", [128, TCH], BF16) for i in range(2)]
        hh = [sbuf(nc, ctx, f"h{i}", [128, TCH], BF16) for i in range(2)]
        gg = [sbuf(nc, ctx, f"g{i}", [128, TCH], BF16) for i in range(2)]
        carry = [sbuf(nc, ctx, f"carry{i}", [128, DS], F32) for i in range(NDT)]
        yo = [sbuf(nc, ctx, f"yo{i}", [128, TCH], BF16) for i in range(NDT)]
        pdl_h, pdl_b = psum(nc, ctx, "pdl", [128, 512])
        pbc = [psum(nc, ctx, f"pbc{i}", [128, 512]) for i in range(2)]
        py = [[psum(nc, ctx, f"py{i}_{s}", [128, 512]) for s in range(NSUB)]
              for i in range(NDT)]

        S.dma("sp", lambda e: e.dma_start(out=ones_h[:, :], in_=ones_d.ap()),
              writes=[ones_b], key="wload")
        S.dma("sp", lambda e: e.dma_start(out=ident_h[:, :], in_=ident_d.ap()),
              writes=[ident_b], key="wload")
        S.dma("pool", lambda e: e.dma_start(out=wdt_h[:, :], in_=wdt_d.ap()),
              writes=[wdt_b], key="wloadp")
        S.dma("sp", lambda e: e.dma_start(out=bdt_h[:, :], in_=bdt_d.ap()),
              writes=[bdt_b], key="wload")
        al_bufs = []
        for i in range(NDT):
            ah, ab = sbuf(nc, ctx, f"alog{i}", [128, DS], F32)
            al_bufs.append((ah, ab))
            S.dma("sp", lambda e, ah=ah, i=i: e.dma_start(
                out=ah[:, :], in_=alog_d.ap()[128 * i:128 * (i + 1), :]),
                writes=[ab], key="wload")
        for i in range(NDT):
            ah, ab = al_bufs[i]
            nh, nb = an_h[i]
            S.op("act", lambda e, ah=ah, nh=nh: e.activation(
                out=nh[:, :], in_=ah[:, :], func=AF.Exp), reads=[ab], writes=[nb])
            S.op("dve", lambda e, nh=nh: e.tensor_scalar(
                out=nh[:, :], in0=nh[:, :], scalar1=-1.0, scalar2=None,
                op0=ALU.mult), writes=[nb])
        for i in range(NDT):
            ch, cb_ = carry[i]
            S.op("dve", lambda e, ch=ch: e.memset(ch[:, :], 0.0), writes=[cb_])

        for b in range(nbatch):
            if b > 0:
                for i in range(NDT):
                    ch, cb_ = carry[i]
                    S.op("dve", lambda e, ch=ch: e.memset(ch[:, :], 0.0),
                         writes=[cb_])
            for chk in range(NCHU):
                c0 = b * L + chk * TCH
                rr = c0 // TC
                cl = c0 % TC
                cp = (b * NCHU + chk) % 2
                dtlr_h, dtlr_b = dtlr_p[cp]
                bc_h, bc_b = bc_p[cp]
                ut, dl, du = ut_p[cp], dl_p[cp], du_p[cp]
                S.dma("pool", lambda e, rr=rr, cl=cl: e.dma_start(
                    out=dtlr_h[:, :],
                    in_=xdbl_d.ap()[rr * 96:rr * 96 + DTR, cl:cl + TCH]),
                    writes=[dtlr_b], key="bcld")
                S.dma("pool", lambda e, rr=rr, cl=cl: e.dma_start(
                    out=bc_h[:, :].rearrange("p (a t) -> p a t", a=32),
                    in_=bass.AP(xdbl_d, (rr * 96 + DTR) * TC + cl,
                                [[1, 1], [TC, 32], [1, TCH]])),
                    writes=[bc_b], key="bcld")
                for i in range(NDT):
                    uh, ub = ut[i]
                    S.dma("sp", lambda e, uh=uh, i=i, rr=rr, cl=cl: e.dma_start(
                        out=uh[:, :],
                        in_=u_d.ap()[rr * DDI + 128 * i:rr * DDI + 128 * (i + 1),
                                     cl:cl + TCH]),
                        writes=[ub], key="uld")
                for i in range(NDT) if "delta" not in skip else []:
                    dlh, dlb = dl[i]
                    for s in range(NSUB):
                        if "deltamm" not in skip:
                            S.op("pe", lambda e, i=i, s=s: e.matmul(
                                pdl_h[:, :], wdt_h[:, 128 * i:128 * (i + 1)],
                                dtlr_h[:, 512 * s:512 * (s + 1)], start=True, stop=True),
                                reads=[dtlr_b, wdt_b], writes=[pdl_b])
                        else:
                            S.op("pe", lambda e, i=i, s=s: e.matmul(
                                pdl_h[:, :], ident_h[:, :],
                                ut[i][0][:, 512 * s:512 * (s + 1)], start=True, stop=True),
                                reads=[ut[i][1], ident_b], writes=[pdl_b])
                        _af_abs = AF.Square if "abs" in skip else AF.Abs
                        if "spact" not in skip:
                            S.op("act", lambda e, i=i, _af_abs=_af_abs: e.activation(
                                out=spa_h[:, :], in_=pdl_h[:, :], func=_af_abs,
                                bias=bdt_h[:, i:i + 1]),
                                reads=[pdl_b, bdt_b], writes=[spa_b])
                            S.op("act", lambda e: e.activation(
                                out=spa_h[:, :], in_=spa_h[:, :], func=AF.Exp,
                                scale=-1.0), writes=[spa_b])
                            S.op("act", lambda e: e.activation(
                                out=spa_h[:, :], in_=spa_h[:, :], func=AF.Ln,
                                bias=1.0), writes=[spa_b])
                        else:
                            S.op("act", lambda e: e.activation(
                                out=spa_h[:, :], in_=pdl_h[:, :], func=AF.Copy),
                                reads=[pdl_b], writes=[spa_b])
                        if "spdve" not in skip:
                            S.op("dve", lambda e, i=i: e.tensor_scalar(
                                out=spb_h[:, :], in0=pdl_h[:, :],
                                scalar1=bdt_h[:, i:i + 1], scalar2=0.0,
                                op0=ALU.add, op1=ALU.max),
                                reads=[pdl_b, bdt_b], writes=[spb_b])
                        else:
                            S.op("dve", lambda e: e.tensor_copy(
                                out=spb_h[:, :], in_=pdl_h[:, :]),
                                reads=[pdl_b], writes=[spb_b])
                        S.op("dve", lambda e, dlh=dlh, s=s: e.tensor_tensor(
                            out=dlh[:, 512 * s:512 * (s + 1)], in0=spa_h[:, :],
                            in1=spb_h[:, :], op=ALU.add),
                            reads=[spa_b, spb_b], writes=[dlb])
                    duh, dub = du[i]
                    uh, ub = ut[i]
                    S.op("dve", lambda e, duh=duh, dlh=dlh, uh=uh: e.tensor_tensor(
                        out=duh[:, :], in0=dlh[:, :], in1=uh[:, :], op=ALU.mult),
                        reads=[dlb, ub], writes=[dub])
                for n in range(DS) if "nloop" not in skip else []:
                    bh, bb = brep[n % 2]
                    ch2, cb2 = crep[n % 2]
                    for s in range(NSUB):
                        ph, pb = pbc[0]
                        S.op("pe", lambda e, ph=ph, n=n, s=s: e.matmul(
                            ph[:, :], ones_h[0:1, :],
                            bc_h[0:1, n * TCH + 512 * s:n * TCH + 512 * (s + 1)],
                            start=True, stop=True),
                            reads=[bc_b, ones_b], writes=[pb])
                        S.op("act", lambda e, ph=ph, bh=bh, s=s: e.activation(
                            out=bh[:, 512 * s:512 * (s + 1)], in_=ph[:, :],
                            func=AF.Copy), reads=[pb], writes=[bb])
                        ph2, pb2 = pbc[1]
                        S.op("pe", lambda e, ph2=ph2, n=n, s=s: e.matmul(
                            ph2[:, :], ones_h[0:1, :],
                            bc_h[0:1, (16 + n) * TCH + 512 * s:(16 + n) * TCH + 512 * (s + 1)],
                            start=True, stop=True),
                            reads=[bc_b, ones_b], writes=[pb2])
                        S.op("act", lambda e, ph2=ph2, ch2=ch2, s=s: e.activation(
                            out=ch2[:, 512 * s:512 * (s + 1)], in_=ph2[:, :],
                            func=AF.Copy), reads=[pb2], writes=[cb2])
                    for i in range(NDT):
                        pp = (n * NDT + i) % 2
                        dAh, dAb = dA[pp]
                        dBh, dBb = dBu[pp]
                        hhh, hhb = hh[pp]
                        ggh, ggb = gg[pp]
                        dlh, dlb = dl[i]
                        duh, dub = du[i]
                        anh, anb = an_h[i]
                        ch, cb_ = carry[i]
                        S.op("act", lambda e, dAh=dAh, dlh=dlh, anh=anh, n=n: e.activation(
                            out=dAh[:, :], in_=dlh[:, :], func=AF.Exp,
                            scale=anh[:, n:n + 1]), reads=[dlb, anb], writes=[dAb])
                        S.op("pool", lambda e, dBh=dBh, duh=duh, bh=bh: e.tensor_tensor(
                            out=dBh[:, :], in0=duh[:, :], in1=bh[:, :], op=ALU.mult),
                            reads=[dub, bb], writes=[dBb])
                        if "scan" in skip:
                            S.op("dve", lambda e, hhh=hhh, dBh=dBh: e.tensor_copy(
                                out=hhh[:, :], in_=dBh[:, :]),
                                reads=[dBb], writes=[hhb])
                        else:
                            S.op("dve", lambda e, hhh=hhh, dAh=dAh, dBh=dBh, ch=ch, n=n: e.tensor_tensor_scan(
                                out=hhh[:, :], data0=dAh[:, :], data1=dBh[:, :],
                                initial=ch[:, n:n + 1], op0=ALU.mult, op1=ALU.add),
                                reads=[dAb, dBb, cb_], writes=[hhb])
                        if "carry" not in skip:
                            S.op("dve", lambda e, hhh=hhh, ch=ch, n=n: e.tensor_copy(
                                out=ch[:, n:n + 1], in_=hhh[:, TCH - 1:TCH]),
                                reads=[hhb], writes=[cb_])
                        S.op("dve", lambda e, ggh=ggh, hhh=hhh, ch2=ch2: e.tensor_tensor(
                            out=ggh[:, :], in0=hhh[:, :], in1=ch2[:, :], op=ALU.mult),
                            reads=[hhb, cb2], writes=[ggb])
                        for s in range(NSUB):
                            pyh, pyb = py[i][s]
                            S.op("pe", lambda e, pyh=pyh, ggh=ggh, s=s, n=n: e.matmul(
                                pyh[:, :], ident_h[:, :],
                                ggh[:, 512 * s:512 * (s + 1)],
                                start=(n == 0), stop=(n == DS - 1)),
                                reads=[ggb, ident_b], writes=[pyb])
                for i in range(NDT) if "yevac" not in skip else []:
                    yh, yb = yo[i]
                    for s in range(NSUB):
                        pyh, pyb = py[i][s]
                        S.op("act", lambda e, yh=yh, pyh=pyh, s=s: e.activation(
                            out=yh[:, 512 * s:512 * (s + 1)], in_=pyh[:, :],
                            func=AF.Copy), reads=[pyb], writes=[yb])
                    S.dma("sp", lambda e, yh=yh, i=i, rr=rr, cl=cl: e.dma_start(
                        out=y_d.ap()[rr * DDI + 128 * i:rr * DDI + 128 * (i + 1),
                                     cl:cl + TCH],
                        in_=yh[:, :]), reads=[yb], key="yst")


def ln_rowmath_sb(S, pm_h, pm_b, psq_h, psq_b, negm_h, negm_b, rstd_h, rstd_b,
                  tmp_h, tmp_b, icol0, ocol0, w, nfeat):
    isl = slice(icol0, icol0 + w)
    osl = slice(ocol0, ocol0 + w)
    S.op("act", lambda e: e.activation(out=negm_h[0:1, osl], in_=pm_h[0:1, isl],
                                       func=AF.Copy, scale=-1.0 / nfeat),
         reads=[pm_b], writes=[negm_b])
    S.op("act", lambda e: e.activation(out=tmp_h[0:1, 0:w], in_=pm_h[0:1, isl],
                                       func=AF.Square, scale=1.0 / nfeat),
         reads=[pm_b], writes=[tmp_b])
    S.op("act", lambda e: e.activation(out=rstd_h[0:1, osl], in_=psq_h[0:1, isl],
                                       func=AF.Copy, scale=1.0 / nfeat),
         reads=[psq_b], writes=[rstd_b])
    S.op("dve", lambda e: e.tensor_tensor(out=rstd_h[0:1, osl], in0=rstd_h[0:1, osl],
                                          in1=tmp_h[0:1, 0:w], op=ALU.subtract),
         reads=[tmp_b], writes=[rstd_b])
    S.op("dve", lambda e: e.tensor_scalar(out=rstd_h[0:1, osl], in0=rstd_h[0:1, osl],
                                          scalar1=EPS, scalar2=None, op0=ALU.add),
         writes=[rstd_b])
    S.op("act", lambda e: e.activation(out=rstd_h[0:1, osl], in_=rstd_h[0:1, osl],
                                       func=AF.Ln), writes=[rstd_b])
    S.op("act", lambda e: e.activation(out=rstd_h[0:1, osl], in_=rstd_h[0:1, osl],
                                       func=AF.Exp, scale=-0.5), writes=[rstd_b])


def stage_l3a(nc, S, sems, E, nchunks=None):
    W = 512
    NCH = TC // W if nchunks is None else nchunks
    KTI = DI // 128
    KTM = D // 128
    y_dd = E["y_my"]
    u_dd = E["u_scr"]
    zs_dd = E["zs_scr"]
    x_dd = E["x_nohalo"]
    dsk_d = E["dskip"]
    wout_d = E["wout"]
    rtr_d = E["router"]
    ones_d = E["ones_in"]
    ident_d = E["ident_in"]
    x2_d = E["x2_scr"]
    xn3_d = E["xn3_scr"]
    coef_d = E["coef_scr"]

    with ExitStack() as ctx:
        ones_h, ones_b = sbuf(nc, ctx, "ones_sb", [128, 128], BF16)
        ident_h, ident_b = sbuf(nc, ctx, "ident_sb", [128, 128], BF16)
        dsk_h, dsk_b = sbuf(nc, ctx, "dsk", [128, KTI], F32)
        wout_h, wout_b = sbuf(nc, ctx, "wout_bf", [128, KTI * D], BF16)
        rtr_h, rtr_b = sbuf(nc, ctx, "rtr_bf", [128, KTM * NE], BF16)
        ysb_h, ysb_b = sbuf(nc, ctx, "ysb", [128, KTI * W], BF16)
        usb_h, usb_b = sbuf(nc, ctx, "usb", [128, KTI * W], BF16)
        zsb_h, zsb_b = sbuf(nc, ctx, "zsb", [128, KTI * W], BF16)
        xsb_h, xsb_b = sbuf(nc, ctx, "xsb", [128, KTM * W], F32)
        tg_h, tg_b = sbuf(nc, ctx, "tg", [128, W], F32)
        yg_h, yg_b = sbuf(nc, ctx, "yg", [128, KTI * W], BF16)
        mo_h, mo_b = sbuf(nc, ctx, "mo_bf", [128, KTM * W], BF16)
        sq_h, sq_b = sbuf(nc, ctx, "sq_bf", [128, KTM * W], BF16)
        x2sb_h, x2sb_b = sbuf(nc, ctx, "x2sb", [128, KTM * W], F32)
        x2bf_h, x2bf_b = sbuf(nc, ctx, "x2bf", [128, KTM * W], BF16)
        xn3_h, xn3_b = sbuf(nc, ctx, "xn3sb", [128, KTM * W], BF16)
        nm2_h, nm2_b = sbuf(nc, ctx, "nm2", [1, TC], BF16)
        rs2_h, rs2_b = sbuf(nc, ctx, "rs2", [1, TC], BF16)
        nm3_h, nm3_b = sbuf(nc, ctx, "nm3", [1, TC], BF16)
        rs3_h, rs3_b = sbuf(nc, ctx, "rs3", [1, TC], BF16)
        nmf_h, nmf_b = sbuf(nc, ctx, "nmf", [1, W], F32)
        rsf_h, rsf_b = sbuf(nc, ctx, "rsf", [1, W], F32)
        rtf_h, rtf_b = sbuf(nc, ctx, "rtf", [1, W], F32)
        lg_h, lg_b = sbuf(nc, ctx, "lg_bf", [8, W], BF16)
        lgT_h, lgT_b = sbuf(nc, ctx, "lgT", [128, 8], F32)
        r1_h, r1_b = sbuf(nc, ctx, "r1", [128, 1], F32)
        r2_h, r2_b = sbuf(nc, ctx, "r2", [128, 1], F32)
        ee_h, ee_b = sbuf(nc, ctx, "ee", [128, 8], F32)
        pp_h, pp_b = sbuf(nc, ctx, "ppb", [128, 8], F32)
        gx_h, gx_b = sbuf(nc, ctx, "gx", [128, 8], F32)
        cf_h, cf_b = sbuf(nc, ctx, "cfbf", [128, 8], BF16)
        cT_h, cT_b = sbuf(nc, ctx, "coefT", [8, W], BF16)
        pwo = [psum(nc, ctx, f"pwo{i}", [128, W]) for i in range(2)]
        pm_h, pm_b = psum(nc, ctx, "pm", [1, W])
        pq_h, pq_b = psum(nc, ctx, "pq", [1, W])
        pnm_h, pnm_b = psum(nc, ctx, "pnm", [128, W])
        prs_h, prs_b = psum(nc, ctx, "prs", [128, W])
        plg_h, plg_b = psum(nc, ctx, "plg", [8, W])
        ptb_h, ptb_b = psum(nc, ctx, "ptb", [128, 128], BF16)

        S.dma("sp", lambda e: e.dma_start(out=ones_h[:, :], in_=ones_d.ap()),
              writes=[ones_b], key="wload")
        S.dma("sp", lambda e: e.dma_start(out=ident_h[:, :], in_=ident_d.ap()),
              writes=[ident_b], key="wload")
        S.dma("sp", lambda e: e.dma_start(out=dsk_h[:, :], in_=dsk_d.ap()),
              writes=[dsk_b], key="wload")
        S.dma("pool", lambda e: e.dma_start(
            out=wout_h[:, :].rearrange("p (a m) -> p a m", a=KTI),
            in_=wout_d.ap().rearrange("(a p) m -> p a m", p=128)),
            writes=[wout_b], key="wloadp")
        S.dma("pool", lambda e: e.dma_start(
            out=rtr_h[:, :].rearrange("p (a m) -> p a m", a=KTM),
            in_=rtr_d.ap().rearrange("(a p) m -> p a m", p=128)),
            writes=[rtr_b], key="wloadp")

        for chn in range(NCH):
            c0 = chn * W
            for nm_, hh_, bb_, dd_, kt_, ky_, xo_ in (
                    ("y", ysb_h, ysb_b, y_dd, KTI, "yld", 0),
                    ("u", usb_h, usb_b, u_dd, KTI, "uld", 0),
                    ("z", zsb_h, zsb_b, zs_dd, KTI, "zld", 0),
                    ("x", xsb_h, xsb_b, x_dd, KTM, "xld", 4)):
                S.dma("sp", lambda e, hh_=hh_, dd_=dd_, kt_=kt_, c0=c0, xo_=xo_: e.dma_start(
                    out=hh_[:, 0:kt_ * W].rearrange("p (a t) -> p a t", a=kt_),
                    in_=dd_.ap()[:, xo_ + c0:xo_ + c0 + W].rearrange(
                        "(a p) t -> p a t", p=128)),
                    writes=[bb_], key=ky_)
            for k in range(KTI):
                sl = slice(k * W, (k + 1) * W)
                S.op("dve", lambda e, k=k, sl=sl: e.scalar_tensor_tensor(
                    out=tg_h[:, :], in0=usb_h[:, sl], scalar=dsk_h[:, k:k + 1],
                    in1=ysb_h[:, sl], op0=ALU.mult, op1=ALU.add),
                    reads=[usb_b, dsk_b, ysb_b], writes=[tg_b])
                S.op("dve", lambda e, k=k, sl=sl: e.tensor_tensor(
                    out=yg_h[:, sl], in0=tg_h[:, :], in1=zsb_h[:, sl],
                    op=ALU.mult), reads=[tg_b, zsb_b], writes=[yg_b])
            for mi in range(KTM):
                ph, pb = pwo[mi % 2]
                for k in range(KTI):
                    S.op("pe", lambda e, mi=mi, k=k, ph=ph: e.matmul(
                        ph[:, :], wout_h[:, k * D + mi * 128:k * D + mi * 128 + 128],
                        yg_h[:, k * W:(k + 1) * W],
                        start=(k == 0), stop=(k == KTI - 1)),
                        reads=[yg_b, wout_b], writes=[pb])
                msl = slice(mi * W, (mi + 1) * W)
                S.op("act", lambda e, ph=ph, msl=msl: e.activation(
                    out=mo_h[:, msl], in_=ph[:, :], func=AF.Copy),
                    reads=[pb], writes=[mo_b])
                S.op("act", lambda e, ph=ph, msl=msl: e.activation(
                    out=sq_h[:, msl], in_=ph[:, :], func=AF.Square),
                    reads=[pb], writes=[sq_b])
            for mi in range(KTM):
                msl = slice(mi * W, (mi + 1) * W)
                S.op("pe", lambda e, mi=mi, msl=msl: e.matmul(
                    pm_h[0:1, :], ones_h[:, 0:1], mo_h[:, msl],
                    start=(mi == 0), stop=(mi == KTM - 1)),
                    reads=[mo_b, ones_b], writes=[pm_b])
                S.op("pe", lambda e, mi=mi, msl=msl: e.matmul(
                    pq_h[0:1, :], ones_h[:, 0:1], sq_h[:, msl],
                    start=(mi == 0), stop=(mi == KTM - 1)),
                    reads=[sq_b, ones_b], writes=[pq_b])
            ln_rowmath(S, pm_h, pm_b, pq_h, pq_b, nmf_h, nmf_b, rsf_h, rsf_b,
                       rtf_h, rtf_b, 0, W, D)
            S.op("act", lambda e, c0=c0: e.activation(
                out=nm2_h[0:1, c0:c0 + W], in_=nmf_h[0:1, 0:W], func=AF.Copy),
                reads=[nmf_b], writes=[nm2_b])
            S.op("act", lambda e, c0=c0: e.activation(
                out=rs2_h[0:1, c0:c0 + W], in_=rsf_h[0:1, 0:W], func=AF.Copy),
                reads=[rsf_b], writes=[rs2_b])
            S.op("pe", lambda e, c0=c0: e.matmul(
                pnm_h[:, :], ones_h[0:1, :], nm2_h[0:1, c0:c0 + W],
                start=True, stop=True), reads=[nm2_b, ones_b], writes=[pnm_b])
            S.op("pe", lambda e, c0=c0: e.matmul(
                prs_h[:, :], ones_h[0:1, :], rs2_h[0:1, c0:c0 + W],
                start=True, stop=True), reads=[rs2_b, ones_b], writes=[prs_b])
            for mi in range(KTM):
                msl = slice(mi * W, (mi + 1) * W)
                S.op("dve", lambda e, msl=msl: e.tensor_tensor(
                    out=tg_h[:, :], in0=mo_h[:, msl], in1=pnm_h[:, :], op=ALU.add),
                    reads=[mo_b, pnm_b], writes=[tg_b])
                S.op("dve", lambda e, msl=msl: e.tensor_tensor(
                    out=tg_h[:, :], in0=tg_h[:, :], in1=prs_h[:, :], op=ALU.mult),
                    reads=[prs_b], writes=[tg_b])
                S.op("dve", lambda e, msl=msl: e.tensor_tensor(
                    out=x2sb_h[:, msl], in0=tg_h[:, :], in1=xsb_h[:, msl],
                    op=ALU.add), reads=[tg_b, xsb_b], writes=[x2sb_b])
                S.op("act", lambda e, msl=msl: e.activation(
                    out=x2bf_h[:, msl], in_=x2sb_h[:, msl], func=AF.Copy),
                    reads=[x2sb_b], writes=[x2bf_b])
                S.op("act", lambda e, msl=msl: e.activation(
                    out=sq_h[:, msl], in_=x2sb_h[:, msl], func=AF.Square),
                    reads=[x2sb_b], writes=[sq_b])
            S.dma("sp", lambda e, c0=c0: e.dma_start(
                out=x2_d.ap()[:, c0:c0 + W].rearrange("(a p) t -> p a t", p=128),
                in_=x2sb_h[:, :].rearrange("p (a t) -> p a t", a=KTM)),
                reads=[x2sb_b], key="ost")
            for mi in range(KTM):
                msl = slice(mi * W, (mi + 1) * W)
                S.op("pe", lambda e, msl=msl, mi=mi: e.matmul(
                    pm_h[0:1, :], ones_h[:, 0:1], x2bf_h[:, msl],
                    start=(mi == 0), stop=(mi == KTM - 1)),
                    reads=[x2bf_b, ones_b], writes=[pm_b])
                S.op("pe", lambda e, msl=msl, mi=mi: e.matmul(
                    pq_h[0:1, :], ones_h[:, 0:1], sq_h[:, msl],
                    start=(mi == 0), stop=(mi == KTM - 1)),
                    reads=[sq_b, ones_b], writes=[pq_b])
            ln_rowmath(S, pm_h, pm_b, pq_h, pq_b, nmf_h, nmf_b, rsf_h, rsf_b,
                       rtf_h, rtf_b, 0, W, D)
            S.op("act", lambda e, c0=c0: e.activation(
                out=nm3_h[0:1, c0:c0 + W], in_=nmf_h[0:1, 0:W], func=AF.Copy),
                reads=[nmf_b], writes=[nm3_b])
            S.op("act", lambda e, c0=c0: e.activation(
                out=rs3_h[0:1, c0:c0 + W], in_=rsf_h[0:1, 0:W], func=AF.Copy),
                reads=[rsf_b], writes=[rs3_b])
            S.op("pe", lambda e, c0=c0: e.matmul(
                pnm_h[:, :], ones_h[0:1, :], nm3_h[0:1, c0:c0 + W],
                start=True, stop=True), reads=[nm3_b, ones_b], writes=[pnm_b])
            S.op("pe", lambda e, c0=c0: e.matmul(
                prs_h[:, :], ones_h[0:1, :], rs3_h[0:1, c0:c0 + W],
                start=True, stop=True), reads=[rs3_b, ones_b], writes=[prs_b])
            for mi in range(KTM):
                msl = slice(mi * W, (mi + 1) * W)
                S.op("dve", lambda e, msl=msl: e.tensor_tensor(
                    out=tg_h[:, :], in0=x2sb_h[:, msl], in1=pnm_h[:, :],
                    op=ALU.add), reads=[x2sb_b, pnm_b], writes=[tg_b])
                S.op("dve", lambda e, msl=msl: e.tensor_tensor(
                    out=xn3_h[:, msl], in0=tg_h[:, :], in1=prs_h[:, :],
                    op=ALU.mult), reads=[tg_b, prs_b], writes=[xn3_b])
            S.dma("sp", lambda e, c0=c0: e.dma_start(
                out=xn3_d.ap()[:, c0:c0 + W].rearrange("(a p) t -> p a t", p=128),
                in_=xn3_h[:, :].rearrange("p (a t) -> p a t", a=KTM)),
                reads=[xn3_b], key="ost")
            for k in range(KTM):
                S.op("pe", lambda e, k=k: e.matmul(
                    plg_h[0:8, :], rtr_h[:, k * NE:(k + 1) * NE],
                    xn3_h[:, k * W:(k + 1) * W],
                    start=(k == 0), stop=(k == KTM - 1)),
                    reads=[xn3_b, rtr_b], writes=[plg_b])
            S.op("act", lambda e: e.activation(
                out=lg_h[0:8, :], in_=plg_h[0:8, :], func=AF.Copy),
                reads=[plg_b], writes=[lg_b])
            for j in range(4):
                S.op("pe", lambda e, j=j: e.transpose(
                    ptb_h[:, 0:8], lg_h[0:8, j * 128:(j + 1) * 128],
                    ident_h[0:8, 0:8]), reads=[lg_b, ident_b], writes=[ptb_b])
                S.op("act", lambda e: e.activation(
                    out=lgT_h[:, :], in_=ptb_h[:, 0:8], func=AF.Copy),
                    reads=[ptb_b], writes=[lgT_b])
                S.op("dve", lambda e: e.reduce_max(
                    out=r1_h[:, :], in_=lgT_h[:, :], axis=mybir.AxisListType.X),
                    reads=[lgT_b], writes=[r1_b])
                S.op("dve", lambda e: e.tensor_scalar(
                    out=r1_h[:, :], in0=r1_h[:, :], scalar1=-1.0, scalar2=None,
                    op0=ALU.mult), writes=[r1_b])
                S.op("act", lambda e: e.activation(
                    out=ee_h[:, :], in_=lgT_h[:, :], func=AF.Exp,
                    bias=r1_h[:, 0:1]), reads=[lgT_b, r1_b], writes=[ee_b])
                S.op("dve", lambda e: e.reduce_sum(
                    out=r2_h[:, :], in_=ee_h[:, :], axis=mybir.AxisListType.X),
                    reads=[ee_b], writes=[r2_b])
                S.op("dve", lambda e: e.reciprocal(out=r2_h[:, :], in_=r2_h[:, :]),
                     writes=[r2_b])
                S.op("dve", lambda e: e.tensor_scalar(
                    out=pp_h[:, :], in0=ee_h[:, :], scalar1=r2_h[:, 0:1],
                    scalar2=None, op0=ALU.mult),
                    reads=[ee_b, r2_b], writes=[pp_b])
                S.op("dve", lambda e: e.reduce_max(
                    out=r1_h[:, :], in_=pp_h[:, :], axis=mybir.AxisListType.X),
                    reads=[pp_b], writes=[r1_b])
                S.op("dve", lambda e: e.tensor_scalar(
                    out=gx_h[:, :], in0=pp_h[:, :], scalar1=r1_h[:, 0:1],
                    scalar2=None, op0=ALU.is_ge),
                    reads=[pp_b, r1_b], writes=[gx_b])
                S.op("dve", lambda e: e.tensor_tensor(
                    out=gx_h[:, :], in0=pp_h[:, :], in1=gx_h[:, :], op=ALU.mult),
                    reads=[pp_b], writes=[gx_b])
                S.op("dve", lambda e: e.tensor_tensor(
                    out=gx_h[:, :], in0=pp_h[:, :], in1=gx_h[:, :],
                    op=ALU.subtract), reads=[pp_b], writes=[gx_b])
                S.op("dve", lambda e: e.reduce_max(
                    out=r2_h[:, :], in_=gx_h[:, :], axis=mybir.AxisListType.X),
                    reads=[gx_b], writes=[r2_b])
                S.op("dve", lambda e: e.tensor_scalar(
                    out=gx_h[:, :], in0=pp_h[:, :], scalar1=r2_h[:, 0:1],
                    scalar2=None, op0=ALU.is_ge),
                    reads=[pp_b, r2_b], writes=[gx_b])
                S.op("dve", lambda e: e.tensor_tensor(
                    out=pp_h[:, :], in0=pp_h[:, :], in1=gx_h[:, :], op=ALU.mult),
                    reads=[gx_b], writes=[pp_b])
                S.op("act", lambda e: e.activation(
                    out=cf_h[:, :], in_=pp_h[:, :], func=AF.Copy),
                    reads=[pp_b], writes=[cf_b])
                S.op("pe", lambda e: e.transpose(
                    ptb_h[0:8, :], cf_h[:, 0:8], ident_h[:, :]),
                    reads=[cf_b, ident_b], writes=[ptb_b])
                S.op("act", lambda e, j=j: e.activation(
                    out=cT_h[0:8, j * 128:(j + 1) * 128], in_=ptb_h[0:8, :],
                    func=AF.Copy), reads=[ptb_b], writes=[cT_b])
            S.dma("sp", lambda e, c0=c0: e.dma_start(
                out=bass.AP(coef_d, c0, [[TC, 8], [1, W]]), in_=cT_h[0:8, :]),
                reads=[cT_b], key="ost")


def stage_l3b(nc, S, sems, E, nchunks=None):
    W = 512
    NCH = TC // W if nchunks is None else nchunks
    KTM = D // 128
    KTF = DFF // 128
    xn3_d = E["xn3_scr"]
    coef_d = E["coef_scr"]
    x2_d = E["x2_scr"]
    weg_d = E["weg"]
    weu_d = E["weu"]
    wed_d = E["wed"]
    wsg_d = E["wsg"]
    wsu_d = E["wsu"]
    wsd_d = E["wsd"]
    ones_d = E["ones_in"]
    out_d = E["out"]
    ffn_d = E["ffn_scr"]

    with ExitStack() as ctx:
        ones_h, ones_b = sbuf(nc, ctx, "ones_sb", [128, 128], BF16)
        wsg_h, wsg_b = sbuf(nc, ctx, "wsg_bf", [128, KTM * D], BF16)
        wsu_h, wsu_b = sbuf(nc, ctx, "wsu_bf", [128, KTM * D], BF16)
        wsd_h, wsd_b = sbuf(nc, ctx, "wsd_bf", [128, KTM * D], BF16)
        xn3_h, xn3_b = sbuf(nc, ctx, "xn3sb", [128, KTM * W], BF16)
        ffn_h, ffn_b = sbuf(nc, ctx, "ffnb", [128, KTM * W], BF16)
        sq_h, sq_b = sbuf(nc, ctx, "sqb", [128, KTM * W], BF16)
        m4_h, m4_b = sbuf(nc, ctx, "m4row", [1, TC], BF16)
        q4_h, q4_b = sbuf(nc, ctx, "q4row", [1, TC], BF16)
        nm4_h, nm4_b = sbuf(nc, ctx, "nm4", [1, TC], BF16)
        rs4_h, rs4_b = sbuf(nc, ctx, "rs4", [1, TC], BF16)
        rtf_h, rtf_b = sbuf(nc, ctx, "rtf", [1, W], F32)
        tg_h, tg_b = sbuf(nc, ctx, "tg", [128, W], F32)
        ctxB = ExitStack()
        wgp = [sbuf(nc, ctxB, f"weg{i}", [128, KTM * DFF], BF16) for i in range(2)]
        wup = [sbuf(nc, ctxB, f"weu{i}", [128, KTM * DFF], BF16) for i in range(2)]
        wdp = [sbuf(nc, ctxB, f"wed{i}", [128, KTF * D], BF16) for i in range(2)]
        coef_h, coef_b = sbuf(nc, ctxB, "coefsb", [1, NE * W], BF16)
        gsb_h, gsb_b = sbuf(nc, ctxB, "gsb", [128, KTF * W], BF16)
        usb_h, usb_b = sbuf(nc, ctxB, "uub", [128, KTF * W], BF16)
        cme_h, cme_b = sbuf(nc, ctxB, "cme", [128, KTF * W], BF16)
        sm_h, sm_b = sbuf(nc, ctxB, "smb", [128, KTM * W], BF16)
        rsb_h, rsb_b = sbuf(nc, ctxB, "routed", [128, KTM * W], F32)
        pg_h, pg_b = psum(nc, ctx, "pg", [128, W])
        pu_h, pu_b = psum(nc, ctx, "pu", [128, W])
        pc_h, pc_b = psum(nc, ctx, "pc", [128, W])
        pd_h, pd_b = psum(nc, ctx, "pd", [128, W])
        pm_h, pm_b = psum(nc, ctx, "pm", [1, W])
        pq_h, pq_b = psum(nc, ctx, "pq", [1, W])
        pnm_h, pnm_b = psum(nc, ctx, "pnm", [128, W])
        prs_h, prs_b = psum(nc, ctx, "prs", [128, W])

        S.dma("pool", lambda e: e.dma_start(
            out=wsg_h[:, :].rearrange("p (a m) -> p a m", a=KTM),
            in_=wsg_d.ap().rearrange("(a p) m -> p a m", p=128)),
            writes=[wsg_b], key="wloadp")
        S.dma("pool", lambda e: e.dma_start(
            out=wsu_h[:, :].rearrange("p (a m) -> p a m", a=KTM),
            in_=wsu_d.ap().rearrange("(a p) m -> p a m", p=128)),
            writes=[wsu_b], key="wloadp")
        S.dma("pool", lambda e: e.dma_start(
            out=wsd_h[:, :].rearrange("p (a m) -> p a m", a=KTM),
            in_=wsd_d.ap().rearrange("(a p) m -> p a m", p=128)),
            writes=[wsd_b], key="wloadp")
        S.dma("sp", lambda e: e.dma_start(out=ones_h[:, :], in_=ones_d.ap()),
              writes=[ones_b], key="wload")

        for chn in range(NCH):
            c0 = chn * W
            S.dma("sp", lambda e, c0=c0: e.dma_start(
                out=xn3_h[:, :].rearrange("p (a t) -> p a t", a=KTM),
                in_=xn3_d.ap()[:, c0:c0 + W].rearrange("(a p) t -> p a t", p=128)),
                writes=[xn3_b], key="xnld")
            S.dma("sp", lambda e, c0=c0: e.dma_start(
                out=coef_h[:, :].rearrange("p (a t) -> p a t", a=NE),
                in_=bass.AP(coef_d, c0, [[1, 1], [TC, NE], [1, W]])),
                writes=[coef_b], key="cfld")
            for ei in range(NE):
                wg_h, wg_b = wgp[ei % 2]
                wu_h, wu_b = wup[ei % 2]
                wd_h, wd_b = wdp[ei % 2]
                ky = "we" + str(ei % 2)
                S.dma("pool", lambda e, ei=ei, wg_h=wg_h: e.dma_start(
                    out=wg_h[:, :].rearrange("p (a m) -> p a m", a=KTM),
                    in_=weg_d.ap()[ei * D:(ei + 1) * D, :].rearrange(
                        "(a p) m -> p a m", p=128)), writes=[wg_b], key=ky)
                S.dma("pool", lambda e, ei=ei, wu_h=wu_h: e.dma_start(
                    out=wu_h[:, :].rearrange("p (a m) -> p a m", a=KTM),
                    in_=weu_d.ap()[ei * D:(ei + 1) * D, :].rearrange(
                        "(a p) m -> p a m", p=128)), writes=[wu_b], key=ky)
                S.dma("pool", lambda e, ei=ei, wd_h=wd_h: e.dma_start(
                    out=wd_h[:, :].rearrange("p (a m) -> p a m", a=KTF),
                    in_=wed_d.ap()[ei * DFF:(ei + 1) * DFF, :].rearrange(
                        "(a p) m -> p a m", p=128)), writes=[wd_b], key=ky)
                S.op("pe", lambda e, ei=ei: e.matmul(
                    pc_h[:, :], ones_h[0:1, :],
                    coef_h[0:1, ei * W:(ei + 1) * W],
                    start=True, stop=True), reads=[coef_b, ones_b], writes=[pc_b])
                for mi2 in range(KTF):
                    fsl = slice(mi2 * W, (mi2 + 1) * W)
                    for k in range(KTM):
                        S.op("pe", lambda e, mi2=mi2, k=k, wg_h=wg_h: e.matmul(
                            pg_h[:, :], wg_h[:, k * DFF + mi2 * 128:k * DFF + mi2 * 128 + 128],
                            xn3_h[:, k * W:(k + 1) * W],
                            start=(k == 0), stop=(k == KTM - 1)),
                            reads=[xn3_b, wg_b], writes=[pg_b])
                    S.op("act", lambda e, fsl=fsl: e.activation(
                        out=gsb_h[:, fsl], in_=pg_h[:, :], func=AF.Silu),
                        reads=[pg_b], writes=[gsb_b])
                    for k in range(KTM):
                        S.op("pe", lambda e, mi2=mi2, k=k, wu_h=wu_h: e.matmul(
                            pu_h[:, :], wu_h[:, k * DFF + mi2 * 128:k * DFF + mi2 * 128 + 128],
                            xn3_h[:, k * W:(k + 1) * W],
                            start=(k == 0), stop=(k == KTM - 1)),
                            reads=[xn3_b, wu_b], writes=[pu_b])
                    S.op("act", lambda e, fsl=fsl: e.activation(
                        out=usb_h[:, fsl], in_=pu_h[:, :], func=AF.Copy),
                        reads=[pu_b], writes=[usb_b])
                    S.op("dve", lambda e, fsl=fsl: e.tensor_tensor(
                        out=gsb_h[:, fsl], in0=gsb_h[:, fsl], in1=usb_h[:, fsl],
                        op=ALU.mult), reads=[usb_b], writes=[gsb_b])
                    S.op("dve", lambda e, fsl=fsl: e.tensor_tensor(
                        out=cme_h[:, fsl], in0=gsb_h[:, fsl], in1=pc_h[:, :],
                        op=ALU.mult), reads=[gsb_b, pc_b], writes=[cme_b])
                for mo in range(KTM):
                    for k in range(KTF):
                        S.op("pe", lambda e, mo=mo, k=k, wd_h=wd_h: e.matmul(
                            pd_h[:, :], wd_h[:, k * D + mo * 128:k * D + mo * 128 + 128],
                            cme_h[:, k * W:(k + 1) * W],
                            start=(k == 0), stop=(k == KTF - 1)),
                            reads=[cme_b, wd_b], writes=[pd_b])
                    msl = slice(mo * W, (mo + 1) * W)
                    if ei == 0:
                        S.op("act", lambda e, msl=msl: e.activation(
                            out=rsb_h[:, msl], in_=pd_h[:, :], func=AF.Copy),
                            reads=[pd_b], writes=[rsb_b])
                    else:
                        S.op("dve", lambda e, msl=msl: e.tensor_tensor(
                            out=rsb_h[:, msl], in0=rsb_h[:, msl], in1=pd_h[:, :],
                            op=ALU.add), reads=[pd_b], writes=[rsb_b])
            for mi in range(KTM):
                msl = slice(mi * W, (mi + 1) * W)
                for k in range(KTM):
                    S.op("pe", lambda e, mi=mi, k=k: e.matmul(
                        pg_h[:, :], wsg_h[:, k * D + mi * 128:k * D + mi * 128 + 128],
                        xn3_h[:, k * W:(k + 1) * W],
                        start=(k == 0), stop=(k == KTM - 1)),
                        reads=[xn3_b, wsg_b], writes=[pg_b])
                S.op("act", lambda e, msl=msl: e.activation(
                    out=sq_h[:, msl], in_=pg_h[:, :], func=AF.Silu),
                    reads=[pg_b], writes=[sq_b])
                for k in range(KTM):
                    S.op("pe", lambda e, mi=mi, k=k: e.matmul(
                        pu_h[:, :], wsu_h[:, k * D + mi * 128:k * D + mi * 128 + 128],
                        xn3_h[:, k * W:(k + 1) * W],
                        start=(k == 0), stop=(k == KTM - 1)),
                        reads=[xn3_b, wsu_b], writes=[pu_b])
                S.op("act", lambda e, msl=msl: e.activation(
                    out=ffn_h[:, msl], in_=pu_h[:, :], func=AF.Copy),
                    reads=[pu_b], writes=[ffn_b])
                S.op("dve", lambda e, msl=msl: e.tensor_tensor(
                    out=sm_h[:, msl], in0=sq_h[:, msl], in1=ffn_h[:, msl],
                    op=ALU.mult), reads=[sq_b, ffn_b], writes=[sm_b])
            for mo in range(KTM):
                msl = slice(mo * W, (mo + 1) * W)
                for k in range(KTM):
                    S.op("pe", lambda e, mo=mo, k=k: e.matmul(
                        pd_h[:, :], wsd_h[:, k * D + mo * 128:k * D + mo * 128 + 128],
                        sm_h[:, k * W:(k + 1) * W],
                        start=(k == 0), stop=(k == KTM - 1)),
                        reads=[sm_b, wsd_b], writes=[pd_b])
                S.op("dve", lambda e, msl=msl: e.tensor_tensor(
                    out=rsb_h[:, msl], in0=rsb_h[:, msl], in1=pd_h[:, :],
                    op=ALU.add), reads=[pd_b], writes=[rsb_b])
                S.op("act", lambda e, msl=msl: e.activation(
                    out=ffn_h[:, msl], in_=rsb_h[:, msl], func=AF.Copy),
                    reads=[rsb_b], writes=[ffn_b])
                S.op("act", lambda e, msl=msl: e.activation(
                    out=sq_h[:, msl], in_=rsb_h[:, msl], func=AF.Square),
                    reads=[rsb_b], writes=[sq_b])
            for mo in range(KTM):
                msl = slice(mo * W, (mo + 1) * W)
                S.op("pe", lambda e, msl=msl, mo=mo: e.matmul(
                    pm_h[0:1, :], ones_h[:, 0:1], ffn_h[:, msl],
                    start=(mo == 0), stop=(mo == KTM - 1)),
                    reads=[ffn_b, ones_b], writes=[pm_b])
                S.op("pe", lambda e, msl=msl, mo=mo: e.matmul(
                    pq_h[0:1, :], ones_h[:, 0:1], sq_h[:, msl],
                    start=(mo == 0), stop=(mo == KTM - 1)),
                    reads=[sq_b, ones_b], writes=[pq_b])
            S.op("act", lambda e, c0=c0: e.activation(
                out=m4_h[0:1, c0:c0 + W], in_=pm_h[0:1, :], func=AF.Copy),
                reads=[pm_b], writes=[m4_b])
            S.op("act", lambda e, c0=c0: e.activation(
                out=q4_h[0:1, c0:c0 + W], in_=pq_h[0:1, :], func=AF.Copy),
                reads=[pq_b], writes=[q4_b])
            S.dma("sp", lambda e, c0=c0: e.dma_start(
                out=ffn_d.ap()[:, c0:c0 + W].rearrange("(a p) t -> p a t", p=128),
                in_=ffn_h[:, :].rearrange("p (a t) -> p a t", a=KTM)),
                reads=[ffn_b], key="ffst")

        S.barrier()
        ctxB.close()
        x2sb_h, x2sb_b = sbuf(nc, ctx, "x2sb", [128, KTM * W], F32)
        for chn in range(NCH):
            c0 = chn * W
            ln_rowmath_sb(S, m4_h, m4_b, q4_h, q4_b, nm4_h, nm4_b, rs4_h, rs4_b,
                          rtf_h, rtf_b, c0, c0, W, D)
        for chn in range(NCH):
            c0 = chn * W
            S.dma("sp", lambda e, c0=c0: e.dma_start(
                out=x2sb_h[:, :].rearrange("p (a t) -> p a t", a=KTM),
                in_=x2_d.ap()[:, c0:c0 + W].rearrange("(a p) t -> p a t", p=128)),
                writes=[x2sb_b], key="x2ld")
            S.dma("sp", lambda e, c0=c0: e.dma_start(
                out=ffn_h[:, :].rearrange("p (a t) -> p a t", a=KTM),
                in_=ffn_d.ap()[:, c0:c0 + W].rearrange("(a p) t -> p a t", p=128)),
                writes=[ffn_b], key="ffld")
            S.op("pe", lambda e, c0=c0: e.matmul(
                pnm_h[:, :], ones_h[0:1, :], nm4_h[0:1, c0:c0 + W],
                start=True, stop=True), reads=[nm4_b, ones_b], writes=[pnm_b])
            S.op("pe", lambda e, c0=c0: e.matmul(
                prs_h[:, :], ones_h[0:1, :], rs4_h[0:1, c0:c0 + W],
                start=True, stop=True), reads=[rs4_b, ones_b], writes=[prs_b])
            for mi in range(KTM):
                msl = slice(mi * W, (mi + 1) * W)
                S.op("dve", lambda e, msl=msl: e.tensor_tensor(
                    out=tg_h[:, :], in0=ffn_h[:, msl], in1=pnm_h[:, :],
                    op=ALU.add), reads=[ffn_b, pnm_b], writes=[tg_b])
                S.op("dve", lambda e, msl=msl: e.tensor_tensor(
                    out=tg_h[:, :], in0=tg_h[:, :], in1=prs_h[:, :], op=ALU.mult),
                    reads=[prs_b], writes=[tg_b])
                S.op("dve", lambda e, msl=msl: e.tensor_tensor(
                    out=x2sb_h[:, msl], in0=tg_h[:, :], in1=x2sb_h[:, msl],
                    op=ALU.add), reads=[tg_b], writes=[x2sb_b])
            S.dma("sp", lambda e, c0=c0: e.dma_start(
                out=out_d.ap()[:, c0:c0 + W].rearrange("(a p) t -> p a t", p=128),
                in_=x2sb_h[:, :].rearrange("p (a t) -> p a t", a=KTM)),
                reads=[x2sb_b], key="ost")




def build_l3():
    nc = bass.Bass("TRN2", target_bir_lowering=False)
    E = {}
    E["y_my"] = nc.dram_tensor("y", [DI, TC], BF16, kind="ExternalInput")
    E["u_scr"] = nc.dram_tensor("u", [DI, TC], BF16, kind="ExternalInput")
    E["zs_scr"] = nc.dram_tensor("zs", [DI, TC], BF16, kind="ExternalInput")
    E["x_nohalo"] = nc.dram_tensor("x", [D, TC + 4], F32, kind="ExternalInput")
    E["dskip"] = nc.dram_tensor("dskip", [128, DI // 128], F32, kind="ExternalInput")
    E["wout"] = nc.dram_tensor("wout", [DI, D], F32, kind="ExternalInput")
    E["router"] = nc.dram_tensor("router", [D, NE], F32, kind="ExternalInput")
    E["weg"] = nc.dram_tensor("weg", [NE * D, DFF], F32, kind="ExternalInput")
    E["weu"] = nc.dram_tensor("weu", [NE * D, DFF], F32, kind="ExternalInput")
    E["wed"] = nc.dram_tensor("wed", [NE * DFF, D], F32, kind="ExternalInput")
    E["wsg"] = nc.dram_tensor("wsg", [D, D], F32, kind="ExternalInput")
    E["wsu"] = nc.dram_tensor("wsu", [D, D], F32, kind="ExternalInput")
    E["wsd"] = nc.dram_tensor("wsd", [D, D], F32, kind="ExternalInput")
    E["ones_in"] = nc.dram_tensor("ones", [128, 128], BF16, kind="ExternalInput")
    E["ident_in"] = nc.dram_tensor("ident", [128, 128], BF16, kind="ExternalInput")
    E["out"] = nc.dram_tensor("out", [D, TC], F32, kind="ExternalOutput")
    E["x2_scr"] = nc.dram_tensor("x2_scr", [D, TC], F32, kind="Internal")
    E["xn3_scr"] = nc.dram_tensor("xn3_scr", [D, TC], BF16, kind="Internal")
    E["coef_scr"] = nc.dram_tensor("coef_scr", [1, NE * TC], BF16, kind="Internal")
    E["ffn_scr"] = nc.dram_tensor("ffn_scr", [D, TC], BF16, kind="Internal")
    with ExitStack() as ctx:
        S = Sched(nc)
        _PREFIX[0] = "c_"
        stage_l3a(nc, S, None, E)
        S.barrier()
        _PREFIX[0] = "d_"
        stage_l3b(nc, S, None, E)
        _PREFIX[0] = ""
        sems = S.create_sems(nc, ctx)
        block = ctx.enter_context(nc.Block())
        S.emit(block, sems)
    return nc


def build_l1_only():
    nc = bass.Bass("TRN2", target_bir_lowering=False)
    E = {}
    E["x"] = nc.dram_tensor("x", [D, TC + 4], F32, kind="ExternalInput")
    E["w_in"] = nc.dram_tensor("w_in", [D, 2 * DI], F32, kind="ExternalInput")
    E["w_xp"] = nc.dram_tensor("w_xp", [DI, 96], F32, kind="ExternalInput")
    E["conv_w"] = nc.dram_tensor("conv_w", [128, (DI // 128) * DCONV], F32, kind="ExternalInput")
    E["conv_b"] = nc.dram_tensor("conv_b", [128, DI // 128], F32, kind="ExternalInput")
    E["ones_in"] = nc.dram_tensor("ones", [128, 128], BF16, kind="ExternalInput")
    E["u_scr"] = nc.dram_tensor("u", [DI, TC], BF16, kind="ExternalOutput")
    E["zs_scr"] = nc.dram_tensor("zs", [DI, TC], BF16, kind="ExternalOutput")
    E["xdbl_scr"] = nc.dram_tensor("xdbl", [96, TC], F32, kind="ExternalOutput")
    with ExitStack() as ctx:
        S = Sched(nc)
        _PREFIX[0] = "a_"
        stage_l1(nc, S, None, E)
        _PREFIX[0] = ""
        sems = S.create_sems(nc, ctx)
        block = ctx.enter_context(nc.Block())
        S.emit(block, sems)
    return nc


def build_l2_only():
    nc = bass.Bass("TRN2", target_bir_lowering=False)
    E = {}
    E["u_my"] = nc.dram_tensor("u", [DI, TC], BF16, kind="ExternalInput")
    E["xdbl_ag"] = nc.dram_tensor("xdbl", [NC * 96, TC], F32, kind="ExternalInput")
    E["wdt"] = nc.dram_tensor("wdt", [DTR, DDI], F32, kind="ExternalInput")
    E["bdt"] = nc.dram_tensor("bdt", [128, DDI // 128], F32, kind="ExternalInput")
    E["alog"] = nc.dram_tensor("alog", [DDI, DS], F32, kind="ExternalInput")
    E["ones_in"] = nc.dram_tensor("ones", [128, 128], BF16, kind="ExternalInput")
    E["ident_in"] = nc.dram_tensor("ident", [128, 128], BF16, kind="ExternalInput")
    E["y_scr"] = nc.dram_tensor("y", [DI, TC], BF16, kind="ExternalOutput")
    with ExitStack() as ctx:
        S = Sched(nc)
        _PREFIX[0] = "b_"
        stage_l2(nc, S, None, E)
        _PREFIX[0] = ""
        sems = S.create_sems(nc, ctx)
        block = ctx.enter_context(nc.Block())
        S.emit(block, sems)
    return nc


RG = [[0, 1, 2, 3, 4, 5, 6, 7]]


def build_full():
    nc = bass.Bass("TRN2", target_bir_lowering=False)
    E = {}
    E["x"] = nc.dram_tensor("x", [D, TC + 4], F32, kind="ExternalInput")
    E["x_nohalo"] = E["x"]
    E["w_in"] = nc.dram_tensor("w_in", [D, 2 * DI], F32, kind="ExternalInput")
    E["w_xp"] = nc.dram_tensor("w_xp", [DI, 96], F32, kind="ExternalInput")
    E["conv_w"] = nc.dram_tensor("conv_w", [128, (DI // 128) * DCONV], F32, kind="ExternalInput")
    E["conv_b"] = nc.dram_tensor("conv_b", [128, DI // 128], F32, kind="ExternalInput")
    E["wdt"] = nc.dram_tensor("wdt", [DTR, DDI], F32, kind="ExternalInput")
    E["bdt"] = nc.dram_tensor("bdt", [128, DDI // 128], F32, kind="ExternalInput")
    E["alog"] = nc.dram_tensor("alog", [DDI, DS], F32, kind="ExternalInput")
    E["dskip"] = nc.dram_tensor("dskip", [128, DI // 128], F32, kind="ExternalInput")
    E["wout"] = nc.dram_tensor("wout", [DI, D], F32, kind="ExternalInput")
    E["router"] = nc.dram_tensor("router", [D, NE], F32, kind="ExternalInput")
    E["weg"] = nc.dram_tensor("weg", [NE * D, DFF], F32, kind="ExternalInput")
    E["weu"] = nc.dram_tensor("weu", [NE * D, DFF], F32, kind="ExternalInput")
    E["wed"] = nc.dram_tensor("wed", [NE * DFF, D], F32, kind="ExternalInput")
    E["wsg"] = nc.dram_tensor("wsg", [D, D], F32, kind="ExternalInput")
    E["wsu"] = nc.dram_tensor("wsu", [D, D], F32, kind="ExternalInput")
    E["wsd"] = nc.dram_tensor("wsd", [D, D], F32, kind="ExternalInput")
    E["ones_in"] = nc.dram_tensor("ones", [128, 128], BF16, kind="ExternalInput")
    E["ident_in"] = nc.dram_tensor("ident", [128, 128], BF16, kind="ExternalInput")
    E["out"] = nc.dram_tensor("out", [D, TC], F32, kind="ExternalOutput")
    E["u_scr"] = nc.dram_tensor("u_scr", [DI, TC], BF16, kind="Internal")
    E["zs_scr"] = nc.dram_tensor("zs_scr", [DI, TC], BF16, kind="Internal")
    E["xdbl_scr"] = nc.dram_tensor("xdbl_scr", [96, TC], F32, kind="Internal")
    E["u_ag"] = nc.dram_tensor("u_ag", [NC * DI, TC], BF16, kind="Internal")
    E["cofs_in"] = nc.dram_tensor("cofs", [1, 1], mybir.dt.int32, kind="ExternalInput")
    E["xdbl_ag"] = nc.dram_tensor("xdbl_ag", [NC * 96, TC], F32, kind="Internal")
    E["y_scr"] = nc.dram_tensor("y_scr", [DI, TC], BF16, kind="Internal")
    E["y_ag"] = nc.dram_tensor("y_ag", [NC * DI, TC], BF16, kind="Internal")
    E["u_my"] = nc.dram_tensor("u_my", [DI, TC], BF16, kind="Internal")
    E["y_my"] = nc.dram_tensor("y_my", [DI, TC], BF16, kind="Internal")
    E["x2_scr"] = nc.dram_tensor("x2_scr", [D, TC], F32, kind="Internal")
    E["xn3_scr"] = nc.dram_tensor("xn3_scr", [D, TC], BF16, kind="Internal")
    E["coef_scr"] = nc.dram_tensor("coef_scr", [1, NE * TC], BF16, kind="Internal")
    E["ffn_scr"] = nc.dram_tensor("ffn_scr", [D, TC], BF16, kind="Internal")

    with ExitStack() as ctx:
        sems = None
        S = Sched(nc)
        cofs_sb = ctx.enter_context(nc.sbuf_tensor("cofs_sb", [1, 1], mybir.dt.int32))
        cofs_buf = Buf("cofs")
        E["cofs"] = {}

        def _cofs_setup(e, E=E, cofs_sb=cofs_sb):
            reg = e.alloc_register("cofs_reg")
            inst = e.reg_load(reg, cofs_sb[0:1, 0:1])
            E["cofs"]["v"] = e.snap(reg)
            return inst

        S.dma("pool", lambda e: e.dma_start(out=cofs_sb[0:1, 0:1],
                                            in_=E["cofs_in"].ap()),
              writes=[cofs_buf], key="wloadp")
        S.op("pool", _cofs_setup, reads=[cofs_buf])
        _PREFIX[0] = "a_"
        stage_l1(nc, S, sems, E)
        S.barrier()
        b_uag, b_umy = Buf("u_ag"), Buf("u_my")
        S.dma("pool", lambda e: e.collective_compute(
            "AllGather", ALU.bypass, replica_groups=RG,
            ins=[E["xdbl_scr"].ap()], outs=[E["xdbl_ag"].ap()]), key="coll")
        S.dma("pool", lambda e: e.collective_compute(
            "AllGather", ALU.bypass, replica_groups=RG,
            ins=[E["u_scr"].ap()], outs=[E["u_ag"].ap()]), key="coll",
            writes=[b_uag])
        for rr in range(NC):
            S.dma("pool", lambda e, rr=rr: e.dma_start(
                out=E["u_my"].ap()[rr * DDI:(rr + 1) * DDI, :],
                in_=E["u_ag"].ap()[bass.ds(E["cofs"]["v"] + rr * DI, DDI), :]),
                key="uld", reads=[b_uag], writes=[b_umy])
        S.barrier()
        _PREFIX[0] = "b_"
        stage_l2(nc, S, sems, E)
        S.barrier()
        b_yag, b_ymy = Buf("y_ag"), Buf("y_my")
        S.dma("pool", lambda e: e.collective_compute(
            "AllGather", ALU.bypass, replica_groups=RG,
            ins=[E["y_scr"].ap()], outs=[E["y_ag"].ap()]), key="coll",
            writes=[b_yag])
        for j in range(NC):
            S.dma("pool", lambda e, j=j: e.dma_start(
                out=E["y_my"].ap()[j * DDI:(j + 1) * DDI, :],
                in_=E["y_ag"].ap()[bass.ds(E["cofs"]["v"] + j * DI, DDI), :]),
                key="yld", reads=[b_yag], writes=[b_ymy])
        S.barrier()
        _PREFIX[0] = "c_"
        stage_l3a(nc, S, sems, E)
        S.barrier()
        _PREFIX[0] = "d_"
        stage_l3b(nc, S, sems, E)
        _PREFIX[0] = ""
        sems = S.create_sems(nc, ctx)
        block = ctx.enter_context(nc.Block())
        S.emit(block, sems)
    return nc


_FULL = [None, None, None]


def _prep_common(inputs):
    p = {}
    p["w_in"] = np.ascontiguousarray(_np(inputs["L0_attn_W_in"]), np.float32)
    p["w_xp"] = np.ascontiguousarray(_np(inputs["L0_attn_W_xproj"]), np.float32)
    p["cw"] = np.ascontiguousarray(_np(inputs["L0_attn_conv_w"]).reshape(DI // 128, 128, DCONV)
                                   .transpose(1, 0, 2).reshape(128, -1), np.float32)
    p["cb"] = np.ascontiguousarray(_np(inputs["L0_attn_conv_b"]).reshape(DI // 128, 128).T,
                                   np.float32)
    p["wdt"] = np.asarray(inputs["L0_attn_W_dt"], np.float32)
    p["bdt"] = np.asarray(inputs["L0_attn_b_dt"], np.float32)
    p["alog"] = np.asarray(inputs["L0_attn_A_log"], np.float32)
    p["dskip"] = np.ascontiguousarray(
        np.asarray(inputs["L0_attn_D_skip"], np.float32).reshape(DI // 128, 128).T)
    p["wout"] = np.ascontiguousarray(_np(inputs["L0_attn_W_out"]), np.float32)
    p["router"] = np.ascontiguousarray(_np(inputs["L0_ffn_router"]), np.float32)
    p["weg"] = np.ascontiguousarray(_np(inputs["L0_ffn_experts_gate"]).reshape(NE * D, DFF), np.float32)
    p["weu"] = np.ascontiguousarray(_np(inputs["L0_ffn_experts_up"]).reshape(NE * D, DFF), np.float32)
    p["wed"] = np.ascontiguousarray(_np(inputs["L0_ffn_experts_down"]).reshape(NE * DFF, D), np.float32)
    p["wsg"] = np.ascontiguousarray(_np(inputs["L0_ffn_shared_gate"]), np.float32)
    p["wsu"] = np.ascontiguousarray(_np(inputs["L0_ffn_shared_up"]), np.float32)
    p["wsd"] = np.ascontiguousarray(_np(inputs["L0_ffn_shared_down"]), np.float32)
    p["ones"] = np.ones((128, 128), BF)
    p["ident"] = np.eye(128).astype(BF)
    return p


def kernel(**inputs):
    if _FULL[0] is None:
        _FULL[0] = build_l1_only()
        _FULL[1] = build_l2_only()
        _FULL[2] = build_l3()
    p = _prep_common(inputs)
    x = _np(inputs["L0_attn_x"])
    xs_all = []
    for c in range(NC):
        b, s = c // 4, (c % 4) * TC
        xs = np.zeros((D, TC + 4), np.float32)
        lo = max(0, s - 4)
        xs[:, 4 - (s - lo):] = x[b, lo:s + TC, :].T
        xs_all.append(np.ascontiguousarray(xs))
    in1 = [dict(x=xs_all[c], w_in=p["w_in"], w_xp=p["w_xp"], conv_w=p["cw"],
                conv_b=p["cb"], ones=p["ones"]) for c in range(NC)]
    r1 = run_bass_kernel_spmd(_FULL[0], in1, core_ids=list(range(NC)))
    u = np.concatenate([np.asarray(r["u"]) for r in r1.results], axis=1)
    zs = [np.asarray(r["zs"]) for r in r1.results]
    xdbl = np.ascontiguousarray(
        np.concatenate([np.asarray(r["xdbl"]) for r in r1.results], axis=0))
    in2 = []
    for c in range(NC):
        dsl = slice(c * DDI, (c + 1) * DDI)
        u_my = np.concatenate([u[dsl][:, k * TC:(k + 1) * TC] for k in range(NC)],
                              axis=0)
        in2.append(dict(
            u=np.ascontiguousarray(u_my), xdbl=xdbl,
            wdt=np.ascontiguousarray(p["wdt"][:, dsl]),
            bdt=np.ascontiguousarray(p["bdt"][dsl].reshape(DDI // 128, 128).T),
            alog=np.ascontiguousarray(p["alog"][dsl]),
            ones=p["ones"], ident=p["ident"]))
    r2 = run_bass_kernel_spmd(_FULL[1], in2, core_ids=list(range(NC)))
    yfull = np.concatenate([np.asarray(r["y"]) for r in r2.results], axis=0)
    in3 = []
    for c in range(NC):
        yv = np.concatenate([yfull[k * DI + c * DDI:k * DI + (c + 1) * DDI]
                             for k in range(NC)], axis=0)
        tsl = slice(c * TC, (c + 1) * TC)
        in3.append(dict(
            y=np.ascontiguousarray(yv),
            u=np.ascontiguousarray(u[:, tsl]), zs=zs[c], x=xs_all[c],
            dskip=p["dskip"], wout=p["wout"], router=p["router"],
            weg=p["weg"], weu=p["weu"], wed=p["wed"], wsg=p["wsg"],
            wsu=p["wsu"], wsd=p["wsd"], ones=p["ones"], ident=p["ident"]))
    r3 = run_bass_kernel_spmd(_FULL[2], in3, core_ids=list(range(NC)))
    out_fm = np.concatenate([np.asarray(r["out"]) for r in r3.results], axis=1)
    return np.ascontiguousarray(out_fm.T.reshape(B, L, D)).astype(np.float32)


# version 07
# speedup vs baseline: 1.1640x; 1.1329x over previous
import numpy as np
import ml_dtypes
import concourse.bass as bass
import concourse.mybir as mybir
from concourse.bass_utils import run_bass_kernel_spmd
from contextlib import ExitStack

F32 = mybir.dt.float32
BF16 = mybir.dt.bfloat16
AF = mybir.ActivationFunctionType
ALU = mybir.AluOpType
BF = ml_dtypes.bfloat16

B, L, D = 2, 16384, 1024
DI, DS, DCONV, DTR = 2048, 16, 4, 64
NE, DFF = 8, 512
NC = 8
TC = (B * L) // NC
DDI = DI // NC
EPS = 1e-5

ENGINES = ("sp", "act", "dve", "pool", "pe")


class Op:
    __slots__ = ("eng", "fn", "deps", "dma", "idx", "inc_val", "needed", "key",
                 "seq", "wait_val")

    def __init__(self, eng, fn, deps, dma):
        self.eng, self.fn, self.deps, self.dma = eng, fn, deps, dma
        self.idx = -1
        self.inc_val = None
        self.needed = False


class Buf:
    __slots__ = ("name", "last_writer", "readers", "exclusive")

    def __init__(self, name="", exclusive=False):
        self.name = name
        self.last_writer = None
        self.readers = []
        self.exclusive = exclusive


SYNC_SAME_ENGINE = [True]


class Sched:

    def __init__(self, nc):
        self.nc = nc
        self.streams = {e: [] for e in ENGINES}
        self._seq = 0
        self._last_dma = {}

    def op(self, eng, fn, reads=(), writes=(), deps=(), dma=False, key=None):
        reads = list(reads)
        writes = list(writes)
        excl = [b for b in reads if b.exclusive]
        if excl:
            reads = [b for b in reads if not b.exclusive]
            writes = writes + excl
        dd = list(deps)
        for b in reads:
            if b.last_writer is not None:
                dd.append(b.last_writer)
        for b in writes:
            if b.last_writer is not None:
                dd.append(b.last_writer)
            dd.extend(b.readers)
        o = Op(eng, fn, dd, dma)
        o.key = key
        o.idx = len(self.streams[eng])
        o.seq = self._seq
        self._seq += 1
        self.streams[eng].append(o)
        for b in reads:
            b.readers.append(o)
        for b in writes:
            b.last_writer = o
            b.readers = []
        return o

    def dma(self, eng, fn, reads=(), writes=(), deps=(), key=None):
        o = self.op(eng, fn, reads=reads, writes=writes, deps=deps, dma=True,
                    key=key)
        self._last_dma[self._key(o)] = o
        return o

    def create_sems(self, nc, ctx):
        sems = {}
        keys = set(ENGINES)
        for e in ENGINES:
            for o in self.streams[e]:
                if o.dma:
                    keys.add(self._key(o))
        for k in sorted(keys):
            sems[k] = ctx.enter_context(nc.semaphore("s_" + k.replace(".", "_")))
        return sems

    def barrier(self):
        lasts = [s[-1] for s in self.streams.values() if s]
        lasts += list(self._last_dma.values())
        nops = [self.op(e, lambda eng: eng.nop(), deps=lasts) for e in ENGINES]
        for e in ENGINES:
            self.op(e, lambda eng: eng.nop(), deps=nops)

    def _key(self, o):
        return o.eng + "." + (o.key or "dma")

    def _resolve(self):
        for e in ENGINES:
            for o in self.streams[e]:
                for d in o.deps:
                    if (not SYNC_SAME_ENGINE[0] and d.eng == o.eng
                            and not d.dma and not o.dma):
                        continue
                    d.needed = True
        self._dma_hist = {}
        for e in ENGINES:
            c_comp = 0
            c_dma = {}
            for o in self.streams[e]:
                if o.dma:
                    kk = self._key(o)
                    c_dma[kk] = c_dma.get(kk, 0) + 16
                    o.inc_val = c_dma[kk]
                    o.needed = True
                    self._dma_hist.setdefault(kk, []).append((o.seq, o.inc_val))
                elif o.needed:
                    c_comp += 1
                    o.inc_val = c_comp
                else:
                    o.inc_val = None

    def emit(self, block, sems, final_eng="dve"):
        self._resolve()
        sched = self
        finals = {}
        for e in ENGINES:
            for o in self.streams[e]:
                if o.dma:
                    kk = self._key(o)
                    finals[kk] = max(finals.get(kk, 0), o.inc_val)
                elif o.inc_val is not None:
                    finals[e] = max(finals.get(e, 0), o.inc_val)

        def mk_runner(ename):
            def run(eng):
                water = {}
                prev_completed = 0
                for o in sched.streams[ename]:
                    want = {}
                    for dp in o.deps:
                        if dp.inc_val is None:
                            continue
                        if dp.eng == o.eng and not dp.dma:
                            if not SYNC_SAME_ENGINE[0] and not o.dma:
                                continue
                            if dp.inc_val <= prev_completed:
                                continue
                        if dp.dma:
                            key = sched._key(dp)
                            v = 0
                            for sq, cum in sched._dma_hist[key]:
                                if sq < o.seq:
                                    v = cum
                                else:
                                    break
                            if v > want.get(key, 0):
                                want[key] = v
                            continue
                        key = dp.eng
                        if dp.inc_val > want.get(key, 0):
                            want[key] = dp.inc_val
                    for key in sorted(want):
                        v = want[key]
                        if key == ename:
                            if v > prev_completed:
                                eng.wait_ge(sems[key], v)
                                prev_completed = v
                            continue
                        if water.get(key, 0) >= v:
                            continue
                        eng.wait_ge(sems[key], v)
                        water[key] = v
                    inst = o.fn(eng)
                    if o.inc_val is not None:
                        key = sched._key(o) if o.dma else o.eng
                        inst.then_inc(sems[key], 16 if o.dma else 1)
                if ename == final_eng:
                    for key in sorted(finals):
                        if key in ENGINES:
                            continue
                        if finals[key] > water.get(key, 0) and finals[key] > 0:
                            eng.wait_ge(sems[key], finals[key])
            return run

        block.sync(mk_runner("sp"))
        block.scalar(mk_runner("act"))
        block.vector(mk_runner("dve"))
        block.gpsimd(mk_runner("pool"))
        block.tensor(mk_runner("pe"))


def make_sems(nc, ctx, extra=("wload",)):
    sems = {}
    for e in ENGINES:
        sems[e] = ctx.enter_context(nc.semaphore(f"s_{e}"))
        sems[e + "_dma"] = ctx.enter_context(nc.semaphore(f"s_{e}_dma"))
    for k in extra:
        sems[k] = ctx.enter_context(nc.semaphore(f"s_{k}"))
    return sems


_PREFIX = [""]


def sbuf(nc, ctx, name, shape, dtype):
    name = _PREFIX[0] + name
    h = ctx.enter_context(nc.sbuf_tensor(name, list(shape), dtype))
    return h, Buf(name)


def psum(nc, ctx, name, shape, dtype=F32):
    name = _PREFIX[0] + name
    h = ctx.enter_context(nc.psum_tensor(name, list(shape), dtype))
    return h, Buf(name, exclusive=True)


def ln_rowmath(S, pm_h, pm_b, psq_h, psq_b, negm_h, negm_b, rstd_h, rstd_b,
               tmp_h, tmp_b, col0, w, nfeat):
    sl = slice(col0, col0 + w)
    S.op("act", lambda e: e.activation(out=negm_h[0:1, sl], in_=pm_h[0:1, 0:w],
                                       func=AF.Copy, scale=-1.0 / nfeat),
         reads=[pm_b], writes=[negm_b])
    S.op("act", lambda e: e.activation(out=tmp_h[0:1, sl], in_=pm_h[0:1, 0:w],
                                       func=AF.Square, scale=1.0 / nfeat),
         reads=[pm_b], writes=[tmp_b])
    S.op("act", lambda e: e.activation(out=rstd_h[0:1, sl], in_=psq_h[0:1, 0:w],
                                       func=AF.Copy, scale=1.0 / nfeat),
         reads=[psq_b], writes=[rstd_b])
    S.op("dve", lambda e: e.tensor_tensor(out=rstd_h[0:1, sl], in0=rstd_h[0:1, sl],
                                          in1=tmp_h[0:1, sl], op=ALU.subtract),
         reads=[tmp_b], writes=[rstd_b])
    S.op("dve", lambda e: e.tensor_scalar(out=rstd_h[0:1, sl], in0=rstd_h[0:1, sl],
                                          scalar1=EPS, scalar2=None, op0=ALU.add),
         writes=[rstd_b])
    S.op("act", lambda e: e.activation(out=rstd_h[0:1, sl], in_=rstd_h[0:1, sl],
                                       func=AF.Ln),
         writes=[rstd_b])
    S.op("act", lambda e: e.activation(out=rstd_h[0:1, sl], in_=rstd_h[0:1, sl],
                                       func=AF.Exp, scale=-0.5),
         writes=[rstd_b])


def stage_l1(nc, S, sems, E, debug=False, p2_nchunks=None):
    x_d = E["x"]
    win_d = E["w_in"]
    wxp_d = E["w_xp"]
    cw_d = E["conv_w"]
    cb_d = E["conv_b"]
    ones_d = E["ones_in"]
    u_d = E["u_scr"]
    zs_d = E["zs_scr"]
    xdbl_d = E["xdbl_scr"]

    KT = D // 128
    MT = 2 * DI // 128
    NCH = TC // 512
    W = 512
    chunks = [(0, 4)] + [(4 + W * c, W) for c in range(NCH)]
    x_ap = x_d.ap()

    with ExitStack() as ctx:
        win_h, win_b = sbuf(nc, ctx, "win_bf", [128, KT * 2 * DI], BF16)
        wxp_h, wxp_b = sbuf(nc, ctx, "wxp_bf", [128, (DI // 128) * 96], BF16)
        cw_h, cw_b = sbuf(nc, ctx, "cw", [128, (DI // 128) * DCONV], F32)
        cb_h, cb_b = sbuf(nc, ctx, "cb", [128, DI // 128], F32)
        ones_h, ones_b = sbuf(nc, ctx, "ones_sb", [128, 128], BF16)
        negmb_h, negmb_b = sbuf(nc, ctx, "negm_bf", [1, TC + 4], BF16)
        rstdb_h, rstdb_b = sbuf(nc, ctx, "rstd_bf", [1, TC + 4], BF16)
        nm_s, nm_sb = sbuf(nc, ctx, "nm_s", [1, W], F32)
        rs_s, rs_sb = sbuf(nc, ctx, "rs_s", [1, W], F32)
        rt_s, rt_sb = sbuf(nc, ctx, "rt_s", [1, W], F32)
        xbf_h, xbf_b = sbuf(nc, ctx, "xbf", [128, KT * W], BF16)
        sqf_h, sqf_b = sbuf(nc, ctx, "sqf", [128, KT * W], BF16)
        xn_h, xn_b = sbuf(nc, ctx, "xn", [128, KT * W], BF16)
        tfa_h, tfa_b = sbuf(nc, ctx, "tfa", [128, W], F32)
        xin = [sbuf(nc, ctx, f"xin{i}", [128, 4 + W], F32) for i in range(DI // 128)]
        u_h, u_b = sbuf(nc, ctx, "usb", [128, (DI // 128) * W], BF16)
        ur_h, ur_b = sbuf(nc, ctx, "uraw", [128, W], F32)
        zs_h, zs_b = sbuf(nc, ctx, "zsb", [128, (DI // 128) * W], BF16)
        xdb_h, xdb_b = sbuf(nc, ctx, "xdbuf", [96, W], F32)
        pmean_h, pmean_b = psum(nc, ctx, "pmean", [1, W])
        psq_h, psq_b = psum(nc, ctx, "psq", [1, W])
        pnm_h, pnm_b = psum(nc, ctx, "pnm", [128, W])
        prs_h, prs_b = psum(nc, ctx, "prs", [128, W])
        pxz = [psum(nc, ctx, f"pxz{i}", [128, W]) for i in range(2)]
        pxd_h, pxd_b = psum(nc, ctx, "pxd", [96, W])

        S.dma("pool", lambda e: e.dma_start(
            out=win_h[:, :].rearrange("p (a m) -> p a m", a=KT),
            in_=win_d.ap().rearrange("(a p) m -> p a m", p=128)),
            writes=[win_b], key="wload")
        S.dma("pool", lambda e: e.dma_start(
            out=wxp_h[:, :].rearrange("p (a m) -> p a m", a=DI // 128),
            in_=wxp_d.ap().rearrange("(a p) m -> p a m", p=128)),
            writes=[wxp_b], key="wload")
        S.dma("sp", lambda e: e.dma_start(out=cw_h[:, :], in_=cw_d.ap()),
              writes=[cw_b])
        S.dma("sp", lambda e: e.dma_start(out=cb_h[:, :], in_=cb_d.ap()),
              writes=[cb_b])
        S.dma("sp", lambda e: e.dma_start(out=ones_h[:, :], in_=ones_d.ap()),
              writes=[ones_b])

        for col0, w in chunks:
            S.dma("pool", lambda e, col0=col0, w=w: e.dma_start(
                out=xbf_h[:, 0:KT * w].rearrange("p (a t) -> p a t", a=KT),
                in_=x_ap[:, col0:col0 + w].rearrange("(a p) t -> p a t", p=128)),
                writes=[xbf_b])
            S.op("act", lambda e, w=w: e.activation(
                out=sqf_h[:, 0:KT * w], in_=xbf_h[:, 0:KT * w], func=AF.Square),
                reads=[xbf_b], writes=[sqf_b])
            for k in range(KT):
                S.op("pe", lambda e, k=k, w=w: e.matmul(
                    pmean_h[0:1, 0:w], ones_h[:, 0:1], xbf_h[:, k * w:(k + 1) * w],
                    start=(k == 0), stop=(k == KT - 1)),
                    reads=[xbf_b, ones_b], writes=[pmean_b])
                S.op("pe", lambda e, k=k, w=w: e.matmul(
                    psq_h[0:1, 0:w], ones_h[:, 0:1], sqf_h[:, k * w:(k + 1) * w],
                    start=(k == 0), stop=(k == KT - 1)),
                    reads=[sqf_b, ones_b], writes=[psq_b])
            ln_rowmath(S, pmean_h, pmean_b, psq_h, psq_b, nm_s, nm_sb,
                       rs_s, rs_sb, rt_s, rt_sb, 0, w, D)
            S.op("act", lambda e, col0=col0, w=w: e.activation(
                out=negmb_h[0:1, col0:col0 + w], in_=nm_s[0:1, 0:w], func=AF.Copy),
                reads=[nm_sb], writes=[negmb_b])
            S.op("act", lambda e, col0=col0, w=w: e.activation(
                out=rstdb_h[0:1, col0:col0 + w], in_=rs_s[0:1, 0:w], func=AF.Copy),
                reads=[rs_sb], writes=[rstdb_b])

        p2chunks = chunks if p2_nchunks is None else chunks[:p2_nchunks]
        for col0, w in p2chunks:
            mini = (w == 4)
            S.dma("pool", lambda e, col0=col0, w=w: e.dma_start(
                out=xbf_h[:, 0:KT * w].rearrange("p (a t) -> p a t", a=KT),
                in_=x_ap[:, col0:col0 + w].rearrange("(a p) t -> p a t", p=128)),
                writes=[xbf_b])
            S.op("pe", lambda e, col0=col0, w=w: e.matmul(
                pnm_h[:, 0:w], ones_h[0:1, :], negmb_h[0:1, col0:col0 + w],
                start=True, stop=True), reads=[negmb_b, ones_b], writes=[pnm_b])
            S.op("pe", lambda e, col0=col0, w=w: e.matmul(
                prs_h[:, 0:w], ones_h[0:1, :], rstdb_h[0:1, col0:col0 + w],
                start=True, stop=True), reads=[rstdb_b, ones_b], writes=[prs_b])
            for k in range(KT):
                S.op("dve", lambda e, k=k, w=w: e.tensor_tensor(
                    out=tfa_h[:, 0:w], in0=xbf_h[:, k * w:(k + 1) * w],
                    in1=pnm_h[:, 0:w], op=ALU.add),
                    reads=[xbf_b, pnm_b], writes=[tfa_b])
                S.op("dve", lambda e, k=k, w=w: e.tensor_tensor(
                    out=xn_h[:, k * w:(k + 1) * w], in0=tfa_h[:, 0:w],
                    in1=prs_h[:, 0:w], op=ALU.mult),
                    reads=[tfa_b, prs_b], writes=[xn_b])
            mt_range = range(16) if mini else range(MT)
            for mi in mt_range:
                ph, pb = pxz[mi % 2]
                for k in range(KT):
                    S.op("pe", lambda e, mi=mi, k=k, ph=ph, w=w: e.matmul(
                        ph[:, 0:w],
                        win_h[:, k * 2 * DI + mi * 128:k * 2 * DI + mi * 128 + 128],
                        xn_h[:, k * w:(k + 1) * w],
                        start=(k == 0), stop=(k == KT - 1)),
                        reads=[xn_b, win_b], writes=[pb])
                if mi < 16:
                    xih, xib = xin[mi]
                    if mini:
                        S.op("act", lambda e, ph=ph, xih=xih: e.activation(
                            out=xih[:, 0:4], in_=ph[:, 0:4], func=AF.Copy),
                            reads=[pb], writes=[xib])
                    else:
                        S.op("act", lambda e, ph=ph, xih=xih, w=w: e.activation(
                            out=xih[:, 4:4 + w], in_=ph[:, 0:w], func=AF.Copy),
                            reads=[pb], writes=[xib])
                        S.op("dve", lambda e, xih=xih, mi=mi, w=w: e.tensor_scalar(
                            out=ur_h[:, 0:w], in0=xih[:, 1:1 + w],
                            scalar1=cw_h[:, mi * DCONV:mi * DCONV + 1],
                            scalar2=cb_h[:, mi:mi + 1], op0=ALU.mult, op1=ALU.add),
                            reads=[xib, cw_b, cb_b], writes=[ur_b])
                        for j in range(1, DCONV):
                            S.op("dve", lambda e, xih=xih, mi=mi, j=j, w=w: e.scalar_tensor_tensor(
                                out=ur_h[:, 0:w], in0=xih[:, 1 + j:1 + j + w],
                                scalar=cw_h[:, mi * DCONV + j:mi * DCONV + j + 1],
                                in1=ur_h[:, 0:w], op0=ALU.mult, op1=ALU.add),
                                reads=[xib, cw_b], writes=[ur_b])
                        S.op("act", lambda e, mi=mi, w=w: e.activation(
                            out=u_h[:, mi * w:(mi + 1) * w], in_=ur_h[:, 0:w],
                            func=AF.Silu), reads=[ur_b], writes=[u_b])
                        S.op("dve", lambda e, xih=xih, w=w: e.tensor_copy(
                            out=xih[:, 0:4], in_=xih[:, w:w + 4]),
                            reads=[xib], writes=[xib])
                        S.op("pe", lambda e, mi=mi, w=w: e.matmul(
                            pxd_h[0:96, 0:w], wxp_h[:, mi * 96:(mi + 1) * 96],
                            u_h[:, mi * w:(mi + 1) * w],
                            start=(mi == 0), stop=(mi == 15)),
                            reads=[u_b, wxp_b], writes=[pxd_b])

                else:
                    S.op("act", lambda e, ph=ph, mi=mi, w=w: e.activation(
                        out=zs_h[:, (mi - 16) * w:(mi - 15) * w], in_=ph[:, 0:w],
                        func=AF.Silu), reads=[pb], writes=[zs_b])
            if not mini:
                S.dma("sp", lambda e, col0=col0, w=w: e.dma_start(
                    out=u_d.ap()[:, col0 - 4:col0 - 4 + w].rearrange(
                        "(a p) t -> p a t", p=128),
                    in_=u_h[:, 0:16 * w].rearrange("p (a t) -> p a t", a=16)),
                    reads=[u_b])
                S.dma("act", lambda e, col0=col0, w=w: e.dma_start(
                    out=zs_d.ap()[:, col0 - 4:col0 - 4 + w].rearrange(
                        "(a p) t -> p a t", p=128),
                    in_=zs_h[:, 0:16 * w].rearrange("p (a t) -> p a t", a=16)),
                    reads=[zs_b])
                S.op("act", lambda e, w=w: e.activation(
                    out=xdb_h[0:96, 0:w], in_=pxd_h[0:96, 0:w], func=AF.Copy),
                    reads=[pxd_b], writes=[xdb_b])
                S.dma("act", lambda e, col0=col0, w=w: e.dma_start(
                    out=xdbl_d.ap()[0:96, col0 - 4:col0 - 4 + w],
                    in_=xdb_h[0:96, 0:w]), reads=[xdb_b])


def _np(a):
    return np.asarray(a)


def stage_l2(nc, S, sems, E, nchu=None, nbatch=B, skip=()):
    TCH = 1024
    NSUB = TCH // 512
    NCHU = (L // TCH) if nchu is None else nchu
    NDT = DDI // 128
    u_d = E["u_my"]
    xdbl_d = E["xdbl_ag"]
    wdt_d = E["wdt"]
    bdt_d = E["bdt"]
    alog_d = E["alog"]
    ones_d = E["ones_in"]
    ident_d = E["ident_in"]
    y_d = E["y_scr"]

    with ExitStack() as ctx:
        ones_h, ones_b = sbuf(nc, ctx, "l2_ones", [128, 128], BF16)
        ident_h, ident_b = sbuf(nc, ctx, "ident_sb", [128, 128], BF16)
        wdt_h, wdt_b = sbuf(nc, ctx, "wdt_bf", [DTR, DDI], BF16)
        bdt_h, bdt_b = sbuf(nc, ctx, "bdt_sb", [128, NDT], F32)
        an_h = [sbuf(nc, ctx, f"aneg{i}", [128, DS], F32) for i in range(NDT)]
        dtlr_p = [sbuf(nc, ctx, f"dtlr_bf{p}", [DTR, TCH], BF16) for p in range(2)]
        brep = [sbuf(nc, ctx, f"brep{i}", [128, TCH], F32) for i in range(2)]
        crep = [sbuf(nc, ctx, f"crep{i}", [128, TCH], F32) for i in range(2)]
        ut_p = [[sbuf(nc, ctx, f"ut{i}_{p}", [128, TCH], BF16) for i in range(NDT)]
                for p in range(2)]
        dl_p = [[sbuf(nc, ctx, f"dl{i}_{p}", [128, TCH], F32) for i in range(NDT)]
                for p in range(2)]
        du_p = [[sbuf(nc, ctx, f"du{i}_{p}", [128, TCH], BF16) for i in range(NDT)]
                for p in range(2)]
        spa_h, spa_b = sbuf(nc, ctx, "sp_abs", [128, 512], F32)
        spb_h, spb_b = sbuf(nc, ctx, "sp_mx", [128, 512], F32)
        dA = [sbuf(nc, ctx, f"dA{i}", [128, TCH], F32) for i in range(2)]
        dBu = [sbuf(nc, ctx, f"dBu{i}", [128, TCH], BF16) for i in range(2)]
        hh = [sbuf(nc, ctx, f"h{i}", [128, TCH], BF16) for i in range(2)]
        gg = [sbuf(nc, ctx, f"g{i}", [128, TCH], BF16) for i in range(2)]
        carry = [sbuf(nc, ctx, f"carry{i}", [128, DS], F32) for i in range(NDT)]
        yo = [sbuf(nc, ctx, f"yo{i}", [128, TCH], BF16) for i in range(NDT)]
        pdl_h, pdl_b = psum(nc, ctx, "pdl", [128, 512])
        py = [[psum(nc, ctx, f"py{i}_{s}", [128, 512]) for s in range(NSUB)]
              for i in range(NDT)]

        S.dma("sp", lambda e: e.dma_start(out=ones_h[:, :], in_=ones_d.ap()),
              writes=[ones_b], key="wload")
        S.dma("sp", lambda e: e.dma_start(out=ident_h[:, :], in_=ident_d.ap()),
              writes=[ident_b], key="wload")
        S.dma("pool", lambda e: e.dma_start(out=wdt_h[:, :], in_=wdt_d.ap()),
              writes=[wdt_b], key="wloadp")
        S.dma("sp", lambda e: e.dma_start(out=bdt_h[:, :], in_=bdt_d.ap()),
              writes=[bdt_b], key="wload")
        al_bufs = []
        for i in range(NDT):
            ah, ab = sbuf(nc, ctx, f"alog{i}", [128, DS], F32)
            al_bufs.append((ah, ab))
            S.dma("sp", lambda e, ah=ah, i=i: e.dma_start(
                out=ah[:, :], in_=alog_d.ap()[128 * i:128 * (i + 1), :]),
                writes=[ab], key="wload")
        for i in range(NDT):
            ah, ab = al_bufs[i]
            nh, nb = an_h[i]
            S.op("act", lambda e, ah=ah, nh=nh: e.activation(
                out=nh[:, :], in_=ah[:, :], func=AF.Exp), reads=[ab], writes=[nb])
            S.op("dve", lambda e, nh=nh: e.tensor_scalar(
                out=nh[:, :], in0=nh[:, :], scalar1=-1.0, scalar2=None,
                op0=ALU.mult), writes=[nb])
        for i in range(NDT):
            ch, cb_ = carry[i]
            S.op("dve", lambda e, ch=ch: e.memset(ch[:, :], 0.0), writes=[cb_])

        for b in range(nbatch):
            if b > 0:
                for i in range(NDT):
                    ch, cb_ = carry[i]
                    S.op("dve", lambda e, ch=ch: e.memset(ch[:, :], 0.0),
                         writes=[cb_])
            for chk in range(NCHU):
                c0 = b * L + chk * TCH
                rr = c0 // TC
                cl = c0 % TC
                cp = (b * NCHU + chk) % 2
                dtlr_h, dtlr_b = dtlr_p[cp]
                ut, dl, du = ut_p[cp], dl_p[cp], du_p[cp]
                S.dma("pool", lambda e, rr=rr, cl=cl: e.dma_start(
                    out=dtlr_h[:, :],
                    in_=xdbl_d.ap()[rr * 96:rr * 96 + DTR, cl:cl + TCH]),
                    writes=[dtlr_b], key="bcld")
                for i in range(NDT):
                    uh, ub = ut[i]
                    S.dma("sp", lambda e, uh=uh, i=i, rr=rr, cl=cl: e.dma_start(
                        out=uh[:, :],
                        in_=u_d.ap()[rr * DDI + 128 * i:rr * DDI + 128 * (i + 1),
                                     cl:cl + TCH]),
                        writes=[ub], key="uld")
                for i in range(NDT) if "delta" not in skip else []:
                    dlh, dlb = dl[i]
                    for s in range(NSUB):
                        if "deltamm" not in skip:
                            S.op("pe", lambda e, i=i, s=s: e.matmul(
                                pdl_h[:, :], wdt_h[:, 128 * i:128 * (i + 1)],
                                dtlr_h[:, 512 * s:512 * (s + 1)], start=True, stop=True),
                                reads=[dtlr_b, wdt_b], writes=[pdl_b])
                        else:
                            S.op("pe", lambda e, i=i, s=s: e.matmul(
                                pdl_h[:, :], ident_h[:, :],
                                ut[i][0][:, 512 * s:512 * (s + 1)], start=True, stop=True),
                                reads=[ut[i][1], ident_b], writes=[pdl_b])
                        _af_abs = AF.Square if "abs" in skip else AF.Abs
                        if "spact" not in skip:
                            S.op("act", lambda e, i=i, _af_abs=_af_abs: e.activation(
                                out=spa_h[:, :], in_=pdl_h[:, :], func=_af_abs,
                                bias=bdt_h[:, i:i + 1]),
                                reads=[pdl_b, bdt_b], writes=[spa_b])
                            S.op("act", lambda e: e.activation(
                                out=spa_h[:, :], in_=spa_h[:, :], func=AF.Exp,
                                scale=-1.0), writes=[spa_b])
                            S.op("act", lambda e: e.activation(
                                out=spa_h[:, :], in_=spa_h[:, :], func=AF.Ln,
                                bias=1.0), writes=[spa_b])
                        else:
                            S.op("act", lambda e: e.activation(
                                out=spa_h[:, :], in_=pdl_h[:, :], func=AF.Copy),
                                reads=[pdl_b], writes=[spa_b])
                        if "spdve" not in skip:
                            S.op("dve", lambda e, i=i: e.tensor_scalar(
                                out=spb_h[:, :], in0=pdl_h[:, :],
                                scalar1=bdt_h[:, i:i + 1], scalar2=0.0,
                                op0=ALU.add, op1=ALU.max),
                                reads=[pdl_b, bdt_b], writes=[spb_b])
                        else:
                            S.op("dve", lambda e: e.tensor_copy(
                                out=spb_h[:, :], in_=pdl_h[:, :]),
                                reads=[pdl_b], writes=[spb_b])
                        S.op("dve", lambda e, dlh=dlh, s=s: e.tensor_tensor(
                            out=dlh[:, 512 * s:512 * (s + 1)], in0=spa_h[:, :],
                            in1=spb_h[:, :], op=ALU.add),
                            reads=[spa_b, spb_b], writes=[dlb])
                    duh, dub = du[i]
                    uh, ub = ut[i]
                    S.op("dve", lambda e, duh=duh, dlh=dlh, uh=uh: e.tensor_tensor(
                        out=duh[:, :], in0=dlh[:, :], in1=uh[:, :], op=ALU.mult),
                        reads=[dlb, ub], writes=[dub])
                for n in range(DS) if "nloop" not in skip else []:
                    bh, bb = brep[n % 2]
                    ch2, cb2 = crep[n % 2]
                    boff = (rr * 96 + DTR + n) * TC + cl
                    coff = (rr * 96 + DTR + 16 + n) * TC + cl
                    S.dma("sp", lambda e, bh=bh, boff=boff: e.dma_start(
                        out=bh[:, :],
                        in_=bass.AP(xdbl_d, boff, [[0, 128], [1, TCH]])),
                        writes=[bb], key="brd")
                    S.dma("act", lambda e, ch2=ch2, coff=coff: e.dma_start(
                        out=ch2[:, :],
                        in_=bass.AP(xdbl_d, coff, [[0, 128], [1, TCH]])),
                        writes=[cb2], key="crd")
                    for i in range(NDT):
                        pp = (n * NDT + i) % 2
                        dAh, dAb = dA[pp]
                        dBh, dBb = dBu[pp]
                        hhh, hhb = hh[pp]
                        ggh, ggb = gg[pp]
                        dlh, dlb = dl[i]
                        duh, dub = du[i]
                        anh, anb = an_h[i]
                        ch, cb_ = carry[i]
                        S.op("act", lambda e, dAh=dAh, dlh=dlh, anh=anh, n=n: e.activation(
                            out=dAh[:, :], in_=dlh[:, :], func=AF.Exp,
                            scale=anh[:, n:n + 1]), reads=[dlb, anb], writes=[dAb])
                        S.op("pool", lambda e, dBh=dBh, duh=duh, bh=bh: e.tensor_tensor(
                            out=dBh[:, :], in0=duh[:, :], in1=bh[:, :], op=ALU.mult),
                            reads=[dub, bb], writes=[dBb])
                        if "scan" in skip:
                            S.op("dve", lambda e, hhh=hhh, dBh=dBh: e.tensor_copy(
                                out=hhh[:, :], in_=dBh[:, :]),
                                reads=[dBb], writes=[hhb])
                        else:
                            S.op("dve", lambda e, hhh=hhh, dAh=dAh, dBh=dBh, ch=ch, n=n: e.tensor_tensor_scan(
                                out=hhh[:, :], data0=dAh[:, :], data1=dBh[:, :],
                                initial=ch[:, n:n + 1], op0=ALU.mult, op1=ALU.add),
                                reads=[dAb, dBb, cb_], writes=[hhb])
                        if "carry" not in skip:
                            S.op("dve", lambda e, hhh=hhh, ch=ch, n=n: e.tensor_copy(
                                out=ch[:, n:n + 1], in_=hhh[:, TCH - 1:TCH]),
                                reads=[hhb], writes=[cb_])
                        S.op("dve", lambda e, ggh=ggh, hhh=hhh, ch2=ch2: e.tensor_tensor(
                            out=ggh[:, :], in0=hhh[:, :], in1=ch2[:, :], op=ALU.mult),
                            reads=[hhb, cb2], writes=[ggb])
                        for s in range(NSUB):
                            pyh, pyb = py[i][s]
                            S.op("pe", lambda e, pyh=pyh, ggh=ggh, s=s, n=n: e.matmul(
                                pyh[:, :], ident_h[:, :],
                                ggh[:, 512 * s:512 * (s + 1)],
                                start=(n == 0), stop=(n == DS - 1)),
                                reads=[ggb, ident_b], writes=[pyb])
                for i in range(NDT) if "yevac" not in skip else []:
                    yh, yb = yo[i]
                    for s in range(NSUB):
                        pyh, pyb = py[i][s]
                        S.op("act", lambda e, yh=yh, pyh=pyh, s=s: e.activation(
                            out=yh[:, 512 * s:512 * (s + 1)], in_=pyh[:, :],
                            func=AF.Copy), reads=[pyb], writes=[yb])
                    S.dma("sp", lambda e, yh=yh, i=i, rr=rr, cl=cl: e.dma_start(
                        out=y_d.ap()[rr * DDI + 128 * i:rr * DDI + 128 * (i + 1),
                                     cl:cl + TCH],
                        in_=yh[:, :]), reads=[yb], key="yst")


def ln_rowmath_sb(S, pm_h, pm_b, psq_h, psq_b, negm_h, negm_b, rstd_h, rstd_b,
                  tmp_h, tmp_b, icol0, ocol0, w, nfeat):
    isl = slice(icol0, icol0 + w)
    osl = slice(ocol0, ocol0 + w)
    S.op("act", lambda e: e.activation(out=negm_h[0:1, osl], in_=pm_h[0:1, isl],
                                       func=AF.Copy, scale=-1.0 / nfeat),
         reads=[pm_b], writes=[negm_b])
    S.op("act", lambda e: e.activation(out=tmp_h[0:1, 0:w], in_=pm_h[0:1, isl],
                                       func=AF.Square, scale=1.0 / nfeat),
         reads=[pm_b], writes=[tmp_b])
    S.op("act", lambda e: e.activation(out=rstd_h[0:1, osl], in_=psq_h[0:1, isl],
                                       func=AF.Copy, scale=1.0 / nfeat),
         reads=[psq_b], writes=[rstd_b])
    S.op("dve", lambda e: e.tensor_tensor(out=rstd_h[0:1, osl], in0=rstd_h[0:1, osl],
                                          in1=tmp_h[0:1, 0:w], op=ALU.subtract),
         reads=[tmp_b], writes=[rstd_b])
    S.op("dve", lambda e: e.tensor_scalar(out=rstd_h[0:1, osl], in0=rstd_h[0:1, osl],
                                          scalar1=EPS, scalar2=None, op0=ALU.add),
         writes=[rstd_b])
    S.op("act", lambda e: e.activation(out=rstd_h[0:1, osl], in_=rstd_h[0:1, osl],
                                       func=AF.Ln), writes=[rstd_b])
    S.op("act", lambda e: e.activation(out=rstd_h[0:1, osl], in_=rstd_h[0:1, osl],
                                       func=AF.Exp, scale=-0.5), writes=[rstd_b])


def stage_l3a(nc, S, sems, E, nchunks=None):
    W = 512
    NCH = TC // W if nchunks is None else nchunks
    KTI = DI // 128
    KTM = D // 128
    y_dd = E["y_my"]
    u_dd = E["u_scr"]
    zs_dd = E["zs_scr"]
    x_dd = E["x_nohalo"]
    dsk_d = E["dskip"]
    wout_d = E["wout"]
    rtr_d = E["router"]
    ones_d = E["ones_in"]
    ident_d = E["ident_in"]
    x2_d = E["x2_scr"]
    xn3_d = E["xn3_scr"]
    coef_d = E["coef_scr"]

    with ExitStack() as ctx:
        ones_h, ones_b = sbuf(nc, ctx, "ones_sb", [128, 128], BF16)
        ident_h, ident_b = sbuf(nc, ctx, "ident_sb", [128, 128], BF16)
        dsk_h, dsk_b = sbuf(nc, ctx, "dsk", [128, KTI], F32)
        wout_h, wout_b = sbuf(nc, ctx, "wout_bf", [128, KTI * D], BF16)
        rtr_h, rtr_b = sbuf(nc, ctx, "rtr_bf", [128, KTM * NE], BF16)
        ysb_h, ysb_b = sbuf(nc, ctx, "ysb", [128, KTI * W], BF16)
        usb_h, usb_b = sbuf(nc, ctx, "usb", [128, KTI * W], BF16)
        zsb_h, zsb_b = sbuf(nc, ctx, "zsb", [128, KTI * W], BF16)
        xsb_h, xsb_b = sbuf(nc, ctx, "xsb", [128, KTM * W], F32)
        tg_h, tg_b = sbuf(nc, ctx, "tg", [128, W], F32)
        yg_h, yg_b = sbuf(nc, ctx, "yg", [128, KTI * W], BF16)
        mo_h, mo_b = sbuf(nc, ctx, "mo_bf", [128, KTM * W], BF16)
        sq_h, sq_b = sbuf(nc, ctx, "sq_bf", [128, KTM * W], BF16)
        x2sb_h, x2sb_b = sbuf(nc, ctx, "x2sb", [128, KTM * W], F32)
        x2bf_h, x2bf_b = sbuf(nc, ctx, "x2bf", [128, KTM * W], BF16)
        xn3_h, xn3_b = sbuf(nc, ctx, "xn3sb", [128, KTM * W], BF16)
        nm2_h, nm2_b = sbuf(nc, ctx, "nm2", [1, TC], BF16)
        rs2_h, rs2_b = sbuf(nc, ctx, "rs2", [1, TC], BF16)
        nm3_h, nm3_b = sbuf(nc, ctx, "nm3", [1, TC], BF16)
        rs3_h, rs3_b = sbuf(nc, ctx, "rs3", [1, TC], BF16)
        nmf_h, nmf_b = sbuf(nc, ctx, "nmf", [1, W], F32)
        rsf_h, rsf_b = sbuf(nc, ctx, "rsf", [1, W], F32)
        rtf_h, rtf_b = sbuf(nc, ctx, "rtf", [1, W], F32)
        lg_h, lg_b = sbuf(nc, ctx, "lg_bf", [8, W], BF16)
        lgT_h, lgT_b = sbuf(nc, ctx, "lgT", [128, 8], F32)
        r1_h, r1_b = sbuf(nc, ctx, "r1", [128, 1], F32)
        r2_h, r2_b = sbuf(nc, ctx, "r2", [128, 1], F32)
        ee_h, ee_b = sbuf(nc, ctx, "ee", [128, 8], F32)
        pp_h, pp_b = sbuf(nc, ctx, "ppb", [128, 8], F32)
        gx_h, gx_b = sbuf(nc, ctx, "gx", [128, 8], F32)
        cf_h, cf_b = sbuf(nc, ctx, "cfbf", [128, 8], BF16)
        cT_h, cT_b = sbuf(nc, ctx, "coefT", [8, W], BF16)
        pwo = [psum(nc, ctx, f"pwo{i}", [128, W]) for i in range(2)]
        pm_h, pm_b = psum(nc, ctx, "pm", [1, W])
        pq_h, pq_b = psum(nc, ctx, "pq", [1, W])
        pnm_h, pnm_b = psum(nc, ctx, "pnm", [128, W])
        prs_h, prs_b = psum(nc, ctx, "prs", [128, W])
        plg_h, plg_b = psum(nc, ctx, "plg", [8, W])
        ptb_h, ptb_b = psum(nc, ctx, "ptb", [128, 128], BF16)

        S.dma("sp", lambda e: e.dma_start(out=ones_h[:, :], in_=ones_d.ap()),
              writes=[ones_b], key="wload")
        S.dma("sp", lambda e: e.dma_start(out=ident_h[:, :], in_=ident_d.ap()),
              writes=[ident_b], key="wload")
        S.dma("sp", lambda e: e.dma_start(out=dsk_h[:, :], in_=dsk_d.ap()),
              writes=[dsk_b], key="wload")
        S.dma("pool", lambda e: e.dma_start(
            out=wout_h[:, :].rearrange("p (a m) -> p a m", a=KTI),
            in_=wout_d.ap().rearrange("(a p) m -> p a m", p=128)),
            writes=[wout_b], key="wloadp")
        S.dma("pool", lambda e: e.dma_start(
            out=rtr_h[:, :].rearrange("p (a m) -> p a m", a=KTM),
            in_=rtr_d.ap().rearrange("(a p) m -> p a m", p=128)),
            writes=[rtr_b], key="wloadp")

        for chn in range(NCH):
            c0 = chn * W
            for nm_, hh_, bb_, dd_, kt_, ky_, xo_ in (
                    ("y", ysb_h, ysb_b, y_dd, KTI, "yld", 0),
                    ("u", usb_h, usb_b, u_dd, KTI, "uld", 0),
                    ("z", zsb_h, zsb_b, zs_dd, KTI, "zld", 0),
                    ("x", xsb_h, xsb_b, x_dd, KTM, "xld", 4)):
                S.dma("sp", lambda e, hh_=hh_, dd_=dd_, kt_=kt_, c0=c0, xo_=xo_: e.dma_start(
                    out=hh_[:, 0:kt_ * W].rearrange("p (a t) -> p a t", a=kt_),
                    in_=dd_.ap()[:, xo_ + c0:xo_ + c0 + W].rearrange(
                        "(a p) t -> p a t", p=128)),
                    writes=[bb_], key=ky_)
            for k in range(KTI):
                sl = slice(k * W, (k + 1) * W)
                S.op("dve", lambda e, k=k, sl=sl: e.scalar_tensor_tensor(
                    out=tg_h[:, :], in0=usb_h[:, sl], scalar=dsk_h[:, k:k + 1],
                    in1=ysb_h[:, sl], op0=ALU.mult, op1=ALU.add),
                    reads=[usb_b, dsk_b, ysb_b], writes=[tg_b])
                S.op("dve", lambda e, k=k, sl=sl: e.tensor_tensor(
                    out=yg_h[:, sl], in0=tg_h[:, :], in1=zsb_h[:, sl],
                    op=ALU.mult), reads=[tg_b, zsb_b], writes=[yg_b])
            for mi in range(KTM):
                ph, pb = pwo[mi % 2]
                for k in range(KTI):
                    S.op("pe", lambda e, mi=mi, k=k, ph=ph: e.matmul(
                        ph[:, :], wout_h[:, k * D + mi * 128:k * D + mi * 128 + 128],
                        yg_h[:, k * W:(k + 1) * W],
                        start=(k == 0), stop=(k == KTI - 1)),
                        reads=[yg_b, wout_b], writes=[pb])
                msl = slice(mi * W, (mi + 1) * W)
                S.op("act", lambda e, ph=ph, msl=msl: e.activation(
                    out=mo_h[:, msl], in_=ph[:, :], func=AF.Copy),
                    reads=[pb], writes=[mo_b])
                S.op("act", lambda e, ph=ph, msl=msl: e.activation(
                    out=sq_h[:, msl], in_=ph[:, :], func=AF.Square),
                    reads=[pb], writes=[sq_b])
            for mi in range(KTM):
                msl = slice(mi * W, (mi + 1) * W)
                S.op("pe", lambda e, mi=mi, msl=msl: e.matmul(
                    pm_h[0:1, :], ones_h[:, 0:1], mo_h[:, msl],
                    start=(mi == 0), stop=(mi == KTM - 1)),
                    reads=[mo_b, ones_b], writes=[pm_b])
                S.op("pe", lambda e, mi=mi, msl=msl: e.matmul(
                    pq_h[0:1, :], ones_h[:, 0:1], sq_h[:, msl],
                    start=(mi == 0), stop=(mi == KTM - 1)),
                    reads=[sq_b, ones_b], writes=[pq_b])
            ln_rowmath(S, pm_h, pm_b, pq_h, pq_b, nmf_h, nmf_b, rsf_h, rsf_b,
                       rtf_h, rtf_b, 0, W, D)
            S.op("act", lambda e, c0=c0: e.activation(
                out=nm2_h[0:1, c0:c0 + W], in_=nmf_h[0:1, 0:W], func=AF.Copy),
                reads=[nmf_b], writes=[nm2_b])
            S.op("act", lambda e, c0=c0: e.activation(
                out=rs2_h[0:1, c0:c0 + W], in_=rsf_h[0:1, 0:W], func=AF.Copy),
                reads=[rsf_b], writes=[rs2_b])
            S.op("pe", lambda e, c0=c0: e.matmul(
                pnm_h[:, :], ones_h[0:1, :], nm2_h[0:1, c0:c0 + W],
                start=True, stop=True), reads=[nm2_b, ones_b], writes=[pnm_b])
            S.op("pe", lambda e, c0=c0: e.matmul(
                prs_h[:, :], ones_h[0:1, :], rs2_h[0:1, c0:c0 + W],
                start=True, stop=True), reads=[rs2_b, ones_b], writes=[prs_b])
            for mi in range(KTM):
                msl = slice(mi * W, (mi + 1) * W)
                S.op("dve", lambda e, msl=msl: e.tensor_tensor(
                    out=tg_h[:, :], in0=mo_h[:, msl], in1=pnm_h[:, :], op=ALU.add),
                    reads=[mo_b, pnm_b], writes=[tg_b])
                S.op("dve", lambda e, msl=msl: e.tensor_tensor(
                    out=tg_h[:, :], in0=tg_h[:, :], in1=prs_h[:, :], op=ALU.mult),
                    reads=[prs_b], writes=[tg_b])
                S.op("dve", lambda e, msl=msl: e.tensor_tensor(
                    out=x2sb_h[:, msl], in0=tg_h[:, :], in1=xsb_h[:, msl],
                    op=ALU.add), reads=[tg_b, xsb_b], writes=[x2sb_b])
                S.op("act", lambda e, msl=msl: e.activation(
                    out=x2bf_h[:, msl], in_=x2sb_h[:, msl], func=AF.Copy),
                    reads=[x2sb_b], writes=[x2bf_b])
                S.op("act", lambda e, msl=msl: e.activation(
                    out=sq_h[:, msl], in_=x2sb_h[:, msl], func=AF.Square),
                    reads=[x2sb_b], writes=[sq_b])
            S.dma("sp", lambda e, c0=c0: e.dma_start(
                out=x2_d.ap()[:, c0:c0 + W].rearrange("(a p) t -> p a t", p=128),
                in_=x2sb_h[:, :].rearrange("p (a t) -> p a t", a=KTM)),
                reads=[x2sb_b], key="ost")
            for mi in range(KTM):
                msl = slice(mi * W, (mi + 1) * W)
                S.op("pe", lambda e, msl=msl, mi=mi: e.matmul(
                    pm_h[0:1, :], ones_h[:, 0:1], x2bf_h[:, msl],
                    start=(mi == 0), stop=(mi == KTM - 1)),
                    reads=[x2bf_b, ones_b], writes=[pm_b])
                S.op("pe", lambda e, msl=msl, mi=mi: e.matmul(
                    pq_h[0:1, :], ones_h[:, 0:1], sq_h[:, msl],
                    start=(mi == 0), stop=(mi == KTM - 1)),
                    reads=[sq_b, ones_b], writes=[pq_b])
            ln_rowmath(S, pm_h, pm_b, pq_h, pq_b, nmf_h, nmf_b, rsf_h, rsf_b,
                       rtf_h, rtf_b, 0, W, D)
            S.op("act", lambda e, c0=c0: e.activation(
                out=nm3_h[0:1, c0:c0 + W], in_=nmf_h[0:1, 0:W], func=AF.Copy),
                reads=[nmf_b], writes=[nm3_b])
            S.op("act", lambda e, c0=c0: e.activation(
                out=rs3_h[0:1, c0:c0 + W], in_=rsf_h[0:1, 0:W], func=AF.Copy),
                reads=[rsf_b], writes=[rs3_b])
            S.op("pe", lambda e, c0=c0: e.matmul(
                pnm_h[:, :], ones_h[0:1, :], nm3_h[0:1, c0:c0 + W],
                start=True, stop=True), reads=[nm3_b, ones_b], writes=[pnm_b])
            S.op("pe", lambda e, c0=c0: e.matmul(
                prs_h[:, :], ones_h[0:1, :], rs3_h[0:1, c0:c0 + W],
                start=True, stop=True), reads=[rs3_b, ones_b], writes=[prs_b])
            for mi in range(KTM):
                msl = slice(mi * W, (mi + 1) * W)
                S.op("dve", lambda e, msl=msl: e.tensor_tensor(
                    out=tg_h[:, :], in0=x2sb_h[:, msl], in1=pnm_h[:, :],
                    op=ALU.add), reads=[x2sb_b, pnm_b], writes=[tg_b])
                S.op("dve", lambda e, msl=msl: e.tensor_tensor(
                    out=xn3_h[:, msl], in0=tg_h[:, :], in1=prs_h[:, :],
                    op=ALU.mult), reads=[tg_b, prs_b], writes=[xn3_b])
            S.dma("sp", lambda e, c0=c0: e.dma_start(
                out=xn3_d.ap()[:, c0:c0 + W].rearrange("(a p) t -> p a t", p=128),
                in_=xn3_h[:, :].rearrange("p (a t) -> p a t", a=KTM)),
                reads=[xn3_b], key="ost")
            for k in range(KTM):
                S.op("pe", lambda e, k=k: e.matmul(
                    plg_h[0:8, :], rtr_h[:, k * NE:(k + 1) * NE],
                    xn3_h[:, k * W:(k + 1) * W],
                    start=(k == 0), stop=(k == KTM - 1)),
                    reads=[xn3_b, rtr_b], writes=[plg_b])
            S.op("act", lambda e: e.activation(
                out=lg_h[0:8, :], in_=plg_h[0:8, :], func=AF.Copy),
                reads=[plg_b], writes=[lg_b])
            for j in range(4):
                S.op("pe", lambda e, j=j: e.transpose(
                    ptb_h[:, 0:8], lg_h[0:8, j * 128:(j + 1) * 128],
                    ident_h[0:8, 0:8]), reads=[lg_b, ident_b], writes=[ptb_b])
                S.op("act", lambda e: e.activation(
                    out=lgT_h[:, :], in_=ptb_h[:, 0:8], func=AF.Copy),
                    reads=[ptb_b], writes=[lgT_b])
                S.op("dve", lambda e: e.reduce_max(
                    out=r1_h[:, :], in_=lgT_h[:, :], axis=mybir.AxisListType.X),
                    reads=[lgT_b], writes=[r1_b])
                S.op("dve", lambda e: e.tensor_scalar(
                    out=r1_h[:, :], in0=r1_h[:, :], scalar1=-1.0, scalar2=None,
                    op0=ALU.mult), writes=[r1_b])
                S.op("act", lambda e: e.activation(
                    out=ee_h[:, :], in_=lgT_h[:, :], func=AF.Exp,
                    bias=r1_h[:, 0:1]), reads=[lgT_b, r1_b], writes=[ee_b])
                S.op("dve", lambda e: e.reduce_sum(
                    out=r2_h[:, :], in_=ee_h[:, :], axis=mybir.AxisListType.X),
                    reads=[ee_b], writes=[r2_b])
                S.op("dve", lambda e: e.reciprocal(out=r2_h[:, :], in_=r2_h[:, :]),
                     writes=[r2_b])
                S.op("dve", lambda e: e.tensor_scalar(
                    out=pp_h[:, :], in0=ee_h[:, :], scalar1=r2_h[:, 0:1],
                    scalar2=None, op0=ALU.mult),
                    reads=[ee_b, r2_b], writes=[pp_b])
                S.op("dve", lambda e: e.reduce_max(
                    out=r1_h[:, :], in_=pp_h[:, :], axis=mybir.AxisListType.X),
                    reads=[pp_b], writes=[r1_b])
                S.op("dve", lambda e: e.tensor_scalar(
                    out=gx_h[:, :], in0=pp_h[:, :], scalar1=r1_h[:, 0:1],
                    scalar2=None, op0=ALU.is_ge),
                    reads=[pp_b, r1_b], writes=[gx_b])
                S.op("dve", lambda e: e.tensor_tensor(
                    out=gx_h[:, :], in0=pp_h[:, :], in1=gx_h[:, :], op=ALU.mult),
                    reads=[pp_b], writes=[gx_b])
                S.op("dve", lambda e: e.tensor_tensor(
                    out=gx_h[:, :], in0=pp_h[:, :], in1=gx_h[:, :],
                    op=ALU.subtract), reads=[pp_b], writes=[gx_b])
                S.op("dve", lambda e: e.reduce_max(
                    out=r2_h[:, :], in_=gx_h[:, :], axis=mybir.AxisListType.X),
                    reads=[gx_b], writes=[r2_b])
                S.op("dve", lambda e: e.tensor_scalar(
                    out=gx_h[:, :], in0=pp_h[:, :], scalar1=r2_h[:, 0:1],
                    scalar2=None, op0=ALU.is_ge),
                    reads=[pp_b, r2_b], writes=[gx_b])
                S.op("dve", lambda e: e.tensor_tensor(
                    out=pp_h[:, :], in0=pp_h[:, :], in1=gx_h[:, :], op=ALU.mult),
                    reads=[gx_b], writes=[pp_b])
                S.op("act", lambda e: e.activation(
                    out=cf_h[:, :], in_=pp_h[:, :], func=AF.Copy),
                    reads=[pp_b], writes=[cf_b])
                S.op("pe", lambda e: e.transpose(
                    ptb_h[0:8, :], cf_h[:, 0:8], ident_h[:, :]),
                    reads=[cf_b, ident_b], writes=[ptb_b])
                S.op("act", lambda e, j=j: e.activation(
                    out=cT_h[0:8, j * 128:(j + 1) * 128], in_=ptb_h[0:8, :],
                    func=AF.Copy), reads=[ptb_b], writes=[cT_b])
            S.dma("sp", lambda e, c0=c0: e.dma_start(
                out=bass.AP(coef_d, c0, [[TC, 8], [1, W]]), in_=cT_h[0:8, :]),
                reads=[cT_b], key="ost")


def stage_l3b(nc, S, sems, E, nchunks=None):
    W = 512
    NCH = TC // W if nchunks is None else nchunks
    KTM = D // 128
    KTF = DFF // 128
    xn3_d = E["xn3_scr"]
    coef_d = E["coef_scr"]
    x2_d = E["x2_scr"]
    weg_d = E["weg"]
    weu_d = E["weu"]
    wed_d = E["wed"]
    wsg_d = E["wsg"]
    wsu_d = E["wsu"]
    wsd_d = E["wsd"]
    ones_d = E["ones_in"]
    out_d = E["out"]
    ffn_d = E["ffn_scr"]

    with ExitStack() as ctx:
        ones_h, ones_b = sbuf(nc, ctx, "ones_sb", [128, 128], BF16)
        wsg_h, wsg_b = sbuf(nc, ctx, "wsg_bf", [128, KTM * D], BF16)
        wsu_h, wsu_b = sbuf(nc, ctx, "wsu_bf", [128, KTM * D], BF16)
        wsd_h, wsd_b = sbuf(nc, ctx, "wsd_bf", [128, KTM * D], BF16)
        xn3_h, xn3_b = sbuf(nc, ctx, "xn3sb", [128, KTM * W], BF16)
        ffn_h, ffn_b = sbuf(nc, ctx, "ffnb", [128, KTM * W], BF16)
        sq_h, sq_b = sbuf(nc, ctx, "sqb", [128, KTM * W], BF16)
        m4_h, m4_b = sbuf(nc, ctx, "m4row", [1, TC], BF16)
        q4_h, q4_b = sbuf(nc, ctx, "q4row", [1, TC], BF16)
        nm4_h, nm4_b = sbuf(nc, ctx, "nm4", [1, TC], BF16)
        rs4_h, rs4_b = sbuf(nc, ctx, "rs4", [1, TC], BF16)
        rtf_h, rtf_b = sbuf(nc, ctx, "rtf", [1, W], F32)
        tg_h, tg_b = sbuf(nc, ctx, "tg", [128, W], F32)
        ctxB = ExitStack()
        wgp = [sbuf(nc, ctxB, f"weg{i}", [128, KTM * DFF], BF16) for i in range(2)]
        wup = [sbuf(nc, ctxB, f"weu{i}", [128, KTM * DFF], BF16) for i in range(2)]
        wdp = [sbuf(nc, ctxB, f"wed{i}", [128, KTF * D], BF16) for i in range(2)]
        coef_h, coef_b = sbuf(nc, ctxB, "coefsb", [1, NE * W], BF16)
        gsb_h, gsb_b = sbuf(nc, ctxB, "gsb", [128, KTF * W], BF16)
        usb_h, usb_b = sbuf(nc, ctxB, "uub", [128, KTF * W], BF16)
        cme_h, cme_b = sbuf(nc, ctxB, "cme", [128, KTF * W], BF16)
        sm_h, sm_b = sbuf(nc, ctxB, "smb", [128, KTM * W], BF16)
        rsb_h, rsb_b = sbuf(nc, ctxB, "routed", [128, KTM * W], F32)
        pg_h, pg_b = psum(nc, ctxB, "pg", [128, W])
        pu_h, pu_b = psum(nc, ctxB, "pu", [128, W])
        pc_h, pc_b = psum(nc, ctxB, "pc", [128, W])
        pdp = [psum(nc, ctxB, f"pd{i}", [128, W]) for i in range(2)]
        pm_h, pm_b = psum(nc, ctxB, "pm", [1, W])
        pq_h, pq_b = psum(nc, ctxB, "pq", [1, W])

        S.dma("pool", lambda e: e.dma_start(
            out=wsg_h[:, :].rearrange("p (a m) -> p a m", a=KTM),
            in_=wsg_d.ap().rearrange("(a p) m -> p a m", p=128)),
            writes=[wsg_b], key="wloadp")
        S.dma("pool", lambda e: e.dma_start(
            out=wsu_h[:, :].rearrange("p (a m) -> p a m", a=KTM),
            in_=wsu_d.ap().rearrange("(a p) m -> p a m", p=128)),
            writes=[wsu_b], key="wloadp")
        S.dma("pool", lambda e: e.dma_start(
            out=wsd_h[:, :].rearrange("p (a m) -> p a m", a=KTM),
            in_=wsd_d.ap().rearrange("(a p) m -> p a m", p=128)),
            writes=[wsd_b], key="wloadp")
        S.dma("sp", lambda e: e.dma_start(out=ones_h[:, :], in_=ones_d.ap()),
              writes=[ones_b], key="wload")

        for chn in range(NCH):
            c0 = chn * W
            S.dma("sp", lambda e, c0=c0: e.dma_start(
                out=xn3_h[:, :].rearrange("p (a t) -> p a t", a=KTM),
                in_=xn3_d.ap()[:, c0:c0 + W].rearrange("(a p) t -> p a t", p=128)),
                writes=[xn3_b], key="xnld")
            S.dma("sp", lambda e, c0=c0: e.dma_start(
                out=coef_h[:, :].rearrange("p (a t) -> p a t", a=NE),
                in_=bass.AP(coef_d, c0, [[1, 1], [TC, NE], [1, W]])),
                writes=[coef_b], key="cfld")
            for ei in range(NE):
                wg_h, wg_b = wgp[ei % 2]
                wu_h, wu_b = wup[ei % 2]
                wd_h, wd_b = wdp[ei % 2]
                ky = "we" + str(ei % 2)
                S.dma("pool", lambda e, ei=ei, wg_h=wg_h: e.dma_start(
                    out=wg_h[:, :].rearrange("p (a m) -> p a m", a=KTM),
                    in_=weg_d.ap()[ei * D:(ei + 1) * D, :].rearrange(
                        "(a p) m -> p a m", p=128)), writes=[wg_b], key=ky)
                S.dma("pool", lambda e, ei=ei, wu_h=wu_h: e.dma_start(
                    out=wu_h[:, :].rearrange("p (a m) -> p a m", a=KTM),
                    in_=weu_d.ap()[ei * D:(ei + 1) * D, :].rearrange(
                        "(a p) m -> p a m", p=128)), writes=[wu_b], key=ky)
                S.dma("pool", lambda e, ei=ei, wd_h=wd_h: e.dma_start(
                    out=wd_h[:, :].rearrange("p (a m) -> p a m", a=KTF),
                    in_=wed_d.ap()[ei * DFF:(ei + 1) * DFF, :].rearrange(
                        "(a p) m -> p a m", p=128)), writes=[wd_b], key=ky)
                S.op("pe", lambda e, ei=ei: e.matmul(
                    pc_h[:, :], ones_h[0:1, :],
                    coef_h[0:1, ei * W:(ei + 1) * W],
                    start=True, stop=True), reads=[coef_b, ones_b], writes=[pc_b])
                for mi2 in range(KTF):
                    fsl = slice(mi2 * W, (mi2 + 1) * W)
                    for k in range(KTM):
                        S.op("pe", lambda e, mi2=mi2, k=k, wg_h=wg_h: e.matmul(
                            pg_h[:, :], wg_h[:, k * DFF + mi2 * 128:k * DFF + mi2 * 128 + 128],
                            xn3_h[:, k * W:(k + 1) * W],
                            start=(k == 0), stop=(k == KTM - 1)),
                            reads=[xn3_b, wg_b], writes=[pg_b])
                    S.op("act", lambda e, fsl=fsl: e.activation(
                        out=gsb_h[:, fsl], in_=pg_h[:, :], func=AF.Silu),
                        reads=[pg_b], writes=[gsb_b])
                    for k in range(KTM):
                        S.op("pe", lambda e, mi2=mi2, k=k, wu_h=wu_h: e.matmul(
                            pu_h[:, :], wu_h[:, k * DFF + mi2 * 128:k * DFF + mi2 * 128 + 128],
                            xn3_h[:, k * W:(k + 1) * W],
                            start=(k == 0), stop=(k == KTM - 1)),
                            reads=[xn3_b, wu_b], writes=[pu_b])
                    S.op("act", lambda e, fsl=fsl: e.activation(
                        out=usb_h[:, fsl], in_=pu_h[:, :], func=AF.Copy),
                        reads=[pu_b], writes=[usb_b])
                    S.op("dve", lambda e, fsl=fsl: e.tensor_tensor(
                        out=gsb_h[:, fsl], in0=gsb_h[:, fsl], in1=usb_h[:, fsl],
                        op=ALU.mult), reads=[usb_b], writes=[gsb_b])
                    S.op("dve", lambda e, fsl=fsl: e.tensor_tensor(
                        out=cme_h[:, fsl], in0=gsb_h[:, fsl], in1=pc_h[:, :],
                        op=ALU.mult), reads=[gsb_b, pc_b], writes=[cme_b])
                for mo in range(KTM):
                    pd_h, pd_b = pdp[mo % 2]
                    for k in range(KTF):
                        S.op("pe", lambda e, mo=mo, k=k, wd_h=wd_h, pd_h=pd_h: e.matmul(
                            pd_h[:, :], wd_h[:, k * D + mo * 128:k * D + mo * 128 + 128],
                            cme_h[:, k * W:(k + 1) * W],
                            start=(k == 0), stop=(k == KTF - 1)),
                            reads=[cme_b, wd_b], writes=[pd_b])
                    msl = slice(mo * W, (mo + 1) * W)
                    if ei == 0:
                        S.op("act", lambda e, msl=msl, pd_h=pd_h: e.activation(
                            out=rsb_h[:, msl], in_=pd_h[:, :], func=AF.Copy),
                            reads=[pd_b], writes=[rsb_b])
                    else:
                        S.op("dve", lambda e, msl=msl, pd_h=pd_h: e.tensor_tensor(
                            out=rsb_h[:, msl], in0=rsb_h[:, msl], in1=pd_h[:, :],
                            op=ALU.add), reads=[pd_b], writes=[rsb_b])
            for mi in range(KTM):
                msl = slice(mi * W, (mi + 1) * W)
                for k in range(KTM):
                    S.op("pe", lambda e, mi=mi, k=k: e.matmul(
                        pg_h[:, :], wsg_h[:, k * D + mi * 128:k * D + mi * 128 + 128],
                        xn3_h[:, k * W:(k + 1) * W],
                        start=(k == 0), stop=(k == KTM - 1)),
                        reads=[xn3_b, wsg_b], writes=[pg_b])
                S.op("act", lambda e, msl=msl: e.activation(
                    out=sq_h[:, msl], in_=pg_h[:, :], func=AF.Silu),
                    reads=[pg_b], writes=[sq_b])
                for k in range(KTM):
                    S.op("pe", lambda e, mi=mi, k=k: e.matmul(
                        pu_h[:, :], wsu_h[:, k * D + mi * 128:k * D + mi * 128 + 128],
                        xn3_h[:, k * W:(k + 1) * W],
                        start=(k == 0), stop=(k == KTM - 1)),
                        reads=[xn3_b, wsu_b], writes=[pu_b])
                S.op("act", lambda e, msl=msl: e.activation(
                    out=ffn_h[:, msl], in_=pu_h[:, :], func=AF.Copy),
                    reads=[pu_b], writes=[ffn_b])
                S.op("dve", lambda e, msl=msl: e.tensor_tensor(
                    out=sm_h[:, msl], in0=sq_h[:, msl], in1=ffn_h[:, msl],
                    op=ALU.mult), reads=[sq_b, ffn_b], writes=[sm_b])
            for mo in range(KTM):
                pd_h, pd_b = pdp[mo % 2]
                msl = slice(mo * W, (mo + 1) * W)
                for k in range(KTM):
                    S.op("pe", lambda e, mo=mo, k=k, pd_h=pd_h: e.matmul(
                        pd_h[:, :], wsd_h[:, k * D + mo * 128:k * D + mo * 128 + 128],
                        sm_h[:, k * W:(k + 1) * W],
                        start=(k == 0), stop=(k == KTM - 1)),
                        reads=[sm_b, wsd_b], writes=[pd_b])
                S.op("dve", lambda e, msl=msl, pd_h=pd_h: e.tensor_tensor(
                    out=rsb_h[:, msl], in0=rsb_h[:, msl], in1=pd_h[:, :],
                    op=ALU.add), reads=[pd_b], writes=[rsb_b])
                S.op("act", lambda e, msl=msl: e.activation(
                    out=ffn_h[:, msl], in_=rsb_h[:, msl], func=AF.Copy),
                    reads=[rsb_b], writes=[ffn_b])
                S.op("act", lambda e, msl=msl: e.activation(
                    out=sq_h[:, msl], in_=rsb_h[:, msl], func=AF.Square),
                    reads=[rsb_b], writes=[sq_b])
            for mo in range(KTM):
                msl = slice(mo * W, (mo + 1) * W)
                S.op("pe", lambda e, msl=msl, mo=mo: e.matmul(
                    pm_h[0:1, :], ones_h[:, 0:1], ffn_h[:, msl],
                    start=(mo == 0), stop=(mo == KTM - 1)),
                    reads=[ffn_b, ones_b], writes=[pm_b])
                S.op("pe", lambda e, msl=msl, mo=mo: e.matmul(
                    pq_h[0:1, :], ones_h[:, 0:1], sq_h[:, msl],
                    start=(mo == 0), stop=(mo == KTM - 1)),
                    reads=[sq_b, ones_b], writes=[pq_b])
            S.op("act", lambda e, c0=c0: e.activation(
                out=m4_h[0:1, c0:c0 + W], in_=pm_h[0:1, :], func=AF.Copy),
                reads=[pm_b], writes=[m4_b])
            S.op("act", lambda e, c0=c0: e.activation(
                out=q4_h[0:1, c0:c0 + W], in_=pq_h[0:1, :], func=AF.Copy),
                reads=[pq_b], writes=[q4_b])
            S.dma("sp", lambda e, c0=c0: e.dma_start(
                out=ffn_d.ap()[:, c0:c0 + W].rearrange("(a p) t -> p a t", p=128),
                in_=ffn_h[:, :].rearrange("p (a t) -> p a t", a=KTM)),
                reads=[ffn_b], key="ffst")

        S.barrier()
        ctxB.close()
        x2sb_h, x2sb_b = sbuf(nc, ctx, "x2sb", [128, KTM * W], F32)
        pnm_h, pnm_b = psum(nc, ctx, "pnm", [128, W])
        prs_h, prs_b = psum(nc, ctx, "prs", [128, W])
        for chn in range(NCH):
            c0 = chn * W
            ln_rowmath_sb(S, m4_h, m4_b, q4_h, q4_b, nm4_h, nm4_b, rs4_h, rs4_b,
                          rtf_h, rtf_b, c0, c0, W, D)
        for chn in range(NCH):
            c0 = chn * W
            S.dma("sp", lambda e, c0=c0: e.dma_start(
                out=x2sb_h[:, :].rearrange("p (a t) -> p a t", a=KTM),
                in_=x2_d.ap()[:, c0:c0 + W].rearrange("(a p) t -> p a t", p=128)),
                writes=[x2sb_b], key="x2ld")
            S.dma("sp", lambda e, c0=c0: e.dma_start(
                out=ffn_h[:, :].rearrange("p (a t) -> p a t", a=KTM),
                in_=ffn_d.ap()[:, c0:c0 + W].rearrange("(a p) t -> p a t", p=128)),
                writes=[ffn_b], key="ffld")
            S.op("pe", lambda e, c0=c0: e.matmul(
                pnm_h[:, :], ones_h[0:1, :], nm4_h[0:1, c0:c0 + W],
                start=True, stop=True), reads=[nm4_b, ones_b], writes=[pnm_b])
            S.op("pe", lambda e, c0=c0: e.matmul(
                prs_h[:, :], ones_h[0:1, :], rs4_h[0:1, c0:c0 + W],
                start=True, stop=True), reads=[rs4_b, ones_b], writes=[prs_b])
            for mi in range(KTM):
                msl = slice(mi * W, (mi + 1) * W)
                S.op("dve", lambda e, msl=msl: e.tensor_tensor(
                    out=tg_h[:, :], in0=ffn_h[:, msl], in1=pnm_h[:, :],
                    op=ALU.add), reads=[ffn_b, pnm_b], writes=[tg_b])
                S.op("dve", lambda e, msl=msl: e.tensor_tensor(
                    out=tg_h[:, :], in0=tg_h[:, :], in1=prs_h[:, :], op=ALU.mult),
                    reads=[prs_b], writes=[tg_b])
                S.op("dve", lambda e, msl=msl: e.tensor_tensor(
                    out=x2sb_h[:, msl], in0=tg_h[:, :], in1=x2sb_h[:, msl],
                    op=ALU.add), reads=[tg_b], writes=[x2sb_b])
            S.dma("sp", lambda e, c0=c0: e.dma_start(
                out=out_d.ap()[:, c0:c0 + W].rearrange("(a p) t -> p a t", p=128),
                in_=x2sb_h[:, :].rearrange("p (a t) -> p a t", a=KTM)),
                reads=[x2sb_b], key="ost")




def build_l3():
    nc = bass.Bass("TRN2", target_bir_lowering=False)
    E = {}
    E["y_my"] = nc.dram_tensor("y", [DI, TC], BF16, kind="ExternalInput")
    E["u_scr"] = nc.dram_tensor("u", [DI, TC], BF16, kind="ExternalInput")
    E["zs_scr"] = nc.dram_tensor("zs", [DI, TC], BF16, kind="ExternalInput")
    E["x_nohalo"] = nc.dram_tensor("x", [D, TC + 4], F32, kind="ExternalInput")
    E["dskip"] = nc.dram_tensor("dskip", [128, DI // 128], F32, kind="ExternalInput")
    E["wout"] = nc.dram_tensor("wout", [DI, D], F32, kind="ExternalInput")
    E["router"] = nc.dram_tensor("router", [D, NE], F32, kind="ExternalInput")
    E["weg"] = nc.dram_tensor("weg", [NE * D, DFF], F32, kind="ExternalInput")
    E["weu"] = nc.dram_tensor("weu", [NE * D, DFF], F32, kind="ExternalInput")
    E["wed"] = nc.dram_tensor("wed", [NE * DFF, D], F32, kind="ExternalInput")
    E["wsg"] = nc.dram_tensor("wsg", [D, D], F32, kind="ExternalInput")
    E["wsu"] = nc.dram_tensor("wsu", [D, D], F32, kind="ExternalInput")
    E["wsd"] = nc.dram_tensor("wsd", [D, D], F32, kind="ExternalInput")
    E["ones_in"] = nc.dram_tensor("ones", [128, 128], BF16, kind="ExternalInput")
    E["ident_in"] = nc.dram_tensor("ident", [128, 128], BF16, kind="ExternalInput")
    E["out"] = nc.dram_tensor("out", [D, TC], F32, kind="ExternalOutput")
    E["x2_scr"] = nc.dram_tensor("x2_scr", [D, TC], F32, kind="Internal")
    E["xn3_scr"] = nc.dram_tensor("xn3_scr", [D, TC], BF16, kind="Internal")
    E["coef_scr"] = nc.dram_tensor("coef_scr", [1, NE * TC], BF16, kind="Internal")
    E["ffn_scr"] = nc.dram_tensor("ffn_scr", [D, TC], BF16, kind="Internal")
    with ExitStack() as ctx:
        S = Sched(nc)
        _PREFIX[0] = "c_"
        stage_l3a(nc, S, None, E)
        S.barrier()
        _PREFIX[0] = "d_"
        stage_l3b(nc, S, None, E)
        _PREFIX[0] = ""
        sems = S.create_sems(nc, ctx)
        block = ctx.enter_context(nc.Block())
        S.emit(block, sems)
    return nc


def build_l1_only():
    nc = bass.Bass("TRN2", target_bir_lowering=False)
    E = {}
    E["x"] = nc.dram_tensor("x", [D, TC + 4], F32, kind="ExternalInput")
    E["w_in"] = nc.dram_tensor("w_in", [D, 2 * DI], F32, kind="ExternalInput")
    E["w_xp"] = nc.dram_tensor("w_xp", [DI, 96], F32, kind="ExternalInput")
    E["conv_w"] = nc.dram_tensor("conv_w", [128, (DI // 128) * DCONV], F32, kind="ExternalInput")
    E["conv_b"] = nc.dram_tensor("conv_b", [128, DI // 128], F32, kind="ExternalInput")
    E["ones_in"] = nc.dram_tensor("ones", [128, 128], BF16, kind="ExternalInput")
    E["u_scr"] = nc.dram_tensor("u", [DI, TC], BF16, kind="ExternalOutput")
    E["zs_scr"] = nc.dram_tensor("zs", [DI, TC], BF16, kind="ExternalOutput")
    E["xdbl_scr"] = nc.dram_tensor("xdbl", [96, TC], F32, kind="ExternalOutput")
    with ExitStack() as ctx:
        S = Sched(nc)
        _PREFIX[0] = "a_"
        stage_l1(nc, S, None, E)
        _PREFIX[0] = ""
        sems = S.create_sems(nc, ctx)
        block = ctx.enter_context(nc.Block())
        S.emit(block, sems)
    return nc


def build_l2_only():
    nc = bass.Bass("TRN2", target_bir_lowering=False)
    E = {}
    E["u_my"] = nc.dram_tensor("u", [DI, TC], BF16, kind="ExternalInput")
    E["xdbl_ag"] = nc.dram_tensor("xdbl", [NC * 96, TC], F32, kind="ExternalInput")
    E["wdt"] = nc.dram_tensor("wdt", [DTR, DDI], F32, kind="ExternalInput")
    E["bdt"] = nc.dram_tensor("bdt", [128, DDI // 128], F32, kind="ExternalInput")
    E["alog"] = nc.dram_tensor("alog", [DDI, DS], F32, kind="ExternalInput")
    E["ones_in"] = nc.dram_tensor("ones", [128, 128], BF16, kind="ExternalInput")
    E["ident_in"] = nc.dram_tensor("ident", [128, 128], BF16, kind="ExternalInput")
    E["y_scr"] = nc.dram_tensor("y", [DI, TC], BF16, kind="ExternalOutput")
    with ExitStack() as ctx:
        S = Sched(nc)
        _PREFIX[0] = "b_"
        stage_l2(nc, S, None, E)
        _PREFIX[0] = ""
        sems = S.create_sems(nc, ctx)
        block = ctx.enter_context(nc.Block())
        S.emit(block, sems)
    return nc


RG = [[0, 1, 2, 3, 4, 5, 6, 7]]


def build_full():
    nc = bass.Bass("TRN2", target_bir_lowering=False)
    E = {}
    E["x"] = nc.dram_tensor("x", [D, TC + 4], F32, kind="ExternalInput")
    E["x_nohalo"] = E["x"]
    E["w_in"] = nc.dram_tensor("w_in", [D, 2 * DI], F32, kind="ExternalInput")
    E["w_xp"] = nc.dram_tensor("w_xp", [DI, 96], F32, kind="ExternalInput")
    E["conv_w"] = nc.dram_tensor("conv_w", [128, (DI // 128) * DCONV], F32, kind="ExternalInput")
    E["conv_b"] = nc.dram_tensor("conv_b", [128, DI // 128], F32, kind="ExternalInput")
    E["wdt"] = nc.dram_tensor("wdt", [DTR, DDI], F32, kind="ExternalInput")
    E["bdt"] = nc.dram_tensor("bdt", [128, DDI // 128], F32, kind="ExternalInput")
    E["alog"] = nc.dram_tensor("alog", [DDI, DS], F32, kind="ExternalInput")
    E["dskip"] = nc.dram_tensor("dskip", [128, DI // 128], F32, kind="ExternalInput")
    E["wout"] = nc.dram_tensor("wout", [DI, D], F32, kind="ExternalInput")
    E["router"] = nc.dram_tensor("router", [D, NE], F32, kind="ExternalInput")
    E["weg"] = nc.dram_tensor("weg", [NE * D, DFF], F32, kind="ExternalInput")
    E["weu"] = nc.dram_tensor("weu", [NE * D, DFF], F32, kind="ExternalInput")
    E["wed"] = nc.dram_tensor("wed", [NE * DFF, D], F32, kind="ExternalInput")
    E["wsg"] = nc.dram_tensor("wsg", [D, D], F32, kind="ExternalInput")
    E["wsu"] = nc.dram_tensor("wsu", [D, D], F32, kind="ExternalInput")
    E["wsd"] = nc.dram_tensor("wsd", [D, D], F32, kind="ExternalInput")
    E["ones_in"] = nc.dram_tensor("ones", [128, 128], BF16, kind="ExternalInput")
    E["ident_in"] = nc.dram_tensor("ident", [128, 128], BF16, kind="ExternalInput")
    E["out"] = nc.dram_tensor("out", [D, TC], F32, kind="ExternalOutput")
    E["u_scr"] = nc.dram_tensor("u_scr", [DI, TC], BF16, kind="Internal")
    E["zs_scr"] = nc.dram_tensor("zs_scr", [DI, TC], BF16, kind="Internal")
    E["xdbl_scr"] = nc.dram_tensor("xdbl_scr", [96, TC], F32, kind="Internal")
    E["u_ag"] = nc.dram_tensor("u_ag", [NC * DI, TC], BF16, kind="Internal")
    E["cofs_in"] = nc.dram_tensor("cofs", [1, 1], mybir.dt.int32, kind="ExternalInput")
    E["xdbl_ag"] = nc.dram_tensor("xdbl_ag", [NC * 96, TC], F32, kind="Internal")
    E["y_scr"] = nc.dram_tensor("y_scr", [DI, TC], BF16, kind="Internal")
    E["y_ag"] = nc.dram_tensor("y_ag", [NC * DI, TC], BF16, kind="Internal")
    E["u_my"] = nc.dram_tensor("u_my", [DI, TC], BF16, kind="Internal")
    E["y_my"] = nc.dram_tensor("y_my", [DI, TC], BF16, kind="Internal")
    E["x2_scr"] = nc.dram_tensor("x2_scr", [D, TC], F32, kind="Internal")
    E["xn3_scr"] = nc.dram_tensor("xn3_scr", [D, TC], BF16, kind="Internal")
    E["coef_scr"] = nc.dram_tensor("coef_scr", [1, NE * TC], BF16, kind="Internal")
    E["ffn_scr"] = nc.dram_tensor("ffn_scr", [D, TC], BF16, kind="Internal")

    with ExitStack() as ctx:
        sems = None
        S = Sched(nc)
        cofs_sb = ctx.enter_context(nc.sbuf_tensor("cofs_sb", [1, 1], mybir.dt.int32))
        cofs_buf = Buf("cofs")
        E["cofs"] = {}

        def _cofs_setup(e, E=E, cofs_sb=cofs_sb):
            reg = e.alloc_register("cofs_reg")
            inst = e.reg_load(reg, cofs_sb[0:1, 0:1])
            E["cofs"]["v"] = e.snap(reg)
            return inst

        S.dma("pool", lambda e: e.dma_start(out=cofs_sb[0:1, 0:1],
                                            in_=E["cofs_in"].ap()),
              writes=[cofs_buf], key="wloadp")
        S.op("pool", _cofs_setup, reads=[cofs_buf])
        _PREFIX[0] = "a_"
        stage_l1(nc, S, sems, E)
        S.barrier()
        b_uag, b_umy = Buf("u_ag"), Buf("u_my")
        S.dma("pool", lambda e: e.collective_compute(
            "AllGather", ALU.bypass, replica_groups=RG,
            ins=[E["xdbl_scr"].ap()], outs=[E["xdbl_ag"].ap()]), key="coll")
        S.dma("pool", lambda e: e.collective_compute(
            "AllGather", ALU.bypass, replica_groups=RG,
            ins=[E["u_scr"].ap()], outs=[E["u_ag"].ap()]), key="coll",
            writes=[b_uag])
        for rr in range(NC):
            S.dma("pool", lambda e, rr=rr: e.dma_start(
                out=E["u_my"].ap()[rr * DDI:(rr + 1) * DDI, :],
                in_=E["u_ag"].ap()[bass.ds(E["cofs"]["v"] + rr * DI, DDI), :]),
                key="uld", reads=[b_uag], writes=[b_umy])
        S.barrier()
        _PREFIX[0] = "b_"
        stage_l2(nc, S, sems, E)
        S.barrier()
        b_yag, b_ymy = Buf("y_ag"), Buf("y_my")
        S.dma("pool", lambda e: e.collective_compute(
            "AllGather", ALU.bypass, replica_groups=RG,
            ins=[E["y_scr"].ap()], outs=[E["y_ag"].ap()]), key="coll",
            writes=[b_yag])
        for j in range(NC):
            S.dma("pool", lambda e, j=j: e.dma_start(
                out=E["y_my"].ap()[j * DDI:(j + 1) * DDI, :],
                in_=E["y_ag"].ap()[bass.ds(E["cofs"]["v"] + j * DI, DDI), :]),
                key="yld", reads=[b_yag], writes=[b_ymy])
        S.barrier()
        _PREFIX[0] = "c_"
        stage_l3a(nc, S, sems, E)
        S.barrier()
        _PREFIX[0] = "d_"
        stage_l3b(nc, S, sems, E)
        _PREFIX[0] = ""
        sems = S.create_sems(nc, ctx)
        block = ctx.enter_context(nc.Block())
        S.emit(block, sems)
    return nc


_FULL = [None, None, None]


def _prep_common(inputs):
    p = {}
    p["w_in"] = np.ascontiguousarray(_np(inputs["L0_attn_W_in"]), np.float32)
    p["w_xp"] = np.ascontiguousarray(_np(inputs["L0_attn_W_xproj"]), np.float32)
    p["cw"] = np.ascontiguousarray(_np(inputs["L0_attn_conv_w"]).reshape(DI // 128, 128, DCONV)
                                   .transpose(1, 0, 2).reshape(128, -1), np.float32)
    p["cb"] = np.ascontiguousarray(_np(inputs["L0_attn_conv_b"]).reshape(DI // 128, 128).T,
                                   np.float32)
    p["wdt"] = np.asarray(inputs["L0_attn_W_dt"], np.float32)
    p["bdt"] = np.asarray(inputs["L0_attn_b_dt"], np.float32)
    p["alog"] = np.asarray(inputs["L0_attn_A_log"], np.float32)
    p["dskip"] = np.ascontiguousarray(
        np.asarray(inputs["L0_attn_D_skip"], np.float32).reshape(DI // 128, 128).T)
    p["wout"] = np.ascontiguousarray(_np(inputs["L0_attn_W_out"]), np.float32)
    p["router"] = np.ascontiguousarray(_np(inputs["L0_ffn_router"]), np.float32)
    p["weg"] = np.ascontiguousarray(_np(inputs["L0_ffn_experts_gate"]).reshape(NE * D, DFF), np.float32)
    p["weu"] = np.ascontiguousarray(_np(inputs["L0_ffn_experts_up"]).reshape(NE * D, DFF), np.float32)
    p["wed"] = np.ascontiguousarray(_np(inputs["L0_ffn_experts_down"]).reshape(NE * DFF, D), np.float32)
    p["wsg"] = np.ascontiguousarray(_np(inputs["L0_ffn_shared_gate"]), np.float32)
    p["wsu"] = np.ascontiguousarray(_np(inputs["L0_ffn_shared_up"]), np.float32)
    p["wsd"] = np.ascontiguousarray(_np(inputs["L0_ffn_shared_down"]), np.float32)
    p["ones"] = np.ones((128, 128), BF)
    p["ident"] = np.eye(128).astype(BF)
    return p


def kernel(**inputs):
    if _FULL[0] is None:
        _FULL[0] = build_l1_only()
        _FULL[1] = build_l2_only()
        _FULL[2] = build_l3()
    p = _prep_common(inputs)
    x = _np(inputs["L0_attn_x"])
    xs_all = []
    for c in range(NC):
        b, s = c // 4, (c % 4) * TC
        xs = np.zeros((D, TC + 4), np.float32)
        lo = max(0, s - 4)
        xs[:, 4 - (s - lo):] = x[b, lo:s + TC, :].T
        xs_all.append(np.ascontiguousarray(xs))
    in1 = [dict(x=xs_all[c], w_in=p["w_in"], w_xp=p["w_xp"], conv_w=p["cw"],
                conv_b=p["cb"], ones=p["ones"]) for c in range(NC)]
    r1 = run_bass_kernel_spmd(_FULL[0], in1, core_ids=list(range(NC)))
    u = np.concatenate([np.asarray(r["u"]) for r in r1.results], axis=1)
    zs = [np.asarray(r["zs"]) for r in r1.results]
    xdbl = np.ascontiguousarray(
        np.concatenate([np.asarray(r["xdbl"]) for r in r1.results], axis=0))
    in2 = []
    for c in range(NC):
        dsl = slice(c * DDI, (c + 1) * DDI)
        u_my = np.concatenate([u[dsl][:, k * TC:(k + 1) * TC] for k in range(NC)],
                              axis=0)
        in2.append(dict(
            u=np.ascontiguousarray(u_my), xdbl=xdbl,
            wdt=np.ascontiguousarray(p["wdt"][:, dsl]),
            bdt=np.ascontiguousarray(p["bdt"][dsl].reshape(DDI // 128, 128).T),
            alog=np.ascontiguousarray(p["alog"][dsl]),
            ones=p["ones"], ident=p["ident"]))
    r2 = run_bass_kernel_spmd(_FULL[1], in2, core_ids=list(range(NC)))
    yfull = np.concatenate([np.asarray(r["y"]) for r in r2.results], axis=0)
    in3 = []
    for c in range(NC):
        yv = np.concatenate([yfull[k * DI + c * DDI:k * DI + (c + 1) * DDI]
                             for k in range(NC)], axis=0)
        tsl = slice(c * TC, (c + 1) * TC)
        in3.append(dict(
            y=np.ascontiguousarray(yv),
            u=np.ascontiguousarray(u[:, tsl]), zs=zs[c], x=xs_all[c],
            dskip=p["dskip"], wout=p["wout"], router=p["router"],
            weg=p["weg"], weu=p["weu"], wed=p["wed"], wsg=p["wsg"],
            wsu=p["wsu"], wsd=p["wsd"], ones=p["ones"], ident=p["ident"]))
    r3 = run_bass_kernel_spmd(_FULL[2], in3, core_ids=list(range(NC)))
    out_fm = np.concatenate([np.asarray(r["out"]) for r in r3.results], axis=1)
    return np.ascontiguousarray(out_fm.T.reshape(B, L, D)).astype(np.float32)


# version 08
# speedup vs baseline: 1.2021x; 1.0327x over previous
import numpy as np
import ml_dtypes
import concourse.bass as bass
import concourse.mybir as mybir
from concourse.bass_utils import run_bass_kernel_spmd
from contextlib import ExitStack

F32 = mybir.dt.float32
BF16 = mybir.dt.bfloat16
AF = mybir.ActivationFunctionType
ALU = mybir.AluOpType
BF = ml_dtypes.bfloat16

B, L, D = 2, 16384, 1024
DI, DS, DCONV, DTR = 2048, 16, 4, 64
NE, DFF = 8, 512
NC = 8
TC = (B * L) // NC
DDI = DI // NC
EPS = 1e-5

ENGINES = ("sp", "act", "dve", "pool", "pe")


class Op:
    __slots__ = ("eng", "fn", "deps", "dma", "idx", "inc_val", "needed", "key",
                 "seq", "wait_val")

    def __init__(self, eng, fn, deps, dma):
        self.eng, self.fn, self.deps, self.dma = eng, fn, deps, dma
        self.idx = -1
        self.inc_val = None
        self.needed = False


class Buf:
    __slots__ = ("name", "last_writer", "readers", "exclusive")

    def __init__(self, name="", exclusive=False):
        self.name = name
        self.last_writer = None
        self.readers = []
        self.exclusive = exclusive


SYNC_SAME_ENGINE = [True]


class Sched:

    def __init__(self, nc):
        self.nc = nc
        self.streams = {e: [] for e in ENGINES}
        self._seq = 0
        self._last_dma = {}

    def op(self, eng, fn, reads=(), writes=(), deps=(), dma=False, key=None):
        reads = list(reads)
        writes = list(writes)
        excl = [b for b in reads if b.exclusive]
        if excl:
            reads = [b for b in reads if not b.exclusive]
            writes = writes + excl
        dd = list(deps)
        for b in reads:
            if b.last_writer is not None:
                dd.append(b.last_writer)
        for b in writes:
            if b.last_writer is not None:
                dd.append(b.last_writer)
            dd.extend(b.readers)
        o = Op(eng, fn, dd, dma)
        o.key = key
        o.idx = len(self.streams[eng])
        o.seq = self._seq
        self._seq += 1
        self.streams[eng].append(o)
        for b in reads:
            b.readers.append(o)
        for b in writes:
            b.last_writer = o
            b.readers = []
        return o

    def dma(self, eng, fn, reads=(), writes=(), deps=(), key=None):
        o = self.op(eng, fn, reads=reads, writes=writes, deps=deps, dma=True,
                    key=key)
        self._last_dma[self._key(o)] = o
        return o

    def create_sems(self, nc, ctx):
        sems = {}
        keys = set(ENGINES)
        for e in ENGINES:
            for o in self.streams[e]:
                if o.dma:
                    keys.add(self._key(o))
        for k in sorted(keys):
            sems[k] = ctx.enter_context(nc.semaphore("s_" + k.replace(".", "_")))
        return sems

    def barrier(self):
        lasts = [s[-1] for s in self.streams.values() if s]
        lasts += list(self._last_dma.values())
        nops = [self.op(e, lambda eng: eng.nop(), deps=lasts) for e in ENGINES]
        for e in ENGINES:
            self.op(e, lambda eng: eng.nop(), deps=nops)

    def _key(self, o):
        return o.eng + "." + (o.key or "dma")

    def _resolve(self):
        for e in ENGINES:
            for o in self.streams[e]:
                for d in o.deps:
                    if (not SYNC_SAME_ENGINE[0] and d.eng == o.eng
                            and not d.dma and not o.dma):
                        continue
                    d.needed = True
        self._dma_hist = {}
        for e in ENGINES:
            c_comp = 0
            c_dma = {}
            for o in self.streams[e]:
                if o.dma:
                    kk = self._key(o)
                    c_dma[kk] = c_dma.get(kk, 0) + 16
                    o.inc_val = c_dma[kk]
                    o.needed = True
                    self._dma_hist.setdefault(kk, []).append((o.seq, o.inc_val))
                elif o.needed:
                    c_comp += 1
                    o.inc_val = c_comp
                else:
                    o.inc_val = None

    def emit(self, block, sems, final_eng="dve"):
        self._resolve()
        sched = self
        finals = {}
        for e in ENGINES:
            for o in self.streams[e]:
                if o.dma:
                    kk = self._key(o)
                    finals[kk] = max(finals.get(kk, 0), o.inc_val)
                elif o.inc_val is not None:
                    finals[e] = max(finals.get(e, 0), o.inc_val)

        def mk_runner(ename):
            def run(eng):
                water = {}
                prev_completed = 0
                for o in sched.streams[ename]:
                    want = {}
                    for dp in o.deps:
                        if dp.inc_val is None:
                            continue
                        if dp.eng == o.eng and not dp.dma:
                            if not SYNC_SAME_ENGINE[0] and not o.dma:
                                continue
                            if dp.inc_val <= prev_completed:
                                continue
                        if dp.dma:
                            key = sched._key(dp)
                            v = 0
                            for sq, cum in sched._dma_hist[key]:
                                if sq < o.seq:
                                    v = cum
                                else:
                                    break
                            if v > want.get(key, 0):
                                want[key] = v
                            continue
                        key = dp.eng
                        if dp.inc_val > want.get(key, 0):
                            want[key] = dp.inc_val
                    for key in sorted(want):
                        v = want[key]
                        if key == ename:
                            if v > prev_completed:
                                eng.wait_ge(sems[key], v)
                                prev_completed = v
                            continue
                        if water.get(key, 0) >= v:
                            continue
                        eng.wait_ge(sems[key], v)
                        water[key] = v
                    inst = o.fn(eng)
                    if o.inc_val is not None:
                        key = sched._key(o) if o.dma else o.eng
                        inst.then_inc(sems[key], 16 if o.dma else 1)
                if ename == final_eng:
                    for key in sorted(finals):
                        if key in ENGINES:
                            continue
                        if finals[key] > water.get(key, 0) and finals[key] > 0:
                            eng.wait_ge(sems[key], finals[key])
            return run

        block.sync(mk_runner("sp"))
        block.scalar(mk_runner("act"))
        block.vector(mk_runner("dve"))
        block.gpsimd(mk_runner("pool"))
        block.tensor(mk_runner("pe"))


def make_sems(nc, ctx, extra=("wload",)):
    sems = {}
    for e in ENGINES:
        sems[e] = ctx.enter_context(nc.semaphore(f"s_{e}"))
        sems[e + "_dma"] = ctx.enter_context(nc.semaphore(f"s_{e}_dma"))
    for k in extra:
        sems[k] = ctx.enter_context(nc.semaphore(f"s_{k}"))
    return sems


_PREFIX = [""]


def sbuf(nc, ctx, name, shape, dtype):
    name = _PREFIX[0] + name
    h = ctx.enter_context(nc.sbuf_tensor(name, list(shape), dtype))
    return h, Buf(name)


def psum(nc, ctx, name, shape, dtype=F32):
    name = _PREFIX[0] + name
    h = ctx.enter_context(nc.psum_tensor(name, list(shape), dtype))
    return h, Buf(name, exclusive=True)


def ln_rowmath(S, pm_h, pm_b, psq_h, psq_b, negm_h, negm_b, rstd_h, rstd_b,
               tmp_h, tmp_b, col0, w, nfeat):
    sl = slice(col0, col0 + w)
    S.op("act", lambda e: e.activation(out=negm_h[0:1, sl], in_=pm_h[0:1, 0:w],
                                       func=AF.Copy, scale=-1.0 / nfeat),
         reads=[pm_b], writes=[negm_b])
    S.op("act", lambda e: e.activation(out=tmp_h[0:1, sl], in_=pm_h[0:1, 0:w],
                                       func=AF.Square, scale=1.0 / nfeat),
         reads=[pm_b], writes=[tmp_b])
    S.op("act", lambda e: e.activation(out=rstd_h[0:1, sl], in_=psq_h[0:1, 0:w],
                                       func=AF.Copy, scale=1.0 / nfeat),
         reads=[psq_b], writes=[rstd_b])
    S.op("dve", lambda e: e.tensor_tensor(out=rstd_h[0:1, sl], in0=rstd_h[0:1, sl],
                                          in1=tmp_h[0:1, sl], op=ALU.subtract),
         reads=[tmp_b], writes=[rstd_b])
    S.op("dve", lambda e: e.tensor_scalar(out=rstd_h[0:1, sl], in0=rstd_h[0:1, sl],
                                          scalar1=EPS, scalar2=None, op0=ALU.add),
         writes=[rstd_b])
    S.op("act", lambda e: e.activation(out=rstd_h[0:1, sl], in_=rstd_h[0:1, sl],
                                       func=AF.Ln),
         writes=[rstd_b])
    S.op("act", lambda e: e.activation(out=rstd_h[0:1, sl], in_=rstd_h[0:1, sl],
                                       func=AF.Exp, scale=-0.5),
         writes=[rstd_b])


def stage_l1(nc, S, sems, E, debug=False, p2_nchunks=None):
    x_d = E["x"]
    win_d = E["w_in"]
    wxp_d = E["w_xp"]
    cw_d = E["conv_w"]
    cb_d = E["conv_b"]
    ones_d = E["ones_in"]
    u_d = E["u_scr"]
    zs_d = E["zs_scr"]
    xdbl_d = E["xdbl_scr"]

    KT = D // 128
    MT = 2 * DI // 128
    NCH = TC // 512
    W = 512
    chunks = [(0, 4)] + [(4 + W * c, W) for c in range(NCH)]
    x_ap = x_d.ap()

    with ExitStack() as ctx:
        win_h, win_b = sbuf(nc, ctx, "win_bf", [128, KT * 2 * DI], BF16)
        wxp_h, wxp_b = sbuf(nc, ctx, "wxp_bf", [128, (DI // 128) * 96], BF16)
        cw_h, cw_b = sbuf(nc, ctx, "cw", [128, (DI // 128) * DCONV], F32)
        cb_h, cb_b = sbuf(nc, ctx, "cb", [128, DI // 128], F32)
        ones_h, ones_b = sbuf(nc, ctx, "ones_sb", [128, 128], BF16)
        negmb_h, negmb_b = sbuf(nc, ctx, "negm_bf", [1, TC + 4], BF16)
        rstdb_h, rstdb_b = sbuf(nc, ctx, "rstd_bf", [1, TC + 4], BF16)
        nm_s, nm_sb = sbuf(nc, ctx, "nm_s", [1, W], F32)
        rs_s, rs_sb = sbuf(nc, ctx, "rs_s", [1, W], F32)
        rt_s, rt_sb = sbuf(nc, ctx, "rt_s", [1, W], F32)
        xbf_h, xbf_b = sbuf(nc, ctx, "xbf", [128, KT * W], BF16)
        sqf_h, sqf_b = sbuf(nc, ctx, "sqf", [128, KT * W], BF16)
        xn_h, xn_b = sbuf(nc, ctx, "xn", [128, KT * W], BF16)
        tfa_h, tfa_b = sbuf(nc, ctx, "tfa", [128, W], F32)
        xin = [sbuf(nc, ctx, f"xin{i}", [128, 4 + W], F32) for i in range(DI // 128)]
        u_h, u_b = sbuf(nc, ctx, "usb", [128, (DI // 128) * W], BF16)
        ur_h, ur_b = sbuf(nc, ctx, "uraw", [128, W], F32)
        zs_h, zs_b = sbuf(nc, ctx, "zsb", [128, (DI // 128) * W], BF16)
        xdb_h, xdb_b = sbuf(nc, ctx, "xdbuf", [96, W], F32)
        pmean_h, pmean_b = psum(nc, ctx, "pmean", [1, W])
        psq_h, psq_b = psum(nc, ctx, "psq", [1, W])
        pnm_h, pnm_b = psum(nc, ctx, "pnm", [128, W])
        prs_h, prs_b = psum(nc, ctx, "prs", [128, W])
        pxz = [psum(nc, ctx, f"pxz{i}", [128, W]) for i in range(2)]
        pxd_h, pxd_b = psum(nc, ctx, "pxd", [96, W])

        S.dma("pool", lambda e: e.dma_start(
            out=win_h[:, :].rearrange("p (a m) -> p a m", a=KT),
            in_=win_d.ap().rearrange("(a p) m -> p a m", p=128)),
            writes=[win_b], key="wload")
        S.dma("pool", lambda e: e.dma_start(
            out=wxp_h[:, :].rearrange("p (a m) -> p a m", a=DI // 128),
            in_=wxp_d.ap().rearrange("(a p) m -> p a m", p=128)),
            writes=[wxp_b], key="wload")
        S.dma("sp", lambda e: e.dma_start(out=cw_h[:, :], in_=cw_d.ap()),
              writes=[cw_b])
        S.dma("sp", lambda e: e.dma_start(out=cb_h[:, :], in_=cb_d.ap()),
              writes=[cb_b])
        S.dma("sp", lambda e: e.dma_start(out=ones_h[:, :], in_=ones_d.ap()),
              writes=[ones_b])

        for col0, w in chunks:
            S.dma("pool", lambda e, col0=col0, w=w: e.dma_start(
                out=xbf_h[:, 0:KT * w].rearrange("p (a t) -> p a t", a=KT),
                in_=x_ap[:, col0:col0 + w].rearrange("(a p) t -> p a t", p=128)),
                writes=[xbf_b])
            S.op("act", lambda e, w=w: e.activation(
                out=sqf_h[:, 0:KT * w], in_=xbf_h[:, 0:KT * w], func=AF.Square),
                reads=[xbf_b], writes=[sqf_b])
            for k in range(KT):
                S.op("pe", lambda e, k=k, w=w: e.matmul(
                    pmean_h[0:1, 0:w], ones_h[:, 0:1], xbf_h[:, k * w:(k + 1) * w],
                    start=(k == 0), stop=(k == KT - 1)),
                    reads=[xbf_b, ones_b], writes=[pmean_b])
                S.op("pe", lambda e, k=k, w=w: e.matmul(
                    psq_h[0:1, 0:w], ones_h[:, 0:1], sqf_h[:, k * w:(k + 1) * w],
                    start=(k == 0), stop=(k == KT - 1)),
                    reads=[sqf_b, ones_b], writes=[psq_b])
            ln_rowmath(S, pmean_h, pmean_b, psq_h, psq_b, nm_s, nm_sb,
                       rs_s, rs_sb, rt_s, rt_sb, 0, w, D)
            S.op("act", lambda e, col0=col0, w=w: e.activation(
                out=negmb_h[0:1, col0:col0 + w], in_=nm_s[0:1, 0:w], func=AF.Copy),
                reads=[nm_sb], writes=[negmb_b])
            S.op("act", lambda e, col0=col0, w=w: e.activation(
                out=rstdb_h[0:1, col0:col0 + w], in_=rs_s[0:1, 0:w], func=AF.Copy),
                reads=[rs_sb], writes=[rstdb_b])

        p2chunks = chunks if p2_nchunks is None else chunks[:p2_nchunks]
        for col0, w in p2chunks:
            mini = (w == 4)
            S.dma("pool", lambda e, col0=col0, w=w: e.dma_start(
                out=xbf_h[:, 0:KT * w].rearrange("p (a t) -> p a t", a=KT),
                in_=x_ap[:, col0:col0 + w].rearrange("(a p) t -> p a t", p=128)),
                writes=[xbf_b])
            S.op("pe", lambda e, col0=col0, w=w: e.matmul(
                pnm_h[:, 0:w], ones_h[0:1, :], negmb_h[0:1, col0:col0 + w],
                start=True, stop=True), reads=[negmb_b, ones_b], writes=[pnm_b])
            S.op("pe", lambda e, col0=col0, w=w: e.matmul(
                prs_h[:, 0:w], ones_h[0:1, :], rstdb_h[0:1, col0:col0 + w],
                start=True, stop=True), reads=[rstdb_b, ones_b], writes=[prs_b])
            for k in range(KT):
                S.op("dve", lambda e, k=k, w=w: e.tensor_tensor(
                    out=tfa_h[:, 0:w], in0=xbf_h[:, k * w:(k + 1) * w],
                    in1=pnm_h[:, 0:w], op=ALU.add),
                    reads=[xbf_b, pnm_b], writes=[tfa_b])
                S.op("dve", lambda e, k=k, w=w: e.tensor_tensor(
                    out=xn_h[:, k * w:(k + 1) * w], in0=tfa_h[:, 0:w],
                    in1=prs_h[:, 0:w], op=ALU.mult),
                    reads=[tfa_b, prs_b], writes=[xn_b])
            mt_range = range(16) if mini else range(MT)
            for mi in mt_range:
                ph, pb = pxz[mi % 2]
                for k in range(KT):
                    S.op("pe", lambda e, mi=mi, k=k, ph=ph, w=w: e.matmul(
                        ph[:, 0:w],
                        win_h[:, k * 2 * DI + mi * 128:k * 2 * DI + mi * 128 + 128],
                        xn_h[:, k * w:(k + 1) * w],
                        start=(k == 0), stop=(k == KT - 1)),
                        reads=[xn_b, win_b], writes=[pb])
                if mi < 16:
                    xih, xib = xin[mi]
                    if mini:
                        S.op("act", lambda e, ph=ph, xih=xih: e.activation(
                            out=xih[:, 0:4], in_=ph[:, 0:4], func=AF.Copy),
                            reads=[pb], writes=[xib])
                    else:
                        S.op("act", lambda e, ph=ph, xih=xih, w=w: e.activation(
                            out=xih[:, 4:4 + w], in_=ph[:, 0:w], func=AF.Copy),
                            reads=[pb], writes=[xib])
                        S.op("dve", lambda e, xih=xih, mi=mi, w=w: e.tensor_scalar(
                            out=ur_h[:, 0:w], in0=xih[:, 1:1 + w],
                            scalar1=cw_h[:, mi * DCONV:mi * DCONV + 1],
                            scalar2=cb_h[:, mi:mi + 1], op0=ALU.mult, op1=ALU.add),
                            reads=[xib, cw_b, cb_b], writes=[ur_b])
                        for j in range(1, DCONV):
                            S.op("dve", lambda e, xih=xih, mi=mi, j=j, w=w: e.scalar_tensor_tensor(
                                out=ur_h[:, 0:w], in0=xih[:, 1 + j:1 + j + w],
                                scalar=cw_h[:, mi * DCONV + j:mi * DCONV + j + 1],
                                in1=ur_h[:, 0:w], op0=ALU.mult, op1=ALU.add),
                                reads=[xib, cw_b], writes=[ur_b])
                        S.op("act", lambda e, mi=mi, w=w: e.activation(
                            out=u_h[:, mi * w:(mi + 1) * w], in_=ur_h[:, 0:w],
                            func=AF.Silu), reads=[ur_b], writes=[u_b])
                        S.op("dve", lambda e, xih=xih, w=w: e.tensor_copy(
                            out=xih[:, 0:4], in_=xih[:, w:w + 4]),
                            reads=[xib], writes=[xib])
                        S.op("pe", lambda e, mi=mi, w=w: e.matmul(
                            pxd_h[0:96, 0:w], wxp_h[:, mi * 96:(mi + 1) * 96],
                            u_h[:, mi * w:(mi + 1) * w],
                            start=(mi == 0), stop=(mi == 15)),
                            reads=[u_b, wxp_b], writes=[pxd_b])

                else:
                    S.op("act", lambda e, ph=ph, mi=mi, w=w: e.activation(
                        out=zs_h[:, (mi - 16) * w:(mi - 15) * w], in_=ph[:, 0:w],
                        func=AF.Silu), reads=[pb], writes=[zs_b])
            if not mini:
                S.dma("sp", lambda e, col0=col0, w=w: e.dma_start(
                    out=u_d.ap()[:, col0 - 4:col0 - 4 + w].rearrange(
                        "(a p) t -> p a t", p=128),
                    in_=u_h[:, 0:16 * w].rearrange("p (a t) -> p a t", a=16)),
                    reads=[u_b])
                S.dma("act", lambda e, col0=col0, w=w: e.dma_start(
                    out=zs_d.ap()[:, col0 - 4:col0 - 4 + w].rearrange(
                        "(a p) t -> p a t", p=128),
                    in_=zs_h[:, 0:16 * w].rearrange("p (a t) -> p a t", a=16)),
                    reads=[zs_b])
                S.op("act", lambda e, w=w: e.activation(
                    out=xdb_h[0:96, 0:w], in_=pxd_h[0:96, 0:w], func=AF.Copy),
                    reads=[pxd_b], writes=[xdb_b])
                S.dma("act", lambda e, col0=col0, w=w: e.dma_start(
                    out=xdbl_d.ap()[0:96, col0 - 4:col0 - 4 + w],
                    in_=xdb_h[0:96, 0:w]), reads=[xdb_b])


def _np(a):
    return np.asarray(a)


def stage_l2(nc, S, sems, E, nchu=None, nbatch=B, skip=()):
    TCH = 1024
    NSUB = TCH // 512
    NCHU = (L // TCH) if nchu is None else nchu
    NDT = DDI // 128
    u_d = E["u_my"]
    xdbl_d = E["xdbl_ag"]
    wdt_d = E["wdt"]
    bdt_d = E["bdt"]
    alog_d = E["alog"]
    ones_d = E["ones_in"]
    ident_d = E["ident_in"]
    y_d = E["y_scr"]

    with ExitStack() as ctx:
        ones_h, ones_b = sbuf(nc, ctx, "l2_ones", [128, 128], BF16)
        ident_h, ident_b = sbuf(nc, ctx, "ident_sb", [128, 128], BF16)
        wdt_h, wdt_b = sbuf(nc, ctx, "wdt_bf", [DTR, DDI], BF16)
        bdt_h, bdt_b = sbuf(nc, ctx, "bdt_sb", [128, NDT], F32)
        an_h = [sbuf(nc, ctx, f"aneg{i}", [128, DS], F32) for i in range(NDT)]
        dtlr_p = [sbuf(nc, ctx, f"dtlr_bf{p}", [DTR, TCH], BF16) for p in range(2)]
        brep = [sbuf(nc, ctx, f"brep{i}", [128, TCH], F32) for i in range(2)]
        crep = [sbuf(nc, ctx, f"crep{i}", [128, TCH], BF16) for i in range(2)]
        ut_p = [[sbuf(nc, ctx, f"ut{i}_{p}", [128, TCH], BF16) for i in range(NDT)]
                for p in range(2)]
        dl_p = [[sbuf(nc, ctx, f"dl{i}_{p}", [128, TCH], F32) for i in range(NDT)]
                for p in range(2)]
        du_p = [[sbuf(nc, ctx, f"du{i}_{p}", [128, TCH], BF16) for i in range(NDT)]
                for p in range(2)]
        spa_h, spa_b = sbuf(nc, ctx, "sp_abs", [128, 512], F32)
        spb_h, spb_b = sbuf(nc, ctx, "sp_mx", [128, 512], F32)
        dA = [sbuf(nc, ctx, f"dA{i}", [128, TCH], F32) for i in range(2)]
        dBu = [sbuf(nc, ctx, f"dBu{i}", [128, TCH], BF16) for i in range(2)]
        hh = [sbuf(nc, ctx, f"h{i}", [128, TCH], BF16) for i in range(2)]
        gg = [sbuf(nc, ctx, f"g{i}", [128, TCH], BF16) for i in range(2)]
        carry = [sbuf(nc, ctx, f"carry{i}", [128, DS], F32) for i in range(NDT)]
        yo = [sbuf(nc, ctx, f"yo{i}", [128, TCH], BF16) for i in range(NDT)]
        pdl_h, pdl_b = psum(nc, ctx, "pdl", [128, 512])
        py = [[psum(nc, ctx, f"py{i}_{s}", [128, 512]) for s in range(NSUB)]
              for i in range(NDT)]

        S.dma("sp", lambda e: e.dma_start(out=ones_h[:, :], in_=ones_d.ap()),
              writes=[ones_b], key="wload")
        S.dma("sp", lambda e: e.dma_start(out=ident_h[:, :], in_=ident_d.ap()),
              writes=[ident_b], key="wload")
        S.dma("pool", lambda e: e.dma_start(out=wdt_h[:, :], in_=wdt_d.ap()),
              writes=[wdt_b], key="wloadp")
        S.dma("sp", lambda e: e.dma_start(out=bdt_h[:, :], in_=bdt_d.ap()),
              writes=[bdt_b], key="wload")
        al_bufs = []
        for i in range(NDT):
            ah, ab = sbuf(nc, ctx, f"alog{i}", [128, DS], F32)
            al_bufs.append((ah, ab))
            S.dma("sp", lambda e, ah=ah, i=i: e.dma_start(
                out=ah[:, :], in_=alog_d.ap()[128 * i:128 * (i + 1), :]),
                writes=[ab], key="wload")
        for i in range(NDT):
            ah, ab = al_bufs[i]
            nh, nb = an_h[i]
            S.op("act", lambda e, ah=ah, nh=nh: e.activation(
                out=nh[:, :], in_=ah[:, :], func=AF.Exp), reads=[ab], writes=[nb])
            S.op("dve", lambda e, nh=nh: e.tensor_scalar(
                out=nh[:, :], in0=nh[:, :], scalar1=-1.0, scalar2=None,
                op0=ALU.mult), writes=[nb])
        for i in range(NDT):
            ch, cb_ = carry[i]
            S.op("dve", lambda e, ch=ch: e.memset(ch[:, :], 0.0), writes=[cb_])

        for b in range(nbatch):
            if b > 0:
                for i in range(NDT):
                    ch, cb_ = carry[i]
                    S.op("dve", lambda e, ch=ch: e.memset(ch[:, :], 0.0),
                         writes=[cb_])
            for chk in range(NCHU):
                c0 = b * L + chk * TCH
                rr = c0 // TC
                cl = c0 % TC
                cp = (b * NCHU + chk) % 2
                dtlr_h, dtlr_b = dtlr_p[cp]
                ut, dl, du = ut_p[cp], dl_p[cp], du_p[cp]
                S.dma("pool", lambda e, rr=rr, cl=cl: e.dma_start(
                    out=dtlr_h[:, :],
                    in_=xdbl_d.ap()[rr * 96:rr * 96 + DTR, cl:cl + TCH]),
                    writes=[dtlr_b], key="bcld")
                for i in range(NDT):
                    uh, ub = ut[i]
                    S.dma("sp", lambda e, uh=uh, i=i, rr=rr, cl=cl: e.dma_start(
                        out=uh[:, :],
                        in_=u_d.ap()[rr * DDI + 128 * i:rr * DDI + 128 * (i + 1),
                                     cl:cl + TCH]),
                        writes=[ub], key="uld")
                for i in range(NDT) if "delta" not in skip else []:
                    dlh, dlb = dl[i]
                    for s in range(NSUB):
                        if "deltamm" not in skip:
                            S.op("pe", lambda e, i=i, s=s: e.matmul(
                                pdl_h[:, :], wdt_h[:, 128 * i:128 * (i + 1)],
                                dtlr_h[:, 512 * s:512 * (s + 1)], start=True, stop=True),
                                reads=[dtlr_b, wdt_b], writes=[pdl_b])
                        else:
                            S.op("pe", lambda e, i=i, s=s: e.matmul(
                                pdl_h[:, :], ident_h[:, :],
                                ut[i][0][:, 512 * s:512 * (s + 1)], start=True, stop=True),
                                reads=[ut[i][1], ident_b], writes=[pdl_b])
                        _af_abs = AF.Square if "abs" in skip else AF.Abs
                        if "spact" not in skip:
                            S.op("act", lambda e, i=i, _af_abs=_af_abs: e.activation(
                                out=spa_h[:, :], in_=pdl_h[:, :], func=_af_abs,
                                bias=bdt_h[:, i:i + 1]),
                                reads=[pdl_b, bdt_b], writes=[spa_b])
                            S.op("act", lambda e: e.activation(
                                out=spa_h[:, :], in_=spa_h[:, :], func=AF.Exp,
                                scale=-1.0), writes=[spa_b])
                            S.op("act", lambda e: e.activation(
                                out=spa_h[:, :], in_=spa_h[:, :], func=AF.Ln,
                                bias=1.0), writes=[spa_b])
                        else:
                            S.op("act", lambda e: e.activation(
                                out=spa_h[:, :], in_=pdl_h[:, :], func=AF.Copy),
                                reads=[pdl_b], writes=[spa_b])
                        if "spdve" not in skip:
                            S.op("dve", lambda e, i=i: e.tensor_scalar(
                                out=spb_h[:, :], in0=pdl_h[:, :],
                                scalar1=bdt_h[:, i:i + 1], scalar2=0.0,
                                op0=ALU.add, op1=ALU.max),
                                reads=[pdl_b, bdt_b], writes=[spb_b])
                        else:
                            S.op("dve", lambda e: e.tensor_copy(
                                out=spb_h[:, :], in_=pdl_h[:, :]),
                                reads=[pdl_b], writes=[spb_b])
                        S.op("dve", lambda e, dlh=dlh, s=s: e.tensor_tensor(
                            out=dlh[:, 512 * s:512 * (s + 1)], in0=spa_h[:, :],
                            in1=spb_h[:, :], op=ALU.add),
                            reads=[spa_b, spb_b], writes=[dlb])
                    duh, dub = du[i]
                    uh, ub = ut[i]
                    S.op("dve", lambda e, duh=duh, dlh=dlh, uh=uh: e.tensor_tensor(
                        out=duh[:, :], in0=dlh[:, :], in1=uh[:, :], op=ALU.mult),
                        reads=[dlb, ub], writes=[dub])
                for n in range(DS) if "nloop" not in skip else []:
                    bh, bb = brep[n % 2]
                    ch2, cb2 = crep[n % 2]
                    boff = (rr * 96 + DTR + n) * TC + cl
                    coff = (rr * 96 + DTR + 16 + n) * TC + cl
                    S.dma("sp", lambda e, bh=bh, boff=boff: e.dma_start(
                        out=bh[:, :],
                        in_=bass.AP(xdbl_d, boff, [[0, 128], [1, TCH]])),
                        writes=[bb], key="brd")
                    S.dma("pool", lambda e, ch2=ch2, coff=coff: e.dma_start(
                        out=ch2[:, :],
                        in_=bass.AP(xdbl_d, coff, [[0, 128], [1, TCH]])),
                        writes=[cb2], key="crd")
                    for i in range(NDT):
                        pp = (n * NDT + i) % 2
                        dAh, dAb = dA[pp]
                        dBh, dBb = dBu[pp]
                        hhh, hhb = hh[pp]
                        ggh, ggb = gg[pp]
                        dlh, dlb = dl[i]
                        duh, dub = du[i]
                        anh, anb = an_h[i]
                        ch, cb_ = carry[i]
                        S.op("act", lambda e, dAh=dAh, dlh=dlh, anh=anh, n=n: e.activation(
                            out=dAh[:, :], in_=dlh[:, :], func=AF.Exp,
                            scale=anh[:, n:n + 1]), reads=[dlb, anb], writes=[dAb])
                        S.op("pool", lambda e, dBh=dBh, duh=duh, bh=bh: e.tensor_tensor(
                            out=dBh[:, :], in0=duh[:, :], in1=bh[:, :], op=ALU.mult),
                            reads=[dub, bb], writes=[dBb])
                        if "scan" in skip:
                            S.op("dve", lambda e, hhh=hhh, dBh=dBh: e.tensor_copy(
                                out=hhh[:, :], in_=dBh[:, :]),
                                reads=[dBb], writes=[hhb])
                        else:
                            S.op("dve", lambda e, hhh=hhh, dAh=dAh, dBh=dBh, ch=ch, n=n: e.tensor_tensor_scan(
                                out=hhh[:, :], data0=dAh[:, :], data1=dBh[:, :],
                                initial=ch[:, n:n + 1], op0=ALU.mult, op1=ALU.add),
                                reads=[dAb, dBb, cb_], writes=[hhb])
                        if "carry" not in skip:
                            S.op("dve", lambda e, hhh=hhh, ch=ch, n=n: e.tensor_copy(
                                out=ch[:, n:n + 1], in_=hhh[:, TCH - 1:TCH]),
                                reads=[hhb], writes=[cb_])
                        S.op("dve", lambda e, ggh=ggh, hhh=hhh, ch2=ch2: e.tensor_tensor(
                            out=ggh[:, :], in0=hhh[:, :], in1=ch2[:, :], op=ALU.mult),
                            reads=[hhb, cb2], writes=[ggb])
                        for s in range(NSUB):
                            pyh, pyb = py[i][s]
                            S.op("pe", lambda e, pyh=pyh, ggh=ggh, s=s, n=n: e.matmul(
                                pyh[:, :], ident_h[:, :],
                                ggh[:, 512 * s:512 * (s + 1)],
                                start=(n == 0), stop=(n == DS - 1)),
                                reads=[ggb, ident_b], writes=[pyb])
                for i in range(NDT) if "yevac" not in skip else []:
                    yh, yb = yo[i]
                    for s in range(NSUB):
                        pyh, pyb = py[i][s]
                        S.op("act", lambda e, yh=yh, pyh=pyh, s=s: e.activation(
                            out=yh[:, 512 * s:512 * (s + 1)], in_=pyh[:, :],
                            func=AF.Copy), reads=[pyb], writes=[yb])
                    S.dma("sp", lambda e, yh=yh, i=i, rr=rr, cl=cl: e.dma_start(
                        out=y_d.ap()[rr * DDI + 128 * i:rr * DDI + 128 * (i + 1),
                                     cl:cl + TCH],
                        in_=yh[:, :]), reads=[yb], key="yst")


def ln_rowmath_sb(S, pm_h, pm_b, psq_h, psq_b, negm_h, negm_b, rstd_h, rstd_b,
                  tmp_h, tmp_b, icol0, ocol0, w, nfeat):
    isl = slice(icol0, icol0 + w)
    osl = slice(ocol0, ocol0 + w)
    S.op("act", lambda e: e.activation(out=negm_h[0:1, osl], in_=pm_h[0:1, isl],
                                       func=AF.Copy, scale=-1.0 / nfeat),
         reads=[pm_b], writes=[negm_b])
    S.op("act", lambda e: e.activation(out=tmp_h[0:1, 0:w], in_=pm_h[0:1, isl],
                                       func=AF.Square, scale=1.0 / nfeat),
         reads=[pm_b], writes=[tmp_b])
    S.op("act", lambda e: e.activation(out=rstd_h[0:1, osl], in_=psq_h[0:1, isl],
                                       func=AF.Copy, scale=1.0 / nfeat),
         reads=[psq_b], writes=[rstd_b])
    S.op("dve", lambda e: e.tensor_tensor(out=rstd_h[0:1, osl], in0=rstd_h[0:1, osl],
                                          in1=tmp_h[0:1, 0:w], op=ALU.subtract),
         reads=[tmp_b], writes=[rstd_b])
    S.op("dve", lambda e: e.tensor_scalar(out=rstd_h[0:1, osl], in0=rstd_h[0:1, osl],
                                          scalar1=EPS, scalar2=None, op0=ALU.add),
         writes=[rstd_b])
    S.op("act", lambda e: e.activation(out=rstd_h[0:1, osl], in_=rstd_h[0:1, osl],
                                       func=AF.Ln), writes=[rstd_b])
    S.op("act", lambda e: e.activation(out=rstd_h[0:1, osl], in_=rstd_h[0:1, osl],
                                       func=AF.Exp, scale=-0.5), writes=[rstd_b])


def stage_l3a(nc, S, sems, E, nchunks=None):
    W = 512
    NCH = TC // W if nchunks is None else nchunks
    KTI = DI // 128
    KTM = D // 128
    y_dd = E["y_my"]
    u_dd = E["u_scr"]
    zs_dd = E["zs_scr"]
    x_dd = E["x_nohalo"]
    dsk_d = E["dskip"]
    wout_d = E["wout"]
    rtr_d = E["router"]
    ones_d = E["ones_in"]
    ident_d = E["ident_in"]
    x2_d = E["x2_scr"]
    xn3_d = E["xn3_scr"]
    coef_d = E["coef_scr"]

    with ExitStack() as ctx:
        ones_h, ones_b = sbuf(nc, ctx, "ones_sb", [128, 128], BF16)
        ident_h, ident_b = sbuf(nc, ctx, "ident_sb", [128, 128], BF16)
        dsk_h, dsk_b = sbuf(nc, ctx, "dsk", [128, KTI], F32)
        wout_h, wout_b = sbuf(nc, ctx, "wout_bf", [128, KTI * D], BF16)
        rtr_h, rtr_b = sbuf(nc, ctx, "rtr_bf", [128, KTM * NE], BF16)
        ysb_h, ysb_b = sbuf(nc, ctx, "ysb", [128, KTI * W], BF16)
        usb_h, usb_b = sbuf(nc, ctx, "usb", [128, KTI * W], BF16)
        zsb_h, zsb_b = sbuf(nc, ctx, "zsb", [128, KTI * W], BF16)
        xsb_h, xsb_b = sbuf(nc, ctx, "xsb", [128, KTM * W], F32)
        tg_h, tg_b = sbuf(nc, ctx, "tg", [128, W], F32)
        yg_h, yg_b = sbuf(nc, ctx, "yg", [128, KTI * W], BF16)
        mo_h, mo_b = sbuf(nc, ctx, "mo_bf", [128, KTM * W], BF16)
        sq_h, sq_b = sbuf(nc, ctx, "sq_bf", [128, KTM * W], BF16)
        x2sb_h, x2sb_b = sbuf(nc, ctx, "x2sb", [128, KTM * W], F32)
        x2bf_h, x2bf_b = sbuf(nc, ctx, "x2bf", [128, KTM * W], BF16)
        xn3_h, xn3_b = sbuf(nc, ctx, "xn3sb", [128, KTM * W], BF16)
        nm2_h, nm2_b = sbuf(nc, ctx, "nm2", [1, TC], BF16)
        rs2_h, rs2_b = sbuf(nc, ctx, "rs2", [1, TC], BF16)
        nm3_h, nm3_b = sbuf(nc, ctx, "nm3", [1, TC], BF16)
        rs3_h, rs3_b = sbuf(nc, ctx, "rs3", [1, TC], BF16)
        nmf_h, nmf_b = sbuf(nc, ctx, "nmf", [1, W], F32)
        rsf_h, rsf_b = sbuf(nc, ctx, "rsf", [1, W], F32)
        rtf_h, rtf_b = sbuf(nc, ctx, "rtf", [1, W], F32)
        lg_h, lg_b = sbuf(nc, ctx, "lg_bf", [8, W], BF16)
        lgT_h, lgT_b = sbuf(nc, ctx, "lgT", [128, 8], F32)
        r1_h, r1_b = sbuf(nc, ctx, "r1", [128, 1], F32)
        r2_h, r2_b = sbuf(nc, ctx, "r2", [128, 1], F32)
        ee_h, ee_b = sbuf(nc, ctx, "ee", [128, 8], F32)
        pp_h, pp_b = sbuf(nc, ctx, "ppb", [128, 8], F32)
        gx_h, gx_b = sbuf(nc, ctx, "gx", [128, 8], F32)
        cf_h, cf_b = sbuf(nc, ctx, "cfbf", [128, 8], BF16)
        cT_h, cT_b = sbuf(nc, ctx, "coefT", [8, W], BF16)
        pwo = [psum(nc, ctx, f"pwo{i}", [128, W]) for i in range(2)]
        pm_h, pm_b = psum(nc, ctx, "pm", [1, W])
        pq_h, pq_b = psum(nc, ctx, "pq", [1, W])
        pnm_h, pnm_b = psum(nc, ctx, "pnm", [128, W])
        prs_h, prs_b = psum(nc, ctx, "prs", [128, W])
        plg_h, plg_b = psum(nc, ctx, "plg", [8, W])
        ptb_h, ptb_b = psum(nc, ctx, "ptb", [128, 128], BF16)

        S.dma("sp", lambda e: e.dma_start(out=ones_h[:, :], in_=ones_d.ap()),
              writes=[ones_b], key="wload")
        S.dma("sp", lambda e: e.dma_start(out=ident_h[:, :], in_=ident_d.ap()),
              writes=[ident_b], key="wload")
        S.dma("sp", lambda e: e.dma_start(out=dsk_h[:, :], in_=dsk_d.ap()),
              writes=[dsk_b], key="wload")
        S.dma("pool", lambda e: e.dma_start(
            out=wout_h[:, :].rearrange("p (a m) -> p a m", a=KTI),
            in_=wout_d.ap().rearrange("(a p) m -> p a m", p=128)),
            writes=[wout_b], key="wloadp")
        S.dma("pool", lambda e: e.dma_start(
            out=rtr_h[:, :].rearrange("p (a m) -> p a m", a=KTM),
            in_=rtr_d.ap().rearrange("(a p) m -> p a m", p=128)),
            writes=[rtr_b], key="wloadp")

        for chn in range(NCH):
            c0 = chn * W
            for nm_, hh_, bb_, dd_, kt_, ky_, xo_ in (
                    ("y", ysb_h, ysb_b, y_dd, KTI, "yld", 0),
                    ("u", usb_h, usb_b, u_dd, KTI, "uld", 0),
                    ("z", zsb_h, zsb_b, zs_dd, KTI, "zld", 0),
                    ("x", xsb_h, xsb_b, x_dd, KTM, "xld", 4)):
                S.dma("sp", lambda e, hh_=hh_, dd_=dd_, kt_=kt_, c0=c0, xo_=xo_: e.dma_start(
                    out=hh_[:, 0:kt_ * W].rearrange("p (a t) -> p a t", a=kt_),
                    in_=dd_.ap()[:, xo_ + c0:xo_ + c0 + W].rearrange(
                        "(a p) t -> p a t", p=128)),
                    writes=[bb_], key=ky_)
            for k in range(KTI):
                sl = slice(k * W, (k + 1) * W)
                S.op("dve", lambda e, k=k, sl=sl: e.scalar_tensor_tensor(
                    out=tg_h[:, :], in0=usb_h[:, sl], scalar=dsk_h[:, k:k + 1],
                    in1=ysb_h[:, sl], op0=ALU.mult, op1=ALU.add),
                    reads=[usb_b, dsk_b, ysb_b], writes=[tg_b])
                S.op("dve", lambda e, k=k, sl=sl: e.tensor_tensor(
                    out=yg_h[:, sl], in0=tg_h[:, :], in1=zsb_h[:, sl],
                    op=ALU.mult), reads=[tg_b, zsb_b], writes=[yg_b])
            for mi in range(KTM):
                ph, pb = pwo[mi % 2]
                for k in range(KTI):
                    S.op("pe", lambda e, mi=mi, k=k, ph=ph: e.matmul(
                        ph[:, :], wout_h[:, k * D + mi * 128:k * D + mi * 128 + 128],
                        yg_h[:, k * W:(k + 1) * W],
                        start=(k == 0), stop=(k == KTI - 1)),
                        reads=[yg_b, wout_b], writes=[pb])
                msl = slice(mi * W, (mi + 1) * W)
                S.op("act", lambda e, ph=ph, msl=msl: e.activation(
                    out=mo_h[:, msl], in_=ph[:, :], func=AF.Copy),
                    reads=[pb], writes=[mo_b])
                S.op("act", lambda e, ph=ph, msl=msl: e.activation(
                    out=sq_h[:, msl], in_=ph[:, :], func=AF.Square),
                    reads=[pb], writes=[sq_b])
            for mi in range(KTM):
                msl = slice(mi * W, (mi + 1) * W)
                S.op("pe", lambda e, mi=mi, msl=msl: e.matmul(
                    pm_h[0:1, :], ones_h[:, 0:1], mo_h[:, msl],
                    start=(mi == 0), stop=(mi == KTM - 1)),
                    reads=[mo_b, ones_b], writes=[pm_b])
                S.op("pe", lambda e, mi=mi, msl=msl: e.matmul(
                    pq_h[0:1, :], ones_h[:, 0:1], sq_h[:, msl],
                    start=(mi == 0), stop=(mi == KTM - 1)),
                    reads=[sq_b, ones_b], writes=[pq_b])
            ln_rowmath(S, pm_h, pm_b, pq_h, pq_b, nmf_h, nmf_b, rsf_h, rsf_b,
                       rtf_h, rtf_b, 0, W, D)
            S.op("act", lambda e, c0=c0: e.activation(
                out=nm2_h[0:1, c0:c0 + W], in_=nmf_h[0:1, 0:W], func=AF.Copy),
                reads=[nmf_b], writes=[nm2_b])
            S.op("act", lambda e, c0=c0: e.activation(
                out=rs2_h[0:1, c0:c0 + W], in_=rsf_h[0:1, 0:W], func=AF.Copy),
                reads=[rsf_b], writes=[rs2_b])
            S.op("pe", lambda e, c0=c0: e.matmul(
                pnm_h[:, :], ones_h[0:1, :], nm2_h[0:1, c0:c0 + W],
                start=True, stop=True), reads=[nm2_b, ones_b], writes=[pnm_b])
            S.op("pe", lambda e, c0=c0: e.matmul(
                prs_h[:, :], ones_h[0:1, :], rs2_h[0:1, c0:c0 + W],
                start=True, stop=True), reads=[rs2_b, ones_b], writes=[prs_b])
            for mi in range(KTM):
                msl = slice(mi * W, (mi + 1) * W)
                S.op("dve", lambda e, msl=msl: e.tensor_tensor(
                    out=tg_h[:, :], in0=mo_h[:, msl], in1=pnm_h[:, :], op=ALU.add),
                    reads=[mo_b, pnm_b], writes=[tg_b])
                S.op("dve", lambda e, msl=msl: e.tensor_tensor(
                    out=tg_h[:, :], in0=tg_h[:, :], in1=prs_h[:, :], op=ALU.mult),
                    reads=[prs_b], writes=[tg_b])
                S.op("dve", lambda e, msl=msl: e.tensor_tensor(
                    out=x2sb_h[:, msl], in0=tg_h[:, :], in1=xsb_h[:, msl],
                    op=ALU.add), reads=[tg_b, xsb_b], writes=[x2sb_b])
                S.op("act", lambda e, msl=msl: e.activation(
                    out=x2bf_h[:, msl], in_=x2sb_h[:, msl], func=AF.Copy),
                    reads=[x2sb_b], writes=[x2bf_b])
                S.op("act", lambda e, msl=msl: e.activation(
                    out=sq_h[:, msl], in_=x2sb_h[:, msl], func=AF.Square),
                    reads=[x2sb_b], writes=[sq_b])
            S.dma("sp", lambda e, c0=c0: e.dma_start(
                out=x2_d.ap()[:, c0:c0 + W].rearrange("(a p) t -> p a t", p=128),
                in_=x2sb_h[:, :].rearrange("p (a t) -> p a t", a=KTM)),
                reads=[x2sb_b], key="ost")
            for mi in range(KTM):
                msl = slice(mi * W, (mi + 1) * W)
                S.op("pe", lambda e, msl=msl, mi=mi: e.matmul(
                    pm_h[0:1, :], ones_h[:, 0:1], x2bf_h[:, msl],
                    start=(mi == 0), stop=(mi == KTM - 1)),
                    reads=[x2bf_b, ones_b], writes=[pm_b])
                S.op("pe", lambda e, msl=msl, mi=mi: e.matmul(
                    pq_h[0:1, :], ones_h[:, 0:1], sq_h[:, msl],
                    start=(mi == 0), stop=(mi == KTM - 1)),
                    reads=[sq_b, ones_b], writes=[pq_b])
            ln_rowmath(S, pm_h, pm_b, pq_h, pq_b, nmf_h, nmf_b, rsf_h, rsf_b,
                       rtf_h, rtf_b, 0, W, D)
            S.op("act", lambda e, c0=c0: e.activation(
                out=nm3_h[0:1, c0:c0 + W], in_=nmf_h[0:1, 0:W], func=AF.Copy),
                reads=[nmf_b], writes=[nm3_b])
            S.op("act", lambda e, c0=c0: e.activation(
                out=rs3_h[0:1, c0:c0 + W], in_=rsf_h[0:1, 0:W], func=AF.Copy),
                reads=[rsf_b], writes=[rs3_b])
            S.op("pe", lambda e, c0=c0: e.matmul(
                pnm_h[:, :], ones_h[0:1, :], nm3_h[0:1, c0:c0 + W],
                start=True, stop=True), reads=[nm3_b, ones_b], writes=[pnm_b])
            S.op("pe", lambda e, c0=c0: e.matmul(
                prs_h[:, :], ones_h[0:1, :], rs3_h[0:1, c0:c0 + W],
                start=True, stop=True), reads=[rs3_b, ones_b], writes=[prs_b])
            for mi in range(KTM):
                msl = slice(mi * W, (mi + 1) * W)
                S.op("dve", lambda e, msl=msl: e.tensor_tensor(
                    out=tg_h[:, :], in0=x2sb_h[:, msl], in1=pnm_h[:, :],
                    op=ALU.add), reads=[x2sb_b, pnm_b], writes=[tg_b])
                S.op("dve", lambda e, msl=msl: e.tensor_tensor(
                    out=xn3_h[:, msl], in0=tg_h[:, :], in1=prs_h[:, :],
                    op=ALU.mult), reads=[tg_b, prs_b], writes=[xn3_b])
            S.dma("sp", lambda e, c0=c0: e.dma_start(
                out=xn3_d.ap()[:, c0:c0 + W].rearrange("(a p) t -> p a t", p=128),
                in_=xn3_h[:, :].rearrange("p (a t) -> p a t", a=KTM)),
                reads=[xn3_b], key="ost")
            for k in range(KTM):
                S.op("pe", lambda e, k=k: e.matmul(
                    plg_h[0:8, :], rtr_h[:, k * NE:(k + 1) * NE],
                    xn3_h[:, k * W:(k + 1) * W],
                    start=(k == 0), stop=(k == KTM - 1)),
                    reads=[xn3_b, rtr_b], writes=[plg_b])
            S.op("act", lambda e: e.activation(
                out=lg_h[0:8, :], in_=plg_h[0:8, :], func=AF.Copy),
                reads=[plg_b], writes=[lg_b])
            for j in range(4):
                S.op("pe", lambda e, j=j: e.transpose(
                    ptb_h[:, 0:8], lg_h[0:8, j * 128:(j + 1) * 128],
                    ident_h[0:8, 0:8]), reads=[lg_b, ident_b], writes=[ptb_b])
                S.op("act", lambda e: e.activation(
                    out=lgT_h[:, :], in_=ptb_h[:, 0:8], func=AF.Copy),
                    reads=[ptb_b], writes=[lgT_b])
                S.op("dve", lambda e: e.reduce_max(
                    out=r1_h[:, :], in_=lgT_h[:, :], axis=mybir.AxisListType.X),
                    reads=[lgT_b], writes=[r1_b])
                S.op("dve", lambda e: e.tensor_scalar(
                    out=r1_h[:, :], in0=r1_h[:, :], scalar1=-1.0, scalar2=None,
                    op0=ALU.mult), writes=[r1_b])
                S.op("act", lambda e: e.activation(
                    out=ee_h[:, :], in_=lgT_h[:, :], func=AF.Exp,
                    bias=r1_h[:, 0:1]), reads=[lgT_b, r1_b], writes=[ee_b])
                S.op("dve", lambda e: e.reduce_sum(
                    out=r2_h[:, :], in_=ee_h[:, :], axis=mybir.AxisListType.X),
                    reads=[ee_b], writes=[r2_b])
                S.op("dve", lambda e: e.reciprocal(out=r2_h[:, :], in_=r2_h[:, :]),
                     writes=[r2_b])
                S.op("dve", lambda e: e.tensor_scalar(
                    out=pp_h[:, :], in0=ee_h[:, :], scalar1=r2_h[:, 0:1],
                    scalar2=None, op0=ALU.mult),
                    reads=[ee_b, r2_b], writes=[pp_b])
                S.op("dve", lambda e: e.reduce_max(
                    out=r1_h[:, :], in_=pp_h[:, :], axis=mybir.AxisListType.X),
                    reads=[pp_b], writes=[r1_b])
                S.op("dve", lambda e: e.tensor_scalar(
                    out=gx_h[:, :], in0=pp_h[:, :], scalar1=r1_h[:, 0:1],
                    scalar2=None, op0=ALU.is_ge),
                    reads=[pp_b, r1_b], writes=[gx_b])
                S.op("dve", lambda e: e.tensor_tensor(
                    out=gx_h[:, :], in0=pp_h[:, :], in1=gx_h[:, :], op=ALU.mult),
                    reads=[pp_b], writes=[gx_b])
                S.op("dve", lambda e: e.tensor_tensor(
                    out=gx_h[:, :], in0=pp_h[:, :], in1=gx_h[:, :],
                    op=ALU.subtract), reads=[pp_b], writes=[gx_b])
                S.op("dve", lambda e: e.reduce_max(
                    out=r2_h[:, :], in_=gx_h[:, :], axis=mybir.AxisListType.X),
                    reads=[gx_b], writes=[r2_b])
                S.op("dve", lambda e: e.tensor_scalar(
                    out=gx_h[:, :], in0=pp_h[:, :], scalar1=r2_h[:, 0:1],
                    scalar2=None, op0=ALU.is_ge),
                    reads=[pp_b, r2_b], writes=[gx_b])
                S.op("dve", lambda e: e.tensor_tensor(
                    out=pp_h[:, :], in0=pp_h[:, :], in1=gx_h[:, :], op=ALU.mult),
                    reads=[gx_b], writes=[pp_b])
                S.op("act", lambda e: e.activation(
                    out=cf_h[:, :], in_=pp_h[:, :], func=AF.Copy),
                    reads=[pp_b], writes=[cf_b])
                S.op("pe", lambda e: e.transpose(
                    ptb_h[0:8, :], cf_h[:, 0:8], ident_h[:, :]),
                    reads=[cf_b, ident_b], writes=[ptb_b])
                S.op("act", lambda e, j=j: e.activation(
                    out=cT_h[0:8, j * 128:(j + 1) * 128], in_=ptb_h[0:8, :],
                    func=AF.Copy), reads=[ptb_b], writes=[cT_b])
            S.dma("sp", lambda e, c0=c0: e.dma_start(
                out=bass.AP(coef_d, c0, [[TC, 8], [1, W]]), in_=cT_h[0:8, :]),
                reads=[cT_b], key="ost")


def stage_l3b(nc, S, sems, E, nchunks=None):
    W = 512
    NCH = TC // W if nchunks is None else nchunks
    KTM = D // 128
    KTF = DFF // 128
    xn3_d = E["xn3_scr"]
    coef_d = E["coef_scr"]
    x2_d = E["x2_scr"]
    weg_d = E["weg"]
    weu_d = E["weu"]
    wed_d = E["wed"]
    wsg_d = E["wsg"]
    wsu_d = E["wsu"]
    wsd_d = E["wsd"]
    ones_d = E["ones_in"]
    out_d = E["out"]
    ffn_d = E["ffn_scr"]

    with ExitStack() as ctx:
        ones_h, ones_b = sbuf(nc, ctx, "ones_sb", [128, 128], BF16)
        wsg_h, wsg_b = sbuf(nc, ctx, "wsg_bf", [128, KTM * D], BF16)
        wsu_h, wsu_b = sbuf(nc, ctx, "wsu_bf", [128, KTM * D], BF16)
        wsd_h, wsd_b = sbuf(nc, ctx, "wsd_bf", [128, KTM * D], BF16)
        xn3_h, xn3_b = sbuf(nc, ctx, "xn3sb", [128, KTM * W], BF16)
        ffn_h, ffn_b = sbuf(nc, ctx, "ffnb", [128, KTM * W], BF16)
        sq_h, sq_b = sbuf(nc, ctx, "sqb", [128, KTM * W], BF16)
        m4_h, m4_b = sbuf(nc, ctx, "m4row", [1, TC], BF16)
        q4_h, q4_b = sbuf(nc, ctx, "q4row", [1, TC], BF16)
        nm4_h, nm4_b = sbuf(nc, ctx, "nm4", [1, TC], BF16)
        rs4_h, rs4_b = sbuf(nc, ctx, "rs4", [1, TC], BF16)
        rtf_h, rtf_b = sbuf(nc, ctx, "rtf", [1, W], F32)
        tg_h, tg_b = sbuf(nc, ctx, "tg", [128, W], F32)
        ctxB = ExitStack()
        wgp = [sbuf(nc, ctxB, f"weg{i}", [128, KTM * DFF], BF16) for i in range(2)]
        wup = [sbuf(nc, ctxB, f"weu{i}", [128, KTM * DFF], BF16) for i in range(2)]
        wdp = [sbuf(nc, ctxB, f"wed{i}", [128, KTF * D], BF16) for i in range(2)]
        coef_h, coef_b = sbuf(nc, ctxB, "coefsb", [1, NE * W], BF16)
        gsb_h, gsb_b = sbuf(nc, ctxB, "gsb", [128, KTF * W], BF16)
        usb_h, usb_b = sbuf(nc, ctxB, "uub", [128, KTF * W], BF16)
        cme_h, cme_b = sbuf(nc, ctxB, "cme", [128, KTF * W], BF16)
        sm_h, sm_b = sbuf(nc, ctxB, "smb", [128, KTM * W], BF16)
        rsb_h, rsb_b = sbuf(nc, ctxB, "routed", [128, KTM * W], F32)
        pgp = [psum(nc, ctxB, f"pg{i}", [128, W]) for i in range(2)]
        pup = [psum(nc, ctxB, f"pu{i}", [128, W]) for i in range(2)]
        pc_h, pc_b = psum(nc, ctxB, "pc", [128, W])
        pdp = [psum(nc, ctxB, f"pd{i}", [128, W]) for i in range(2)]
        pm_h, pm_b = pgp[0][0], pgp[0][1]
        pq_h, pq_b = pup[0][0], pup[0][1]

        S.dma("pool", lambda e: e.dma_start(
            out=wsg_h[:, :].rearrange("p (a m) -> p a m", a=KTM),
            in_=wsg_d.ap().rearrange("(a p) m -> p a m", p=128)),
            writes=[wsg_b], key="wloadp")
        S.dma("pool", lambda e: e.dma_start(
            out=wsu_h[:, :].rearrange("p (a m) -> p a m", a=KTM),
            in_=wsu_d.ap().rearrange("(a p) m -> p a m", p=128)),
            writes=[wsu_b], key="wloadp")
        S.dma("pool", lambda e: e.dma_start(
            out=wsd_h[:, :].rearrange("p (a m) -> p a m", a=KTM),
            in_=wsd_d.ap().rearrange("(a p) m -> p a m", p=128)),
            writes=[wsd_b], key="wloadp")
        S.dma("sp", lambda e: e.dma_start(out=ones_h[:, :], in_=ones_d.ap()),
              writes=[ones_b], key="wload")

        for chn in range(NCH):
            c0 = chn * W
            S.dma("sp", lambda e, c0=c0: e.dma_start(
                out=xn3_h[:, :].rearrange("p (a t) -> p a t", a=KTM),
                in_=xn3_d.ap()[:, c0:c0 + W].rearrange("(a p) t -> p a t", p=128)),
                writes=[xn3_b], key="xnld")
            S.dma("sp", lambda e, c0=c0: e.dma_start(
                out=coef_h[:, :].rearrange("p (a t) -> p a t", a=NE),
                in_=bass.AP(coef_d, c0, [[1, 1], [TC, NE], [1, W]])),
                writes=[coef_b], key="cfld")
            for ei in range(NE):
                wg_h, wg_b = wgp[ei % 2]
                wu_h, wu_b = wup[ei % 2]
                wd_h, wd_b = wdp[ei % 2]
                ky = "we" + str(ei % 2)
                S.dma("pool", lambda e, ei=ei, wg_h=wg_h: e.dma_start(
                    out=wg_h[:, :].rearrange("p (a m) -> p a m", a=KTM),
                    in_=weg_d.ap()[ei * D:(ei + 1) * D, :].rearrange(
                        "(a p) m -> p a m", p=128)), writes=[wg_b], key=ky)
                S.dma("pool", lambda e, ei=ei, wu_h=wu_h: e.dma_start(
                    out=wu_h[:, :].rearrange("p (a m) -> p a m", a=KTM),
                    in_=weu_d.ap()[ei * D:(ei + 1) * D, :].rearrange(
                        "(a p) m -> p a m", p=128)), writes=[wu_b], key=ky)
                S.dma("pool", lambda e, ei=ei, wd_h=wd_h: e.dma_start(
                    out=wd_h[:, :].rearrange("p (a m) -> p a m", a=KTF),
                    in_=wed_d.ap()[ei * DFF:(ei + 1) * DFF, :].rearrange(
                        "(a p) m -> p a m", p=128)), writes=[wd_b], key=ky)
                S.op("pe", lambda e, ei=ei: e.matmul(
                    pc_h[:, :], ones_h[0:1, :],
                    coef_h[0:1, ei * W:(ei + 1) * W],
                    start=True, stop=True), reads=[coef_b, ones_b], writes=[pc_b])
                for mi2 in range(KTF):
                    fsl = slice(mi2 * W, (mi2 + 1) * W)
                    pg_h, pg_b = pgp[mi2 % 2]
                    pu_h, pu_b = pup[mi2 % 2]
                    for k in range(KTM):
                        S.op("pe", lambda e, mi2=mi2, k=k, wg_h=wg_h, pg_h=pg_h: e.matmul(
                            pg_h[:, :], wg_h[:, k * DFF + mi2 * 128:k * DFF + mi2 * 128 + 128],
                            xn3_h[:, k * W:(k + 1) * W],
                            start=(k == 0), stop=(k == KTM - 1)),
                            reads=[xn3_b, wg_b], writes=[pg_b])
                    S.op("act", lambda e, fsl=fsl, pg_h=pg_h: e.activation(
                        out=gsb_h[:, fsl], in_=pg_h[:, :], func=AF.Silu),
                        reads=[pg_b], writes=[gsb_b])
                    for k in range(KTM):
                        S.op("pe", lambda e, mi2=mi2, k=k, wu_h=wu_h, pu_h=pu_h: e.matmul(
                            pu_h[:, :], wu_h[:, k * DFF + mi2 * 128:k * DFF + mi2 * 128 + 128],
                            xn3_h[:, k * W:(k + 1) * W],
                            start=(k == 0), stop=(k == KTM - 1)),
                            reads=[xn3_b, wu_b], writes=[pu_b])
                    S.op("act", lambda e, fsl=fsl, pu_h=pu_h: e.activation(
                        out=usb_h[:, fsl], in_=pu_h[:, :], func=AF.Copy),
                        reads=[pu_b], writes=[usb_b])
                    S.op("dve", lambda e, fsl=fsl: e.tensor_tensor(
                        out=gsb_h[:, fsl], in0=gsb_h[:, fsl], in1=usb_h[:, fsl],
                        op=ALU.mult), reads=[usb_b], writes=[gsb_b])
                    S.op("dve", lambda e, fsl=fsl: e.tensor_tensor(
                        out=cme_h[:, fsl], in0=gsb_h[:, fsl], in1=pc_h[:, :],
                        op=ALU.mult), reads=[gsb_b, pc_b], writes=[cme_b])
                for mo in range(KTM):
                    pd_h, pd_b = pdp[mo % 2]
                    for k in range(KTF):
                        S.op("pe", lambda e, mo=mo, k=k, wd_h=wd_h, pd_h=pd_h: e.matmul(
                            pd_h[:, :], wd_h[:, k * D + mo * 128:k * D + mo * 128 + 128],
                            cme_h[:, k * W:(k + 1) * W],
                            start=(k == 0), stop=(k == KTF - 1)),
                            reads=[cme_b, wd_b], writes=[pd_b])
                    msl = slice(mo * W, (mo + 1) * W)
                    if ei == 0:
                        S.op("act", lambda e, msl=msl, pd_h=pd_h: e.activation(
                            out=rsb_h[:, msl], in_=pd_h[:, :], func=AF.Copy),
                            reads=[pd_b], writes=[rsb_b])
                    else:
                        S.op("dve", lambda e, msl=msl, pd_h=pd_h: e.tensor_tensor(
                            out=rsb_h[:, msl], in0=rsb_h[:, msl], in1=pd_h[:, :],
                            op=ALU.add), reads=[pd_b], writes=[rsb_b])
            for mi in range(KTM):
                msl = slice(mi * W, (mi + 1) * W)
                pg_h, pg_b = pgp[mi % 2]
                pu_h, pu_b = pup[mi % 2]
                for k in range(KTM):
                    S.op("pe", lambda e, mi=mi, k=k, pg_h=pg_h: e.matmul(
                        pg_h[:, :], wsg_h[:, k * D + mi * 128:k * D + mi * 128 + 128],
                        xn3_h[:, k * W:(k + 1) * W],
                        start=(k == 0), stop=(k == KTM - 1)),
                        reads=[xn3_b, wsg_b], writes=[pg_b])
                S.op("act", lambda e, msl=msl, pg_h=pg_h: e.activation(
                    out=sq_h[:, msl], in_=pg_h[:, :], func=AF.Silu),
                    reads=[pg_b], writes=[sq_b])
                for k in range(KTM):
                    S.op("pe", lambda e, mi=mi, k=k, pu_h=pu_h: e.matmul(
                        pu_h[:, :], wsu_h[:, k * D + mi * 128:k * D + mi * 128 + 128],
                        xn3_h[:, k * W:(k + 1) * W],
                        start=(k == 0), stop=(k == KTM - 1)),
                        reads=[xn3_b, wsu_b], writes=[pu_b])
                S.op("act", lambda e, msl=msl, pu_h=pu_h: e.activation(
                    out=ffn_h[:, msl], in_=pu_h[:, :], func=AF.Copy),
                    reads=[pu_b], writes=[ffn_b])
                S.op("dve", lambda e, msl=msl: e.tensor_tensor(
                    out=sm_h[:, msl], in0=sq_h[:, msl], in1=ffn_h[:, msl],
                    op=ALU.mult), reads=[sq_b, ffn_b], writes=[sm_b])
            for mo in range(KTM):
                pd_h, pd_b = pdp[mo % 2]
                msl = slice(mo * W, (mo + 1) * W)
                for k in range(KTM):
                    S.op("pe", lambda e, mo=mo, k=k, pd_h=pd_h: e.matmul(
                        pd_h[:, :], wsd_h[:, k * D + mo * 128:k * D + mo * 128 + 128],
                        sm_h[:, k * W:(k + 1) * W],
                        start=(k == 0), stop=(k == KTM - 1)),
                        reads=[sm_b, wsd_b], writes=[pd_b])
                S.op("dve", lambda e, msl=msl, pd_h=pd_h: e.tensor_tensor(
                    out=rsb_h[:, msl], in0=rsb_h[:, msl], in1=pd_h[:, :],
                    op=ALU.add), reads=[pd_b], writes=[rsb_b])
                S.op("act", lambda e, msl=msl: e.activation(
                    out=ffn_h[:, msl], in_=rsb_h[:, msl], func=AF.Copy),
                    reads=[rsb_b], writes=[ffn_b])
                S.op("act", lambda e, msl=msl: e.activation(
                    out=sq_h[:, msl], in_=rsb_h[:, msl], func=AF.Square),
                    reads=[rsb_b], writes=[sq_b])
            for mo in range(KTM):
                msl = slice(mo * W, (mo + 1) * W)
                S.op("pe", lambda e, msl=msl, mo=mo: e.matmul(
                    pm_h[0:1, :], ones_h[:, 0:1], ffn_h[:, msl],
                    start=(mo == 0), stop=(mo == KTM - 1)),
                    reads=[ffn_b, ones_b], writes=[pm_b])
                S.op("pe", lambda e, msl=msl, mo=mo: e.matmul(
                    pq_h[0:1, :], ones_h[:, 0:1], sq_h[:, msl],
                    start=(mo == 0), stop=(mo == KTM - 1)),
                    reads=[sq_b, ones_b], writes=[pq_b])
            S.op("act", lambda e, c0=c0: e.activation(
                out=m4_h[0:1, c0:c0 + W], in_=pm_h[0:1, :], func=AF.Copy),
                reads=[pm_b], writes=[m4_b])
            S.op("act", lambda e, c0=c0: e.activation(
                out=q4_h[0:1, c0:c0 + W], in_=pq_h[0:1, :], func=AF.Copy),
                reads=[pq_b], writes=[q4_b])
            S.dma("sp", lambda e, c0=c0: e.dma_start(
                out=ffn_d.ap()[:, c0:c0 + W].rearrange("(a p) t -> p a t", p=128),
                in_=ffn_h[:, :].rearrange("p (a t) -> p a t", a=KTM)),
                reads=[ffn_b], key="ffst")

        S.barrier()
        ctxB.close()
        x2sb_h, x2sb_b = sbuf(nc, ctx, "x2sb", [128, KTM * W], F32)
        pnm_h, pnm_b = psum(nc, ctx, "pnm", [128, W])
        prs_h, prs_b = psum(nc, ctx, "prs", [128, W])
        for chn in range(NCH):
            c0 = chn * W
            ln_rowmath_sb(S, m4_h, m4_b, q4_h, q4_b, nm4_h, nm4_b, rs4_h, rs4_b,
                          rtf_h, rtf_b, c0, c0, W, D)
        for chn in range(NCH):
            c0 = chn * W
            S.dma("sp", lambda e, c0=c0: e.dma_start(
                out=x2sb_h[:, :].rearrange("p (a t) -> p a t", a=KTM),
                in_=x2_d.ap()[:, c0:c0 + W].rearrange("(a p) t -> p a t", p=128)),
                writes=[x2sb_b], key="x2ld")
            S.dma("sp", lambda e, c0=c0: e.dma_start(
                out=ffn_h[:, :].rearrange("p (a t) -> p a t", a=KTM),
                in_=ffn_d.ap()[:, c0:c0 + W].rearrange("(a p) t -> p a t", p=128)),
                writes=[ffn_b], key="ffld")
            S.op("pe", lambda e, c0=c0: e.matmul(
                pnm_h[:, :], ones_h[0:1, :], nm4_h[0:1, c0:c0 + W],
                start=True, stop=True), reads=[nm4_b, ones_b], writes=[pnm_b])
            S.op("pe", lambda e, c0=c0: e.matmul(
                prs_h[:, :], ones_h[0:1, :], rs4_h[0:1, c0:c0 + W],
                start=True, stop=True), reads=[rs4_b, ones_b], writes=[prs_b])
            for mi in range(KTM):
                msl = slice(mi * W, (mi + 1) * W)
                S.op("dve", lambda e, msl=msl: e.tensor_tensor(
                    out=tg_h[:, :], in0=ffn_h[:, msl], in1=pnm_h[:, :],
                    op=ALU.add), reads=[ffn_b, pnm_b], writes=[tg_b])
                S.op("dve", lambda e, msl=msl: e.tensor_tensor(
                    out=tg_h[:, :], in0=tg_h[:, :], in1=prs_h[:, :], op=ALU.mult),
                    reads=[prs_b], writes=[tg_b])
                S.op("dve", lambda e, msl=msl: e.tensor_tensor(
                    out=x2sb_h[:, msl], in0=tg_h[:, :], in1=x2sb_h[:, msl],
                    op=ALU.add), reads=[tg_b], writes=[x2sb_b])
            S.dma("sp", lambda e, c0=c0: e.dma_start(
                out=out_d.ap()[:, c0:c0 + W].rearrange("(a p) t -> p a t", p=128),
                in_=x2sb_h[:, :].rearrange("p (a t) -> p a t", a=KTM)),
                reads=[x2sb_b], key="ost")




def build_l3():
    nc = bass.Bass("TRN2", target_bir_lowering=False)
    E = {}
    E["y_my"] = nc.dram_tensor("y", [DI, TC], BF16, kind="ExternalInput")
    E["u_scr"] = nc.dram_tensor("u", [DI, TC], BF16, kind="ExternalInput")
    E["zs_scr"] = nc.dram_tensor("zs", [DI, TC], BF16, kind="ExternalInput")
    E["x_nohalo"] = nc.dram_tensor("x", [D, TC + 4], F32, kind="ExternalInput")
    E["dskip"] = nc.dram_tensor("dskip", [128, DI // 128], F32, kind="ExternalInput")
    E["wout"] = nc.dram_tensor("wout", [DI, D], F32, kind="ExternalInput")
    E["router"] = nc.dram_tensor("router", [D, NE], F32, kind="ExternalInput")
    E["weg"] = nc.dram_tensor("weg", [NE * D, DFF], F32, kind="ExternalInput")
    E["weu"] = nc.dram_tensor("weu", [NE * D, DFF], F32, kind="ExternalInput")
    E["wed"] = nc.dram_tensor("wed", [NE * DFF, D], F32, kind="ExternalInput")
    E["wsg"] = nc.dram_tensor("wsg", [D, D], F32, kind="ExternalInput")
    E["wsu"] = nc.dram_tensor("wsu", [D, D], F32, kind="ExternalInput")
    E["wsd"] = nc.dram_tensor("wsd", [D, D], F32, kind="ExternalInput")
    E["ones_in"] = nc.dram_tensor("ones", [128, 128], BF16, kind="ExternalInput")
    E["ident_in"] = nc.dram_tensor("ident", [128, 128], BF16, kind="ExternalInput")
    E["out"] = nc.dram_tensor("out", [D, TC], F32, kind="ExternalOutput")
    E["x2_scr"] = nc.dram_tensor("x2_scr", [D, TC], F32, kind="Internal")
    E["xn3_scr"] = nc.dram_tensor("xn3_scr", [D, TC], BF16, kind="Internal")
    E["coef_scr"] = nc.dram_tensor("coef_scr", [1, NE * TC], BF16, kind="Internal")
    E["ffn_scr"] = nc.dram_tensor("ffn_scr", [D, TC], BF16, kind="Internal")
    with ExitStack() as ctx:
        S = Sched(nc)
        _PREFIX[0] = "c_"
        stage_l3a(nc, S, None, E)
        S.barrier()
        _PREFIX[0] = "d_"
        stage_l3b(nc, S, None, E)
        _PREFIX[0] = ""
        sems = S.create_sems(nc, ctx)
        block = ctx.enter_context(nc.Block())
        S.emit(block, sems)
    return nc


def build_l1_only():
    nc = bass.Bass("TRN2", target_bir_lowering=False)
    E = {}
    E["x"] = nc.dram_tensor("x", [D, TC + 4], F32, kind="ExternalInput")
    E["w_in"] = nc.dram_tensor("w_in", [D, 2 * DI], F32, kind="ExternalInput")
    E["w_xp"] = nc.dram_tensor("w_xp", [DI, 96], F32, kind="ExternalInput")
    E["conv_w"] = nc.dram_tensor("conv_w", [128, (DI // 128) * DCONV], F32, kind="ExternalInput")
    E["conv_b"] = nc.dram_tensor("conv_b", [128, DI // 128], F32, kind="ExternalInput")
    E["ones_in"] = nc.dram_tensor("ones", [128, 128], BF16, kind="ExternalInput")
    E["u_scr"] = nc.dram_tensor("u", [DI, TC], BF16, kind="ExternalOutput")
    E["zs_scr"] = nc.dram_tensor("zs", [DI, TC], BF16, kind="ExternalOutput")
    E["xdbl_scr"] = nc.dram_tensor("xdbl", [96, TC], F32, kind="ExternalOutput")
    with ExitStack() as ctx:
        S = Sched(nc)
        _PREFIX[0] = "a_"
        stage_l1(nc, S, None, E)
        _PREFIX[0] = ""
        sems = S.create_sems(nc, ctx)
        block = ctx.enter_context(nc.Block())
        S.emit(block, sems)
    return nc


def build_l2_only():
    nc = bass.Bass("TRN2", target_bir_lowering=False)
    E = {}
    E["u_my"] = nc.dram_tensor("u", [DI, TC], BF16, kind="ExternalInput")
    E["xdbl_ag"] = nc.dram_tensor("xdbl", [NC * 96, TC], F32, kind="ExternalInput")
    E["wdt"] = nc.dram_tensor("wdt", [DTR, DDI], F32, kind="ExternalInput")
    E["bdt"] = nc.dram_tensor("bdt", [128, DDI // 128], F32, kind="ExternalInput")
    E["alog"] = nc.dram_tensor("alog", [DDI, DS], F32, kind="ExternalInput")
    E["ones_in"] = nc.dram_tensor("ones", [128, 128], BF16, kind="ExternalInput")
    E["ident_in"] = nc.dram_tensor("ident", [128, 128], BF16, kind="ExternalInput")
    E["y_scr"] = nc.dram_tensor("y", [DI, TC], BF16, kind="ExternalOutput")
    with ExitStack() as ctx:
        S = Sched(nc)
        _PREFIX[0] = "b_"
        stage_l2(nc, S, None, E)
        _PREFIX[0] = ""
        sems = S.create_sems(nc, ctx)
        block = ctx.enter_context(nc.Block())
        S.emit(block, sems)
    return nc


RG = [[0, 1, 2, 3, 4, 5, 6, 7]]


def build_full():
    nc = bass.Bass("TRN2", target_bir_lowering=False)
    E = {}
    E["x"] = nc.dram_tensor("x", [D, TC + 4], F32, kind="ExternalInput")
    E["x_nohalo"] = E["x"]
    E["w_in"] = nc.dram_tensor("w_in", [D, 2 * DI], F32, kind="ExternalInput")
    E["w_xp"] = nc.dram_tensor("w_xp", [DI, 96], F32, kind="ExternalInput")
    E["conv_w"] = nc.dram_tensor("conv_w", [128, (DI // 128) * DCONV], F32, kind="ExternalInput")
    E["conv_b"] = nc.dram_tensor("conv_b", [128, DI // 128], F32, kind="ExternalInput")
    E["wdt"] = nc.dram_tensor("wdt", [DTR, DDI], F32, kind="ExternalInput")
    E["bdt"] = nc.dram_tensor("bdt", [128, DDI // 128], F32, kind="ExternalInput")
    E["alog"] = nc.dram_tensor("alog", [DDI, DS], F32, kind="ExternalInput")
    E["dskip"] = nc.dram_tensor("dskip", [128, DI // 128], F32, kind="ExternalInput")
    E["wout"] = nc.dram_tensor("wout", [DI, D], F32, kind="ExternalInput")
    E["router"] = nc.dram_tensor("router", [D, NE], F32, kind="ExternalInput")
    E["weg"] = nc.dram_tensor("weg", [NE * D, DFF], F32, kind="ExternalInput")
    E["weu"] = nc.dram_tensor("weu", [NE * D, DFF], F32, kind="ExternalInput")
    E["wed"] = nc.dram_tensor("wed", [NE * DFF, D], F32, kind="ExternalInput")
    E["wsg"] = nc.dram_tensor("wsg", [D, D], F32, kind="ExternalInput")
    E["wsu"] = nc.dram_tensor("wsu", [D, D], F32, kind="ExternalInput")
    E["wsd"] = nc.dram_tensor("wsd", [D, D], F32, kind="ExternalInput")
    E["ones_in"] = nc.dram_tensor("ones", [128, 128], BF16, kind="ExternalInput")
    E["ident_in"] = nc.dram_tensor("ident", [128, 128], BF16, kind="ExternalInput")
    E["out"] = nc.dram_tensor("out", [D, TC], F32, kind="ExternalOutput")
    E["u_scr"] = nc.dram_tensor("u_scr", [DI, TC], BF16, kind="Internal")
    E["zs_scr"] = nc.dram_tensor("zs_scr", [DI, TC], BF16, kind="Internal")
    E["xdbl_scr"] = nc.dram_tensor("xdbl_scr", [96, TC], F32, kind="Internal")
    E["u_ag"] = nc.dram_tensor("u_ag", [NC * DI, TC], BF16, kind="Internal")
    E["cofs_in"] = nc.dram_tensor("cofs", [1, 1], mybir.dt.int32, kind="ExternalInput")
    E["xdbl_ag"] = nc.dram_tensor("xdbl_ag", [NC * 96, TC], F32, kind="Internal")
    E["y_scr"] = nc.dram_tensor("y_scr", [DI, TC], BF16, kind="Internal")
    E["y_ag"] = nc.dram_tensor("y_ag", [NC * DI, TC], BF16, kind="Internal")
    E["u_my"] = nc.dram_tensor("u_my", [DI, TC], BF16, kind="Internal")
    E["y_my"] = nc.dram_tensor("y_my", [DI, TC], BF16, kind="Internal")
    E["x2_scr"] = nc.dram_tensor("x2_scr", [D, TC], F32, kind="Internal")
    E["xn3_scr"] = nc.dram_tensor("xn3_scr", [D, TC], BF16, kind="Internal")
    E["coef_scr"] = nc.dram_tensor("coef_scr", [1, NE * TC], BF16, kind="Internal")
    E["ffn_scr"] = nc.dram_tensor("ffn_scr", [D, TC], BF16, kind="Internal")

    with ExitStack() as ctx:
        sems = None
        S = Sched(nc)
        cofs_sb = ctx.enter_context(nc.sbuf_tensor("cofs_sb", [1, 1], mybir.dt.int32))
        cofs_buf = Buf("cofs")
        E["cofs"] = {}

        def _cofs_setup(e, E=E, cofs_sb=cofs_sb):
            reg = e.alloc_register("cofs_reg")
            inst = e.reg_load(reg, cofs_sb[0:1, 0:1])
            E["cofs"]["v"] = e.snap(reg)
            return inst

        S.dma("pool", lambda e: e.dma_start(out=cofs_sb[0:1, 0:1],
                                            in_=E["cofs_in"].ap()),
              writes=[cofs_buf], key="wloadp")
        S.op("pool", _cofs_setup, reads=[cofs_buf])
        _PREFIX[0] = "a_"
        stage_l1(nc, S, sems, E)
        S.barrier()
        b_uag, b_umy = Buf("u_ag"), Buf("u_my")
        S.dma("pool", lambda e: e.collective_compute(
            "AllGather", ALU.bypass, replica_groups=RG,
            ins=[E["xdbl_scr"].ap()], outs=[E["xdbl_ag"].ap()]), key="coll")
        S.dma("pool", lambda e: e.collective_compute(
            "AllGather", ALU.bypass, replica_groups=RG,
            ins=[E["u_scr"].ap()], outs=[E["u_ag"].ap()]), key="coll",
            writes=[b_uag])
        for rr in range(NC):
            S.dma("pool", lambda e, rr=rr: e.dma_start(
                out=E["u_my"].ap()[rr * DDI:(rr + 1) * DDI, :],
                in_=E["u_ag"].ap()[bass.ds(E["cofs"]["v"] + rr * DI, DDI), :]),
                key="uld", reads=[b_uag], writes=[b_umy])
        S.barrier()
        _PREFIX[0] = "b_"
        stage_l2(nc, S, sems, E)
        S.barrier()
        b_yag, b_ymy = Buf("y_ag"), Buf("y_my")
        S.dma("pool", lambda e: e.collective_compute(
            "AllGather", ALU.bypass, replica_groups=RG,
            ins=[E["y_scr"].ap()], outs=[E["y_ag"].ap()]), key="coll",
            writes=[b_yag])
        for j in range(NC):
            S.dma("pool", lambda e, j=j: e.dma_start(
                out=E["y_my"].ap()[j * DDI:(j + 1) * DDI, :],
                in_=E["y_ag"].ap()[bass.ds(E["cofs"]["v"] + j * DI, DDI), :]),
                key="yld", reads=[b_yag], writes=[b_ymy])
        S.barrier()
        _PREFIX[0] = "c_"
        stage_l3a(nc, S, sems, E)
        S.barrier()
        _PREFIX[0] = "d_"
        stage_l3b(nc, S, sems, E)
        _PREFIX[0] = ""
        sems = S.create_sems(nc, ctx)
        block = ctx.enter_context(nc.Block())
        S.emit(block, sems)
    return nc


_FULL = [None, None, None]


def _prep_common(inputs):
    p = {}
    p["w_in"] = np.ascontiguousarray(_np(inputs["L0_attn_W_in"]), np.float32)
    p["w_xp"] = np.ascontiguousarray(_np(inputs["L0_attn_W_xproj"]), np.float32)
    p["cw"] = np.ascontiguousarray(_np(inputs["L0_attn_conv_w"]).reshape(DI // 128, 128, DCONV)
                                   .transpose(1, 0, 2).reshape(128, -1), np.float32)
    p["cb"] = np.ascontiguousarray(_np(inputs["L0_attn_conv_b"]).reshape(DI // 128, 128).T,
                                   np.float32)
    p["wdt"] = np.asarray(inputs["L0_attn_W_dt"], np.float32)
    p["bdt"] = np.asarray(inputs["L0_attn_b_dt"], np.float32)
    p["alog"] = np.asarray(inputs["L0_attn_A_log"], np.float32)
    p["dskip"] = np.ascontiguousarray(
        np.asarray(inputs["L0_attn_D_skip"], np.float32).reshape(DI // 128, 128).T)
    p["wout"] = np.ascontiguousarray(_np(inputs["L0_attn_W_out"]), np.float32)
    p["router"] = np.ascontiguousarray(_np(inputs["L0_ffn_router"]), np.float32)
    p["weg"] = np.ascontiguousarray(_np(inputs["L0_ffn_experts_gate"]).reshape(NE * D, DFF), np.float32)
    p["weu"] = np.ascontiguousarray(_np(inputs["L0_ffn_experts_up"]).reshape(NE * D, DFF), np.float32)
    p["wed"] = np.ascontiguousarray(_np(inputs["L0_ffn_experts_down"]).reshape(NE * DFF, D), np.float32)
    p["wsg"] = np.ascontiguousarray(_np(inputs["L0_ffn_shared_gate"]), np.float32)
    p["wsu"] = np.ascontiguousarray(_np(inputs["L0_ffn_shared_up"]), np.float32)
    p["wsd"] = np.ascontiguousarray(_np(inputs["L0_ffn_shared_down"]), np.float32)
    p["ones"] = np.ones((128, 128), BF)
    p["ident"] = np.eye(128).astype(BF)
    return p


def kernel(**inputs):
    if _FULL[0] is None:
        _FULL[0] = build_l1_only()
        _FULL[1] = build_l2_only()
        _FULL[2] = build_l3()
    p = _prep_common(inputs)
    x = _np(inputs["L0_attn_x"])
    xs_all = []
    for c in range(NC):
        b, s = c // 4, (c % 4) * TC
        xs = np.zeros((D, TC + 4), np.float32)
        lo = max(0, s - 4)
        xs[:, 4 - (s - lo):] = x[b, lo:s + TC, :].T
        xs_all.append(np.ascontiguousarray(xs))
    in1 = [dict(x=xs_all[c], w_in=p["w_in"], w_xp=p["w_xp"], conv_w=p["cw"],
                conv_b=p["cb"], ones=p["ones"]) for c in range(NC)]
    r1 = run_bass_kernel_spmd(_FULL[0], in1, core_ids=list(range(NC)))
    u = np.concatenate([np.asarray(r["u"]) for r in r1.results], axis=1)
    zs = [np.asarray(r["zs"]) for r in r1.results]
    xdbl = np.ascontiguousarray(
        np.concatenate([np.asarray(r["xdbl"]) for r in r1.results], axis=0))
    in2 = []
    for c in range(NC):
        dsl = slice(c * DDI, (c + 1) * DDI)
        u_my = np.concatenate([u[dsl][:, k * TC:(k + 1) * TC] for k in range(NC)],
                              axis=0)
        in2.append(dict(
            u=np.ascontiguousarray(u_my), xdbl=xdbl,
            wdt=np.ascontiguousarray(p["wdt"][:, dsl]),
            bdt=np.ascontiguousarray(p["bdt"][dsl].reshape(DDI // 128, 128).T),
            alog=np.ascontiguousarray(p["alog"][dsl]),
            ones=p["ones"], ident=p["ident"]))
    r2 = run_bass_kernel_spmd(_FULL[1], in2, core_ids=list(range(NC)))
    yfull = np.concatenate([np.asarray(r["y"]) for r in r2.results], axis=0)
    in3 = []
    for c in range(NC):
        yv = np.concatenate([yfull[k * DI + c * DDI:k * DI + (c + 1) * DDI]
                             for k in range(NC)], axis=0)
        tsl = slice(c * TC, (c + 1) * TC)
        in3.append(dict(
            y=np.ascontiguousarray(yv),
            u=np.ascontiguousarray(u[:, tsl]), zs=zs[c], x=xs_all[c],
            dskip=p["dskip"], wout=p["wout"], router=p["router"],
            weg=p["weg"], weu=p["weu"], wed=p["wed"], wsg=p["wsg"],
            wsu=p["wsu"], wsd=p["wsd"], ones=p["ones"], ident=p["ident"]))
    r3 = run_bass_kernel_spmd(_FULL[2], in3, core_ids=list(range(NC)))
    out_fm = np.concatenate([np.asarray(r["out"]) for r in r3.results], axis=1)
    return np.ascontiguousarray(out_fm.T.reshape(B, L, D)).astype(np.float32)
